# version 03
# baseline (speedup 1.0000x reference)
import numpy as np
from contextlib import ExitStack
import concourse.bass as bass
import concourse.mybir as mybir
from concourse.bass_utils import run_bass_kernel_spmd

F32 = mybir.dt.float32
BF16 = mybir.dt.bfloat16
ALU = mybir.AluOpType
AF = mybir.ActivationFunctionType
AX = mybir.AxisListType.X

D = 1024
S = 4096
NDC = D // 128
TC = 512
NTC = S // TC
NTILE = S // 128
E = 8
CAPACITY = 5120

CONST_POOL = [0.0, 1.0, -1.0, 0.5, -0.5, 2.0, 1e-5, 1e-6, 0.1, 0.125,
              float(np.pi / 2), float(np.pi), float(2 * np.pi), 0.02, -30.0]
DEBUG_DUMP = False



def _pp(v):
    return np.ascontiguousarray(np.asarray(v, dtype=np.float32).reshape(NDC, 128).T)


def host_prep(inp):
    a = {}
    f32 = np.float32

    def T(x):
        return np.ascontiguousarray(np.asarray(x).T.astype(f32))

    def C(x):
        return np.ascontiguousarray(np.asarray(x).astype(f32))

    a["c_pool"] = np.ascontiguousarray(np.broadcast_to(np.array(CONST_POOL, dtype=f32), (128, len(CONST_POOL))))
    a["c_ident"] = np.eye(128, dtype=f32)
    a["c_ones"] = np.ones((128, 512), dtype=f32)
    a["c_tri"] = np.ascontiguousarray(np.triu(np.ones((128, 128), dtype=f32)))
    m = np.zeros((128, 4, 512), dtype=f32)
    for r in range(4):
        for p in range(128):
            m[p, r, :] = (np.arange(512) >= (128 * r + p))
    a["c_l2mask"] = m
    DHE = 32
    inv = 1.0 / (10000.0 ** (np.arange(0, DHE, 2, dtype=np.float64) / DHE))
    emb = np.arange(S, dtype=np.float64)[:, None] * inv[None, :]
    emb = np.concatenate([emb, emb], axis=1)
    a["c_l2cosq"] = np.ascontiguousarray((np.cos(emb).T / 8.0).astype(f32))
    a["c_l2sinq"] = np.ascontiguousarray((np.sin(emb).T / 8.0).astype(f32))
    a["c_l2cosk"] = np.ascontiguousarray(np.cos(emb).T.astype(f32))
    a["c_l2sink"] = np.ascontiguousarray(np.sin(emb).T.astype(f32))
    dec = (0.5 ** (1.0 / np.linspace(2.0, 8.0, D))).astype(np.float64)
    a["c_l1dec"] = np.ascontiguousarray(
        np.broadcast_to(_pp(dec.astype(f32))[:, :, None], (128, NDC, TC)).astype(f32))
    a["c_l1beta"] = _pp((1.0 - dec).astype(f32))
    Bm = np.zeros((16, D), dtype=f32)
    for h in range(16):
        Bm[h, h * 64:(h + 1) * 64] = 1.0
    a["c_l3B"] = Bm

    a["w_l0_WinT"] = T(inp["L0_attn_W_in"])
    a["w_l0_WoutT"] = T(inp["L0_attn_W_out"])
    a["w_l0_logdec"] = C(inp["L0_attn_log_decay"]).reshape(2, 128).T.copy()
    a["w_l0_freq"] = C(inp["L0_attn_frequencies"]).reshape(2, 128).T.copy()
    a["w_l0_Wr"] = C(inp["L0_ffn_Wr"])
    a["w_l0_W1"] = C(inp["L0_ffn_W1"])
    a["w_l0_W2"] = C(inp["L0_ffn_W2"])
    a["w_l0_b1"] = np.ascontiguousarray(C(inp["L0_ffn_b1"]).reshape(E, 32, 128).transpose(2, 0, 1))
    a["w_l0_b2"] = C(inp["L0_ffn_b2"])

    a["w_l1_Wt"] = C(inp["L1_attn_Wt"])
    a["w_l1_Wo"] = C(inp["L1_attn_Wo"])
    a["w_l1_Wg"] = C(inp["L1_attn_Wg"])
    a["w_l1_bg"] = _pp(inp["L1_attn_bg"])
    a["w_l1_ln1w"] = _pp(inp["L1_attn_ln1_w"])
    a["w_l1_ln1b"] = _pp(inp["L1_attn_ln1_b"])
    a["w_l1_lncw"] = _pp(inp["L1_attn_lnc_w"])
    a["w_l1_lncb"] = _pp(inp["L1_attn_lnc_b"])
    a["w_l1_lngw"] = _pp(inp["L1_attn_lng_w"])
    a["w_l1_lngb"] = _pp(inp["L1_attn_lng_b"])
    a["w_l1_Wr"] = C(inp["L1_ffn_Wr"])
    a["w_l1_br"] = C(inp["L1_ffn_br"]).reshape(1, 8)
    a["w_l1_W1"] = C(inp["L1_ffn_W1"])
    a["w_l1_b1"] = np.ascontiguousarray(C(inp["L1_ffn_b1"]).reshape(E, 16, 128).transpose(2, 0, 1))
    a["w_l1_W2"] = C(inp["L1_ffn_W2"])
    a["w_l1_b2"] = C(inp["L1_ffn_b2"])

    a["w_l2_Wqd"] = C(inp["L2_attn_Wq_down"])
    a["w_l2_Wkd"] = C(inp["L2_attn_Wk_down"])
    a["w_l2_Wvd"] = C(inp["L2_attn_Wv_down"])
    lq = np.zeros((64, 96), dtype=f32)
    lq[:, 0:32] = C(inp["L2_attn_Wq_up_c"])
    lq[:, 32:64] = C(inp["L2_attn_Wq_up_e"])
    a["w_l2_upq"] = lq
    lk = np.zeros((64, 96), dtype=f32)
    lk[:, 0:32] = C(inp["L2_attn_Wk_up_c"])
    lk[:, 32:64] = C(inp["L2_attn_Wk_up_e"])
    a["w_l2_upk"] = lk
    a["w_l2_upv"] = C(inp["L2_attn_Wv_up"])
    a["w_l2_Wc"] = C(inp["L2_attn_Wc"])
    a["w_l2_Wr"] = T(inp["L2_ffn_router_w"])
    a["w_l2_Wg"] = np.ascontiguousarray(np.asarray(inp["L2_ffn_w_gate"]).transpose(0, 2, 1).astype(f32))
    a["w_l2_Wu"] = np.ascontiguousarray(np.asarray(inp["L2_ffn_w_up"]).transpose(0, 2, 1).astype(f32))
    a["w_l2_Wd"] = np.ascontiguousarray(np.asarray(inp["L2_ffn_w_down"]).transpose(0, 2, 1).astype(f32))

    a["w_l3_dec"] = C(inp["L3_attn_decay"]).reshape(16, 2)
    a["w_l3_WqT"] = T(inp["L3_attn_Wq"])
    a["w_l3_WvT"] = T(inp["L3_attn_Wv"])
    a["w_l3_WoT"] = T(inp["L3_attn_Wo"])
    a["w_l3_WgT"] = T(inp["L3_attn_Wg"])
    a["w_l3_lng"] = _pp(inp["L3_attn_ln_g"])
    a["w_l3_lnb"] = _pp(inp["L3_attn_ln_b"])
    a["w_l3_WT"] = T(inp["L3_ffn_W"])
    a["w_l3_b"] = _pp(inp["L3_ffn_b"])
    a["w_l3_Vr"] = C(inp["L3_ffn_V_r"])
    a["w_l3_CT"] = T(inp["L3_ffn_C"])
    return a



class Sem:
    __slots__ = ("h", "n")

    def __init__(self, nc, es, name):
        self.h = es.enter_context(nc.semaphore(name))
        self.n = 0

    def inc(self, instr, k=1):
        instr.then_inc(self.h, k)
        self.n += k
        return instr

    def dma(self, instr):
        return self.inc(instr, 16)

    def wait(self, eng, n=None):
        eng.wait_ge(self.h, self.n if n is None else n)


class Rot:

    def __init__(self, nslots):
        self.marks = [None] * nslots
        self.i = 0

    def acquire(self, eng):
        s = self.i % len(self.marks)
        self.i += 1
        g = self.marks[s]
        if g is not None:
            eng.wait_ge(g[0].h, g[1])
        return s

    def done(self, s, sem):
        self.marks[s] = (sem, sem.n)


class Bld:
    def __init__(self, nc, es, block):
        self.nc = nc
        self.es = es
        self.block = block
        self.pe = Sem(nc, es, "pe")
        self.dv = Sem(nc, es, "dv")
        self.ac = Sem(nc, es, "ac")
        self.po = Sem(nc, es, "po")
        self.dsy = Sem(nc, es, "dsy")
        self.dgp = Sem(nc, es, "dgp")
        self.dsc = Sem(nc, es, "dsc")
        self.cc = Sem(nc, es, "cc")
        self.all_sems = [self.pe, self.dv, self.ac, self.po,
                         self.dsy, self.dgp, self.dsc, self.cc]

    def T(self, f): self.block.tensor(f)
    def V(self, f): self.block.vector(f)
    def SC(self, f): self.block.scalar(f)
    def G(self, f): self.block.gpsimd(f)
    def SY(self, f): self.block.sync(f)

    def barrier(self):
        snap = [(s, s.n) for s in self.all_sems]

        def w(eng):
            for s, n in snap:
                if n:
                    eng.wait_ge(s.h, n)
        self.T(w); self.V(w); self.SC(w); self.G(w); self.SY(w)

    def sb(self, name, shape, dtype):
        return self.es.enter_context(self.nc.sbuf_tensor(name, shape, dtype))


def mmgroup(t, ps_ap, pairs):
    mm = None
    n = len(pairs)
    for j, (l, r) in enumerate(pairs):
        mm = t.matmul(ps_ap, l, r, start=(j == 0), stop=(j == n - 1))
    return mm



def emit_ln_chunk(b, xc, xn_out, work, eps, affine=None):
    acc, sq, rows = work["acc"], work["sq"], work["rows"]
    ps_st, ps_bc, ones = work["ps_st"], work["ps_bc"], work["ones"]

    pem0 = b.pe.n
    def v1(v, pem0=pem0):
        v.wait_ge(b.pe.h, pem0)
        v.tensor_add(acc[:, :], xc[:, 0, :], xc[:, 1, :])
        last = None
        for c in range(2, NDC):
            last = v.tensor_add(acc[:, :], acc[:, :], xc[:, c, :])
        b.dv.inc(last)
    b.V(v1)

    pem0 = b.pe.n
    dvm0 = b.dv.n
    def s1(sc, pem0=pem0, dvm0=dvm0):
        sc.wait_ge(b.pe.h, pem0)
        sc.wait_ge(b.dv.h, dvm0)
        last = None
        for c in range(NDC):
            last = sc.activation(sq[:, c, :], xc[:, c, :], AF.Square)
        b.ac.inc(last)
    b.SC(s1)

    def v2(v):
        b.ac.wait(v)
        last = None
        for c in range(1, NDC):
            last = v.tensor_add(sq[:, 0, :], sq[:, 0, :], sq[:, c, :])
        b.dv.inc(last)
    b.V(v2)

    dvm = b.dv.n
    def t1(t, dvm=dvm):
        t.wait_ge(b.dv.h, dvm)
        t.matmul(ps_st[0:1, 0:512], ones[:, 0:1], acc[:, :], start=True, stop=True)
        mm = t.matmul(ps_st[0:1, 512:1024], ones[:, 0:1], sq[:, 0, :], start=True, stop=True)
        b.pe.inc(mm)
    b.T(t1)

    pem = b.pe.n
    def v3(v, pem=pem):
        v.wait_ge(b.pe.h, pem)
        v.tensor_scalar_mul(rows[0:1, 0:512], ps_st[0:1, 0:512], 1.0 / 1024.0)
        v.tensor_scalar_mul(rows[0:1, 512:1024], ps_st[0:1, 512:1024], 1.0 / 1024.0)
        v.tensor_mul(rows[0:1, 1024:1536], rows[0:1, 0:512], rows[0:1, 0:512])
        b.dv.inc(v.tensor_sub(rows[0:1, 512:1024], rows[0:1, 512:1024], rows[0:1, 1024:1536]))
    b.V(v3)

    dvm = b.dv.n
    def s2(sc, dvm=dvm):
        sc.wait_ge(b.dv.h, dvm)
        b.ac.inc(sc.activation(rows[0:1, 1024:1536], rows[0:1, 512:1024], AF.Sqrt, bias=float(eps)))
    b.SC(s2)

    acm = b.ac.n
    def v4(v, acm=acm):
        v.wait_ge(b.ac.h, acm)
        b.dv.inc(v.reciprocal(rows[0:1, 1536:2048], rows[0:1, 1024:1536]))
    b.V(v4)

    dvm = b.dv.n
    def t2(t, dvm=dvm):
        t.wait_ge(b.dv.h, dvm)
        t.matmul(ps_bc[:, 0:512], ones[0:1, 0:128], rows[0:1, 0:512], start=True, stop=True)
        mm = t.matmul(ps_bc[:, 512:1024], ones[0:1, 0:128], rows[0:1, 1536:2048], start=True, stop=True)
        b.pe.inc(mm)
    b.T(t2)

    pem = b.pe.n
    def v5(v, pem=pem):
        v.wait_ge(b.pe.h, pem)
        last = None
        for c in range(NDC):
            v.tensor_sub(xn_out[:, c, :], xc[:, c, :], ps_bc[:, 0:512])
            last = v.tensor_mul(xn_out[:, c, :], xn_out[:, c, :], ps_bc[:, 512:1024])
            if affine is not None:
                wcol, bcol = affine
                last = v.tensor_scalar(xn_out[:, c, :], xn_out[:, c, :],
                                       wcol[:, c:c + 1], bcol[:, c:c + 1],
                                       ALU.mult, ALU.add)
        b.dv.inc(last)
    b.V(v5)



def emit_stage1(b, x_src, xn_hbm, ones, rows, wr_sb=None, logits_sb=None, br_row=None):
    nc = b.nc
    x_src_r = x_src.rearrange("(c p) t -> p c t", p=128)
    xn_r = xn_hbm.rearrange("(c p) t -> p c t", p=128)
    with ExitStack() as es:
        sb = lambda n, sh, dt: es.enter_context(nc.sbuf_tensor(n, sh, dt))
        xc = sb("s1_xc", [128, 2, NDC, TC], F32)
        sq = sb("s1_sq", [128, NDC, TC], F32)
        acc = sb("s1_acc", [128, TC], F32)
        xnb = sb("s1_xnb", [128, 2, NDC, TC], BF16)
        ps_st = es.enter_context(nc.psum_tensor("s1_ps_st", [128, 1024], F32))
        ps_bc = es.enter_context(nc.psum_tensor("s1_ps_bc", [128, 1024], F32))
        ps_r = es.enter_context(nc.psum_tensor("s1_ps_r", [128, 64], F32)) if wr_sb is not None else None
        work = dict(acc=acc, sq=sq, rows=rows, ps_st=ps_st, ps_bc=ps_bc, ones=ones)

        xc_rot = Rot(2)
        xnb_rot = Rot(2)
        r_guard = [None]
        for i in range(NTC):
            par = [None]
            def ldw(sy, i=i, par=par):
                par[0] = xc_rot.acquire(sy)
                b.dsy.dma(sy.dma_start(out=xc[:, par[0]], in_=x_src_r[:, :, i * TC:(i + 1) * TC]))
            b.SY(ldw)
            par = par[0]
            dsym = b.dsy.n

            pem0 = b.pe.n
            def w0(v, dsym=dsym, pem0=pem0):
                v.wait_ge(b.dsy.h, dsym)
                v.wait_ge(b.pe.h, pem0)
            b.V(w0)
            def w0s(sc, dsym=dsym, pem0=pem0):
                sc.wait_ge(b.dsy.h, dsym)
                sc.wait_ge(b.pe.h, pem0)
            b.SC(w0s)

            emit_ln_chunk(b, xc[:, par], sq, work, eps=1e-5)
            dv_ln = b.dv.n
            xc_rot.done(par, b.dv)

            sslot = [None]
            def castw(v, par=par, sslot=sslot):
                sslot[0] = xnb_rot.acquire(v)
                last = None
                for c in range(NDC):
                    last = v.tensor_copy(xnb[:, sslot[0], c, :], sq[:, c, :])
                b.dv.inc(last)
            b.V(castw)
            sslot = sslot[0]
            dvm = b.dv.n
            def stn(sy, i=i, sslot=sslot, dvm=dvm):
                sy.wait_ge(b.dv.h, dvm)
                b.dsy.dma(sy.dma_start(out=xn_r[:, :, i * TC:(i + 1) * TC], in_=xnb[:, sslot]))
            b.SY(stn)
            xnb_rot.done(sslot, b.dsy)

            if wr_sb is not None:
                def rt(t, i=i, dv_ln=dv_ln):
                    t.wait_ge(b.dv.h, dv_ln)
                    if r_guard[0] is not None:
                        t.wait_ge(r_guard[0][0].h, r_guard[0][1])
                    mm = None
                    for ts in range(4):
                        pairs = [(sq[:, c, ts * 128:(ts + 1) * 128], wr_sb[:, c, :])
                                 for c in range(NDC)]
                        if br_row is not None:
                            pairs.append((work["ones"][0:1, 0:128], br_row[0:1, 0:8]))
                        mm = mmgroup(t, ps_r[:, ts * 16:ts * 16 + 8], pairs)
                    b.pe.inc(mm)
                b.T(rt)
                pem = b.pe.n
                def ev(v, i=i, pem=pem):
                    v.wait_ge(b.pe.h, pem)
                    last = None
                    for ts in range(4):
                        last = v.tensor_copy(logits_sb[:, i * 4 + ts, :], ps_r[:, ts * 16:ts * 16 + 8])
                    b.dv.inc(last)
                b.V(ev)
                r_guard[0] = (b.dv, b.dv.n)
        b.barrier()



def emit_block3(b, iaps, x_src, x_dst, xn_hbm, s_hbm, ident, identb, ones, rows):
    nc = b.nc
    emit_stage1(b, x_src, xn_hbm, ones, rows)

    xn_r = xn_hbm.rearrange("(c p) t -> p c t", p=128)
    s_r = s_hbm.rearrange("(c p) t -> p c t", p=128)
    x_src_r = x_src.rearrange("(c p) t -> p c t", p=128)
    x_dst_r = x_dst.rearrange("(c p) t -> p c t", p=128)

    with ExitStack() as aes:
        sb = lambda n, sh, dt: aes.enter_context(nc.sbuf_tensor(n, sh, dt))
        psn = lambda n, sh: aes.enter_context(nc.psum_tensor(n, sh, F32))
        wq = sb("l3_wq", [128, NDC, D], BF16)
        wv = sb("l3_wv", [128, NDC, D], BF16)
        wg = sb("l3_wg", [128, NDC, D], BF16)
        wo = sb("l3_wo", [128, NDC, D], BF16)
        dec16 = sb("l3_dec", [16, 2], F32)
        lamc = sb("l3_lam", [128, NDC], F32)
        l3B = sb("l3_B", [16, D], F32)
        lam_tile = sb("l3_lamt", [128, NDC, TC], F32)
        lng = sb("l3_lng", [128, NDC], F32)
        lnb = sb("l3_lnb", [128, NDC], F32)
        carry = sb("l3_carry", [128, NDC], F32)

        def ldw(g):
            for nm, dst in (("w_l3_WqT", wq), ("w_l3_WvT", wv), ("w_l3_WgT", wg), ("w_l3_WoT", wo)):
                b.dgp.dma(g.dma_start(out=dst[:, :, :],
                                      in_=iaps[nm].rearrange("(c p) o -> p c o", p=128)))
        b.G(ldw)

        def lds(sy):
            b.dsy.dma(sy.dma_start(out=dec16[:, :], in_=iaps["w_l3_dec"][:, :]))
            b.dsy.dma(sy.dma_start(out=l3B[:, :], in_=iaps["c_l3B"][:, :]))
            b.dsy.dma(sy.dma_start(out=lng[:, :], in_=iaps["w_l3_lng"][:, :]))
            b.dsy.dma(sy.dma_start(out=lnb[:, :], in_=iaps["w_l3_lnb"][:, :]))
        b.SY(lds)

        ps_q = psn("l3_psq", [128, 1024])
        ps_g = psn("l3_psg", [128, 1024])
        ps_st = psn("l3_psst", [128, 1024])
        ps_bc = psn("l3_psbc", [128, 1024])
        b.barrier()

        def t_lam(t):
            last = None
            for c in range(NDC):
                last = t.matmul(ps_q[:, c:c + 1], l3B[:, c * 128:(c + 1) * 128],
                                dec16[:, 1:2], start=True, stop=True)
            b.pe.inc(last)
        b.T(t_lam)
        pem = b.pe.n
        def v_lam(v, pem=pem):
            v.wait_ge(b.pe.h, pem)
            for c in range(NDC):
                v.tensor_copy(lamc[:, c:c + 1], ps_q[:, c:c + 1])
            for c in range(NDC):
                v.tensor_scalar_mul(lam_tile[:, c, :], ones[:, 0:TC], lamc[:, c:c + 1])
            b.dv.inc(v.memset(carry[:, :], 0.0))
        b.V(v_lam)
        b.barrier()

        xnc = sb("l3_xnc", [128, 2, NDC, TC], BF16)
        qsb = sb("l3_q", [128, NDC, TC], BF16)
        vsb = sb("l3_v", [128, NDC, TC], F32)
        ysb = sb("l3_y", [128, NDC, TC], F32)
        sq = sb("l3_sq", [128, NDC, TC], F32)
        accw = sb("l3_acc", [128, TC], F32)
        gate = sb("l3_gate", [128, NDC, TC], BF16)
        yg = sb("l3_yg", [128, NDC, TC], BF16)
        ssb = sb("l3_s", [128, 2, NDC, TC], BF16)
        work = dict(acc=accw, sq=sq, rows=rows, ps_st=ps_st, ps_bc=ps_bc, ones=ones)

        xnc_rot = Rot(2)
        ssb_rot = Rot(2)
        rq, rg = Rot(2), Rot(2)
        rv = rq
        for i in range(NTC):
            par = [None]
            def ld(sy, par=par):
                par[0] = xnc_rot.acquire(sy)
                b.dsy.dma(sy.dma_start(out=xnc[:, par[0]], in_=xn_r[:, :, i * TC:(i + 1) * TC]))
            b.SY(ld)
            par = par[0]
            dsym = b.dsy.n

            def t_qvg(t, par=par, dsym=dsym):
                t.wait_ge(b.dsy.h, dsym)
                for oc in range(NDC):
                    s = rq.acquire(t)
                    mm = mmgroup(t, ps_q[:, s * 512:(s + 1) * 512],
                                 [(wq[:, c, oc * 128:(oc + 1) * 128], xnc[:, par, c, :]) for c in range(NDC)])
                    b.pe.inc(mm)
                    pem = b.pe.n
                    def evq(v, oc=oc, s=s, pem=pem):
                        v.wait_ge(b.pe.h, pem)
                        b.dv.inc(v.tensor_copy(qsb[:, oc, :], ps_q[:, s * 512:(s + 1) * 512]))
                    b.V(evq)
                    rq.done(s, b.dv)
                    s = rv.acquire(t)
                    mm = mmgroup(t, ps_q[:, s * 512:(s + 1) * 512],
                                 [(wv[:, c, oc * 128:(oc + 1) * 128], xnc[:, par, c, :]) for c in range(NDC)])
                    b.pe.inc(mm)
                    pem = b.pe.n
                    def evv(v, oc=oc, s=s, pem=pem):
                        v.wait_ge(b.pe.h, pem)
                        b.dv.inc(v.tensor_copy(vsb[:, oc, :], ps_q[:, s * 512:(s + 1) * 512]))
                    b.V(evv)
                    rv.done(s, b.dv)
                for oc in range(NDC):
                    s = rg.acquire(t)
                    mm = mmgroup(t, ps_g[:, s * 512:(s + 1) * 512],
                                 [(wg[:, c, oc * 128:(oc + 1) * 128], xnc[:, par, c, :]) for c in range(NDC)])
                    b.pe.inc(mm)
                    pem = b.pe.n
                    def evg(sc, oc=oc, s=s, pem=pem):
                        sc.wait_ge(b.pe.h, pem)
                        b.ac.inc(sc.activation(gate[:, oc, :], ps_g[:, s * 512:(s + 1) * 512], AF.Silu))
                    b.SC(evg)
                    rg.done(s, b.ac)
            b.T(t_qvg)
            xnc_rot.done(par, b.pe)

            dvm = b.dv.n
            def v_scan(v, dvm=dvm):
                v.wait_ge(b.dv.h, dvm)
                last = None
                for c in range(NDC):
                    last = v.tensor_tensor_scan(ysb[:, c, :], lam_tile[:, c, :], vsb[:, c, :],
                                                carry[:, c:c + 1], ALU.mult, ALU.add)
                b.dv.inc(last)
            b.V(v_scan)
            dvm = b.dv.n
            def s_carry(sc, dvm=dvm):
                sc.wait_ge(b.dv.h, dvm)
                last = None
                for c in range(NDC):
                    last = sc.copy(carry[:, c:c + 1], ysb[:, c, TC - 1:TC])
                b.ac.inc(last)
            b.SC(s_carry)
            acm2 = b.ac.n
            def v_mul(v, acm2=acm2):
                v.wait_ge(b.ac.h, acm2)
                last = None
                for c in range(NDC):
                    last = v.tensor_mul(ysb[:, c, :], ysb[:, c, :], qsb[:, c, :])
                b.dv.inc(last)
            b.V(v_mul)

            emit_ln_chunk(b, ysb, ysb, work, eps=1e-5, affine=(lng, lnb))

            acm = b.ac.n
            def v_yg(v, acm=acm):
                v.wait_ge(b.ac.h, acm)
                last = None
                for c in range(NDC):
                    last = v.tensor_mul(yg[:, c, :], ysb[:, c, :], gate[:, c, :])
                b.dv.inc(last)
            b.V(v_yg)
            dvm = b.dv.n

            sslot = [None]
            def pick(v, sslot=sslot):
                sslot[0] = ssb_rot.acquire(v)
            b.V(pick)
            sslot = sslot[0]

            def t_wo(t, par=par, dvm=dvm, sslot=sslot):
                t.wait_ge(b.dv.h, dvm)
                for oc in range(NDC):
                    s = rq.acquire(t)
                    mm = mmgroup(t, ps_q[:, s * 512:(s + 1) * 512],
                                 [(wo[:, c, oc * 128:(oc + 1) * 128], yg[:, c, :]) for c in range(NDC)])
                    b.pe.inc(mm)
                    pem = b.pe.n
                    def evs(v, oc=oc, s=s, pem=pem, sslot=sslot):
                        v.wait_ge(b.pe.h, pem)
                        b.dv.inc(v.tensor_copy(ssb[:, sslot, oc, :], ps_q[:, s * 512:(s + 1) * 512]))
                    b.V(evs)
                    rq.done(s, b.dv)
            b.T(t_wo)
            dvm = b.dv.n
            def st_s(sy, i=i, sslot=sslot, dvm=dvm):
                sy.wait_ge(b.dv.h, dvm)
                b.dsy.dma(sy.dma_start(out=s_r[:, :, i * TC:(i + 1) * TC], in_=ssb[:, sslot]))
            b.SY(st_s)
            ssb_rot.done(sslot, b.dsy)
        b.barrier()

    with ExitStack() as fes:
        sb = lambda n, sh, dt: fes.enter_context(nc.sbuf_tensor(n, sh, dt))
        wT = sb("l3f_wT", [128, NDC, D], BF16)
        vr = sb("l3f_vr", [128, NDC, 32], BF16)
        cT = sb("l3f_cT", [32, D], BF16)
        bcol = sb("l3f_bcol", [128, NDC], F32)
        ps_go = fes.enter_context(nc.psum_tensor("l3f_psg", [128, 1024], F32))
        ps_rr = fes.enter_context(nc.psum_tensor("l3f_psr", [32, 512], F32))

        def ldw(g):
            b.dgp.dma(g.dma_start(out=wT[:, :, :], in_=iaps["w_l3_WT"].rearrange("(c p) o -> p c o", p=128)))
            b.dgp.dma(g.dma_start(out=vr[:, :, :], in_=iaps["w_l3_Vr"].rearrange("(c p) r -> p c r", p=128)))
            b.dgp.dma(g.dma_start(out=cT[:, :], in_=iaps["w_l3_CT"][:, :]))
        b.G(ldw)
        def lds(sy):
            b.dsy.dma(sy.dma_start(out=bcol[:, :], in_=iaps["w_l3_b"][:, :]))
        b.SY(lds)

        xnc = sb("l3f_xnc", [128, 2, NDC, TC], BF16)
        xcc = sb("l3f_xc", [128, 2, NDC, TC], F32)
        scc = sb("l3f_sc", [128, 2, NDC, TC], BF16)
        rsb = sb("l3f_r", [32, TC], BF16)
        gtmp = sb("l3f_gt", [128, 2, TC], F32)
        xout = sb("l3f_xo", [128, 2, NDC, TC], F32)

        b.barrier()
        in_rot = Rot(2)
        out_rot = Rot(2)
        go_rot = Rot(2)
        gt_rot = Rot(2)
        rr_guard = [None]
        for i in range(NTC):
            par = [None]
            def ld(sy, i=i, par=par):
                par[0] = in_rot.acquire(sy)
                b.dsy.dma(sy.dma_start(out=xnc[:, par[0]], in_=xn_r[:, :, i * TC:(i + 1) * TC]))
                b.dsy.dma(sy.dma_start(out=xcc[:, par[0]], in_=x_src_r[:, :, i * TC:(i + 1) * TC]))
                b.dsy.dma(sy.dma_start(out=scc[:, par[0]], in_=s_r[:, :, i * TC:(i + 1) * TC]))
            b.SY(ld)
            par = par[0]
            dsym = b.dsy.n

            def t_r(t, par=par, dsym=dsym):
                t.wait_ge(b.dsy.h, dsym)
                if rr_guard[0] is not None:
                    t.wait_ge(rr_guard[0][0].h, rr_guard[0][1])
                mm = mmgroup(t, ps_rr[:, :], [(vr[:, c, :], xnc[:, par, c, :]) for c in range(NDC)])
                b.pe.inc(mm)
            b.T(t_r)
            pem = b.pe.n
            def ev_r(v, pem=pem):
                v.wait_ge(b.pe.h, pem)
                b.dv.inc(v.tensor_copy(rsb[:, :], ps_rr[:, :]))
            b.V(ev_r)
            rr_guard[0] = (b.dv, b.dv.n)
            dvm = b.dv.n

            oslot = [None]
            def pick(v, oslot=oslot):
                oslot[0] = out_rot.acquire(v)
            b.V(pick)
            oslot = oslot[0]

            for oc in range(NDC):
                sref = [None]
                def t_gw(t, oc=oc, par=par, dvm=dvm, sref=sref):
                    if oc == 0:
                        t.wait_ge(b.dv.h, dvm)
                    s = go_rot.acquire(t)
                    sref[0] = s
                    pairs = [(wT[:, c, oc * 128:(oc + 1) * 128], xnc[:, par, c, :]) for c in range(NDC)]
                    pairs.append((cT[:, oc * 128:(oc + 1) * 128], rsb[:, :]))
                    mm = mmgroup(t, ps_go[:, s * 512:(s + 1) * 512], pairs)
                    b.pe.inc(mm)
                b.T(t_gw)
                s = sref[0]
                pem = b.pe.n
                gslot = [None]
                def fuse(sc, oc=oc, s=s, pem=pem, gslot=gslot):
                    sc.wait_ge(b.pe.h, pem)
                    gs = gt_rot.acquire(sc)
                    gslot[0] = gs
                    b.ac.inc(sc.activation(gtmp[:, gs, :], ps_go[:, s * 512:(s + 1) * 512],
                                           AF.Silu, bias=bcol[:, oc:oc + 1]))
                b.SC(fuse)
                gslot = gslot[0]
                go_rot.done(s, b.ac)
                acm = b.ac.n
                def join(v, oc=oc, par=par, acm=acm, gslot=gslot, oslot=oslot):
                    v.wait_ge(b.ac.h, acm)
                    v.tensor_mul(gtmp[:, gslot, :], gtmp[:, gslot, :], scc[:, par, oc, :])
                    b.dv.inc(v.tensor_add(xout[:, oslot, oc, :], xcc[:, par, oc, :], gtmp[:, gslot, :]))
                b.V(join)
                gt_rot.done(gslot, b.dv)
            in_rot.done(par, b.dv)
            dvm = b.dv.n
            def st_x(sy, i=i, oslot=oslot, dvm=dvm):
                sy.wait_ge(b.dv.h, dvm)
                b.dsy.dma(sy.dma_start(out=x_dst_r[:, :, i * TC:(i + 1) * TC], in_=xout[:, oslot]))
            b.SY(st_x)
            out_rot.done(oslot, b.dsy)
        b.barrier()


def emit_first_max_ks(v, src_ap, m_bc, oh, t1, t2, sh):
    P, NT_, Ecnt = sh
    v.tensor_tensor(oh, src_ap, m_bc, ALU.is_ge)
    v.tensor_copy(t1, oh)
    v.tensor_add(t1[:, :, 1:8], oh[:, :, 1:8], oh[:, :, 0:7])
    v.tensor_copy(t2, t1)
    v.tensor_add(t2[:, :, 2:8], t1[:, :, 2:8], t1[:, :, 0:6])
    v.tensor_copy(t1, t2)
    v.tensor_add(t1[:, :, 4:8], t2[:, :, 4:8], t2[:, :, 0:4])
    v.tensor_single_scalar(t1, t1, 1.0, ALU.is_le)
    v.tensor_mul(oh, oh, t1)


def bounce_copy(b, pairs):
    dvm = b.dv.n
    def sc_(sc, dvm=dvm):
        sc.wait_ge(b.dv.h, dvm)
        last = None
        for dst, srcap in pairs:
            last = sc.copy(dst, srcap)
        b.ac.inc(last)
    b.SC(sc_)
    acm = b.ac.n
    def v_(v, acm=acm):
        v.wait_ge(b.ac.h, acm)
    b.V(v_)


def emit_top2(b, logits_sb, scr):
    sh = (128, NTILE, E)
    m1 = scr["m1"][:, :, :]
    m2 = scr["m2"][:, :, :]
    m1b = scr["m1b"][:, :, :]
    m2b = scr["m2b"][:, :, :]
    oh1 = scr["oh1"][:, :, :]
    oh2 = scr["oh2"][:, :, :]
    t1 = scr["t1"][:, :, :]
    t2 = scr["t2"][:, :, :]
    e1 = scr["e1"][:, :, :]
    lg = logits_sb[:, :, :]

    def va(v):
        b.dv.inc(v.tensor_reduce(m1, lg, AX, ALU.max))
    b.V(va)
    bounce_copy(b, [(m1b, m1)])
    def vb(v):
        emit_first_max_ks(v, lg, m1b.broadcast_to(sh), oh1, t1, t2, sh)
        v.scalar_tensor_tensor(t1, oh1, -1e30, lg, ALU.mult, ALU.add)
        b.dv.inc(v.tensor_reduce(m2, t1, AX, ALU.max))
    b.V(vb)
    bounce_copy(b, [(m2b, m2)])
    def vc(v):
        emit_first_max_ks(v, t1, m2b.broadcast_to(sh), oh2, t2, e1, sh)
    b.V(vc)


def emit_routing_topk(b, logits_sb, scr, gates_bf, ident, ps_t, out_also=None):
    nc = b.nc
    sh = (128, NTILE, E)
    m1 = scr["m1"][:, :, :]
    m2 = scr["m2"][:, :, :]
    m1b = scr["m1b"][:, :, :]
    m2b = scr["m2b"][:, :, :]
    oh1 = scr["oh1"][:, :, :]
    oh2 = scr["oh2"][:, :, :]
    t1 = scr["t1"][:, :, :]
    e1 = scr["e1"][:, :, :]
    lg = logits_sb[:, :, :]
    emit_top2(b, logits_sb, scr)
    def vd(v):
        b.dv.inc(v.tensor_tensor(e1, lg, m1b.broadcast_to(sh), ALU.subtract))
    b.V(vd)
    dvm = b.dv.n
    def s1(sc, dvm=dvm):
        sc.wait_ge(b.dv.h, dvm)
        b.ac.inc(sc.activation(e1, e1, AF.Exp))
    b.SC(s1)
    acm = b.ac.n
    def v2(v, acm=acm):
        v.wait_ge(b.ac.h, acm)
        v.tensor_add(t1, oh1, oh2)
        v.tensor_mul(e1, e1, t1)
        b.dv.inc(v.tensor_reduce(m2, e1, AX, ALU.add))
    b.V(v2)
    bounce_copy(b, [(m2b, m2)])
    def v3(v):
        b.dv.inc(v.reciprocal(m1, m2b))
    b.V(v3)
    bounce_copy(b, [(m1b, m1)])
    def v4(v):
        b.dv.inc(v.tensor_mul(e1, e1, m1b.broadcast_to(sh)))
    b.V(v4)
    _gates_transpose(b, e1, gates_bf, ident, ps_t)


def _gates_transpose(b, gtm, gates_bf, ident, ps_t):
    dvm = b.dv.n
    rot = Rot(4)
    for i in range(NTILE):
        def tp(t, i=i, dvm=dvm):
            if i == 0:
                t.wait_ge(b.dv.h, dvm)
            s = rot.acquire(t)
            mm = t.transpose(ps_t[0:8, s * 512:s * 512 + 128], gtm[:, i, :], ident[:, :])
            b.pe.inc(mm)
            pem = b.pe.n
            def ev(v, i=i, s=s, pem=pem):
                v.wait_ge(b.pe.h, pem)
                b.dv.inc(v.tensor_copy(gates_bf[0:8, i * 128:(i + 1) * 128],
                                       ps_t[0:8, s * 512:s * 512 + 128]))
            b.V(ev)
            rot.done(s, b.dv)
        b.T(tp)


def emit_moe_join(b, x_src, x_dst, s_hbm, g_acc):
    nc = b.nc
    x_src_r = x_src.rearrange("(c p) t -> p c t", p=128)
    x_dst_r = x_dst.rearrange("(c p) t -> p c t", p=128)
    s_r = s_hbm.rearrange("(c p) t -> p c t", p=128)
    with ExitStack() as es:
        sb = lambda n, sh, dt: es.enter_context(nc.sbuf_tensor(n, sh, dt))
        xcc = sb("j_xc", [128, 2, NDC, TC], F32)
        scc = sb("j_sc", [128, 2, NDC, TC], BF16)
        gt = sb("j_gt", [128, NDC, TC], F32)
        xo = sb("j_xo", [128, 2, NDC, TC], F32)
        in_rot, out_rot = Rot(2), Rot(2)
        for i in range(NTC):
            par = [None]
            def ld(sy, i=i, par=par):
                par[0] = in_rot.acquire(sy)
                b.dsy.dma(sy.dma_start(out=xcc[:, par[0]], in_=x_src_r[:, :, i * TC:(i + 1) * TC]))
                b.dsy.dma(sy.dma_start(out=scc[:, par[0]], in_=s_r[:, :, i * TC:(i + 1) * TC]))
            b.SY(ld)
            par = par[0]
            dsym = b.dsy.n
            acm0 = b.ac.n
            def s_silu(sc, i=i, acm0=acm0):
                sc.wait_ge(b.ac.h, acm0)
                last = None
                for c in range(NDC):
                    last = sc.activation(gt[:, c, :], g_acc[:, c, i * TC:(i + 1) * TC], AF.Silu)
                b.ac.inc(last)
            b.SC(s_silu)
            acm = b.ac.n
            oslot = [None]
            def vj(v, par=par, acm=acm, dsym=dsym, oslot=oslot):
                v.wait_ge(b.ac.h, acm)
                v.wait_ge(b.dsy.h, dsym)
                oslot[0] = out_rot.acquire(v)
                last = None
                for c in range(NDC):
                    v.tensor_mul(gt[:, c, :], gt[:, c, :], scc[:, par, c, :])
                    last = v.tensor_add(xo[:, oslot[0], c, :], xcc[:, par, c, :], gt[:, c, :])
                b.dv.inc(last)
            b.V(vj)
            in_rot.done(par, b.dv)
            oslot = oslot[0]
            dvj = b.dv.n
            def s_guard(sc, dvj=dvj):
                sc.wait_ge(b.dv.h, dvj)
            b.SC(s_guard)
            dvm = b.dv.n
            def st(sy, i=i, oslot=oslot, dvm=dvm):
                sy.wait_ge(b.dv.h, dvm)
                b.dsy.dma(sy.dma_start(out=x_dst_r[:, :, i * TC:(i + 1) * TC], in_=xo[:, oslot]))
            b.SY(st)
            out_rot.done(oslot, b.dsy)
    b.barrier()


def emit_block1(b, iaps, x_src, x_dst, xn_hbm, s_hbm, ident, identb, ones, rows):
    nc = b.nc
    xn_r = xn_hbm.rearrange("(c p) t -> p c t", p=128)
    s_r = s_hbm.rearrange("(c p) t -> p c t", p=128)

    bes = ExitStack()
    gates_bf = bes.enter_context(nc.sbuf_tensor("l1_gbf", [8, S], BF16))

    with ExitStack() as res_:
        sb = lambda n, sh, dt: res_.enter_context(nc.sbuf_tensor(n, sh, dt))
        wr_sb = sb("l1_wr", [128, NDC, E], F32)
        br_row = sb("l1_br", [1, 8], F32)
        logits_sb = sb("l1_lg", [128, NTILE, E], F32)

        def ldr(sy):
            b.dsy.dma(sy.dma_start(out=wr_sb[:, :, :],
                                   in_=iaps["w_l1_Wr"].rearrange("(c p) e -> p c e", p=128)))
            b.dsy.dma(sy.dma_start(out=br_row[:, :], in_=iaps["w_l1_br"][:, :]))
        b.SY(ldr)
        b.barrier()
        emit_stage1(b, x_src, xn_hbm, ones, rows, wr_sb=wr_sb, logits_sb=logits_sb, br_row=br_row)

        scr = {k: sb("l1_scr_" + k, [128, NTILE, E], F32)
               for k in ("oh1", "oh2", "t1", "t2", "e1")}
        for k in ("m1", "m2", "m1b", "m2b"):
            scr[k] = sb("l1_" + k, [128, NTILE, 1], F32)
        ps_t = res_.enter_context(nc.psum_tensor("l1_pst", [128, 2048], F32))
        emit_routing_topk(b, logits_sb, scr, gates_bf, ident, ps_t)
        b.barrier()

    with ExitStack() as aes:
        sb = lambda n, sh, dt: aes.enter_context(nc.sbuf_tensor(n, sh, dt))
        psn = lambda n, sh: aes.enter_context(nc.psum_tensor(n, sh, F32))
        wt = sb("l1_wt", [128, NDC, D], BF16)
        wo = sb("l1_wo", [128, NDC, D], BF16)
        wgm = sb("l1_wgm", [128, 2 * NDC, D], BF16)
        dec_t = sb("l1_dect", [128, NDC, TC], F32)
        beta = sb("l1_beta", [128, NDC], F32)
        ln1w = sb("l1_ln1w", [128, NDC], F32)
        ln1b = sb("l1_ln1b", [128, NDC], F32)
        lncw = sb("l1_lncw", [128, NDC], F32)
        lncb = sb("l1_lncb", [128, NDC], F32)
        lngw = sb("l1_lngw", [128, NDC], F32)
        lngb = sb("l1_lngb", [128, NDC], F32)
        bg = sb("l1_bg", [128, NDC], F32)
        carry = sb("l1_carry", [128, NDC], F32)

        def ldw(g):
            b.dgp.dma(g.dma_start(out=wt[:, :, :], in_=iaps["w_l1_Wt"].rearrange("(c p) o -> p c o", p=128)))
            b.dgp.dma(g.dma_start(out=wo[:, :, :], in_=iaps["w_l1_Wo"].rearrange("(c p) o -> p c o", p=128)))
            b.dgp.dma(g.dma_start(out=wgm[:, :, :], in_=iaps["w_l1_Wg"].rearrange("(c p) o -> p c o", p=128)))
        b.G(ldw)
        def lds(sy):
            b.dsy.dma(sy.dma_start(out=dec_t[:, :, :], in_=iaps["c_l1dec"][:, :, :]))
            for nm, dst in (("c_l1beta", beta), ("w_l1_ln1w", ln1w), ("w_l1_ln1b", ln1b),
                            ("w_l1_lncw", lncw), ("w_l1_lncb", lncb), ("w_l1_lngw", lngw),
                            ("w_l1_lngb", lngb), ("w_l1_bg", bg)):
                b.dsy.dma(sy.dma_start(out=dst[:, :], in_=iaps[nm][:, :]))
        b.SY(lds)

        xnc = sb("l1_xnc", [128, 1, NDC, TC], BF16)
        xf = sb("l1_xf", [128, NDC, TC], F32)
        xb = sb("l1_xb", [128, NDC, TC], BF16)
        cx = sb("l1_cx", [128, NDC, TC], F32)
        cxnb = sb("l1_cxnb", [128, NDC, TC], BF16)
        outb = sb("l1_outb", [128, NDC, TC], BF16)
        gate = sb("l1_gate", [128, NDC, TC], BF16)
        ssb = sb("l1_ssb", [128, 1, NDC, TC], BF16)
        sq = sb("l1_sq", [128, NDC, TC], F32)
        accw = sb("l1_acc", [128, TC], F32)
        ps_a = psn("l1_psa", [128, 1024])
        ps_st = psn("l1_psst", [128, 1024])
        ps_bc = psn("l1_psbc", [128, 1024])
        work = dict(acc=accw, sq=sq, rows=rows, ps_st=ps_st, ps_bc=ps_bc, ones=ones)

        b.barrier()
        def vz(v):
            b.dv.inc(v.memset(carry[:, :], 0.0))
        b.V(vz)

        xnc_rot = Rot(1)
        ssb_rot = Rot(1)
        ra = Rot(2)
        for i in range(NTC):
            par = [None]
            def ld(sy, i=i, par=par):
                par[0] = xnc_rot.acquire(sy)
                b.dsy.dma(sy.dma_start(out=xnc[:, par[0]], in_=xn_r[:, :, i * TC:(i + 1) * TC]))
            b.SY(ld)
            par = par[0]
            dsym = b.dsy.n
            def w0(v, dsym=dsym):
                v.wait_ge(b.dsy.h, dsym)
            b.V(w0)
            def w0s(sc, dsym=dsym):
                sc.wait_ge(b.dsy.h, dsym)
            b.SC(w0s)

            emit_ln_chunk(b, xnc[:, par], xf, work, eps=1e-6, affine=(ln1w, ln1b))
            def vcast(v):
                last = None
                for c in range(NDC):
                    last = v.tensor_copy(xb[:, c, :], xf[:, c, :])
                b.dv.inc(last)
            b.V(vcast)
            xnc_rot.done(par, b.dv)
            dvm = b.dv.n

            def t_u(t, dvm=dvm):
                t.wait_ge(b.dv.h, dvm)
                for oc in range(NDC):
                    s = ra.acquire(t)
                    mm = mmgroup(t, ps_a[:, s * 512:(s + 1) * 512],
                                 [(wt[:, c, oc * 128:(oc + 1) * 128], xb[:, c, :]) for c in range(NDC)])
                    b.pe.inc(mm)
                    pem = b.pe.n
                    def evu(v, oc=oc, s=s, pem=pem):
                        v.wait_ge(b.pe.h, pem)
                        b.dv.inc(v.tensor_scalar(sq[:, oc, :], ps_a[:, s * 512:(s + 1) * 512],
                                                 beta[:, oc:oc + 1], None, ALU.mult))
                    b.V(evu)
                    ra.done(s, b.dv)
            b.T(t_u)

            dvm = b.dv.n
            def v_scan(v, dvm=dvm):
                v.wait_ge(b.dv.h, dvm)
                last = None
                for c in range(NDC):
                    last = v.tensor_tensor_scan(cx[:, c, :], dec_t[:, c, :], sq[:, c, :],
                                                carry[:, c:c + 1], ALU.mult, ALU.add)
                b.dv.inc(last)
            b.V(v_scan)
            dvm = b.dv.n
            def s_carry(sc, dvm=dvm):
                sc.wait_ge(b.dv.h, dvm)
                last = None
                for c in range(NDC):
                    last = sc.copy(carry[:, c:c + 1], cx[:, c, TC - 1:TC])
                b.ac.inc(last)
            b.SC(s_carry)
            acm = b.ac.n
            def vb1(v, acm=acm):
                v.wait_ge(b.ac.h, acm)
            b.V(vb1)

            emit_ln_chunk(b, cx, cxnb, work, eps=1e-6, affine=(lncw, lncb))
            dvm = b.dv.n

            def t_o(t, dvm=dvm):
                t.wait_ge(b.dv.h, dvm)
                for oc in range(NDC):
                    s = ra.acquire(t)
                    mm = mmgroup(t, ps_a[:, s * 512:(s + 1) * 512],
                                 [(wo[:, c, oc * 128:(oc + 1) * 128], cxnb[:, c, :]) for c in range(NDC)])
                    b.pe.inc(mm)
                    pem = b.pe.n
                    def evo(v, oc=oc, s=s, pem=pem):
                        v.wait_ge(b.pe.h, pem)
                        b.dv.inc(v.tensor_copy(outb[:, oc, :], ps_a[:, s * 512:(s + 1) * 512]))
                    b.V(evo)
                    ra.done(s, b.dv)
            b.T(t_o)
            dvm = b.dv.n

            def t_g(t, dvm=dvm):
                t.wait_ge(b.dv.h, dvm)
                for oc in range(NDC):
                    s = ra.acquire(t)
                    pairs = [(wgm[:, c, oc * 128:(oc + 1) * 128], xb[:, c, :]) for c in range(NDC)]
                    pairs += [(wgm[:, NDC + c, oc * 128:(oc + 1) * 128], outb[:, c, :]) for c in range(NDC)]
                    mm = mmgroup(t, ps_a[:, s * 512:(s + 1) * 512], pairs)
                    b.pe.inc(mm)
                    pem = b.pe.n
                    def evl(v, oc=oc, s=s, pem=pem):
                        v.wait_ge(b.pe.h, pem)
                        b.dv.inc(v.tensor_scalar(cx[:, oc, :], ps_a[:, s * 512:(s + 1) * 512],
                                                 bg[:, oc:oc + 1], None, ALU.add))
                    b.V(evl)
                    ra.done(s, b.dv)
            b.T(t_g)

            emit_ln_chunk(b, cx, cx, work, eps=1e-6, affine=(lngw, lngb))
            dvm = b.dv.n
            def s_sig(sc, dvm=dvm):
                sc.wait_ge(b.dv.h, dvm)
                last = None
                for c in range(NDC):
                    last = sc.activation(gate[:, c, :], cx[:, c, :], AF.Sigmoid)
                b.ac.inc(last)
            b.SC(s_sig)
            acm = b.ac.n

            sslot = [None]
            def v_blend(v, par=par, acm=acm, sslot=sslot):
                v.wait_ge(b.ac.h, acm)
                sslot[0] = ssb_rot.acquire(v)
                last = None
                for c in range(NDC):
                    v.tensor_sub(sq[:, c, :], outb[:, c, :], xf[:, c, :])
                    v.tensor_mul(sq[:, c, :], sq[:, c, :], gate[:, c, :])
                    last = v.tensor_add(ssb[:, sslot[0], c, :], xf[:, c, :], sq[:, c, :])
                b.dv.inc(last)
            b.V(v_blend)
            sslot = sslot[0]
            dvm = b.dv.n
            def st_s(sy, i=i, sslot=sslot, dvm=dvm):
                sy.wait_ge(b.dv.h, dvm)
                b.dsy.dma(sy.dma_start(out=s_r[:, :, i * TC:(i + 1) * TC], in_=ssb[:, sslot]))
            b.SY(st_s)
            ssb_rot.done(sslot, b.dsy)
        b.barrier()

    g_acc = bes.enter_context(nc.sbuf_tensor("l1_gacc", [128, NDC, S], BF16))
    with ExitStack() as fes:
        sb = lambda n, sh, dt: fes.enter_context(nc.sbuf_tensor(n, sh, dt))
        psn = lambda n, sh: fes.enter_context(nc.psum_tensor(n, sh, F32))
        FH = 2048
        NFC = FH // 128
        w1 = sb("l1f_w1", [128, 1, NDC, FH], BF16)
        w2 = sb("l1f_w2", [128, NFC, D], BF16)
        b1c = sb("l1f_b1", [128, E, NFC], F32)
        b2b = sb("l1f_b2", [8, D], BF16)
        xnc = sb("l1f_xnc", [128, 1, NDC, TC], BF16)
        grow = sb("l1f_grow", [1, 2, TC], BF16)
        hsb = sb("l1f_h", [128, 2, NFC, TC], BF16)
        gb = sb("l1f_gb", [128, 2, TC], F32)
        ps_h = psn("l1f_psh", [128, 1024])
        ps_g = psn("l1f_psg", [128, 1024])
        ps_k = psn("l1f_psk", [128, 1024])

        def lds(sy):
            b.dsy.dma(sy.dma_start(out=b1c[:, :, :], in_=iaps["w_l1_b1"][:, :, :]))
        b.SY(lds)
        def ldb2(g):
            b.dgp.dma(g.dma_start(out=b2b[:, :], in_=iaps["w_l1_b2"][:, :]))
        b.G(ldb2)

        onesb = b.onesb
        w1_rot = Rot(1)
        w2_rot = Rot(1)
        xn_rot = Rot(1)
        h_rot = Rot(2)
        gbrot = Rot(2)
        rh = Rot(2)
        rg = Rot(2)
        kg = Rot(2)
        grow_rot = Rot(2)
        def ldw1_0(g):
            w1_rot.acquire(g)
            b.dgp.dma(g.dma_start(out=w1[:, 0],
                                  in_=iaps["w_l1_W1"][0].rearrange("(c p) f -> p c f", p=128)))
        b.G(ldw1_0)
        dgm_w1_next = [b.dgp.n]
        for e in range(E):
            wpar = 0
            dgm_w1 = dgm_w1_next[0]
            def ldw2(g, e=e):
                w2_rot.acquire(g)
                b.dgp.dma(g.dma_start(out=w2[:, :, :],
                                      in_=iaps["w_l1_W2"][e].rearrange("(c p) d -> p c d", p=128)))
            b.G(ldw2)
            dgm_w2 = b.dgp.n
            pend = [None]
            def emit_td(t, info, e=e, dgm_w2=dgm_w2):
                i, hs, dvm_ = info
                t.wait_ge(b.dv.h, dvm_)
                t.wait_ge(b.dgp.h, dgm_w2)
                for dc in range(NDC):
                    s = rg.acquire(t)
                    pairs = []
                    if e == 0:
                        pairs.append((b2b[0:8, dc * 128:(dc + 1) * 128],
                                      gates_bf[0:8, i * TC:(i + 1) * TC]))
                    pairs += [(w2[:, fc, dc * 128:(dc + 1) * 128], hsb[:, hs, fc, :])
                              for fc in range(NFC)]
                    mm = mmgroup(t, ps_g[:, s * 512:(s + 1) * 512], pairs)
                    b.pe.inc(mm)
                    pem = b.pe.n
                    def evg(v, dc=dc, s=s, pem=pem, e=e, i=i):
                        v.wait_ge(b.pe.h, pem)
                        if e == 0:
                            b.dv.inc(v.tensor_copy(g_acc[:, dc, i * TC:(i + 1) * TC],
                                                   ps_g[:, s * 512:(s + 1) * 512]))
                        else:
                            b.dv.inc(v.tensor_tensor(g_acc[:, dc, i * TC:(i + 1) * TC],
                                                     ps_g[:, s * 512:(s + 1) * 512],
                                                     g_acc[:, dc, i * TC:(i + 1) * TC], ALU.add))
                    b.V(evg)
                    rg.done(s, b.dv)
            for i in range(NTC):
                xpar = [None]
                def ldx(sy, i=i, xpar=xpar):
                    xpar[0] = xn_rot.acquire(sy)
                    b.dsy.dma(sy.dma_start(out=xnc[:, xpar[0]], in_=xn_r[:, :, i * TC:(i + 1) * TC]))
                b.SY(ldx)
                xpar = xpar[0]
                dsym = b.dsy.n

                gs2 = [None]
                def ldgr(sy, e=e, i=i, gs2=gs2):
                    gs2[0] = grow_rot.acquire(sy)
                    b.dsy.dma(sy.dma_start(out=grow[:, gs2[0], :],
                                           in_=gates_bf[e:e + 1, i * TC:(i + 1) * TC]))
                b.SY(ldgr)
                gs2 = gs2[0]
                grm = b.dsy.n
                kgs = [None]
                def t_k(t, e=e, i=i, grm=grm, kgs=kgs, gs2=gs2):
                    t.wait_ge(b.dsy.h, grm)
                    kgs[0] = kg.acquire(t)
                    mm = t.matmul(ps_k[:, kgs[0] * 512:(kgs[0] + 1) * 512], onesb[0:1, 0:128],
                                  grow[0:1, gs2, :], start=True, stop=True)
                    b.pe.inc(mm)
                b.T(t_k)
                grow_rot.done(gs2, b.pe)
                kgs = kgs[0]
                pem = b.pe.n
                gbs = [None]
                def ev_k(v, pem=pem, kgs=kgs, gbs=gbs):
                    v.wait_ge(b.pe.h, pem)
                    gbs[0] = gbrot.acquire(v)
                    b.dv.inc(v.tensor_copy(gb[:, gbs[0], :], ps_k[:, kgs * 512:(kgs + 1) * 512]))
                b.V(ev_k)
                gbs = gbs[0]
                kg.done(kgs, b.dv)
                gbm = b.dv.n

                hs = [None]
                def pickh(sc, hs=hs):
                    hs[0] = h_rot.acquire(sc)
                b.SC(pickh)
                hs = hs[0]
                def t_h(t, e=e, i=i, xpar=xpar, dgm_w1=dgm_w1, dsym=dsym, hs=hs, wpar=wpar):
                    t.wait_ge(b.dgp.h, dgm_w1)
                    t.wait_ge(b.dsy.h, dsym)
                    for hc in range(NFC):
                        s = rh.acquire(t)
                        mm = mmgroup(t, ps_h[:, s * 512:(s + 1) * 512],
                                     [(w1[:, wpar, c, hc * 128:(hc + 1) * 128], xnc[:, xpar, c, :])
                                      for c in range(NDC)])
                        b.pe.inc(mm)
                        pem = b.pe.n
                        def evh(sc, hc=hc, s=s, pem=pem, e=e, hs=hs):
                            sc.wait_ge(b.pe.h, pem)
                            b.ac.inc(sc.activation(hsb[:, hs, hc, :], ps_h[:, s * 512:(s + 1) * 512],
                                                   AF.Relu, bias=b1c[:, e, hc:hc + 1]))
                        b.SC(evh)
                        rh.done(s, b.ac)
                    if pend[0] is not None:
                        emit_td(t, pend[0])
                        pend[0] = None
                b.T(t_h)
                xn_rot.done(xpar, b.pe)
                acm = b.ac.n
                def v_hg(v, acm=acm, gbm=gbm, hs=hs, gbs=gbs):
                    v.wait_ge(b.ac.h, acm)
                    v.wait_ge(b.dv.h, gbm)
                    b.dv.inc(v.tensor_tensor(
                        hsb[:, hs, :, :], hsb[:, hs, :, :],
                        gb[:, gbs, :].rearrange("p (o t) -> p o t", o=1).broadcast_to((128, NFC, TC)),
                        ALU.mult))
                b.V(v_hg)
                gbrot.done(gbs, b.dv)
                pend[0] = (i, hs, b.dv.n)
                h_rot.done(hs, b.pe)
            w1_rot.done(0, b.pe)
            if e + 1 < E:
                def ldw1_n(g, e=e):
                    w1_rot.acquire(g)
                    b.dgp.dma(g.dma_start(out=w1[:, 0],
                                          in_=iaps["w_l1_W1"][e + 1].rearrange("(c p) f -> p c f", p=128)))
                b.G(ldw1_n)
                dgm_w1_next[0] = b.dgp.n
            def t_tail(t):
                if pend[0] is not None:
                    emit_td(t, pend[0])
                    pend[0] = None
            b.T(t_tail)
            w2_rot.done(0, b.pe)
        b.barrier()

    emit_moe_join(b, x_src, x_dst, s_hbm, g_acc)
    bes.close()


def emit_block0(b, iaps, x_src, x_dst, xn_hbm, s_hbm, h_hbm, cc_in, cc_out, pfx_sb,
                ident, identb, ones, tri, rows):
    nc = b.nc
    xn_r = xn_hbm.rearrange("(c p) t -> p c t", p=128)
    s_r = s_hbm.rearrange("(c p) t -> p c t", p=128)
    h_r = h_hbm.rearrange("(c p) t -> p c t", p=128)

    bes = ExitStack()
    gates_bf = bes.enter_context(nc.sbuf_tensor("l0_gbf", [8, S], BF16))

    with ExitStack() as res_:
        sb = lambda n, sh, dt: res_.enter_context(nc.sbuf_tensor(n, sh, dt))
        wr_sb = sb("l0_wr", [128, NDC, E], F32)
        logits_sb = sb("l0_lg", [128, NTILE, E], F32)

        def ldr(sy):
            b.dsy.dma(sy.dma_start(out=wr_sb[:, :, :],
                                   in_=iaps["w_l0_Wr"].rearrange("(c p) e -> p c e", p=128)))
        b.SY(ldr)
        b.barrier()
        emit_stage1(b, x_src, xn_hbm, ones, rows, wr_sb=wr_sb, logits_sb=logits_sb)

        scr = {k: sb("l0_scr_" + k, [128, NTILE, E], F32)
               for k in ("oh1", "oh2", "t1", "t2", "e1")}
        for k in ("m1", "m2", "m1b", "m2b"):
            scr[k] = sb("l0_" + k, [128, NTILE, 1], F32)
        emit_top2(b, logits_sb, scr)

        ohs = sb("l0_ohs", [128, NTILE, 16], F32)
        w1t = sb("l0_w1t", [128, NTILE, 1], F32)
        rank_sb = sb("l0_rank", [128, NTILE, 16], F32)
        carry16 = sb("l0_carry16", [1, 16], F32)
        offs_bc = sb("l0_offs", [128, 16], F32)
        ccsb = sb("l0_ccsb", [8, 16], F32)
        ps_rk = res_.enter_context(nc.psum_tensor("l0_psrk", [128, 512], F32))
        ps_ct = res_.enter_context(nc.psum_tensor("l0_psct", [128, 512], F32))

        def v5(v):
            v.tensor_copy(ohs[:, :, 0:8], scr["oh1"][:, :, :])
            v.tensor_copy(ohs[:, :, 8:16], scr["oh2"][:, :, :])
            b.dv.inc(v.tensor_sub(scr["t1"][:, :, 0:1], scr["m1b"][:, :, :], scr["m2b"][:, :, :]))
        b.V(v5)
        dvm = b.dv.n
        def s5(sc, dvm=dvm):
            sc.wait_ge(b.dv.h, dvm)
            b.ac.inc(sc.activation(w1t[:, :, :], scr["t1"][:, :, 0:1], AF.Sigmoid))
        b.SC(s5)

        dvm = b.dv.n
        acm = b.ac.n
        cguard = [None]
        for ti in range(NTILE):
            def t_rk(t, ti=ti, dvm=dvm, acm=acm):
                if ti == 0:
                    t.wait_ge(b.dv.h, dvm)
                    t.wait_ge(b.ac.h, acm)
                else:
                    t.wait_ge(cguard[0][0].h, cguard[0][1])
                pairs = [(tri[:, :], ohs[:, ti, :])]
                if ti > 0:
                    pairs.append((ones[0:1, 0:128], carry16[0:1, :]))
                mm = mmgroup(t, ps_rk[:, 0:16], pairs)
                b.pe.inc(mm)
                mm = t.matmul(ps_ct[0:1, 0:16], ones[:, 0:1], ohs[:, ti, :],
                              start=(ti == 0), stop=True)
                b.pe.inc(mm)
            b.T(t_rk)
            pem = b.pe.n
            def ev_rk(v, ti=ti, pem=pem):
                v.wait_ge(b.pe.h, pem)
                v.tensor_copy(rank_sb[:, ti, :], ps_rk[:, 0:16])
                b.dv.inc(v.tensor_copy(carry16[0:1, :], ps_ct[0:1, 0:16]))
            b.V(ev_rk)
            cguard[0] = (b.dv, b.dv.n)

        dvm = b.dv.n
        def st_cc(sy, dvm=dvm):
            sy.wait_ge(b.dv.h, dvm)
            b.dsy.dma(sy.dma_start(out=cc_in[:, :], in_=carry16[0:1, :]))
        b.SY(st_cc)
        dsym = b.dsy.n
        def do_cc(g, dsym=dsym):
            g.wait_ge(b.dsy.h, dsym)
            b.cc.inc(g.collective_compute(
                "AllGather", ALU.bypass,
                replica_groups=[list(range(8))],
                ins=[cc_in[:, :]],
                outs=[cc_out[:, :]]), 1)
        b.G(do_cc)
        ccm = b.cc.n
        def ld_cc(sy, ccm=ccm):
            sy.wait_ge(b.cc.h, ccm)
            b.dsy.dma(sy.dma_start(out=ccsb[:, :], in_=cc_out[:, :]))
        b.SY(ld_cc)
        dsym = b.dsy.n
        def t_off(t, dsym=dsym):
            t.wait_ge(b.dsy.h, dsym)
            mm = t.matmul(ps_ct[0:1, 32:48], pfx_sb[:, :], ccsb[:, :], start=True, stop=True)
            b.pe.inc(mm)
        b.T(t_off)
        pem = b.pe.n
        def ev_off(v, pem=pem):
            v.wait_ge(b.pe.h, pem)
            b.dv.inc(v.tensor_copy(rows[0:1, 0:16], ps_ct[0:1, 32:48]))
        b.V(ev_off)
        dvm = b.dv.n
        def t_offb(t, dvm=dvm):
            t.wait_ge(b.dv.h, dvm)
            mm = t.matmul(ps_rk[:, 32:48], ones[0:1, 0:128], rows[0:1, 0:16], start=True, stop=True)
            b.pe.inc(mm)
        b.T(t_offb)
        pem = b.pe.n
        def ev_offb(v, pem=pem):
            v.wait_ge(b.pe.h, pem)
            b.dv.inc(v.tensor_copy(offs_bc[:, :], ps_rk[:, 32:48]))
        b.V(ev_offb)

        def v6(v):
            sh16 = (128, NTILE, 16)
            v.tensor_tensor(rank_sb[:, :, :], rank_sb[:, :, :],
                            offs_bc[:, :].rearrange("p (o f) -> p o f", o=1).broadcast_to(sh16),
                            ALU.add)
            v.tensor_single_scalar(rank_sb[:, :, :], rank_sb[:, :, :], float(CAPACITY), ALU.is_le)
            v.tensor_mul(rank_sb[:, :, :], rank_sb[:, :, :], ohs[:, :, :])
            sh8 = (128, NTILE, 8)
            v.tensor_mul(scr["t2"][:, :, :], rank_sb[:, :, 0:8], w1t[:, :, :].broadcast_to(sh8))
            v.tensor_scalar(scr["t1"][:, :, 0:1], w1t[:, :, :], -1.0, 1.0, ALU.mult, ALU.add)
            v.tensor_mul(scr["e1"][:, :, :], rank_sb[:, :, 8:16],
                         scr["t1"][:, :, 0:1].broadcast_to(sh8))
            b.dv.inc(v.tensor_add(scr["e1"][:, :, :], scr["e1"][:, :, :], scr["t2"][:, :, :]))
        b.V(v6)
        ps_t = res_.enter_context(nc.psum_tensor("l0_pst", [128, 2048], F32))
        _gates_transpose(b, scr["e1"][:, :, :], gates_bf, ident, ps_t)
        b.barrier()

    with ExitStack() as aes:
        sb = lambda n, sh, dt: aes.enter_context(nc.sbuf_tensor(n, sh, dt))
        psn = lambda n, sh: aes.enter_context(nc.psum_tensor(n, sh, F32))
        winT = sb("l0_win", [128, NDC, 512], BF16)
        woutT = sb("l0_wout", [128, 4, D], BF16)
        ldec = sb("l0_ldec", [128, 2, 1], F32)
        freq = sb("l0_freq", [128, 2, 1], F32)
        dcol = sb("l0_dcol", [128, 2, 1], F32)
        c1 = sb("l0_c1", [128, 2, 1], F32)
        s1v = sb("l0_s1", [128, 2, 1], F32)
        csc = sb("l0_csc", [128, 2, 1], F32)
        ssc = sb("l0_ssc", [128, 2, 1], F32)
        Ct = sb("l0_C", [128, 2, S], F32)
        St = sb("l0_S", [128, 2, S], F32)
        tmp2 = sb("l0_tmp2", [128, 2, 2048], F32)
        dec_tile = sb("l0_dect", [128, 4, TC], F32)

        def ldw(g):
            b.dgp.dma(g.dma_start(out=winT[:, :, :],
                                  in_=iaps["w_l0_WinT"].rearrange("(c p) f -> p c f", p=128)))
            b.dgp.dma(g.dma_start(out=woutT[:, :, :],
                                  in_=iaps["w_l0_WoutT"].rearrange("(c p) o -> p c o", p=128)))
        b.G(ldw)
        def lds(sy):
            b.dsy.dma(sy.dma_start(out=ldec[:, :, 0], in_=iaps["w_l0_logdec"][:, :]))
            b.dsy.dma(sy.dma_start(out=freq[:, :, 0], in_=iaps["w_l0_freq"][:, :]))
        b.SY(lds)
        dsym = b.dsy.n
        def s_trig(sc, dsym=dsym):
            sc.wait_ge(b.dsy.h, dsym)
            sc.activation(dcol[:, :, :], ldec[:, :, :], AF.Sigmoid)
            sc.activation(c1[:, :, :], freq[:, :, :], AF.Sin, bias=float(np.pi / 2), scale=0.1)
            b.ac.inc(sc.activation(s1v[:, :, :], freq[:, :, :], AF.Sin, scale=0.1))
        b.SC(s_trig)
        acm = b.ac.n
        def v_init(v, acm=acm):
            v.wait_ge(b.ac.h, acm)
            for j in range(4):
                v.tensor_scalar_mul(dec_tile[:, j, :], ones[:, 0:TC], dcol[:, j % 2, 0:1])
            v.memset(Ct[:, :, 0:1], 1.0)
            v.memset(St[:, :, 0:1], 0.0)
            v.tensor_copy(Ct[:, :, 1:2], c1[:, :, :])
            b.dv.inc(v.tensor_copy(St[:, :, 1:2], s1v[:, :, :]))
        b.V(v_init)

        vv = 2
        while vv < S:
            w = min(vv - 1, S - vv)
            dvm = b.dv.n
            def sc_b(sc, vv=vv, dvm=dvm):
                sc.wait_ge(b.dv.h, dvm)
                sc.copy(csc[:, :, :], Ct[:, :, vv - 1:vv])
                b.ac.inc(sc.copy(ssc[:, :, :], St[:, :, vv - 1:vv]))
            b.SC(sc_b)
            acm = b.ac.n
            def v_dbl(v, vv=vv, w=w, acm=acm):
                v.wait_ge(b.ac.h, acm)
                last = None
                for cr in range(2):
                    v.tensor_scalar_mul(tmp2[:, cr, 0:w], St[:, cr, 1:1 + w], ssc[:, cr, 0:1])
                    v.scalar_tensor_tensor(Ct[:, cr, vv:vv + w], Ct[:, cr, 1:1 + w],
                                           csc[:, cr, 0:1], tmp2[:, cr, 0:w],
                                           ALU.mult, ALU.subtract)
                    v.tensor_scalar_mul(tmp2[:, cr, 0:w], Ct[:, cr, 1:1 + w], ssc[:, cr, 0:1])
                    last = v.scalar_tensor_tensor(St[:, cr, vv:vv + w], St[:, cr, 1:1 + w],
                                                  csc[:, cr, 0:1], tmp2[:, cr, 0:w],
                                                  ALU.mult, ALU.add)
                b.dv.inc(last)
            b.V(v_dbl)
            vv = vv + w

        xnc = sb("l0_xnc", [128, 2, NDC, TC], BF16)
        u = sb("l0_u", [128, 4, TC], F32)
        us = sb("l0_us", [128, 4, TC], F32)
        cxs = sb("l0_cx", [128, 4, TC], F32)
        c2t = sb("l0_c2", [128, 2, TC], F32)
        s2t = sb("l0_s2", [128, 2, TC], F32)
        fb = sb("l0_fb", [128, 4, TC], BF16)
        ssb = sb("l0_ssb", [128, NDC, TC], BF16)
        carry = sb("l0_carry", [128, 4], F32)
        ps_a = psn("l0_psa", [128, 1024])
        b.barrier()

        def vz(v):
            b.dv.inc(v.memset(carry[:, :], 0.0))
        b.V(vz)

        xn_rot = Rot(2)
        ssb_rot = Rot(1)
        ra = Rot(2)
        for i in range(NTC):
            par = [None]
            def ld(sy, i=i, par=par):
                par[0] = xn_rot.acquire(sy)
                b.dsy.dma(sy.dma_start(out=xnc[:, par[0]], in_=xn_r[:, :, i * TC:(i + 1) * TC]))
            b.SY(ld)
            par = par[0]
            dsym = b.dsy.n

            def t_sp(t, par=par, dsym=dsym):
                t.wait_ge(b.dsy.h, dsym)
                for fc in range(4):
                    s = ra.acquire(t)
                    mm = mmgroup(t, ps_a[:, s * 512:(s + 1) * 512],
                                 [(winT[:, c, fc * 128:(fc + 1) * 128], xnc[:, par, c, :])
                                  for c in range(NDC)])
                    b.pe.inc(mm)
                    pem = b.pe.n
                    def evs(v, fc=fc, s=s, pem=pem):
                        v.wait_ge(b.pe.h, pem)
                        b.dv.inc(v.tensor_copy(u[:, fc, :], ps_a[:, s * 512:(s + 1) * 512]))
                    b.V(evs)
                    ra.done(s, b.dv)
            b.T(t_sp)
            xn_rot.done(par, b.pe)

            def v_tw(v, i=i):
                sl = slice(i * TC, (i + 1) * TC)
                last = None
                for cr in range(2):
                    v.tensor_mul(us[:, cr, :], u[:, cr, :], Ct[:, cr, sl])
                    v.tensor_mul(cxs[:, cr, :], u[:, 2 + cr, :], St[:, cr, sl])
                    v.tensor_add(us[:, cr, :], us[:, cr, :], cxs[:, cr, :])
                    v.tensor_mul(us[:, 2 + cr, :], u[:, 2 + cr, :], Ct[:, cr, sl])
                    v.tensor_mul(cxs[:, 2 + cr, :], u[:, cr, :], St[:, cr, sl])
                    last = v.tensor_sub(us[:, 2 + cr, :], us[:, 2 + cr, :], cxs[:, 2 + cr, :])
                b.dv.inc(last)
            b.V(v_tw)

            dvm = b.dv.n
            def v_scan(v, dvm=dvm):
                v.wait_ge(b.dv.h, dvm)
                last = None
                for j in range(4):
                    last = v.tensor_tensor_scan(cxs[:, j, :], dec_tile[:, j, :], us[:, j, :],
                                                carry[:, j:j + 1], ALU.mult, ALU.add)
                b.dv.inc(last)
            b.V(v_scan)
            dvm = b.dv.n
            def s_carry(sc, dvm=dvm):
                sc.wait_ge(b.dv.h, dvm)
                last = None
                for j in range(4):
                    last = sc.copy(carry[:, j:j + 1], cxs[:, j, TC - 1:TC])
                b.ac.inc(last)
            b.SC(s_carry)
            acm = b.ac.n

            def v_ut(v, i=i, acm=acm):
                v.wait_ge(b.ac.h, acm)
                sl = slice(i * TC, (i + 1) * TC)
                for cr in range(2):
                    v.tensor_scalar_mul(c2t[:, cr, :], St[:, cr, sl], s1v[:, cr, 0:1])
                    v.scalar_tensor_tensor(c2t[:, cr, :], Ct[:, cr, sl], c1[:, cr, 0:1],
                                           c2t[:, cr, :], ALU.mult, ALU.subtract)
                    v.tensor_scalar_mul(s2t[:, cr, :], Ct[:, cr, sl], s1v[:, cr, 0:1])
                    v.scalar_tensor_tensor(s2t[:, cr, :], St[:, cr, sl], c1[:, cr, 0:1],
                                           s2t[:, cr, :], ALU.mult, ALU.add)
                last = None
                for cr in range(2):
                    v.tensor_mul(us[:, cr, :], cxs[:, cr, :], c2t[:, cr, :])
                    v.tensor_mul(us[:, 2 + cr, :], cxs[:, 2 + cr, :], s2t[:, cr, :])
                    v.tensor_sub(fb[:, cr, :], us[:, cr, :], us[:, 2 + cr, :])
                    v.tensor_mul(us[:, cr, :], cxs[:, cr, :], s2t[:, cr, :])
                    v.tensor_mul(us[:, 2 + cr, :], cxs[:, 2 + cr, :], c2t[:, cr, :])
                    last = v.tensor_add(fb[:, 2 + cr, :], us[:, cr, :], us[:, 2 + cr, :])
                b.dv.inc(last)
            b.V(v_ut)
            dvm = b.dv.n

            sslot = [None]
            def pick(v, sslot=sslot):
                sslot[0] = ssb_rot.acquire(v)
            b.V(pick)
            sslot = sslot[0]
            def t_wo(t, dvm=dvm, sslot=sslot):
                t.wait_ge(b.dv.h, dvm)
                for oc in range(NDC):
                    s = ra.acquire(t)
                    mm = mmgroup(t, ps_a[:, s * 512:(s + 1) * 512],
                                 [(woutT[:, fc, oc * 128:(oc + 1) * 128], fb[:, fc, :])
                                  for fc in range(4)])
                    b.pe.inc(mm)
                    pem = b.pe.n
                    def evo(v, oc=oc, s=s, pem=pem, sslot=sslot):
                        v.wait_ge(b.pe.h, pem)
                        b.dv.inc(v.tensor_copy(ssb[:, oc, :], ps_a[:, s * 512:(s + 1) * 512]))
                    b.V(evo)
                    ra.done(s, b.dv)
            b.T(t_wo)
            dvm = b.dv.n
            def st_s(sy, i=i, dvm=dvm):
                sy.wait_ge(b.dv.h, dvm)
                b.dsy.dma(sy.dma_start(out=s_r[:, :, i * TC:(i + 1) * TC], in_=ssb[:, :, :]))
            b.SY(st_s)
            ssb_rot.done(0, b.dsy)
        b.barrier()

    g_acc = bes.enter_context(nc.sbuf_tensor("l0_gacc", [128, NDC, S], BF16))
    onesb = b.onesb
    FH = 4096
    NFC = FH // 128
    h_guard = [None]
    with ExitStack() as fes:
        sb = lambda n, sh, dt: fes.enter_context(nc.sbuf_tensor(n, sh, dt))
        psn = lambda n, sh: fes.enter_context(nc.psum_tensor(n, sh, F32))
        b1c = sb("l0f_b1", [128, E, NFC], F32)
        b2b = sb("l0f_b2", [8, D], BF16)
        grow = sb("l0f_grow", [1, 2, TC], BF16)
        gb = sb("l0f_gb", [128, TC], F32)
        gt1 = sb("l0f_gt1", [128, TC], F32)
        ps_h = psn("l0f_psh", [128, 1024])
        ps_g = psn("l0f_psg", [128, 1024])
        ps_k = psn("l0f_psk", [128, 512])

        def lds(sy):
            b.dsy.dma(sy.dma_start(out=b1c[:, :, :], in_=iaps["w_l0_b1"][:, :, :]))
        b.SY(lds)
        def ldb2(g):
            b.dgp.dma(g.dma_start(out=b2b[:, :], in_=iaps["w_l0_b2"][:, :]))
        b.G(ldb2)

        w1_rot = Rot(1)
        w2_rot = Rot(1)
        xn_rot = Rot(2)
        hst_rot = Rot(2)
        rh = Rot(2)
        rg = Rot(2)
        kg = Rot(2)
        grow_rot = Rot(1)

        dgm0 = b.dgp.n
        for i in range(NTC):
            def t_b2(t, i=i, dgm0=dgm0):
                if i == 0:
                    t.wait_ge(b.dgp.h, dgm0)
                for dc in range(NDC):
                    s = rg.acquire(t)
                    mm = t.matmul(ps_g[:, s * 512:(s + 1) * 512],
                                  b2b[0:8, dc * 128:(dc + 1) * 128],
                                  gates_bf[0:8, i * TC:(i + 1) * TC], start=True, stop=True)
                    b.pe.inc(mm)
                    pem = b.pe.n
                    def evb(v, dc=dc, s=s, pem=pem, i=i):
                        v.wait_ge(b.pe.h, pem)
                        b.dv.inc(v.tensor_copy(g_acc[:, dc, i * TC:(i + 1) * TC],
                                               ps_g[:, s * 512:(s + 1) * 512]))
                    b.V(evb)
                    rg.done(s, b.dv)
            b.T(t_b2)

        for e in range(E):
            p1es = ExitStack()
            w1 = p1es.enter_context(nc.sbuf_tensor(f"l0f_w1_{e}", [128, NDC, FH], BF16))
            xnc = p1es.enter_context(nc.sbuf_tensor(f"l0f_xnc_{e}", [128, 2, NDC, TC], BF16))
            hst = p1es.enter_context(nc.sbuf_tensor(f"l0f_hst_{e}", [128, 2, 8, TC], BF16))
            def ldw1(g, e=e):
                w1_rot.acquire(g)
                b.dgp.dma(g.dma_start(out=w1[:, :, :],
                                      in_=iaps["w_l0_W1"][e].rearrange("(c p) f -> p c f", p=128)))
            b.G(ldw1)
            dgm = b.dgp.n
            for i in range(NTC):
                xpar = [None]
                def ldx(sy, i=i, xpar=xpar):
                    xpar[0] = xn_rot.acquire(sy)
                    b.dsy.dma(sy.dma_start(out=xnc[:, xpar[0]], in_=xn_r[:, :, i * TC:(i + 1) * TC]))
                b.SY(ldx)
                xpar = xpar[0]
                dsym = b.dsy.n
                for hg in range(4):
                    hpar = [None]
                    def pickh(sc, hpar=hpar):
                        hpar[0] = hst_rot.acquire(sc)
                    b.SC(pickh)
                    hpar = hpar[0]
                    def t_h(t, e=e, i=i, hg=hg, xpar=xpar, dgm=dgm, dsym=dsym, hpar=hpar):
                        if hg == 0:
                            t.wait_ge(b.dgp.h, dgm)
                            t.wait_ge(b.dsy.h, dsym)
                        if h_guard[0] is not None and hg == 0 and i == 0:
                            t.wait_ge(h_guard[0][0].h, h_guard[0][1])
                        for hh in range(8):
                            hc = hg * 8 + hh
                            s = rh.acquire(t)
                            mm = mmgroup(t, ps_h[:, s * 512:(s + 1) * 512],
                                         [(w1[:, c, hc * 128:(hc + 1) * 128], xnc[:, xpar, c, :])
                                          for c in range(NDC)])
                            b.pe.inc(mm)
                            pem = b.pe.n
                            def evh(sc, hh=hh, hc=hc, s=s, pem=pem, e=e, hpar=hpar):
                                sc.wait_ge(b.pe.h, pem)
                                b.ac.inc(sc.activation(hst[:, hpar, hh, :],
                                                       ps_h[:, s * 512:(s + 1) * 512],
                                                       AF.Gelu, bias=b1c[:, e, hc:hc + 1]))
                            b.SC(evh)
                            rh.done(s, b.ac)
                    b.T(t_h)
                    acm = b.ac.n
                    def st_h(sy, i=i, hg=hg, hpar=hpar, acm=acm):
                        sy.wait_ge(b.ac.h, acm)
                        b.dsy.dma(sy.dma_start(
                            out=h_r[:, hg * 8:(hg + 1) * 8, i * TC:(i + 1) * TC],
                            in_=hst[:, hpar]))
                    b.SY(st_h)
                    hst_rot.done(hpar, b.dsy)
                xn_rot.done(xpar, b.pe)
            p1_store_mark = b.dsy.n
            w1_rot.done(0, b.pe)
            b.barrier()
            p1es.close()
            if DEBUG_DUMP and e == 0:
                dbg_hh = nc.dram_tensor("dbg_hh", [4096, S], BF16, kind="ExternalOutput").ap()
                def dmp_hh(sy, p1_store_mark=p1_store_mark):
                    sy.wait_ge(b.dsy.h, p1_store_mark)
                    b.dsy.dma(sy.dma_start(out=dbg_hh[:, :], in_=h_hbm[:, :]))
                b.SY(dmp_hh)

            p2es = ExitStack()
            w2 = p2es.enter_context(nc.sbuf_tensor(f"l0f_w2_{e}", [128, NFC, D], BF16))
            hld = p2es.enter_context(nc.sbuf_tensor(f"l0f_hld_{e}", [128, 3, 16, TC], BF16))
            hld_rot = Rot(3)
            def ldw2(g, e=e):
                w2_rot.acquire(g)
                b.dgp.dma(g.dma_start(out=w2[:, :, :],
                                      in_=iaps["w_l0_W2"][e].rearrange("(c p) d -> p c d", p=128)))
            b.G(ldw2)
            dgm2 = b.dgp.n
            for i in range(NTC):
                hpar = [None, None]
                def ldh(sy, i=i, hpar=hpar, p1_store_mark=p1_store_mark):
                    sy.wait_ge(b.dsy.h, p1_store_mark)
                    hpar[0] = hld_rot.acquire(sy)
                    b.dsy.dma(sy.dma_start(out=hld[:, hpar[0]],
                                           in_=h_r[:, 0:16, i * TC:(i + 1) * TC]))
                    hpar[1] = hld_rot.acquire(sy)
                    b.dsy.dma(sy.dma_start(out=hld[:, hpar[1]],
                                           in_=h_r[:, 16:32, i * TC:(i + 1) * TC]))
                b.SY(ldh)
                dsym = b.dsy.n

                gs2 = [None]
                def ldgr(sy, e=e, i=i, gs2=gs2):
                    gs2[0] = grow_rot.acquire(sy)
                    b.dsy.dma(sy.dma_start(out=grow[:, gs2[0], :],
                                           in_=gates_bf[e:e + 1, i * TC:(i + 1) * TC]))
                b.SY(ldgr)
                gs2 = gs2[0]
                grm = b.dsy.n
                def t_k(t, e=e, i=i, grm=grm, gs2=gs2):
                    t.wait_ge(b.dsy.h, grm)
                    kg.acquire(t)
                    mm = t.matmul(ps_k[:, 0:512], onesb[0:1, 0:128],
                                  grow[0:1, gs2, :], start=True, stop=True)
                    b.pe.inc(mm)
                b.T(t_k)
                grow_rot.done(gs2, b.pe)
                pem = b.pe.n
                def ev_k(v, pem=pem):
                    v.wait_ge(b.pe.h, pem)
                    b.dv.inc(v.tensor_copy(gb[:, :], ps_k[:, 0:512]))
                b.V(ev_k)
                kg.done(0, b.dv)
                gbm = b.dv.n

                def t_d(t, e=e, i=i, hpar=hpar, dgm2=dgm2, dsym=dsym):
                    t.wait_ge(b.dgp.h, dgm2)
                    t.wait_ge(b.dsy.h, dsym)
                    for dc in range(NDC):
                        s = rg.acquire(t)
                        pairs = [(w2[:, fc, dc * 128:(dc + 1) * 128],
                                  hld[:, hpar[fc // 16], fc % 16, :])
                                 for fc in range(NFC)]
                        mm = mmgroup(t, ps_g[:, s * 512:(s + 1) * 512], pairs)
                        b.pe.inc(mm)
                        pem = b.pe.n
                        def evg(v, dc=dc, s=s, pem=pem, e=e, i=i, gbm=gbm):
                            v.wait_ge(b.pe.h, pem)
                            v.wait_ge(b.dv.h, gbm)
                            v.tensor_tensor(gt1[:, :], ps_g[:, s * 512:(s + 1) * 512],
                                            gb[:, :], ALU.mult)
                            b.dv.inc(v.tensor_tensor(g_acc[:, dc, i * TC:(i + 1) * TC],
                                                     gt1[:, :],
                                                     g_acc[:, dc, i * TC:(i + 1) * TC],
                                                     ALU.add))
                        b.V(evg)
                        rg.done(s, b.dv)
                b.T(t_d)
                hld_rot.done(hpar[0], b.pe)
                hld_rot.done(hpar[1], b.pe)
            w2_rot.done(0, b.pe)
            grow_rot.done(0, b.pe)
            h_guard[0] = (b.pe, b.pe.n)
            b.barrier()
            p2es.close()
        b.barrier()
    emit_moe_join(b, x_src, x_dst, s_hbm, g_acc)
    bes.close()


def emit_routing_sparsemixer(b, logits_sb, scr, gates_bf, ident, ps_t):
    sh = (128, NTILE, E)
    m1b = scr["m1b"][:, :, :]
    m2b = scr["m2b"][:, :, :]
    m1 = scr["m1"][:, :, :]
    m2 = scr["m2"][:, :, :]
    oh1 = scr["oh1"][:, :, :]
    oh2 = scr["oh2"][:, :, :]
    t1 = scr["t1"][:, :, :]
    t2 = scr["t2"][:, :, :]
    e1 = scr["e1"][:, :, :]
    k1 = scr["k1"][:, :, :]
    k2 = scr["k2"][:, :, :]
    d1 = scr["d1"][:, :, :]
    lg = logits_sb[:, :, :]

    emit_top2(b, logits_sb, scr)
    dvm = b.dv.n
    def sA(sc, dvm=dvm):
        sc.wait_ge(b.dv.h, dvm)
        b.ac.inc(sc.activation(t2, lg, AF.Abs))
    b.SC(sA)
    acm = b.ac.n
    def v1(v, acm=acm):
        v.wait_ge(b.ac.h, acm)
        v.tensor_tensor(d1, lg, m1b.broadcast_to(sh), ALU.subtract)
        v.tensor_tensor(k1, t2, m1b.broadcast_to(sh), ALU.max)
        v.scalar_tensor_tensor(k1, k1, 0.02, d1, ALU.mult, ALU.add)
        v.tensor_single_scalar(k1, k1, 0.0, ALU.is_ge)
        v.tensor_tensor(e1, lg, m2b.broadcast_to(sh), ALU.subtract)
        v.tensor_tensor(k2, t2, m2b.broadcast_to(sh), ALU.max)
        v.scalar_tensor_tensor(k2, k2, 0.02, e1, ALU.mult, ALU.add)
        v.tensor_single_scalar(k2, k2, 0.0, ALU.is_ge)
        v.tensor_scalar(t1, oh1, -1.0, 1.0, ALU.mult, ALU.add)
        v.tensor_mul(k2, k2, t1)
        b.dv.inc(v.tensor_single_scalar(e1, e1, 0.0, ALU.min))
    b.V(v1)
    dvm = b.dv.n
    def sE(sc, dvm=dvm):
        sc.wait_ge(b.dv.h, dvm)
        sc.activation(d1, d1, AF.Exp)
        b.ac.inc(sc.activation(e1, e1, AF.Exp))
    b.SC(sE)
    acm = b.ac.n
    def v2(v, acm=acm):
        v.wait_ge(b.ac.h, acm)
        v.tensor_mul(d1, d1, k1)
        v.tensor_mul(e1, e1, k2)
        v.tensor_reduce(m1, d1, AX, ALU.add)
        b.dv.inc(v.tensor_reduce(m2, e1, AX, ALU.add))
    b.V(v2)
    bounce_copy(b, [(m1b, m1), (m2b, m2)])
    def v3(v):
        v.reciprocal(m1, m1b)
        b.dv.inc(v.reciprocal(m2, m2b))
    b.V(v3)
    bounce_copy(b, [(m1b, m1), (m2b, m2)])
    def v4(v):
        v.tensor_mul(t1, oh1, m1b.broadcast_to(sh))
        v.tensor_mul(t2, oh2, m2b.broadcast_to(sh))
        b.dv.inc(v.tensor_add(e1, t1, t2))
    b.V(v4)
    _gates_transpose(b, e1, gates_bf, ident, ps_t)


def emit_block2(b, iaps, x_src, x_dst, xn_hbm, s_hbm, h_hbm, y_hbm,
                ident, identb, ones, rows):
    nc = b.nc
    xn_r = xn_hbm.rearrange("(c p) t -> p c t", p=128)
    s_r = s_hbm.rearrange("(c p) t -> p c t", p=128)
    y_r = y_hbm.rearrange("(c p) t -> p c t", p=128)
    lat_r = h_hbm.rearrange("(c p) t -> p c t", p=128)
    lat_rows = h_hbm

    bes = ExitStack()
    gates_bf = bes.enter_context(nc.sbuf_tensor("l2_gbf", [8, S], BF16))

    with ExitStack() as res_:
        sb = lambda n, sh, dt: res_.enter_context(nc.sbuf_tensor(n, sh, dt))
        wr_sb = sb("l2_wr", [128, NDC, E], F32)
        logits_sb = sb("l2_lg", [128, NTILE, E], F32)
        def ldr(sy):
            b.dsy.dma(sy.dma_start(out=wr_sb[:, :, :],
                                   in_=iaps["w_l2_Wr"].rearrange("(c p) e -> p c e", p=128)))
        b.SY(ldr)
        b.barrier()
        emit_stage1(b, x_src, xn_hbm, ones, rows, wr_sb=wr_sb, logits_sb=logits_sb)
        scr = {k: sb("l2_scr_" + k, [128, NTILE, E], F32)
               for k in ("oh1", "oh2", "t1", "t2", "e1", "k1", "k2", "d1")}
        for k in ("m1", "m2", "m1b", "m2b"):
            scr[k] = sb("l2_" + k, [128, NTILE, 1], F32)
        ps_t = res_.enter_context(nc.psum_tensor("l2_pst", [128, 2048], F32))
        emit_routing_sparsemixer(b, logits_sb, scr, gates_bf, ident, ps_t)
        b.barrier()

    with ExitStack() as aes:
        sb = lambda n, sh, dt: aes.enter_context(nc.sbuf_tensor(n, sh, dt))
        wq = sb("l2a_wq", [128, NDC, D], BF16)
        wk = sb("l2a_wk", [128, NDC, D], BF16)
        wv = sb("l2a_wv", [128, NDC, D], BF16)
        xnc = sb("l2a_xnc", [128, 2, NDC, TC], BF16)
        lst = sb("l2a_lst", [128, 2, NDC, TC], BF16)
        ps_a = aes.enter_context(nc.psum_tensor("l2a_ps", [128, 1024], F32))
        def ldw(g):
            b.dgp.dma(g.dma_start(out=wq[:, :, :], in_=iaps["w_l2_Wqd"].rearrange("(c p) o -> p c o", p=128)))
            b.dgp.dma(g.dma_start(out=wk[:, :, :], in_=iaps["w_l2_Wkd"].rearrange("(c p) o -> p c o", p=128)))
            b.dgp.dma(g.dma_start(out=wv[:, :, :], in_=iaps["w_l2_Wvd"].rearrange("(c p) o -> p c o", p=128)))
        b.G(ldw)
        b.barrier()
        xn_rot = Rot(2)
        lst_rot = Rot(2)
        ra = Rot(2)
        for i in range(NTC):
            par = [None]
            def ld(sy, i=i, par=par):
                par[0] = xn_rot.acquire(sy)
                b.dsy.dma(sy.dma_start(out=xnc[:, par[0]], in_=xn_r[:, :, i * TC:(i + 1) * TC]))
            b.SY(ld)
            par = par[0]
            dsym = b.dsy.n
            for pi, wmat in ((0, wq), (1, wk), (2, wv)):
                lpar = [None]
                def pickl(v, lpar=lpar):
                    lpar[0] = lst_rot.acquire(v)
                b.V(pickl)
                lpar = lpar[0]
                def t_l(t, pi=pi, wmat=wmat, par=par, dsym=dsym, lpar=lpar):
                    if pi == 0:
                        t.wait_ge(b.dsy.h, dsym)
                    for oc in range(NDC):
                        s = ra.acquire(t)
                        mm = mmgroup(t, ps_a[:, s * 512:(s + 1) * 512],
                                     [(wmat[:, c, oc * 128:(oc + 1) * 128], xnc[:, par, c, :])
                                      for c in range(NDC)])
                        b.pe.inc(mm)
                        pem = b.pe.n
                        def evl(v, oc=oc, s=s, pem=pem, lpar=lpar):
                            v.wait_ge(b.pe.h, pem)
                            b.dv.inc(v.tensor_copy(lst[:, lpar, oc, :], ps_a[:, s * 512:(s + 1) * 512]))
                        b.V(evl)
                        ra.done(s, b.dv)
                b.T(t_l)
                dvm = b.dv.n
                def st_l(sy, i=i, pi=pi, lpar=lpar, dvm=dvm):
                    sy.wait_ge(b.dv.h, dvm)
                    b.dsy.dma(sy.dma_start(out=lat_r[:, pi * 8:(pi + 1) * 8, i * TC:(i + 1) * TC],
                                           in_=lst[:, lpar]))
                b.SY(st_l)
                lst_rot.done(lpar, b.dsy)
            xn_rot.done(par, b.pe)
        b.barrier()

    with ExitStack() as hes:
        sb = lambda n, sh, dt: hes.enter_context(nc.sbuf_tensor(n, sh, dt))
        psn = lambda n, sh: hes.enter_context(nc.psum_tensor(n, sh, F32))
        upq = sb("l2b_upq", [64, 96], BF16)
        upk = sb("l2b_upk", [64, 96], BF16)
        upv = sb("l2b_upv", [64, 64], BF16)
        cosq = sb("l2b_cosq", [32, S], F32)
        sinq = sb("l2b_sinq", [32, S], F32)
        cosk = sb("l2b_cosk", [32, S], F32)
        sink = sb("l2b_sink", [32, S], F32)
        msk = sb("l2b_msk", [128, 4, TC], BF16)
        qlat = sb("l2b_qlat", [64, S], BF16)
        klat = sb("l2b_klat", [64, S], BF16)
        vlat = sb("l2b_vlat", [64, S], BF16)
        q_aug = sb("l2b_qaug", [128, S], BF16)
        k_aug = sb("l2b_kaug", [128, S], BF16)
        vsb = sb("l2b_vsb", [64, S], BF16)
        vaug = sb("l2b_vaug", [128, NTILE, 65], BF16)
        qsq = sb("l2b_qsq", [64, S], F32)
        rsb = sb("l2b_rsb", [1, 2 * S], F32)
        rtmp = sb("l2b_rt", [32, 2, TC], F32)
        psb = sb("l2b_psb", [128, 2, TC], BF16)
        ytmp = sb("l2b_yt", [64, TC], F32)
        dtmp = sb("l2b_dt", [64, TC], F32)
        dtmp2 = sb("l2b_dt2", [64, TC], F32)
        yout = sb("l2b_yo", [64, 2, TC], BF16)
        ps_s = psn("l2b_pss", [128, 1024])
        ps_vt = hes.enter_context(nc.psum_tensor("l2b_psvt", [128, 2, 1024], BF16))
        ps_o = psn("l2b_pso", [65, 512])
        ps_u = psn("l2b_psu", [96, 512])
        ps_bc = psn("l2b_psbc", [64, 512])
        ps_r = psn("l2b_psr", [1, 512])

        def ldc(g):
            b.dgp.dma(g.dma_start(out=upq[:, :], in_=iaps["w_l2_upq"][:, :]))
            b.dgp.dma(g.dma_start(out=upk[:, :], in_=iaps["w_l2_upk"][:, :]))
            b.dgp.dma(g.dma_start(out=upv[:, :], in_=iaps["w_l2_upv"][:, :]))
            b.dgp.dma(g.dma_start(out=msk[:, :, :], in_=iaps["c_l2mask"][:, :, :]))
        b.G(ldc)
        def ldc2(sy):
            b.dsy.dma(sy.dma_start(out=cosq[:, :], in_=iaps["c_l2cosq"][:, :]))
            b.dsy.dma(sy.dma_start(out=sinq[:, :], in_=iaps["c_l2sinq"][:, :]))
            b.dsy.dma(sy.dma_start(out=cosk[:, :], in_=iaps["c_l2cosk"][:, :]))
            b.dsy.dma(sy.dma_start(out=sink[:, :], in_=iaps["c_l2sink"][:, :]))
        b.SY(ldc2)
        b.barrier()
        def v_rot(v):
            v.tensor_scalar_mul(upq[:, 64:96:2], upq[:, 33:64:2], -1.0)
            v.tensor_copy(upq[:, 65:96:2], upq[:, 32:64:2])
            v.tensor_scalar_mul(upk[:, 64:96:2], upk[:, 33:64:2], -1.0)
            b.dv.inc(v.tensor_copy(upk[:, 65:96:2], upk[:, 32:64:2]))
        b.V(v_rot)
        b.barrier()

        lat_rot = Rot(1)
        rs = Rot(2)
        rp = Rot(2)
        ru = Rot(1)
        yrot = Rot(2)
        for h in range(16):
            def ldl(sy, h=h):
                lat_rot.acquire(sy)
                b.dsy.dma(sy.dma_start(out=qlat[:, :], in_=lat_rows[h * 64:(h + 1) * 64, :]))
                b.dsy.dma(sy.dma_start(out=klat[:, :], in_=lat_rows[1024 + h * 64:1024 + (h + 1) * 64, :]))
                b.dsy.dma(sy.dma_start(out=vlat[:, :], in_=lat_rows[2048 + h * 64:2048 + (h + 1) * 64, :]))
            b.SY(ldl)
            dsym = b.dsy.n

            for pi in range(8):
                sl = slice(pi * TC, (pi + 1) * TC)
                def t_up(t, pi=pi, dsym=dsym):
                    if pi == 0:
                        t.wait_ge(b.dsy.h, dsym)
                    ru.acquire(t)
                    mm = t.matmul(ps_u[:, :], upq[:, :], qlat[:, pi * TC:(pi + 1) * TC],
                                  start=True, stop=True)
                    b.pe.inc(mm)
                b.T(t_up)
                pem = b.pe.n
                def ev_q(sc, pi=pi, pem=pem, sl=sl):
                    sc.wait_ge(b.pe.h, pem)
                    b.ac.inc(sc.activation(q_aug[0:32, sl], ps_u[0:32, :], AF.Copy, scale=0.125))
                b.SC(ev_q)
                acm = b.ac.n
                def ev_qe(v, pi=pi, sl=sl, acm=acm):
                    v.wait_ge(b.ac.h, acm)
                    v.tensor_tensor(rtmp[:, 0, :], ps_u[32:64, :], cosq[:, sl], ALU.mult)
                    v.tensor_tensor(rtmp[:, 1, :], ps_u[64:96, :], sinq[:, sl], ALU.mult)
                    b.dv.inc(v.tensor_add(q_aug[32:64, sl], rtmp[:, 0, :], rtmp[:, 1, :]))
                b.V(ev_qe)
                ru.done(0, b.dv)
                def t_upk(t, pi=pi):
                    ru.acquire(t)
                    mm = t.matmul(ps_u[:, :], upk[:, :], klat[:, pi * TC:(pi + 1) * TC],
                                  start=True, stop=True)
                    b.pe.inc(mm)
                b.T(t_upk)
                pem = b.pe.n
                def ev_k(sc, pi=pi, pem=pem, sl=sl):
                    sc.wait_ge(b.pe.h, pem)
                    b.ac.inc(sc.copy(k_aug[0:32, sl], ps_u[0:32, :]))
                b.SC(ev_k)
                acm = b.ac.n
                def ev_ke(v, pi=pi, sl=sl, acm=acm):
                    v.wait_ge(b.ac.h, acm)
                    v.tensor_tensor(rtmp[:, 0, :], ps_u[32:64, :], cosk[:, sl], ALU.mult)
                    v.tensor_tensor(rtmp[:, 1, :], ps_u[64:96, :], sink[:, sl], ALU.mult)
                    b.dv.inc(v.tensor_add(k_aug[32:64, sl], rtmp[:, 0, :], rtmp[:, 1, :]))
                b.V(ev_ke)
                ru.done(0, b.dv)

            rr_rot = Rot(1)
            for which, src_aug, off in (("q", q_aug, 0), ("k", k_aug, S)):
                dvm0 = b.dv.n
                pem0 = b.pe.n
                def s_sq2(sc, src_aug=src_aug, dvm0=dvm0, pem0=pem0):
                    sc.wait_ge(b.dv.h, dvm0)
                    sc.wait_ge(b.pe.h, pem0)
                    b.ac.inc(sc.activation(qsq[:, :], src_aug[0:64, :], AF.Square))
                b.SC(s_sq2)
                acm = b.ac.n
                for pi in range(8):
                    def t_r(t, pi=pi, acm=acm):
                        if pi == 0:
                            t.wait_ge(b.ac.h, acm)
                        rr_rot.acquire(t)
                        mm = t.matmul(ps_r[:, :], ones[0:64, 0:1], qsq[:, pi * TC:(pi + 1) * TC],
                                      start=True, stop=True)
                        b.pe.inc(mm)
                    b.T(t_r)
                    pem = b.pe.n
                    def ev_r(v, pi=pi, pem=pem, off=off):
                        v.wait_ge(b.pe.h, pem)
                        b.dv.inc(v.tensor_copy(rsb[0:1, off + pi * TC:off + (pi + 1) * TC], ps_r[:, :]))
                    b.V(ev_r)
                    rr_rot.done(0, b.dv)
            dvm = b.dv.n
            def v_km(v, dvm=dvm):
                b.dv.inc(v.tensor_reduce(rows[0:1, 0:1], rsb[0:1, S:2 * S], AX, ALU.max))
            b.V(v_km)
            bounce_copy(b, [(rows[0:1, 4:5], rows[0:1, 0:1])])
            dvm = b.dv.n
            def s_nb(sc, dvm=dvm):
                sc.wait_ge(b.dv.h, dvm)
                sc.activation(rows[0:1, 8:9], rows[0:1, 4:5], AF.Sqrt)
                b.ac.inc(sc.activation(rsb[0:1, 0:S], rsb[0:1, 0:S], AF.Sqrt))
            b.SC(s_nb)
            acm = b.ac.n
            def v_nb(v, acm=acm):
                v.wait_ge(b.ac.h, acm)
                b.dv.inc(v.tensor_scalar(q_aug[64:65, :], rsb[0:1, 0:S],
                                         rows[0:1, 8:9], -1.01, ALU.mult, ALU.mult))
            b.V(v_nb)
            def v_ones(v):
                b.dv.inc(v.memset(k_aug[64:65, :], 1.0))
            b.V(v_ones)

            for pi in range(8):
                def t_uv(t, pi=pi):
                    ru.acquire(t)
                    mm = t.matmul(ps_u[0:64, :], upv[:, :], vlat[:, pi * TC:(pi + 1) * TC],
                                  start=True, stop=True)
                    b.pe.inc(mm)
                b.T(t_uv)
                pem = b.pe.n
                def ev_uv(v, pi=pi, pem=pem):
                    v.wait_ge(b.pe.h, pem)
                    b.dv.inc(v.tensor_copy(vsb[:, pi * TC:(pi + 1) * TC], ps_u[0:64, :]))
                b.V(ev_uv)
                ru.done(0, b.dv)

            def v_vz(v):
                b.dv.inc(v.memset(vaug[:, :, 64:65], 1.0))
            b.V(v_vz)
            dvm = b.dv.n
            rv = Rot(2)
            for kt in range(NTILE):
                def t_v(t, kt=kt, dvm=dvm):
                    if kt == 0:
                        t.wait_ge(b.dv.h, dvm)
                    s = rv.acquire(t)
                    mm = t.transpose(ps_vt[:, s, 0:64],
                                     vsb[:, kt * 128:(kt + 1) * 128], identb[0:64, 0:64])
                    b.pe.inc(mm)
                    pem = b.pe.n
                    def ev_v(v, kt=kt, s=s, pem=pem):
                        v.wait_ge(b.pe.h, pem)
                        b.dv.inc(v.tensor_copy(vaug[:, kt, 0:64], ps_vt[:, s, 0:64]))
                    b.V(ev_v)
                    rv.done(s, b.dv)
                b.T(t_v)

            if DEBUG_DUMP and h == 0:
                dbg_qa = nc.dram_tensor("dbg_qa", [128, S], BF16, kind="ExternalOutput").ap()
                dbg_ka = nc.dram_tensor("dbg_ka", [128, S], BF16, kind="ExternalOutput").ap()
                dbg_va = nc.dram_tensor("dbg_va", [128, NTILE * 65], BF16, kind="ExternalOutput").ap()
                dvq = b.dv.n
                def dmpq(sy, dvq=dvq):
                    sy.wait_ge(b.dv.h, dvq)
                    b.dsy.dma(sy.dma_start(out=dbg_qa[:, :], in_=q_aug[:, :]))
                    b.dsy.dma(sy.dma_start(out=dbg_ka[:, :], in_=k_aug[:, :]))
                    b.dsy.dma(sy.dma_start(out=dbg_va.rearrange("p (a c) -> p a c", c=65), in_=vaug[:, :, :]))
                b.SY(dmpq)
            dvm_all = b.dv.n
            po_rot = globals().get("_l2_po_rot")
            if po_rot is None or h == 0:
                po_rot = Rot(1)
                globals()["_l2_po_rot"] = po_rot
            for qp in range(8):
                nkt = 4 * qp + 4
                pend = [None]
                def emit_av(t, info, nkt=nkt):
                    kt, sl2, acm_, dvm_ = info
                    t.wait_ge(b.ac.h, acm_)
                    if dvm_:
                        t.wait_ge(b.dv.h, dvm_)
                    if kt == 0:
                        po_rot.acquire(t)
                    mm = t.matmul(ps_o[:, :], vaug[:, kt, :], psb[:, sl2, :],
                                  start=(kt == 0), stop=(kt == nkt - 1))
                    b.pe.inc(mm)
                for kt in range(nkt):
                    sref = [None]
                    def t_sw(t, sref=sref, qp=qp, kt=kt, dvm_all=dvm_all):
                        if qp == 0 and kt == 0:
                            t.wait_ge(b.dv.h, dvm_all)
                        s = rs.acquire(t)
                        sref[0] = s
                        mm = t.matmul(ps_s[:, s * 512:(s + 1) * 512],
                                      k_aug[0:65, kt * 128:(kt + 1) * 128],
                                      q_aug[0:65, qp * TC:(qp + 1) * TC],
                                      start=True, stop=True)
                        b.pe.inc(mm)
                        if pend[0] is not None:
                            emit_av(t, pend[0])
                            pend[0] = None
                    b.T(t_sw)
                    s = sref[0]
                    pem = b.pe.n
                    sl2ref = [None]
                    def s_e(sc, s=s, pem=pem, sl2ref=sl2ref):
                        sc.wait_ge(b.pe.h, pem)
                        sl2 = rp.acquire(sc)
                        sl2ref[0] = sl2
                        b.ac.inc(sc.activation(psb[:, sl2, :], ps_s[:, s * 512:(s + 1) * 512], AF.Exp))
                    b.SC(s_e)
                    sl2 = sl2ref[0]
                    rs.done(s, b.ac)
                    acm_ = b.ac.n
                    dvm_ = 0
                    r = kt - 4 * qp
                    if r >= 0:
                        def v_m(v, sl2=sl2, r=r, acm_=acm_):
                            v.wait_ge(b.ac.h, acm_)
                            b.dv.inc(v.tensor_mul(psb[:, sl2, :], psb[:, sl2, :], msk[:, r, :]))
                        b.V(v_m)
                        dvm_ = b.dv.n
                    pend[0] = (kt, sl2, acm_, dvm_)
                    rp.done(sl2, b.pe)
                def t_fl(t):
                    if pend[0] is not None:
                        emit_av(t, pend[0])
                        pend[0] = None
                b.T(t_fl)

                pem = b.pe.n
                def s_den(sc, pem=pem):
                    sc.wait_ge(b.pe.h, pem)
                    sc.copy(rows[0:1, 16:528], ps_o[64:65, :])
                    b.ac.inc(sc.copy(ytmp[:, :], ps_o[0:64, :]))
                b.SC(s_den)
                acm = b.ac.n
                def t_bc(t, acm=acm):
                    t.wait_ge(b.ac.h, acm)
                    mm = t.matmul(ps_bc[:, :], ones[0:1, 0:64], rows[0:1, 16:528],
                                  start=True, stop=True)
                    b.pe.inc(mm)
                b.T(t_bc)
                pem = b.pe.n
                def v_rc(v, pem=pem):
                    v.wait_ge(b.pe.h, pem)
                    b.dv.inc(v.reciprocal(dtmp[:, :], ps_bc[:, :]))
                b.V(v_rc)
                dvm = b.dv.n
                def s_rb(sc, dvm=dvm):
                    sc.wait_ge(b.dv.h, dvm)
                    b.ac.inc(sc.copy(dtmp2[:, :], dtmp[:, :]))
                b.SC(s_rb)
                acm = b.ac.n
                ypar = [None]
                def v_y(v, acm=acm, ypar=ypar):
                    v.wait_ge(b.ac.h, acm)
                    ypar[0] = yrot.acquire(v)
                    b.dv.inc(v.tensor_mul(yout[:, ypar[0], :], ytmp[:, :], dtmp2[:, :]))
                b.V(v_y)
                po_rot.done(0, b.dv)
                ypar = ypar[0]
                dvm = b.dv.n
                def st_y(sy, h=h, qp=qp, ypar=ypar, dvm=dvm):
                    sy.wait_ge(b.dv.h, dvm)
                    b.dsy.dma(sy.dma_start(out=y_hbm[h * 64:(h + 1) * 64, qp * TC:(qp + 1) * TC],
                                           in_=yout[:, ypar, :]))
                b.SY(st_y)
                yrot.done(ypar, b.dsy)
            lat_rot.done(0, b.pe)
        b.barrier()

    with ExitStack() as ces:
        sb = lambda n, sh, dt: ces.enter_context(nc.sbuf_tensor(n, sh, dt))
        wc = sb("l2c_wc", [128, NDC, D], BF16)
        ych = sb("l2c_y", [128, 2, NDC, TC], BF16)
        ssb = sb("l2c_s", [128, 2, NDC, TC], BF16)
        ps_a = ces.enter_context(nc.psum_tensor("l2c_ps", [128, 1024], F32))
        def ldw(g):
            b.dgp.dma(g.dma_start(out=wc[:, :, :], in_=iaps["w_l2_Wc"].rearrange("(c p) o -> p c o", p=128)))
        b.G(ldw)
        b.barrier()
        y_rot = Rot(2)
        s_rot = Rot(2)
        ra = Rot(2)
        for i in range(NTC):
            par = [None]
            def ld(sy, i=i, par=par):
                par[0] = y_rot.acquire(sy)
                b.dsy.dma(sy.dma_start(out=ych[:, par[0]], in_=y_r[:, :, i * TC:(i + 1) * TC]))
            b.SY(ld)
            par = par[0]
            dsym = b.dsy.n
            spar = [None]
            def picks(v, spar=spar):
                spar[0] = s_rot.acquire(v)
            b.V(picks)
            spar = spar[0]
            def t_c(t, par=par, dsym=dsym, spar=spar):
                t.wait_ge(b.dsy.h, dsym)
                for oc in range(NDC):
                    s = ra.acquire(t)
                    mm = mmgroup(t, ps_a[:, s * 512:(s + 1) * 512],
                                 [(wc[:, c, oc * 128:(oc + 1) * 128], ych[:, par, c, :])
                                  for c in range(NDC)])
                    b.pe.inc(mm)
                    pem = b.pe.n
                    def evc(v, oc=oc, s=s, pem=pem, spar=spar):
                        v.wait_ge(b.pe.h, pem)
                        b.dv.inc(v.tensor_copy(ssb[:, spar, oc, :], ps_a[:, s * 512:(s + 1) * 512]))
                    b.V(evc)
                    ra.done(s, b.dv)
            b.T(t_c)
            y_rot.done(par, b.pe)
            dvm = b.dv.n
            def st_s(sy, i=i, spar=spar, dvm=dvm):
                sy.wait_ge(b.dv.h, dvm)
                b.dsy.dma(sy.dma_start(out=s_r[:, :, i * TC:(i + 1) * TC], in_=ssb[:, spar]))
            b.SY(st_s)
            s_rot.done(spar, b.dsy)
        b.barrier()

    g_acc = bes.enter_context(nc.sbuf_tensor("l2_gacc", [128, NDC, S], BF16))
    onesb = b.onesb
    FH = 2048
    NFC = FH // 128
    h_guard = [None]
    with ExitStack() as fes:
        sb = lambda n, sh, dt: fes.enter_context(nc.sbuf_tensor(n, sh, dt))
        psn = lambda n, sh: fes.enter_context(nc.psum_tensor(n, sh, F32))
        grow = sb("l2f_grow", [1, S], BF16)
        gb = sb("l2f_gb", [128, TC], F32)
        gt1 = sb("l2f_gt1", [128, TC], F32)
        st1 = sb("l2f_st1", [128, 2, TC], F32)
        ps_h = psn("l2f_psh", [128, 1024])
        ps_p = psn("l2f_psp", [128, 1024])
        ps_g = psn("l2f_psg", [128, 1024])
        ps_k = psn("l2f_psk", [128, 512])

        w1_rot = Rot(1)
        w2_rot = Rot(1)
        xn_rot = Rot(2)
        hst_rot = Rot(2)
        rh = Rot(2)
        rp2 = Rot(2)
        st1_rot = Rot(2)
        rg = Rot(2)
        kg = Rot(1)
        grow_rot = Rot(1)
        for e in range(E):
            p1es = ExitStack()
            wg1 = p1es.enter_context(nc.sbuf_tensor(f"l2f_wg_{e}", [128, NDC, FH], BF16))
            wu1 = p1es.enter_context(nc.sbuf_tensor(f"l2f_wu_{e}", [128, NDC, FH], BF16))
            xnc = p1es.enter_context(nc.sbuf_tensor(f"l2f_xnc_{e}", [128, 2, NDC, TC], BF16))
            hst = p1es.enter_context(nc.sbuf_tensor(f"l2f_hst_{e}", [128, 2, 8, TC], BF16))
            def ldw1(g, e=e):
                w1_rot.acquire(g)
                b.dgp.dma(g.dma_start(out=wg1[:, :, :],
                                      in_=iaps["w_l2_Wg"][e].rearrange("(c p) f -> p c f", p=128)))
                b.dgp.dma(g.dma_start(out=wu1[:, :, :],
                                      in_=iaps["w_l2_Wu"][e].rearrange("(c p) f -> p c f", p=128)))
            b.G(ldw1)
            dgm = b.dgp.n
            for i in range(NTC):
                xpar = [None]
                def ldx(sy, i=i, xpar=xpar):
                    xpar[0] = xn_rot.acquire(sy)
                    b.dsy.dma(sy.dma_start(out=xnc[:, xpar[0]], in_=xn_r[:, :, i * TC:(i + 1) * TC]))
                b.SY(ldx)
                xpar = xpar[0]
                dsym = b.dsy.n
                for hg in range(2):
                    hpar = [None]
                    def pickh(sc, hpar=hpar):
                        hpar[0] = hst_rot.acquire(sc)
                    b.SC(pickh)
                    hpar = hpar[0]
                    def t_h(t, e=e, i=i, hg=hg, xpar=xpar, dgm=dgm, dsym=dsym, hpar=hpar):
                        if hg == 0:
                            t.wait_ge(b.dgp.h, dgm)
                            t.wait_ge(b.dsy.h, dsym)
                        if h_guard[0] is not None and hg == 0 and i == 0:
                            t.wait_ge(h_guard[0][0].h, h_guard[0][1])
                        for hh in range(8):
                            hc = hg * 8 + hh
                            s = rh.acquire(t)
                            mm = mmgroup(t, ps_h[:, s * 512:(s + 1) * 512],
                                         [(wg1[:, c, hc * 128:(hc + 1) * 128], xnc[:, xpar, c, :])
                                          for c in range(NDC)])
                            b.pe.inc(mm)
                            s2 = rp2.acquire(t)
                            mm = mmgroup(t, ps_p[:, s2 * 512:(s2 + 1) * 512],
                                         [(wu1[:, c, hc * 128:(hc + 1) * 128], xnc[:, xpar, c, :])
                                          for c in range(NDC)])
                            b.pe.inc(mm)
                            pem = b.pe.n
                            s1ref = [None]
                            def evh(sc, hh=hh, s=s, pem=pem, s1ref=s1ref):
                                sc.wait_ge(b.pe.h, pem)
                                s1s = st1_rot.acquire(sc)
                                s1ref[0] = s1s
                                b.ac.inc(sc.activation(st1[:, s1s, :], ps_h[:, s * 512:(s + 1) * 512], AF.Silu))
                            b.SC(evh)
                            s1s = s1ref[0]
                            rh.done(s, b.ac)
                            acm = b.ac.n
                            def evh2(v, hh=hh, s2=s2, acm=acm, hpar=hpar, s1s=s1s):
                                v.wait_ge(b.ac.h, acm)
                                b.dv.inc(v.tensor_tensor(hst[:, hpar, hh, :], st1[:, s1s, :],
                                                         ps_p[:, s2 * 512:(s2 + 1) * 512], ALU.mult))
                            b.V(evh2)
                            rp2.done(s2, b.dv)
                            st1_rot.done(s1s, b.dv)
                    b.T(t_h)
                    dvm = b.dv.n
                    def st_h(sy, i=i, hg=hg, hpar=hpar, dvm=dvm):
                        sy.wait_ge(b.dv.h, dvm)
                        b.dsy.dma(sy.dma_start(
                            out=lat_r[:, hg * 8:(hg + 1) * 8, i * TC:(i + 1) * TC],
                            in_=hst[:, hpar]))
                    b.SY(st_h)
                    hst_rot.done(hpar, b.dsy)
                xn_rot.done(xpar, b.pe)
            p1_store_mark = b.dsy.n
            w1_rot.done(0, b.pe)
            b.barrier()
            p1es.close()

            p2es = ExitStack()
            w2 = p2es.enter_context(nc.sbuf_tensor(f"l2f_wd_{e}", [128, NFC, D], BF16))
            hld = p2es.enter_context(nc.sbuf_tensor(f"l2f_hld_{e}", [128, 2, NFC, TC], BF16))
            hld_rot = Rot(2)
            def ldw2(g, e=e):
                w2_rot.acquire(g)
                b.dgp.dma(g.dma_start(out=w2[:, :, :],
                                      in_=iaps["w_l2_Wd"][e].rearrange("(c p) d -> p c d", p=128)))
            b.G(ldw2)
            dgm2 = b.dgp.n
            def ldgr(sy, e=e):
                grow_rot.acquire(sy)
                b.dsy.dma(sy.dma_start(out=grow[:, :], in_=gates_bf[e:e + 1, :]))
            b.SY(ldgr)
            grm = b.dsy.n
            for i in range(NTC):
                hpar = [None]
                def ldh(sy, i=i, hpar=hpar, p1_store_mark=p1_store_mark):
                    sy.wait_ge(b.dsy.h, p1_store_mark)
                    hpar[0] = hld_rot.acquire(sy)
                    b.dsy.dma(sy.dma_start(out=hld[:, hpar[0]],
                                           in_=lat_r[:, 0:NFC, i * TC:(i + 1) * TC]))
                b.SY(ldh)
                hpar = hpar[0]
                dsym = b.dsy.n
                def t_k(t, e=e, i=i, grm=grm):
                    t.wait_ge(b.dsy.h, grm)
                    kg.acquire(t)
                    mm = t.matmul(ps_k[:, 0:512], onesb[0:1, 0:128],
                                  grow[0:1, i * TC:(i + 1) * TC], start=True, stop=True)
                    b.pe.inc(mm)
                b.T(t_k)
                pem = b.pe.n
                def ev_k(v, pem=pem):
                    v.wait_ge(b.pe.h, pem)
                    b.dv.inc(v.tensor_copy(gb[:, :], ps_k[:, 0:512]))
                b.V(ev_k)
                kg.done(0, b.dv)
                gbm = b.dv.n
                def t_d(t, e=e, i=i, hpar=hpar, dgm2=dgm2, dsym=dsym):
                    t.wait_ge(b.dgp.h, dgm2)
                    t.wait_ge(b.dsy.h, dsym)
                    for dc in range(NDC):
                        s = rg.acquire(t)
                        mm = mmgroup(t, ps_g[:, s * 512:(s + 1) * 512],
                                     [(w2[:, fc, dc * 128:(dc + 1) * 128], hld[:, hpar, fc, :])
                                      for fc in range(NFC)])
                        b.pe.inc(mm)
                        pem = b.pe.n
                        def evg(v, dc=dc, s=s, pem=pem, e=e, i=i, gbm=gbm):
                            v.wait_ge(b.pe.h, pem)
                            v.wait_ge(b.dv.h, gbm)
                            if e == 0:
                                b.dv.inc(v.tensor_tensor(g_acc[:, dc, i * TC:(i + 1) * TC],
                                                         ps_g[:, s * 512:(s + 1) * 512],
                                                         gb[:, :], ALU.mult))
                            else:
                                v.tensor_tensor(gt1[:, :], ps_g[:, s * 512:(s + 1) * 512],
                                                gb[:, :], ALU.mult)
                                b.dv.inc(v.tensor_tensor(g_acc[:, dc, i * TC:(i + 1) * TC],
                                                         gt1[:, :],
                                                         g_acc[:, dc, i * TC:(i + 1) * TC],
                                                         ALU.add))
                        b.V(evg)
                        rg.done(s, b.dv)
                b.T(t_d)
                hld_rot.done(hpar, b.pe)
            w2_rot.done(0, b.pe)
            grow_rot.done(0, b.pe)
            h_guard[0] = (b.pe, b.pe.n)
            b.barrier()
            p2es.close()
        b.barrier()
    emit_moe_join(b, x_src, x_dst, s_hbm, g_acc)
    bes.close()



def _transpose_pass(b, nc, src_kind, i, pb, tin, tout, ident, src_ap, dst_ap,
                    in_rot, out_rot, ps_rot):
    par = [None]
    def ld(sy):
        par[0] = in_rot.acquire(sy)
        b.dsy.dma(sy.dma_start(out=tin[:, par[0]] if src_kind == "tm" else
                               tin[:, par[0]].rearrange("p (c q) -> p c q", q=128),
                               in_=src_ap))
    b.SY(ld)
    par = par[0]
    dsym = b.dsy.n
    oslot = [None]
    def pickv(v):
        oslot[0] = out_rot.acquire(v)
    b.V(pickv)
    def tp(t):
        t.wait_ge(b.dsy.h, dsym)
        for c in range(NDC):
            s = ps_rot.acquire(t)
            mm = t.transpose(pb[:, s * 512:s * 512 + 128], tin[:, par, c * 128:(c + 1) * 128], ident[:, :])
            b.pe.inc(mm)
            pem = b.pe.n
            def ev(v, c=c, s=s, pem=pem):
                v.wait_ge(b.pe.h, pem)
                b.dv.inc(v.tensor_copy(tout[:, oslot[0], c * 128:(c + 1) * 128],
                                       pb[:, s * 512:s * 512 + 128]))
            b.V(ev)
            ps_rot.done(s, b.dv)
    b.T(tp)
    in_rot.done(par, b.pe)
    oslot = oslot[0]
    dvm = b.dv.n
    def st(sy):
        sy.wait_ge(b.dv.h, dvm)
        b.dsy.dma(sy.dma_start(out=dst_ap,
                               in_=tout[:, oslot].rearrange("p (c q) -> p c q", q=128)
                               if src_kind == "tm" else tout[:, oslot]))
    b.SY(st)
    out_rot.done(oslot, b.dsy)


def build_program(host_arrays, blocks=(0, 1, 2, 3)):
    nc = bass.Bass("TRN2", target_bir_lowering=False, debug=False)
    import itertools
    _cnt = itertools.count()
    _orig_sb = nc.sbuf_tensor
    _orig_ps = nc.psum_tensor
    def _sb(name, shape, dtype, **kw):
        return _orig_sb(f"{name}_{next(_cnt)}", shape, dtype, **kw)
    def _ps(name, shape, dtype, **kw):
        return _orig_ps(f"{name}_{next(_cnt)}", shape, dtype, **kw)
    nc.sbuf_tensor = _sb
    nc.psum_tensor = _ps

    iaps = {}
    for name, arr in host_arrays.items():
        iaps[name] = nc.dram_tensor(name, list(arr.shape), F32, kind="ExternalInput").ap()
    x_in = nc.dram_tensor("x", [S, D], F32, kind="ExternalInput").ap()
    pfx = nc.dram_tensor("pfx", [8, 1], F32, kind="ExternalInput").ap()
    out = nc.dram_tensor("out", [S, D], F32, kind="ExternalOutput").ap()

    xfa = nc.dram_tensor("xfa", [D, S], F32).ap()
    xfb = nc.dram_tensor("xfb", [D, S], F32).ap()
    xn_hbm = nc.dram_tensor("xn_hbm", [D, S], BF16).ap()
    s_hbm = nc.dram_tensor("s_hbm", [D, S], BF16).ap()
    y_hbm = nc.dram_tensor("y_hbm", [D, S], BF16).ap()
    h_hbm = nc.dram_tensor("h_hbm", [4096, S], BF16).ap()
    cc_in = nc.dram_tensor("cc_in", [1, 16], F32).ap()
    cc_out = nc.dram_tensor("cc_out", [8, 16], F32).ap()

    es = ExitStack()
    with es:
        block = es.enter_context(nc.Block())
        b = Bld(nc, es, block)

        ident = b.sb("ident", [128, 128], F32)
        identb = b.sb("identb", [128, 128], BF16)
        onesb = b.sb("onesb", [1, 128], BF16)
        ones = b.sb("ones", [128, 512], F32)
        tri = b.sb("tri", [128, 128], F32)
        rows = b.sb("rows", [1, 2048], F32)
        pfx_sb = b.sb("pfx_sb", [8, 1], F32)
        cpool = b.sb("cpool", [128, len(CONST_POOL)], F32)
        for j, val in enumerate(CONST_POOL):
            nc.const_aps.register(F32, float(val), cpool[:, j:j + 1])

        def ld_const(sy):
            b.dsy.dma(sy.dma_start(out=ident[:, :], in_=iaps["c_ident"][:, :]))
            b.dsy.dma(sy.dma_start(out=ones[:, :], in_=iaps["c_ones"][:, :]))
            b.dsy.dma(sy.dma_start(out=tri[:, :], in_=iaps["c_tri"][:, :]))
            b.dsy.dma(sy.dma_start(out=pfx_sb[:, :], in_=pfx[:, :]))
            b.dsy.dma(sy.dma_start(out=cpool[:, :], in_=iaps["c_pool"][:, :]))
        b.SY(ld_const)
        def ld_constg(g):
            b.dgp.dma(g.dma_start(out=identb[:, :], in_=iaps["c_ident"][:, :]))
            b.dgp.dma(g.dma_start(out=onesb[:, :], in_=iaps["c_ones"][0:1, 0:128]))
        b.G(ld_constg)
        b.barrier()

        with ExitStack() as pes:
            pb = pes.enter_context(nc.psum_tensor("ppro", [128, 2048], F32))
            tin = pes.enter_context(nc.sbuf_tensor("p_in", [128, 2, D], F32))
            tout = pes.enter_context(nc.sbuf_tensor("p_out", [128, 2, D], F32))
            in_rot, out_rot, ps_rot = Rot(2), Rot(2), Rot(4)
            xfa_r = xfa.rearrange("(c p) t -> p c t", p=128)
            for i in range(NTILE):
                _transpose_pass(b, nc, "tm", i, pb, tin, tout, ident,
                                x_in[i * 128:(i + 1) * 128, :],
                                xfa_r[:, :, i * 128:(i + 1) * 128],
                                in_rot, out_rot, ps_rot)
            b.barrier()

        b.onesb = onesb
        x_src, x_dst = xfa, xfb
        for li in blocks:
            if li == 3:
                emit_block3(b, iaps, x_src, x_dst, xn_hbm, s_hbm, ident, identb, ones, rows)
            elif li == 1:
                emit_block1(b, iaps, x_src, x_dst, xn_hbm, s_hbm, ident, identb, ones, rows)
            elif li == 0:
                emit_block0(b, iaps, x_src, x_dst, xn_hbm, s_hbm, h_hbm,
                            cc_in, cc_out, pfx_sb, ident, identb, ones, tri, rows)
            elif li == 2:
                emit_block2(b, iaps, x_src, x_dst, xn_hbm, s_hbm, h_hbm, y_hbm,
                            ident, identb, ones, rows)
            x_src, x_dst = x_dst, x_src
            b.barrier()

        if DEBUG_DUMP:
            dbg_xn = nc.dram_tensor("dbg_xn", [D, S], BF16, kind="ExternalOutput").ap()
            dbg_s = nc.dram_tensor("dbg_s", [D, S], BF16, kind="ExternalOutput").ap()
            dbg_y = nc.dram_tensor("dbg_y", [D, S], BF16, kind="ExternalOutput").ap()
            def dmp(sy):
                b.dsy.dma(sy.dma_start(out=dbg_xn[:, :], in_=xn_hbm[:, :]))
                b.dsy.dma(sy.dma_start(out=dbg_s[:, :], in_=s_hbm[:, :]))
                b.dsy.dma(sy.dma_start(out=dbg_y[:, :], in_=y_hbm[:, :]))
                sy.wait_ge(b.dsy.h, b.dsy.n)
            b.SY(dmp)

        with ExitStack() as pes:
            pb = pes.enter_context(nc.psum_tensor("pepi", [128, 2048], F32))
            tin = pes.enter_context(nc.sbuf_tensor("e_in", [128, 2, D], F32))
            tout = pes.enter_context(nc.sbuf_tensor("e_out", [128, 2, D], F32))
            in_rot, out_rot, ps_rot = Rot(2), Rot(2), Rot(4)
            xs_r = x_src.rearrange("(c p) t -> p c t", p=128)
            for i in range(NTILE):
                _transpose_pass(b, nc, "fm", i, pb, tin, tout, ident,
                                xs_r[:, :, i * 128:(i + 1) * 128],
                                out[i * 128:(i + 1) * 128, :],
                                in_rot, out_rot, ps_rot)
            def fin(sy):
                sy.wait_ge(b.dsy.h, b.dsy.n)
            b.SY(fin)
    return nc



def _run(inputs, blocks, x_override=None):
    host = host_prep(inputs)
    nc = build_program(host, blocks=blocks)
    x_full = np.asarray(x_override if x_override is not None else inputs["L0_attn_x"], dtype=np.float32)
    in_maps = []
    for c in range(8):
        m = dict(host)
        m["x"] = np.ascontiguousarray(x_full[c])
        p = np.zeros((8, 1), np.float32)
        p[:c] = 1.0
        m["pfx"] = p
        in_maps.append(m)
    res = run_bass_kernel_spmd(nc, in_maps, core_ids=list(range(8)))
    if DEBUG_DUMP:
        import ml_dtypes
        if "dbg_gb" in res.results[0]:
            np.save("/tmp/dbg_gb.npy", np.stack([np.asarray(r["dbg_gb"]) for r in res.results]))
        for key in ("dbg_den", "dbg_num"):
            if key in res.results[0]:
                np.save(f"/tmp/{key}.npy", np.stack([np.asarray(r[key]) for r in res.results]))
        for key in ("dbg_xn", "dbg_s", "dbg_g", "dbg_gt", "dbg_h", "dbg_w2", "dbg_hh", "dbg_y", "dbg_qa", "dbg_ka", "dbg_va", *[f"dbg_gi{i}" for i in range(8)]):
            if key in res.results[0]:
                np.save(f"/tmp/{key}.npy", np.stack(
                    [np.asarray(r[key]).view(ml_dtypes.bfloat16).astype(np.float32) for r in res.results]))
    return np.stack([np.asarray(r["out"]) for r in res.results])


def kernel(**inputs):
    return _run(inputs, blocks=(0, 1, 2, 3)).astype(np.float32)


# version 05
# speedup vs baseline: 1.1017x; 1.1017x over previous
import numpy as np
from contextlib import ExitStack
import concourse.bass as bass
import concourse.mybir as mybir
from concourse.bass_utils import run_bass_kernel_spmd

F32 = mybir.dt.float32
BF16 = mybir.dt.bfloat16
ALU = mybir.AluOpType
AF = mybir.ActivationFunctionType
AX = mybir.AxisListType.X

D = 1024
S = 4096
NDC = D // 128
TC = 512
NTC = S // TC
NTILE = S // 128
E = 8
CAPACITY = 5120

CONST_POOL = [0.0, 1.0, -1.0, 0.5, -0.5, 2.0, 1e-5, 1e-6, 0.1, 0.125,
              float(np.pi / 2), float(np.pi), float(2 * np.pi), 0.02, -30.0]
DEBUG_DUMP = False



def _pp(v):
    return np.ascontiguousarray(np.asarray(v, dtype=np.float32).reshape(NDC, 128).T)


def host_prep(inp):
    a = {}
    f32 = np.float32

    def T(x):
        return np.ascontiguousarray(np.asarray(x).T.astype(f32))

    def C(x):
        return np.ascontiguousarray(np.asarray(x).astype(f32))

    a["c_pool"] = np.ascontiguousarray(np.broadcast_to(np.array(CONST_POOL, dtype=f32), (128, len(CONST_POOL))))
    a["c_ident"] = np.eye(128, dtype=f32)
    a["c_ones"] = np.ones((128, 512), dtype=f32)
    a["c_onescol"] = np.ones((128, 1), dtype=f32)
    a["c_tri"] = np.ascontiguousarray(np.triu(np.ones((128, 128), dtype=f32)))
    m = np.zeros((128, 4, 512), dtype=f32)
    for r in range(4):
        for p in range(128):
            m[p, r, :] = (np.arange(512) >= (128 * r + p))
    a["c_l2mask"] = m
    DHE = 32
    inv = 1.0 / (10000.0 ** (np.arange(0, DHE, 2, dtype=np.float64) / DHE))
    emb = np.arange(S, dtype=np.float64)[:, None] * inv[None, :]
    emb = np.concatenate([emb, emb], axis=1)
    a["c_l2cosq"] = np.ascontiguousarray((np.cos(emb).T / 8.0).astype(f32))
    a["c_l2sinq"] = np.ascontiguousarray((np.sin(emb).T / 8.0).astype(f32))
    a["c_l2cosk"] = np.ascontiguousarray(np.cos(emb).T.astype(f32))
    a["c_l2sink"] = np.ascontiguousarray(np.sin(emb).T.astype(f32))
    dec = (0.5 ** (1.0 / np.linspace(2.0, 8.0, D))).astype(np.float64)
    a["c_l1dec"] = np.ascontiguousarray(
        np.broadcast_to(_pp(dec.astype(f32))[:, :, None], (128, NDC, TC)).astype(f32))
    a["c_l1beta"] = _pp((1.0 - dec).astype(f32))
    Bm = np.zeros((16, D), dtype=f32)
    for h in range(16):
        Bm[h, h * 64:(h + 1) * 64] = 1.0
    a["c_l3B"] = Bm

    a["w_l0_WinT"] = T(inp["L0_attn_W_in"])
    a["w_l0_WoutT"] = T(inp["L0_attn_W_out"])
    a["w_l0_logdec"] = C(inp["L0_attn_log_decay"]).reshape(2, 128).T.copy()
    a["w_l0_freq"] = C(inp["L0_attn_frequencies"]).reshape(2, 128).T.copy()
    a["w_l0_Wr"] = C(inp["L0_ffn_Wr"])
    a["w_l0_W1"] = C(inp["L0_ffn_W1"])
    a["w_l0_W2"] = C(inp["L0_ffn_W2"])
    a["w_l0_b1"] = np.ascontiguousarray(C(inp["L0_ffn_b1"]).reshape(E, 32, 128).transpose(2, 0, 1))
    a["w_l0_b2"] = C(inp["L0_ffn_b2"])

    a["w_l1_Wt"] = C(inp["L1_attn_Wt"])
    a["w_l1_Wo"] = C(inp["L1_attn_Wo"])
    a["w_l1_Wg"] = C(inp["L1_attn_Wg"])
    a["w_l1_bg"] = _pp(inp["L1_attn_bg"])
    a["w_l1_ln1w"] = _pp(inp["L1_attn_ln1_w"])
    a["w_l1_ln1b"] = _pp(inp["L1_attn_ln1_b"])
    a["w_l1_lncw"] = _pp(inp["L1_attn_lnc_w"])
    a["w_l1_lncb"] = _pp(inp["L1_attn_lnc_b"])
    a["w_l1_lngw"] = _pp(inp["L1_attn_lng_w"])
    a["w_l1_lngb"] = _pp(inp["L1_attn_lng_b"])
    a["w_l1_Wr"] = C(inp["L1_ffn_Wr"])
    a["w_l1_br"] = C(inp["L1_ffn_br"]).reshape(1, 8)
    a["w_l1_W1"] = C(inp["L1_ffn_W1"])
    a["w_l1_b1"] = np.ascontiguousarray(C(inp["L1_ffn_b1"]).reshape(E, 16, 128).transpose(2, 0, 1))
    a["w_l1_W2"] = C(inp["L1_ffn_W2"])
    a["w_l1_b2"] = C(inp["L1_ffn_b2"])

    a["w_l2_Wqd"] = C(inp["L2_attn_Wq_down"])
    a["w_l2_Wkd"] = C(inp["L2_attn_Wk_down"])
    a["w_l2_Wvd"] = C(inp["L2_attn_Wv_down"])
    lq = np.zeros((64, 96), dtype=f32)
    lq[:, 0:32] = C(inp["L2_attn_Wq_up_c"])
    lq[:, 32:64] = C(inp["L2_attn_Wq_up_e"])
    a["w_l2_upq"] = lq
    lk = np.zeros((64, 96), dtype=f32)
    lk[:, 0:32] = C(inp["L2_attn_Wk_up_c"])
    lk[:, 32:64] = C(inp["L2_attn_Wk_up_e"])
    a["w_l2_upk"] = lk
    a["w_l2_upv"] = C(inp["L2_attn_Wv_up"])
    a["w_l2_Wc"] = C(inp["L2_attn_Wc"])
    a["w_l2_Wr"] = T(inp["L2_ffn_router_w"])
    a["w_l2_Wg"] = np.ascontiguousarray(np.asarray(inp["L2_ffn_w_gate"]).transpose(0, 2, 1).astype(f32))
    a["w_l2_Wu"] = np.ascontiguousarray(np.asarray(inp["L2_ffn_w_up"]).transpose(0, 2, 1).astype(f32))
    a["w_l2_Wd"] = np.ascontiguousarray(np.asarray(inp["L2_ffn_w_down"]).transpose(0, 2, 1).astype(f32))

    a["w_l3_dec"] = C(inp["L3_attn_decay"]).reshape(16, 2)
    a["w_l3_WqT"] = T(inp["L3_attn_Wq"])
    a["w_l3_WvT"] = T(inp["L3_attn_Wv"])
    a["w_l3_WoT"] = T(inp["L3_attn_Wo"])
    a["w_l3_WgT"] = T(inp["L3_attn_Wg"])
    a["w_l3_lng"] = _pp(inp["L3_attn_ln_g"])
    a["w_l3_lnb"] = _pp(inp["L3_attn_ln_b"])
    a["w_l3_WT"] = T(inp["L3_ffn_W"])
    a["w_l3_b"] = _pp(inp["L3_ffn_b"])
    a["w_l3_Vr"] = C(inp["L3_ffn_V_r"])
    a["w_l3_CT"] = T(inp["L3_ffn_C"])
    return a



class Sem:
    __slots__ = ("h", "n")

    def __init__(self, nc, es, name):
        self.h = es.enter_context(nc.semaphore(name))
        self.n = 0

    def inc(self, instr, k=1):
        instr.then_inc(self.h, k)
        self.n += k
        return instr

    def dma(self, instr):
        return self.inc(instr, 16)

    def wait(self, eng, n=None):
        eng.wait_ge(self.h, self.n if n is None else n)


class Rot:

    def __init__(self, nslots):
        self.marks = [None] * nslots
        self.i = 0

    def acquire(self, eng):
        s = self.i % len(self.marks)
        self.i += 1
        g = self.marks[s]
        if g is not None:
            eng.wait_ge(g[0].h, g[1])
        return s

    def done(self, s, sem):
        self.marks[s] = (sem, sem.n)


class Bld:
    def __init__(self, nc, es, block):
        self.nc = nc
        self.es = es
        self.block = block
        self.pe = Sem(nc, es, "pe")
        self.dv = Sem(nc, es, "dv")
        self.ac = Sem(nc, es, "ac")
        self.po = Sem(nc, es, "po")
        self.dsy = Sem(nc, es, "dsy")
        self.dgp = Sem(nc, es, "dgp")
        self.dsc = Sem(nc, es, "dsc")
        self.cc = Sem(nc, es, "cc")
        self.all_sems = [self.pe, self.dv, self.ac, self.po,
                         self.dsy, self.dgp, self.dsc, self.cc]

    def T(self, f): self.block.tensor(f)
    def V(self, f): self.block.vector(f)
    def SC(self, f): self.block.scalar(f)
    def G(self, f): self.block.gpsimd(f)
    def SY(self, f): self.block.sync(f)

    def barrier(self):
        snap = [(s, s.n) for s in self.all_sems]

        def w(eng):
            for s, n in snap:
                if n:
                    eng.wait_ge(s.h, n)
        self.T(w); self.V(w); self.SC(w); self.G(w); self.SY(w)

    def sb(self, name, shape, dtype):
        return self.es.enter_context(self.nc.sbuf_tensor(name, shape, dtype))


def mmgroup(t, ps_ap, pairs):
    mm = None
    n = len(pairs)
    for j, (l, r) in enumerate(pairs):
        mm = t.matmul(ps_ap, l, r, start=(j == 0), stop=(j == n - 1))
    return mm



def emit_ln_chunk(b, xc, xn_out, work, eps, affine=None):
    acc, sq, rows = work["acc"], work["sq"], work["rows"]
    ps_st, ps_bc, ones = work["ps_st"], work["ps_bc"], work["ones"]
    onesb = b.onesb
    xc_bf = xc.dtype == BF16 if hasattr(xc, "dtype") else False

    pem0 = b.pe.n
    dvm0 = b.dv.n
    def s1(sc, pem0=pem0, dvm0=dvm0):
        sc.wait_ge(b.pe.h, pem0)
        sc.wait_ge(b.dv.h, dvm0)
        last = None
        for c in range(NDC):
            last = sc.activation(sq[:, c, :], xc[:, c, :], AF.Square)
        b.ac.inc(last)
    b.SC(s1)

    acm = b.ac.n
    try:
        xdt = xc.dtype
    except Exception:
        xdt = F32
    lhs_x = b.onescb[:, 0:1] if xdt == BF16 else ones[:, 0:1]
    dsym0 = b.dsy.n
    def t1(t, acm=acm, lhs_x=lhs_x, pem0=pem0, dvm0=dvm0, dsym0=dsym0):
        t.wait_ge(b.dv.h, dvm0)
        t.wait_ge(b.dsy.h, dsym0)
        for c in range(NDC):
            t.matmul(ps_st[0:1, 0:512], lhs_x, xc[:, c, :],
                     start=(c == 0), stop=(c == NDC - 1))
        t.wait_ge(b.ac.h, acm)
        mm = None
        for c in range(NDC):
            mm = t.matmul(ps_st[0:1, 512:1024], ones[:, 0:1], sq[:, c, :],
                          start=(c == 0), stop=(c == NDC - 1))
        b.pe.inc(mm)
    b.T(t1)

    pem = b.pe.n
    def v3(v, pem=pem):
        v.wait_ge(b.pe.h, pem)
        v.tensor_scalar_mul(rows[0:1, 0:512], ps_st[0:1, 0:512], 1.0 / 1024.0)
        v.tensor_scalar_mul(rows[0:1, 512:1024], ps_st[0:1, 512:1024], 1.0 / 1024.0)
        v.tensor_mul(rows[0:1, 1024:1536], rows[0:1, 0:512], rows[0:1, 0:512])
        b.dv.inc(v.tensor_sub(rows[0:1, 512:1024], rows[0:1, 512:1024], rows[0:1, 1024:1536]))
    b.V(v3)

    dvm = b.dv.n
    def s2(sc, dvm=dvm):
        sc.wait_ge(b.dv.h, dvm)
        b.ac.inc(sc.activation(rows[0:1, 1024:1536], rows[0:1, 512:1024], AF.Sqrt, bias=float(eps)))
    b.SC(s2)

    acm = b.ac.n
    def v4(v, acm=acm):
        v.wait_ge(b.ac.h, acm)
        b.dv.inc(v.reciprocal(rows[0:1, 1536:2048], rows[0:1, 1024:1536]))
    b.V(v4)

    dvm = b.dv.n
    def t2(t, dvm=dvm):
        t.wait_ge(b.dv.h, dvm)
        t.matmul(ps_bc[:, 0:512], ones[0:1, 0:128], rows[0:1, 0:512], start=True, stop=True)
        mm = t.matmul(ps_bc[:, 512:1024], ones[0:1, 0:128], rows[0:1, 1536:2048], start=True, stop=True)
        b.pe.inc(mm)
    b.T(t2)

    pem = b.pe.n
    def v5(v, pem=pem):
        v.wait_ge(b.pe.h, pem)
        last = None
        for c in range(NDC):
            v.tensor_sub(xn_out[:, c, :], xc[:, c, :], ps_bc[:, 0:512])
            last = v.tensor_mul(xn_out[:, c, :], xn_out[:, c, :], ps_bc[:, 512:1024])
            if affine is not None:
                wcol, bcol = affine
                last = v.tensor_scalar(xn_out[:, c, :], xn_out[:, c, :],
                                       wcol[:, c:c + 1], bcol[:, c:c + 1],
                                       ALU.mult, ALU.add)
        b.dv.inc(last)
    b.V(v5)



def emit_stage1(b, x_src, xn_hbm, ones, rows, wr_sb=None, logits_sb=None, br_row=None):
    nc = b.nc
    x_src_r = x_src.rearrange("(c p) t -> p c t", p=128)
    xn_r = xn_hbm.rearrange("(c p) t -> p c t", p=128)
    with ExitStack() as es:
        sb = lambda n, sh, dt: es.enter_context(nc.sbuf_tensor(n, sh, dt))
        xc = sb("s1_xc", [128, 2, NDC, TC], F32)
        sq = sb("s1_sq", [128, NDC, TC], F32)
        acc = sb("s1_acc", [128, TC], F32)
        xnb = sb("s1_xnb", [128, 2, NDC, TC], BF16)
        ps_st = es.enter_context(nc.psum_tensor("s1_ps_st", [128, 1024], F32))
        ps_bc = es.enter_context(nc.psum_tensor("s1_ps_bc", [128, 1024], F32))
        ps_r = es.enter_context(nc.psum_tensor("s1_ps_r", [128, 64], F32)) if wr_sb is not None else None
        work = dict(acc=acc, sq=sq, rows=rows, ps_st=ps_st, ps_bc=ps_bc, ones=ones)

        xc_rot = Rot(2)
        xnb_rot = Rot(2)
        r_guard = [None]
        for i in range(NTC):
            par = [None]
            def ldw(sy, i=i, par=par):
                par[0] = xc_rot.acquire(sy)
                b.dsy.dma(sy.dma_start(out=xc[:, par[0]], in_=x_src_r[:, :, i * TC:(i + 1) * TC]))
            b.SY(ldw)
            par = par[0]
            dsym = b.dsy.n

            pem0 = b.pe.n
            def w0(v, dsym=dsym, pem0=pem0):
                v.wait_ge(b.dsy.h, dsym)
                v.wait_ge(b.pe.h, pem0)
            b.V(w0)
            def w0s(sc, dsym=dsym, pem0=pem0):
                sc.wait_ge(b.dsy.h, dsym)
                sc.wait_ge(b.pe.h, pem0)
            b.SC(w0s)

            emit_ln_chunk(b, xc[:, par], sq, work, eps=1e-5)
            dv_ln = b.dv.n
            xc_rot.done(par, b.dv)

            sslot = [None]
            def castw(v, par=par, sslot=sslot):
                sslot[0] = xnb_rot.acquire(v)
                last = None
                for c in range(NDC):
                    last = v.tensor_copy(xnb[:, sslot[0], c, :], sq[:, c, :])
                b.dv.inc(last)
            b.V(castw)
            sslot = sslot[0]
            dvm = b.dv.n
            def stn(sy, i=i, sslot=sslot, dvm=dvm):
                sy.wait_ge(b.dv.h, dvm)
                b.dsy.dma(sy.dma_start(out=xn_r[:, :, i * TC:(i + 1) * TC], in_=xnb[:, sslot]))
            b.SY(stn)
            xnb_rot.done(sslot, b.dsy)

            if wr_sb is not None:
                def rt(t, i=i, dv_ln=dv_ln):
                    t.wait_ge(b.dv.h, dv_ln)
                    if r_guard[0] is not None:
                        t.wait_ge(r_guard[0][0].h, r_guard[0][1])
                    mm = None
                    for ts in range(4):
                        pairs = [(sq[:, c, ts * 128:(ts + 1) * 128], wr_sb[:, c, :])
                                 for c in range(NDC)]
                        if br_row is not None:
                            pairs.append((work["ones"][0:1, 0:128], br_row[0:1, 0:8]))
                        mm = mmgroup(t, ps_r[:, ts * 16:ts * 16 + 8], pairs)
                    b.pe.inc(mm)
                b.T(rt)
                pem = b.pe.n
                def ev(v, i=i, pem=pem):
                    v.wait_ge(b.pe.h, pem)
                    last = None
                    for ts in range(4):
                        last = v.tensor_copy(logits_sb[:, i * 4 + ts, :], ps_r[:, ts * 16:ts * 16 + 8])
                    b.dv.inc(last)
                b.V(ev)
                r_guard[0] = (b.dv, b.dv.n)
        b.barrier()



def emit_block3(b, iaps, x_src, x_dst, xn_hbm, s_hbm, ident, identb, ones, rows):
    nc = b.nc
    emit_stage1(b, x_src, xn_hbm, ones, rows)

    xn_r = xn_hbm.rearrange("(c p) t -> p c t", p=128)
    s_r = s_hbm.rearrange("(c p) t -> p c t", p=128)
    x_src_r = x_src.rearrange("(c p) t -> p c t", p=128)
    x_dst_r = x_dst.rearrange("(c p) t -> p c t", p=128)

    with ExitStack() as aes:
        sb = lambda n, sh, dt: aes.enter_context(nc.sbuf_tensor(n, sh, dt))
        psn = lambda n, sh: aes.enter_context(nc.psum_tensor(n, sh, F32))
        wq = sb("l3_wq", [128, NDC, D], BF16)
        wv = sb("l3_wv", [128, NDC, D], BF16)
        wg = sb("l3_wg", [128, NDC, D], BF16)
        wo = sb("l3_wo", [128, NDC, D], BF16)
        dec16 = sb("l3_dec", [16, 2], F32)
        lamc = sb("l3_lam", [128, NDC], F32)
        l3B = sb("l3_B", [16, D], F32)
        lam_tile = sb("l3_lamt", [128, NDC, TC], F32)
        lng = sb("l3_lng", [128, NDC], F32)
        lnb = sb("l3_lnb", [128, NDC], F32)
        carry = sb("l3_carry", [128, NDC], F32)

        def ldw(g):
            for nm, dst in (("w_l3_WqT", wq), ("w_l3_WvT", wv), ("w_l3_WgT", wg), ("w_l3_WoT", wo)):
                b.dgp.dma(g.dma_start(out=dst[:, :, :],
                                      in_=iaps[nm].rearrange("(c p) o -> p c o", p=128)))
        b.G(ldw)

        def lds(sy):
            b.dsy.dma(sy.dma_start(out=dec16[:, :], in_=iaps["w_l3_dec"][:, :]))
            b.dsy.dma(sy.dma_start(out=l3B[:, :], in_=iaps["c_l3B"][:, :]))
            b.dsy.dma(sy.dma_start(out=lng[:, :], in_=iaps["w_l3_lng"][:, :]))
            b.dsy.dma(sy.dma_start(out=lnb[:, :], in_=iaps["w_l3_lnb"][:, :]))
        b.SY(lds)

        ps_q = psn("l3_psq", [128, 1024])
        ps_g = psn("l3_psg", [128, 1024])
        ps_st = psn("l3_psst", [128, 1024])
        ps_bc = psn("l3_psbc", [128, 1024])
        b.barrier()

        def t_lam(t):
            last = None
            for c in range(NDC):
                last = t.matmul(ps_q[:, c:c + 1], l3B[:, c * 128:(c + 1) * 128],
                                dec16[:, 1:2], start=True, stop=True)
            b.pe.inc(last)
        b.T(t_lam)
        pem = b.pe.n
        def v_lam(v, pem=pem):
            v.wait_ge(b.pe.h, pem)
            for c in range(NDC):
                v.tensor_copy(lamc[:, c:c + 1], ps_q[:, c:c + 1])
            for c in range(NDC):
                v.tensor_scalar_mul(lam_tile[:, c, :], ones[:, 0:TC], lamc[:, c:c + 1])
            b.dv.inc(v.memset(carry[:, :], 0.0))
        b.V(v_lam)
        b.barrier()

        xnc = sb("l3_xnc", [128, 2, NDC, TC], BF16)
        qsb = sb("l3_q", [128, NDC, TC], BF16)
        vsb = sb("l3_v", [128, NDC, TC], F32)
        ysb = sb("l3_y", [128, NDC, TC], F32)
        sq = sb("l3_sq", [128, NDC, TC], F32)
        accw = sb("l3_acc", [128, TC], F32)
        gate = sb("l3_gate", [128, NDC, TC], BF16)
        yg = sb("l3_yg", [128, NDC, TC], BF16)
        ssb = sb("l3_s", [128, 2, NDC, TC], BF16)
        work = dict(acc=accw, sq=sq, rows=rows, ps_st=ps_st, ps_bc=ps_bc, ones=ones)

        xnc_rot = Rot(2)
        ssb_rot = Rot(2)
        rq, rg = Rot(2), Rot(2)
        rv = rq
        for i in range(NTC):
            par = [None]
            def ld(sy, par=par):
                par[0] = xnc_rot.acquire(sy)
                b.dsy.dma(sy.dma_start(out=xnc[:, par[0]], in_=xn_r[:, :, i * TC:(i + 1) * TC]))
            b.SY(ld)
            par = par[0]
            dsym = b.dsy.n

            def t_qvg(t, par=par, dsym=dsym):
                t.wait_ge(b.dsy.h, dsym)
                for oc in range(NDC):
                    s = rq.acquire(t)
                    mm = mmgroup(t, ps_q[:, s * 512:(s + 1) * 512],
                                 [(wq[:, c, oc * 128:(oc + 1) * 128], xnc[:, par, c, :]) for c in range(NDC)])
                    b.pe.inc(mm)
                    pem = b.pe.n
                    def evq(v, oc=oc, s=s, pem=pem):
                        v.wait_ge(b.pe.h, pem)
                        b.dv.inc(v.tensor_copy(qsb[:, oc, :], ps_q[:, s * 512:(s + 1) * 512]))
                    b.V(evq)
                    rq.done(s, b.dv)
                    s = rv.acquire(t)
                    mm = mmgroup(t, ps_q[:, s * 512:(s + 1) * 512],
                                 [(wv[:, c, oc * 128:(oc + 1) * 128], xnc[:, par, c, :]) for c in range(NDC)])
                    b.pe.inc(mm)
                    pem = b.pe.n
                    def evv(v, oc=oc, s=s, pem=pem):
                        v.wait_ge(b.pe.h, pem)
                        b.dv.inc(v.tensor_copy(vsb[:, oc, :], ps_q[:, s * 512:(s + 1) * 512]))
                    b.V(evv)
                    rv.done(s, b.dv)
                for oc in range(NDC):
                    s = rg.acquire(t)
                    mm = mmgroup(t, ps_g[:, s * 512:(s + 1) * 512],
                                 [(wg[:, c, oc * 128:(oc + 1) * 128], xnc[:, par, c, :]) for c in range(NDC)])
                    b.pe.inc(mm)
                    pem = b.pe.n
                    def evg(sc, oc=oc, s=s, pem=pem):
                        sc.wait_ge(b.pe.h, pem)
                        b.ac.inc(sc.activation(gate[:, oc, :], ps_g[:, s * 512:(s + 1) * 512], AF.Silu))
                    b.SC(evg)
                    rg.done(s, b.ac)
            b.T(t_qvg)
            xnc_rot.done(par, b.pe)

            dvm = b.dv.n
            def v_scan(v, dvm=dvm):
                v.wait_ge(b.dv.h, dvm)
                last = None
                for c in range(NDC):
                    last = v.tensor_tensor_scan(ysb[:, c, :], lam_tile[:, c, :], vsb[:, c, :],
                                                carry[:, c:c + 1], ALU.mult, ALU.add)
                b.dv.inc(last)
            b.V(v_scan)
            dvm = b.dv.n
            def s_carry(sc, dvm=dvm):
                sc.wait_ge(b.dv.h, dvm)
                last = None
                for c in range(NDC):
                    last = sc.copy(carry[:, c:c + 1], ysb[:, c, TC - 1:TC])
                b.ac.inc(last)
            b.SC(s_carry)
            acm2 = b.ac.n
            def v_mul(v, acm2=acm2):
                v.wait_ge(b.ac.h, acm2)
                last = None
                for c in range(NDC):
                    last = v.tensor_mul(ysb[:, c, :], ysb[:, c, :], qsb[:, c, :])
                b.dv.inc(last)
            b.V(v_mul)

            emit_ln_chunk(b, ysb, ysb, work, eps=1e-5, affine=(lng, lnb))

            acm = b.ac.n
            def v_yg(v, acm=acm):
                v.wait_ge(b.ac.h, acm)
                last = None
                for c in range(NDC):
                    last = v.tensor_mul(yg[:, c, :], ysb[:, c, :], gate[:, c, :])
                b.dv.inc(last)
            b.V(v_yg)
            dvm = b.dv.n

            sslot = [None]
            def pick(v, sslot=sslot):
                sslot[0] = ssb_rot.acquire(v)
            b.V(pick)
            sslot = sslot[0]

            def t_wo(t, par=par, dvm=dvm, sslot=sslot):
                t.wait_ge(b.dv.h, dvm)
                for oc in range(NDC):
                    s = rq.acquire(t)
                    mm = mmgroup(t, ps_q[:, s * 512:(s + 1) * 512],
                                 [(wo[:, c, oc * 128:(oc + 1) * 128], yg[:, c, :]) for c in range(NDC)])
                    b.pe.inc(mm)
                    pem = b.pe.n
                    def evs(v, oc=oc, s=s, pem=pem, sslot=sslot):
                        v.wait_ge(b.pe.h, pem)
                        b.dv.inc(v.tensor_copy(ssb[:, sslot, oc, :], ps_q[:, s * 512:(s + 1) * 512]))
                    b.V(evs)
                    rq.done(s, b.dv)
            b.T(t_wo)
            dvm = b.dv.n
            def st_s(sy, i=i, sslot=sslot, dvm=dvm):
                sy.wait_ge(b.dv.h, dvm)
                b.dsy.dma(sy.dma_start(out=s_r[:, :, i * TC:(i + 1) * TC], in_=ssb[:, sslot]))
            b.SY(st_s)
            ssb_rot.done(sslot, b.dsy)
        b.barrier()

    with ExitStack() as fes:
        sb = lambda n, sh, dt: fes.enter_context(nc.sbuf_tensor(n, sh, dt))
        wT = sb("l3f_wT", [128, NDC, D], BF16)
        vr = sb("l3f_vr", [128, NDC, 32], BF16)
        cT = sb("l3f_cT", [32, D], BF16)
        bcol = sb("l3f_bcol", [128, NDC], F32)
        ps_go = fes.enter_context(nc.psum_tensor("l3f_psg", [128, 1024], F32))
        ps_rr = fes.enter_context(nc.psum_tensor("l3f_psr", [32, 512], F32))

        def ldw(g):
            b.dgp.dma(g.dma_start(out=wT[:, :, :], in_=iaps["w_l3_WT"].rearrange("(c p) o -> p c o", p=128)))
            b.dgp.dma(g.dma_start(out=vr[:, :, :], in_=iaps["w_l3_Vr"].rearrange("(c p) r -> p c r", p=128)))
            b.dgp.dma(g.dma_start(out=cT[:, :], in_=iaps["w_l3_CT"][:, :]))
        b.G(ldw)
        def lds(sy):
            b.dsy.dma(sy.dma_start(out=bcol[:, :], in_=iaps["w_l3_b"][:, :]))
        b.SY(lds)

        xnc = sb("l3f_xnc", [128, 2, NDC, TC], BF16)
        xcc = sb("l3f_xc", [128, 2, NDC, TC], F32)
        scc = sb("l3f_sc", [128, 2, NDC, TC], BF16)
        rsb = sb("l3f_r", [32, TC], BF16)
        gtmp = sb("l3f_gt", [128, 2, TC], F32)
        xout = sb("l3f_xo", [128, 2, NDC, TC], F32)

        b.barrier()
        in_rot = Rot(2)
        out_rot = Rot(2)
        go_rot = Rot(2)
        gt_rot = Rot(2)
        rr_guard = [None]
        for i in range(NTC):
            par = [None]
            def ld(sy, i=i, par=par):
                par[0] = in_rot.acquire(sy)
                b.dsy.dma(sy.dma_start(out=xnc[:, par[0]], in_=xn_r[:, :, i * TC:(i + 1) * TC]))
                b.dsy.dma(sy.dma_start(out=xcc[:, par[0]], in_=x_src_r[:, :, i * TC:(i + 1) * TC]))
                b.dsy.dma(sy.dma_start(out=scc[:, par[0]], in_=s_r[:, :, i * TC:(i + 1) * TC]))
            b.SY(ld)
            par = par[0]
            dsym = b.dsy.n

            def t_r(t, par=par, dsym=dsym):
                t.wait_ge(b.dsy.h, dsym)
                if rr_guard[0] is not None:
                    t.wait_ge(rr_guard[0][0].h, rr_guard[0][1])
                mm = mmgroup(t, ps_rr[:, :], [(vr[:, c, :], xnc[:, par, c, :]) for c in range(NDC)])
                b.pe.inc(mm)
            b.T(t_r)
            pem = b.pe.n
            def ev_r(v, pem=pem):
                v.wait_ge(b.pe.h, pem)
                b.dv.inc(v.tensor_copy(rsb[:, :], ps_rr[:, :]))
            b.V(ev_r)
            rr_guard[0] = (b.dv, b.dv.n)
            dvm = b.dv.n

            oslot = [None]
            def pick(v, oslot=oslot):
                oslot[0] = out_rot.acquire(v)
            b.V(pick)
            oslot = oslot[0]

            for oc in range(NDC):
                sref = [None]
                def t_gw(t, oc=oc, par=par, dvm=dvm, sref=sref):
                    if oc == 0:
                        t.wait_ge(b.dv.h, dvm)
                    s = go_rot.acquire(t)
                    sref[0] = s
                    pairs = [(wT[:, c, oc * 128:(oc + 1) * 128], xnc[:, par, c, :]) for c in range(NDC)]
                    pairs.append((cT[:, oc * 128:(oc + 1) * 128], rsb[:, :]))
                    mm = mmgroup(t, ps_go[:, s * 512:(s + 1) * 512], pairs)
                    b.pe.inc(mm)
                b.T(t_gw)
                s = sref[0]
                pem = b.pe.n
                gslot = [None]
                def fuse(sc, oc=oc, s=s, pem=pem, gslot=gslot):
                    sc.wait_ge(b.pe.h, pem)
                    gs = gt_rot.acquire(sc)
                    gslot[0] = gs
                    b.ac.inc(sc.activation(gtmp[:, gs, :], ps_go[:, s * 512:(s + 1) * 512],
                                           AF.Silu, bias=bcol[:, oc:oc + 1]))
                b.SC(fuse)
                gslot = gslot[0]
                go_rot.done(s, b.ac)
                acm = b.ac.n
                def join(v, oc=oc, par=par, acm=acm, gslot=gslot, oslot=oslot):
                    v.wait_ge(b.ac.h, acm)
                    v.tensor_mul(gtmp[:, gslot, :], gtmp[:, gslot, :], scc[:, par, oc, :])
                    b.dv.inc(v.tensor_add(xout[:, oslot, oc, :], xcc[:, par, oc, :], gtmp[:, gslot, :]))
                b.V(join)
                gt_rot.done(gslot, b.dv)
            in_rot.done(par, b.dv)
            dvm = b.dv.n
            def st_x(sy, i=i, oslot=oslot, dvm=dvm):
                sy.wait_ge(b.dv.h, dvm)
                b.dsy.dma(sy.dma_start(out=x_dst_r[:, :, i * TC:(i + 1) * TC], in_=xout[:, oslot]))
            b.SY(st_x)
            out_rot.done(oslot, b.dsy)
        b.barrier()


def emit_first_max_ks(v, src_ap, m_bc, oh, t1, t2, sh):
    P, NT_, Ecnt = sh
    v.tensor_tensor(oh, src_ap, m_bc, ALU.is_ge)
    v.tensor_copy(t1, oh)
    v.tensor_add(t1[:, :, 1:8], oh[:, :, 1:8], oh[:, :, 0:7])
    v.tensor_copy(t2, t1)
    v.tensor_add(t2[:, :, 2:8], t1[:, :, 2:8], t1[:, :, 0:6])
    v.tensor_copy(t1, t2)
    v.tensor_add(t1[:, :, 4:8], t2[:, :, 4:8], t2[:, :, 0:4])
    v.tensor_single_scalar(t1, t1, 1.0, ALU.is_le)
    v.tensor_mul(oh, oh, t1)


def bounce_copy(b, pairs):
    dvm = b.dv.n
    def sc_(sc, dvm=dvm):
        sc.wait_ge(b.dv.h, dvm)
        last = None
        for dst, srcap in pairs:
            last = sc.copy(dst, srcap)
        b.ac.inc(last)
    b.SC(sc_)
    acm = b.ac.n
    def v_(v, acm=acm):
        v.wait_ge(b.ac.h, acm)
    b.V(v_)


def emit_top2(b, logits_sb, scr):
    sh = (128, NTILE, E)
    m1 = scr["m1"][:, :, :]
    m2 = scr["m2"][:, :, :]
    m1b = scr["m1b"][:, :, :]
    m2b = scr["m2b"][:, :, :]
    oh1 = scr["oh1"][:, :, :]
    oh2 = scr["oh2"][:, :, :]
    t1 = scr["t1"][:, :, :]
    t2 = scr["t2"][:, :, :]
    e1 = scr["e1"][:, :, :]
    lg = logits_sb[:, :, :]

    def va(v):
        b.dv.inc(v.tensor_reduce(m1, lg, AX, ALU.max))
    b.V(va)
    bounce_copy(b, [(m1b, m1)])
    def vb(v):
        emit_first_max_ks(v, lg, m1b.broadcast_to(sh), oh1, t1, t2, sh)
        v.scalar_tensor_tensor(t1, oh1, -1e30, lg, ALU.mult, ALU.add)
        b.dv.inc(v.tensor_reduce(m2, t1, AX, ALU.max))
    b.V(vb)
    bounce_copy(b, [(m2b, m2)])
    def vc(v):
        emit_first_max_ks(v, t1, m2b.broadcast_to(sh), oh2, t2, e1, sh)
    b.V(vc)


def emit_routing_topk(b, logits_sb, scr, gates_bf, ident, ps_t, out_also=None):
    nc = b.nc
    sh = (128, NTILE, E)
    m1 = scr["m1"][:, :, :]
    m2 = scr["m2"][:, :, :]
    m1b = scr["m1b"][:, :, :]
    m2b = scr["m2b"][:, :, :]
    oh1 = scr["oh1"][:, :, :]
    oh2 = scr["oh2"][:, :, :]
    t1 = scr["t1"][:, :, :]
    e1 = scr["e1"][:, :, :]
    lg = logits_sb[:, :, :]
    emit_top2(b, logits_sb, scr)
    def vd(v):
        b.dv.inc(v.tensor_tensor(e1, lg, m1b.broadcast_to(sh), ALU.subtract))
    b.V(vd)
    dvm = b.dv.n
    def s1(sc, dvm=dvm):
        sc.wait_ge(b.dv.h, dvm)
        b.ac.inc(sc.activation(e1, e1, AF.Exp))
    b.SC(s1)
    acm = b.ac.n
    def v2(v, acm=acm):
        v.wait_ge(b.ac.h, acm)
        v.tensor_add(t1, oh1, oh2)
        v.tensor_mul(e1, e1, t1)
        b.dv.inc(v.tensor_reduce(m2, e1, AX, ALU.add))
    b.V(v2)
    bounce_copy(b, [(m2b, m2)])
    def v3(v):
        b.dv.inc(v.reciprocal(m1, m2b))
    b.V(v3)
    bounce_copy(b, [(m1b, m1)])
    def v4(v):
        b.dv.inc(v.tensor_mul(e1, e1, m1b.broadcast_to(sh)))
    b.V(v4)
    _gates_transpose(b, e1, gates_bf, ident, ps_t)


def _gates_transpose(b, gtm, gates_bf, ident, ps_t):
    dvm = b.dv.n
    rot = Rot(4)
    for i in range(NTILE):
        def tp(t, i=i, dvm=dvm):
            if i == 0:
                t.wait_ge(b.dv.h, dvm)
            s = rot.acquire(t)
            mm = t.transpose(ps_t[0:8, s * 512:s * 512 + 128], gtm[:, i, :], ident[:, :])
            b.pe.inc(mm)
            pem = b.pe.n
            def ev(v, i=i, s=s, pem=pem):
                v.wait_ge(b.pe.h, pem)
                b.dv.inc(v.tensor_copy(gates_bf[0:8, i * 128:(i + 1) * 128],
                                       ps_t[0:8, s * 512:s * 512 + 128]))
            b.V(ev)
            rot.done(s, b.dv)
        b.T(tp)


def emit_moe_join(b, x_src, x_dst, s_hbm, g_acc):
    nc = b.nc
    x_src_r = x_src.rearrange("(c p) t -> p c t", p=128)
    x_dst_r = x_dst.rearrange("(c p) t -> p c t", p=128)
    s_r = s_hbm.rearrange("(c p) t -> p c t", p=128)
    with ExitStack() as es:
        sb = lambda n, sh, dt: es.enter_context(nc.sbuf_tensor(n, sh, dt))
        xcc = sb("j_xc", [128, 2, NDC, TC], F32)
        scc = sb("j_sc", [128, 2, NDC, TC], BF16)
        gt = sb("j_gt", [128, NDC, TC], F32)
        xo = sb("j_xo", [128, 2, NDC, TC], F32)
        in_rot, out_rot = Rot(2), Rot(2)
        for i in range(NTC):
            par = [None]
            def ld(sy, i=i, par=par):
                par[0] = in_rot.acquire(sy)
                b.dsy.dma(sy.dma_start(out=xcc[:, par[0]], in_=x_src_r[:, :, i * TC:(i + 1) * TC]))
                b.dsy.dma(sy.dma_start(out=scc[:, par[0]], in_=s_r[:, :, i * TC:(i + 1) * TC]))
            b.SY(ld)
            par = par[0]
            dsym = b.dsy.n
            acm0 = b.ac.n
            def s_silu(sc, i=i, acm0=acm0):
                sc.wait_ge(b.ac.h, acm0)
                last = None
                for c in range(NDC):
                    last = sc.activation(gt[:, c, :], g_acc[:, c, i * TC:(i + 1) * TC], AF.Silu)
                b.ac.inc(last)
            b.SC(s_silu)
            acm = b.ac.n
            oslot = [None]
            def vj(v, par=par, acm=acm, dsym=dsym, oslot=oslot):
                v.wait_ge(b.ac.h, acm)
                v.wait_ge(b.dsy.h, dsym)
                oslot[0] = out_rot.acquire(v)
                last = None
                for c in range(NDC):
                    v.tensor_mul(gt[:, c, :], gt[:, c, :], scc[:, par, c, :])
                    last = v.tensor_add(xo[:, oslot[0], c, :], xcc[:, par, c, :], gt[:, c, :])
                b.dv.inc(last)
            b.V(vj)
            in_rot.done(par, b.dv)
            oslot = oslot[0]
            dvj = b.dv.n
            def s_guard(sc, dvj=dvj):
                sc.wait_ge(b.dv.h, dvj)
            b.SC(s_guard)
            dvm = b.dv.n
            def st(sy, i=i, oslot=oslot, dvm=dvm):
                sy.wait_ge(b.dv.h, dvm)
                b.dsy.dma(sy.dma_start(out=x_dst_r[:, :, i * TC:(i + 1) * TC], in_=xo[:, oslot]))
            b.SY(st)
            out_rot.done(oslot, b.dsy)
    b.barrier()


def emit_block1(b, iaps, x_src, x_dst, xn_hbm, s_hbm, ident, identb, ones, rows):
    nc = b.nc
    xn_r = xn_hbm.rearrange("(c p) t -> p c t", p=128)
    s_r = s_hbm.rearrange("(c p) t -> p c t", p=128)

    bes = ExitStack()
    gates_bf = bes.enter_context(nc.sbuf_tensor("l1_gbf", [8, S], BF16))

    with ExitStack() as res_:
        sb = lambda n, sh, dt: res_.enter_context(nc.sbuf_tensor(n, sh, dt))
        wr_sb = sb("l1_wr", [128, NDC, E], F32)
        br_row = sb("l1_br", [1, 8], F32)
        logits_sb = sb("l1_lg", [128, NTILE, E], F32)

        def ldr(sy):
            b.dsy.dma(sy.dma_start(out=wr_sb[:, :, :],
                                   in_=iaps["w_l1_Wr"].rearrange("(c p) e -> p c e", p=128)))
            b.dsy.dma(sy.dma_start(out=br_row[:, :], in_=iaps["w_l1_br"][:, :]))
        b.SY(ldr)
        b.barrier()
        emit_stage1(b, x_src, xn_hbm, ones, rows, wr_sb=wr_sb, logits_sb=logits_sb, br_row=br_row)

        scr = {k: sb("l1_scr_" + k, [128, NTILE, E], F32)
               for k in ("oh1", "oh2", "t1", "t2", "e1")}
        for k in ("m1", "m2", "m1b", "m2b"):
            scr[k] = sb("l1_" + k, [128, NTILE, 1], F32)
        ps_t = res_.enter_context(nc.psum_tensor("l1_pst", [128, 2048], F32))
        emit_routing_topk(b, logits_sb, scr, gates_bf, ident, ps_t)
        b.barrier()

    with ExitStack() as aes:
        sb = lambda n, sh, dt: aes.enter_context(nc.sbuf_tensor(n, sh, dt))
        psn = lambda n, sh: aes.enter_context(nc.psum_tensor(n, sh, F32))
        wt = sb("l1_wt", [128, NDC, D], BF16)
        wo = sb("l1_wo", [128, NDC, D], BF16)
        wgm = sb("l1_wgm", [128, 2 * NDC, D], BF16)
        dec_t = sb("l1_dect", [128, NDC, TC], F32)
        beta = sb("l1_beta", [128, NDC], F32)
        ln1w = sb("l1_ln1w", [128, NDC], F32)
        ln1b = sb("l1_ln1b", [128, NDC], F32)
        lncw = sb("l1_lncw", [128, NDC], F32)
        lncb = sb("l1_lncb", [128, NDC], F32)
        lngw = sb("l1_lngw", [128, NDC], F32)
        lngb = sb("l1_lngb", [128, NDC], F32)
        bg = sb("l1_bg", [128, NDC], F32)
        carry = sb("l1_carry", [128, NDC], F32)

        def ldw(g):
            b.dgp.dma(g.dma_start(out=wt[:, :, :], in_=iaps["w_l1_Wt"].rearrange("(c p) o -> p c o", p=128)))
            b.dgp.dma(g.dma_start(out=wo[:, :, :], in_=iaps["w_l1_Wo"].rearrange("(c p) o -> p c o", p=128)))
            b.dgp.dma(g.dma_start(out=wgm[:, :, :], in_=iaps["w_l1_Wg"].rearrange("(c p) o -> p c o", p=128)))
        b.G(ldw)
        def lds(sy):
            b.dsy.dma(sy.dma_start(out=dec_t[:, :, :], in_=iaps["c_l1dec"][:, :, :]))
            for nm, dst in (("c_l1beta", beta), ("w_l1_ln1w", ln1w), ("w_l1_ln1b", ln1b),
                            ("w_l1_lncw", lncw), ("w_l1_lncb", lncb), ("w_l1_lngw", lngw),
                            ("w_l1_lngb", lngb), ("w_l1_bg", bg)):
                b.dsy.dma(sy.dma_start(out=dst[:, :], in_=iaps[nm][:, :]))
        b.SY(lds)

        xnc = sb("l1_xnc", [128, 1, NDC, TC], BF16)
        xf = sb("l1_xf", [128, NDC, TC], F32)
        xb = sb("l1_xb", [128, NDC, TC], BF16)
        cx = sb("l1_cx", [128, NDC, TC], F32)
        cxnb = sb("l1_cxnb", [128, NDC, TC], BF16)
        outb = sb("l1_outb", [128, NDC, TC], BF16)
        gate = sb("l1_gate", [128, NDC, TC], BF16)
        ssb = sb("l1_ssb", [128, 1, NDC, TC], BF16)
        sq = sb("l1_sq", [128, NDC, TC], F32)
        accw = sb("l1_acc", [128, TC], F32)
        ps_a = psn("l1_psa", [128, 1024])
        ps_st = psn("l1_psst", [128, 1024])
        ps_bc = psn("l1_psbc", [128, 1024])
        work = dict(acc=accw, sq=sq, rows=rows, ps_st=ps_st, ps_bc=ps_bc, ones=ones)

        b.barrier()
        def vz(v):
            b.dv.inc(v.memset(carry[:, :], 0.0))
        b.V(vz)

        xnc_rot = Rot(1)
        ssb_rot = Rot(1)
        ra = Rot(2)
        for i in range(NTC):
            par = [None]
            def ld(sy, i=i, par=par):
                par[0] = xnc_rot.acquire(sy)
                b.dsy.dma(sy.dma_start(out=xnc[:, par[0]], in_=xn_r[:, :, i * TC:(i + 1) * TC]))
            b.SY(ld)
            par = par[0]
            dsym = b.dsy.n
            def w0(v, dsym=dsym):
                v.wait_ge(b.dsy.h, dsym)
            b.V(w0)
            def w0s(sc, dsym=dsym):
                sc.wait_ge(b.dsy.h, dsym)
            b.SC(w0s)

            emit_ln_chunk(b, xnc[:, par], xf, work, eps=1e-6, affine=(ln1w, ln1b))
            def vcast(v):
                last = None
                for c in range(NDC):
                    last = v.tensor_copy(xb[:, c, :], xf[:, c, :])
                b.dv.inc(last)
            b.V(vcast)
            xnc_rot.done(par, b.dv)
            dvm = b.dv.n

            def t_u(t, dvm=dvm):
                t.wait_ge(b.dv.h, dvm)
                for oc in range(NDC):
                    s = ra.acquire(t)
                    mm = mmgroup(t, ps_a[:, s * 512:(s + 1) * 512],
                                 [(wt[:, c, oc * 128:(oc + 1) * 128], xb[:, c, :]) for c in range(NDC)])
                    b.pe.inc(mm)
                    pem = b.pe.n
                    def evu(v, oc=oc, s=s, pem=pem):
                        v.wait_ge(b.pe.h, pem)
                        b.dv.inc(v.tensor_scalar(sq[:, oc, :], ps_a[:, s * 512:(s + 1) * 512],
                                                 beta[:, oc:oc + 1], None, ALU.mult))
                    b.V(evu)
                    ra.done(s, b.dv)
            b.T(t_u)

            dvm = b.dv.n
            def v_scan(v, dvm=dvm):
                v.wait_ge(b.dv.h, dvm)
                last = None
                for c in range(NDC):
                    last = v.tensor_tensor_scan(cx[:, c, :], dec_t[:, c, :], sq[:, c, :],
                                                carry[:, c:c + 1], ALU.mult, ALU.add)
                b.dv.inc(last)
            b.V(v_scan)
            dvm = b.dv.n
            def s_carry(sc, dvm=dvm):
                sc.wait_ge(b.dv.h, dvm)
                last = None
                for c in range(NDC):
                    last = sc.copy(carry[:, c:c + 1], cx[:, c, TC - 1:TC])
                b.ac.inc(last)
            b.SC(s_carry)
            acm = b.ac.n
            def vb1(v, acm=acm):
                v.wait_ge(b.ac.h, acm)
            b.V(vb1)

            emit_ln_chunk(b, cx, cxnb, work, eps=1e-6, affine=(lncw, lncb))
            dvm = b.dv.n

            def t_o(t, dvm=dvm):
                t.wait_ge(b.dv.h, dvm)
                for oc in range(NDC):
                    s = ra.acquire(t)
                    mm = mmgroup(t, ps_a[:, s * 512:(s + 1) * 512],
                                 [(wo[:, c, oc * 128:(oc + 1) * 128], cxnb[:, c, :]) for c in range(NDC)])
                    b.pe.inc(mm)
                    pem = b.pe.n
                    def evo(v, oc=oc, s=s, pem=pem):
                        v.wait_ge(b.pe.h, pem)
                        b.dv.inc(v.tensor_copy(outb[:, oc, :], ps_a[:, s * 512:(s + 1) * 512]))
                    b.V(evo)
                    ra.done(s, b.dv)
            b.T(t_o)
            dvm = b.dv.n

            def t_g(t, dvm=dvm):
                t.wait_ge(b.dv.h, dvm)
                for oc in range(NDC):
                    s = ra.acquire(t)
                    pairs = [(wgm[:, c, oc * 128:(oc + 1) * 128], xb[:, c, :]) for c in range(NDC)]
                    pairs += [(wgm[:, NDC + c, oc * 128:(oc + 1) * 128], outb[:, c, :]) for c in range(NDC)]
                    mm = mmgroup(t, ps_a[:, s * 512:(s + 1) * 512], pairs)
                    b.pe.inc(mm)
                    pem = b.pe.n
                    def evl(v, oc=oc, s=s, pem=pem):
                        v.wait_ge(b.pe.h, pem)
                        b.dv.inc(v.tensor_scalar(cx[:, oc, :], ps_a[:, s * 512:(s + 1) * 512],
                                                 bg[:, oc:oc + 1], None, ALU.add))
                    b.V(evl)
                    ra.done(s, b.dv)
            b.T(t_g)

            emit_ln_chunk(b, cx, cx, work, eps=1e-6, affine=(lngw, lngb))
            dvm = b.dv.n
            def s_sig(sc, dvm=dvm):
                sc.wait_ge(b.dv.h, dvm)
                last = None
                for c in range(NDC):
                    last = sc.activation(gate[:, c, :], cx[:, c, :], AF.Sigmoid)
                b.ac.inc(last)
            b.SC(s_sig)
            acm = b.ac.n

            sslot = [None]
            def v_blend(v, par=par, acm=acm, sslot=sslot):
                v.wait_ge(b.ac.h, acm)
                sslot[0] = ssb_rot.acquire(v)
                last = None
                for c in range(NDC):
                    v.tensor_sub(sq[:, c, :], outb[:, c, :], xf[:, c, :])
                    v.tensor_mul(sq[:, c, :], sq[:, c, :], gate[:, c, :])
                    last = v.tensor_add(ssb[:, sslot[0], c, :], xf[:, c, :], sq[:, c, :])
                b.dv.inc(last)
            b.V(v_blend)
            sslot = sslot[0]
            dvm = b.dv.n
            def st_s(sy, i=i, sslot=sslot, dvm=dvm):
                sy.wait_ge(b.dv.h, dvm)
                b.dsy.dma(sy.dma_start(out=s_r[:, :, i * TC:(i + 1) * TC], in_=ssb[:, sslot]))
            b.SY(st_s)
            ssb_rot.done(sslot, b.dsy)
        b.barrier()

    g_acc = bes.enter_context(nc.sbuf_tensor("l1_gacc", [128, NDC, S], BF16))
    with ExitStack() as fes:
        sb = lambda n, sh, dt: fes.enter_context(nc.sbuf_tensor(n, sh, dt))
        psn = lambda n, sh: fes.enter_context(nc.psum_tensor(n, sh, F32))
        FH = 2048
        NFC = FH // 128
        w1 = sb("l1f_w1", [128, 1, NDC, FH], BF16)
        w2 = sb("l1f_w2", [128, NFC, D], BF16)
        b1c = sb("l1f_b1", [128, E, NFC], F32)
        b2b = sb("l1f_b2", [8, D], BF16)
        xnc = sb("l1f_xnc", [128, 1, NDC, TC], BF16)
        grow = sb("l1f_grow", [1, 2, TC], BF16)
        hsb = sb("l1f_h", [128, 2, NFC, TC], BF16)
        gb = sb("l1f_gb", [128, 2, TC], F32)
        ps_h = psn("l1f_psh", [128, 1024])
        ps_g = psn("l1f_psg", [128, 1024])
        ps_k = psn("l1f_psk", [128, 1024])

        def lds(sy):
            b.dsy.dma(sy.dma_start(out=b1c[:, :, :], in_=iaps["w_l1_b1"][:, :, :]))
        b.SY(lds)
        def ldb2(g):
            b.dgp.dma(g.dma_start(out=b2b[:, :], in_=iaps["w_l1_b2"][:, :]))
        b.G(ldb2)

        onesb = b.onesb
        w1_rot = Rot(1)
        w2_rot = Rot(1)
        xn_rot = Rot(1)
        h_rot = Rot(2)
        gbrot = Rot(2)
        rh = Rot(2)
        rg = Rot(2)
        kg = Rot(2)
        grow_rot = Rot(2)
        def ldw1_0(g):
            w1_rot.acquire(g)
            b.dgp.dma(g.dma_start(out=w1[:, 0],
                                  in_=iaps["w_l1_W1"][0].rearrange("(c p) f -> p c f", p=128)))
        b.G(ldw1_0)
        dgm_w1_next = [b.dgp.n]
        for e in range(E):
            wpar = 0
            dgm_w1 = dgm_w1_next[0]
            def ldw2(g, e=e):
                w2_rot.acquire(g)
                b.dgp.dma(g.dma_start(out=w2[:, :, :],
                                      in_=iaps["w_l1_W2"][e].rearrange("(c p) d -> p c d", p=128)))
            b.G(ldw2)
            dgm_w2 = b.dgp.n
            pend = [None]
            def emit_td(t, info, e=e, dgm_w2=dgm_w2):
                i, hs, dvm_ = info
                t.wait_ge(b.dv.h, dvm_)
                t.wait_ge(b.dgp.h, dgm_w2)
                for dc in range(NDC):
                    s = rg.acquire(t)
                    pairs = []
                    if e == 0:
                        pairs.append((b2b[0:8, dc * 128:(dc + 1) * 128],
                                      gates_bf[0:8, i * TC:(i + 1) * TC]))
                    pairs += [(w2[:, fc, dc * 128:(dc + 1) * 128], hsb[:, hs, fc, :])
                              for fc in range(NFC)]
                    mm = mmgroup(t, ps_g[:, s * 512:(s + 1) * 512], pairs)
                    b.pe.inc(mm)
                    pem = b.pe.n
                    def evg(v, dc=dc, s=s, pem=pem, e=e, i=i):
                        v.wait_ge(b.pe.h, pem)
                        if e == 0:
                            b.dv.inc(v.tensor_copy(g_acc[:, dc, i * TC:(i + 1) * TC],
                                                   ps_g[:, s * 512:(s + 1) * 512]))
                        else:
                            b.dv.inc(v.tensor_tensor(g_acc[:, dc, i * TC:(i + 1) * TC],
                                                     ps_g[:, s * 512:(s + 1) * 512],
                                                     g_acc[:, dc, i * TC:(i + 1) * TC], ALU.add))
                    b.V(evg)
                    rg.done(s, b.dv)
            for i in range(NTC):
                xpar = [None]
                def ldx(sy, i=i, xpar=xpar):
                    xpar[0] = xn_rot.acquire(sy)
                    b.dsy.dma(sy.dma_start(out=xnc[:, xpar[0]], in_=xn_r[:, :, i * TC:(i + 1) * TC]))
                b.SY(ldx)
                xpar = xpar[0]
                dsym = b.dsy.n

                gs2 = [None]
                def ldgr(sc, e=e, i=i, gs2=gs2):
                    gs2[0] = grow_rot.acquire(sc)
                    b.dsc.dma(sc.dma_start(out=grow[:, gs2[0], :],
                                           in_=gates_bf[e:e + 1, i * TC:(i + 1) * TC]))
                b.SC(ldgr)
                gs2 = gs2[0]
                grm = b.dsc.n
                kgs = [None]
                def t_k(t, e=e, i=i, grm=grm, kgs=kgs, gs2=gs2):
                    t.wait_ge(b.dsc.h, grm)
                    kgs[0] = kg.acquire(t)
                    mm = t.matmul(ps_k[:, kgs[0] * 512:(kgs[0] + 1) * 512], onesb[0:1, 0:128],
                                  grow[0:1, gs2, :], start=True, stop=True)
                    b.pe.inc(mm)
                b.T(t_k)
                grow_rot.done(gs2, b.pe)
                kgs = kgs[0]
                pem = b.pe.n
                gbs = [None]
                def ev_k(v, pem=pem, kgs=kgs, gbs=gbs):
                    v.wait_ge(b.pe.h, pem)
                    gbs[0] = gbrot.acquire(v)
                    b.dv.inc(v.tensor_copy(gb[:, gbs[0], :], ps_k[:, kgs * 512:(kgs + 1) * 512]))
                b.V(ev_k)
                gbs = gbs[0]
                kg.done(kgs, b.dv)
                gbm = b.dv.n

                hs = [None]
                def pickh(sc, hs=hs):
                    hs[0] = h_rot.acquire(sc)
                b.SC(pickh)
                hs = hs[0]
                def t_h(t, e=e, i=i, xpar=xpar, dgm_w1=dgm_w1, dsym=dsym, hs=hs, wpar=wpar):
                    t.wait_ge(b.dgp.h, dgm_w1)
                    t.wait_ge(b.dsy.h, dsym)
                    for hc in range(NFC):
                        s = rh.acquire(t)
                        mm = mmgroup(t, ps_h[:, s * 512:(s + 1) * 512],
                                     [(w1[:, wpar, c, hc * 128:(hc + 1) * 128], xnc[:, xpar, c, :])
                                      for c in range(NDC)])
                        b.pe.inc(mm)
                        pem = b.pe.n
                        def evh(sc, hc=hc, s=s, pem=pem, e=e, hs=hs):
                            sc.wait_ge(b.pe.h, pem)
                            b.ac.inc(sc.activation(hsb[:, hs, hc, :], ps_h[:, s * 512:(s + 1) * 512],
                                                   AF.Relu, bias=b1c[:, e, hc:hc + 1]))
                        b.SC(evh)
                        rh.done(s, b.ac)
                    if pend[0] is not None:
                        emit_td(t, pend[0])
                        pend[0] = None
                b.T(t_h)
                xn_rot.done(xpar, b.pe)
                acm = b.ac.n
                def v_hg(v, acm=acm, gbm=gbm, hs=hs, gbs=gbs):
                    v.wait_ge(b.ac.h, acm)
                    v.wait_ge(b.dv.h, gbm)
                    b.dv.inc(v.tensor_tensor(
                        hsb[:, hs, :, :], hsb[:, hs, :, :],
                        gb[:, gbs, :].rearrange("p (o t) -> p o t", o=1).broadcast_to((128, NFC, TC)),
                        ALU.mult))
                b.V(v_hg)
                gbrot.done(gbs, b.dv)
                pend[0] = (i, hs, b.dv.n)
                h_rot.done(hs, b.pe)
            w1_rot.done(0, b.pe)
            if e + 1 < E:
                def ldw1_n(g, e=e):
                    w1_rot.acquire(g)
                    b.dgp.dma(g.dma_start(out=w1[:, 0],
                                          in_=iaps["w_l1_W1"][e + 1].rearrange("(c p) f -> p c f", p=128)))
                b.G(ldw1_n)
                dgm_w1_next[0] = b.dgp.n
            def t_tail(t):
                if pend[0] is not None:
                    emit_td(t, pend[0])
                    pend[0] = None
            b.T(t_tail)
            w2_rot.done(0, b.pe)
        b.barrier()

    emit_moe_join(b, x_src, x_dst, s_hbm, g_acc)
    bes.close()


def emit_block0(b, iaps, x_src, x_dst, xn_hbm, s_hbm, h_hbm, cc_in, cc_out, pfx_sb,
                ident, identb, ones, tri, rows):
    nc = b.nc
    xn_r = xn_hbm.rearrange("(c p) t -> p c t", p=128)
    s_r = s_hbm.rearrange("(c p) t -> p c t", p=128)
    h_r = h_hbm.rearrange("(c p) t -> p c t", p=128)

    bes = ExitStack()
    gates_bf = bes.enter_context(nc.sbuf_tensor("l0_gbf", [8, S], BF16))

    with ExitStack() as res_:
        sb = lambda n, sh, dt: res_.enter_context(nc.sbuf_tensor(n, sh, dt))
        wr_sb = sb("l0_wr", [128, NDC, E], F32)
        logits_sb = sb("l0_lg", [128, NTILE, E], F32)

        def ldr(sy):
            b.dsy.dma(sy.dma_start(out=wr_sb[:, :, :],
                                   in_=iaps["w_l0_Wr"].rearrange("(c p) e -> p c e", p=128)))
        b.SY(ldr)
        b.barrier()
        emit_stage1(b, x_src, xn_hbm, ones, rows, wr_sb=wr_sb, logits_sb=logits_sb)

        scr = {k: sb("l0_scr_" + k, [128, NTILE, E], F32)
               for k in ("oh1", "oh2", "t1", "t2", "e1")}
        for k in ("m1", "m2", "m1b", "m2b"):
            scr[k] = sb("l0_" + k, [128, NTILE, 1], F32)
        emit_top2(b, logits_sb, scr)

        ohs = sb("l0_ohs", [128, NTILE, 16], F32)
        w1t = sb("l0_w1t", [128, NTILE, 1], F32)
        rank_sb = sb("l0_rank", [128, NTILE, 16], F32)
        carry16 = sb("l0_carry16", [1, 16], F32)
        offs_bc = sb("l0_offs", [128, 16], F32)
        ccsb = sb("l0_ccsb", [8, 16], F32)
        ps_rk = res_.enter_context(nc.psum_tensor("l0_psrk", [128, 512], F32))
        ps_ct = res_.enter_context(nc.psum_tensor("l0_psct", [128, 512], F32))

        def v5(v):
            v.tensor_copy(ohs[:, :, 0:8], scr["oh1"][:, :, :])
            v.tensor_copy(ohs[:, :, 8:16], scr["oh2"][:, :, :])
            b.dv.inc(v.tensor_sub(scr["t1"][:, :, 0:1], scr["m1b"][:, :, :], scr["m2b"][:, :, :]))
        b.V(v5)
        dvm = b.dv.n
        def s5(sc, dvm=dvm):
            sc.wait_ge(b.dv.h, dvm)
            b.ac.inc(sc.activation(w1t[:, :, :], scr["t1"][:, :, 0:1], AF.Sigmoid))
        b.SC(s5)

        dvm = b.dv.n
        acm = b.ac.n
        cguard = [None]
        for ti in range(NTILE):
            def t_rk(t, ti=ti, dvm=dvm, acm=acm):
                if ti == 0:
                    t.wait_ge(b.dv.h, dvm)
                    t.wait_ge(b.ac.h, acm)
                else:
                    t.wait_ge(cguard[0][0].h, cguard[0][1])
                pairs = [(tri[:, :], ohs[:, ti, :])]
                if ti > 0:
                    pairs.append((ones[0:1, 0:128], carry16[0:1, :]))
                mm = mmgroup(t, ps_rk[:, 0:16], pairs)
                b.pe.inc(mm)
                mm = t.matmul(ps_ct[0:1, 0:16], ones[:, 0:1], ohs[:, ti, :],
                              start=(ti == 0), stop=True)
                b.pe.inc(mm)
            b.T(t_rk)
            pem = b.pe.n
            def ev_rk(v, ti=ti, pem=pem):
                v.wait_ge(b.pe.h, pem)
                v.tensor_copy(rank_sb[:, ti, :], ps_rk[:, 0:16])
                b.dv.inc(v.tensor_copy(carry16[0:1, :], ps_ct[0:1, 0:16]))
            b.V(ev_rk)
            cguard[0] = (b.dv, b.dv.n)

        dvm = b.dv.n
        def st_cc(sy, dvm=dvm):
            sy.wait_ge(b.dv.h, dvm)
            b.dsy.dma(sy.dma_start(out=cc_in[:, :], in_=carry16[0:1, :]))
        b.SY(st_cc)
        dsym = b.dsy.n
        def do_cc(g, dsym=dsym):
            g.wait_ge(b.dsy.h, dsym)
            b.cc.inc(g.collective_compute(
                "AllGather", ALU.bypass,
                replica_groups=[list(range(8))],
                ins=[cc_in[:, :]],
                outs=[cc_out[:, :]]), 1)
        b.G(do_cc)
        ccm = b.cc.n
        def ld_cc(sy, ccm=ccm):
            sy.wait_ge(b.cc.h, ccm)
            b.dsy.dma(sy.dma_start(out=ccsb[:, :], in_=cc_out[:, :]))
        b.SY(ld_cc)
        dsym = b.dsy.n
        def t_off(t, dsym=dsym):
            t.wait_ge(b.dsy.h, dsym)
            mm = t.matmul(ps_ct[0:1, 32:48], pfx_sb[:, :], ccsb[:, :], start=True, stop=True)
            b.pe.inc(mm)
        b.T(t_off)
        pem = b.pe.n
        def ev_off(v, pem=pem):
            v.wait_ge(b.pe.h, pem)
            b.dv.inc(v.tensor_copy(rows[0:1, 0:16], ps_ct[0:1, 32:48]))
        b.V(ev_off)
        dvm = b.dv.n
        def t_offb(t, dvm=dvm):
            t.wait_ge(b.dv.h, dvm)
            mm = t.matmul(ps_rk[:, 32:48], ones[0:1, 0:128], rows[0:1, 0:16], start=True, stop=True)
            b.pe.inc(mm)
        b.T(t_offb)
        pem = b.pe.n
        def ev_offb(v, pem=pem):
            v.wait_ge(b.pe.h, pem)
            b.dv.inc(v.tensor_copy(offs_bc[:, :], ps_rk[:, 32:48]))
        b.V(ev_offb)

        def v6(v):
            sh16 = (128, NTILE, 16)
            v.tensor_tensor(rank_sb[:, :, :], rank_sb[:, :, :],
                            offs_bc[:, :].rearrange("p (o f) -> p o f", o=1).broadcast_to(sh16),
                            ALU.add)
            v.tensor_single_scalar(rank_sb[:, :, :], rank_sb[:, :, :], float(CAPACITY), ALU.is_le)
            v.tensor_mul(rank_sb[:, :, :], rank_sb[:, :, :], ohs[:, :, :])
            sh8 = (128, NTILE, 8)
            v.tensor_mul(scr["t2"][:, :, :], rank_sb[:, :, 0:8], w1t[:, :, :].broadcast_to(sh8))
            v.tensor_scalar(scr["t1"][:, :, 0:1], w1t[:, :, :], -1.0, 1.0, ALU.mult, ALU.add)
            v.tensor_mul(scr["e1"][:, :, :], rank_sb[:, :, 8:16],
                         scr["t1"][:, :, 0:1].broadcast_to(sh8))
            b.dv.inc(v.tensor_add(scr["e1"][:, :, :], scr["e1"][:, :, :], scr["t2"][:, :, :]))
        b.V(v6)
        ps_t = res_.enter_context(nc.psum_tensor("l0_pst", [128, 2048], F32))
        _gates_transpose(b, scr["e1"][:, :, :], gates_bf, ident, ps_t)
        b.barrier()

    with ExitStack() as aes:
        sb = lambda n, sh, dt: aes.enter_context(nc.sbuf_tensor(n, sh, dt))
        psn = lambda n, sh: aes.enter_context(nc.psum_tensor(n, sh, F32))
        winT = sb("l0_win", [128, NDC, 512], BF16)
        woutT = sb("l0_wout", [128, 4, D], BF16)
        ldec = sb("l0_ldec", [128, 2, 1], F32)
        freq = sb("l0_freq", [128, 2, 1], F32)
        dcol = sb("l0_dcol", [128, 2, 1], F32)
        c1 = sb("l0_c1", [128, 2, 1], F32)
        s1v = sb("l0_s1", [128, 2, 1], F32)
        csc = sb("l0_csc", [128, 2, 1], F32)
        ssc = sb("l0_ssc", [128, 2, 1], F32)
        Ct = sb("l0_C", [128, 2, S], F32)
        St = sb("l0_S", [128, 2, S], F32)
        tmp2 = sb("l0_tmp2", [128, 2, 2048], F32)
        dec_tile = sb("l0_dect", [128, 4, TC], F32)

        def ldw(g):
            b.dgp.dma(g.dma_start(out=winT[:, :, :],
                                  in_=iaps["w_l0_WinT"].rearrange("(c p) f -> p c f", p=128)))
            b.dgp.dma(g.dma_start(out=woutT[:, :, :],
                                  in_=iaps["w_l0_WoutT"].rearrange("(c p) o -> p c o", p=128)))
        b.G(ldw)
        def lds(sy):
            b.dsy.dma(sy.dma_start(out=ldec[:, :, 0], in_=iaps["w_l0_logdec"][:, :]))
            b.dsy.dma(sy.dma_start(out=freq[:, :, 0], in_=iaps["w_l0_freq"][:, :]))
        b.SY(lds)
        dsym = b.dsy.n
        def s_trig(sc, dsym=dsym):
            sc.wait_ge(b.dsy.h, dsym)
            sc.activation(dcol[:, :, :], ldec[:, :, :], AF.Sigmoid)
            sc.activation(c1[:, :, :], freq[:, :, :], AF.Sin, bias=float(np.pi / 2), scale=0.1)
            b.ac.inc(sc.activation(s1v[:, :, :], freq[:, :, :], AF.Sin, scale=0.1))
        b.SC(s_trig)
        acm = b.ac.n
        def v_init(v, acm=acm):
            v.wait_ge(b.ac.h, acm)
            for j in range(4):
                v.tensor_scalar_mul(dec_tile[:, j, :], ones[:, 0:TC], dcol[:, j % 2, 0:1])
            v.memset(Ct[:, :, 0:1], 1.0)
            v.memset(St[:, :, 0:1], 0.0)
            v.tensor_copy(Ct[:, :, 1:2], c1[:, :, :])
            b.dv.inc(v.tensor_copy(St[:, :, 1:2], s1v[:, :, :]))
        b.V(v_init)

        vv = 2
        while vv < S:
            w = min(vv - 1, S - vv)
            dvm = b.dv.n
            def sc_b(sc, vv=vv, dvm=dvm):
                sc.wait_ge(b.dv.h, dvm)
                sc.copy(csc[:, :, :], Ct[:, :, vv - 1:vv])
                b.ac.inc(sc.copy(ssc[:, :, :], St[:, :, vv - 1:vv]))
            b.SC(sc_b)
            acm = b.ac.n
            def v_dbl(v, vv=vv, w=w, acm=acm):
                v.wait_ge(b.ac.h, acm)
                last = None
                for cr in range(2):
                    v.tensor_scalar_mul(tmp2[:, cr, 0:w], St[:, cr, 1:1 + w], ssc[:, cr, 0:1])
                    v.scalar_tensor_tensor(Ct[:, cr, vv:vv + w], Ct[:, cr, 1:1 + w],
                                           csc[:, cr, 0:1], tmp2[:, cr, 0:w],
                                           ALU.mult, ALU.subtract)
                    v.tensor_scalar_mul(tmp2[:, cr, 0:w], Ct[:, cr, 1:1 + w], ssc[:, cr, 0:1])
                    last = v.scalar_tensor_tensor(St[:, cr, vv:vv + w], St[:, cr, 1:1 + w],
                                                  csc[:, cr, 0:1], tmp2[:, cr, 0:w],
                                                  ALU.mult, ALU.add)
                b.dv.inc(last)
            b.V(v_dbl)
            vv = vv + w

        xnc = sb("l0_xnc", [128, 2, NDC, TC], BF16)
        u = sb("l0_u", [128, 4, TC], F32)
        us = sb("l0_us", [128, 4, TC], F32)
        cxs = sb("l0_cx", [128, 4, TC], F32)
        c2t = sb("l0_c2", [128, 2, TC], F32)
        s2t = sb("l0_s2", [128, 2, TC], F32)
        fb = sb("l0_fb", [128, 4, TC], BF16)
        ssb = sb("l0_ssb", [128, NDC, TC], BF16)
        carry = sb("l0_carry", [128, 4], F32)
        ps_a = psn("l0_psa", [128, 1024])
        b.barrier()

        def vz(v):
            b.dv.inc(v.memset(carry[:, :], 0.0))
        b.V(vz)

        xn_rot = Rot(2)
        ssb_rot = Rot(1)
        ra = Rot(2)
        for i in range(NTC):
            par = [None]
            def ld(sy, i=i, par=par):
                par[0] = xn_rot.acquire(sy)
                b.dsy.dma(sy.dma_start(out=xnc[:, par[0]], in_=xn_r[:, :, i * TC:(i + 1) * TC]))
            b.SY(ld)
            par = par[0]
            dsym = b.dsy.n

            def t_sp(t, par=par, dsym=dsym):
                t.wait_ge(b.dsy.h, dsym)
                for fc in range(4):
                    s = ra.acquire(t)
                    mm = mmgroup(t, ps_a[:, s * 512:(s + 1) * 512],
                                 [(winT[:, c, fc * 128:(fc + 1) * 128], xnc[:, par, c, :])
                                  for c in range(NDC)])
                    b.pe.inc(mm)
                    pem = b.pe.n
                    def evs(v, fc=fc, s=s, pem=pem):
                        v.wait_ge(b.pe.h, pem)
                        b.dv.inc(v.tensor_copy(u[:, fc, :], ps_a[:, s * 512:(s + 1) * 512]))
                    b.V(evs)
                    ra.done(s, b.dv)
            b.T(t_sp)
            xn_rot.done(par, b.pe)

            def v_tw(v, i=i):
                sl = slice(i * TC, (i + 1) * TC)
                last = None
                for cr in range(2):
                    v.tensor_mul(us[:, cr, :], u[:, cr, :], Ct[:, cr, sl])
                    v.tensor_mul(cxs[:, cr, :], u[:, 2 + cr, :], St[:, cr, sl])
                    v.tensor_add(us[:, cr, :], us[:, cr, :], cxs[:, cr, :])
                    v.tensor_mul(us[:, 2 + cr, :], u[:, 2 + cr, :], Ct[:, cr, sl])
                    v.tensor_mul(cxs[:, 2 + cr, :], u[:, cr, :], St[:, cr, sl])
                    last = v.tensor_sub(us[:, 2 + cr, :], us[:, 2 + cr, :], cxs[:, 2 + cr, :])
                b.dv.inc(last)
            b.V(v_tw)

            dvm = b.dv.n
            def v_scan(v, dvm=dvm):
                v.wait_ge(b.dv.h, dvm)
                last = None
                for j in range(4):
                    last = v.tensor_tensor_scan(cxs[:, j, :], dec_tile[:, j, :], us[:, j, :],
                                                carry[:, j:j + 1], ALU.mult, ALU.add)
                b.dv.inc(last)
            b.V(v_scan)
            dvm = b.dv.n
            def s_carry(sc, dvm=dvm):
                sc.wait_ge(b.dv.h, dvm)
                last = None
                for j in range(4):
                    last = sc.copy(carry[:, j:j + 1], cxs[:, j, TC - 1:TC])
                b.ac.inc(last)
            b.SC(s_carry)
            acm = b.ac.n

            def v_ut(v, i=i, acm=acm):
                v.wait_ge(b.ac.h, acm)
                sl = slice(i * TC, (i + 1) * TC)
                for cr in range(2):
                    v.tensor_scalar_mul(c2t[:, cr, :], St[:, cr, sl], s1v[:, cr, 0:1])
                    v.scalar_tensor_tensor(c2t[:, cr, :], Ct[:, cr, sl], c1[:, cr, 0:1],
                                           c2t[:, cr, :], ALU.mult, ALU.subtract)
                    v.tensor_scalar_mul(s2t[:, cr, :], Ct[:, cr, sl], s1v[:, cr, 0:1])
                    v.scalar_tensor_tensor(s2t[:, cr, :], St[:, cr, sl], c1[:, cr, 0:1],
                                           s2t[:, cr, :], ALU.mult, ALU.add)
                last = None
                for cr in range(2):
                    v.tensor_mul(us[:, cr, :], cxs[:, cr, :], c2t[:, cr, :])
                    v.tensor_mul(us[:, 2 + cr, :], cxs[:, 2 + cr, :], s2t[:, cr, :])
                    v.tensor_sub(fb[:, cr, :], us[:, cr, :], us[:, 2 + cr, :])
                    v.tensor_mul(us[:, cr, :], cxs[:, cr, :], s2t[:, cr, :])
                    v.tensor_mul(us[:, 2 + cr, :], cxs[:, 2 + cr, :], c2t[:, cr, :])
                    last = v.tensor_add(fb[:, 2 + cr, :], us[:, cr, :], us[:, 2 + cr, :])
                b.dv.inc(last)
            b.V(v_ut)
            dvm = b.dv.n

            sslot = [None]
            def pick(v, sslot=sslot):
                sslot[0] = ssb_rot.acquire(v)
            b.V(pick)
            sslot = sslot[0]
            def t_wo(t, dvm=dvm, sslot=sslot):
                t.wait_ge(b.dv.h, dvm)
                for oc in range(NDC):
                    s = ra.acquire(t)
                    mm = mmgroup(t, ps_a[:, s * 512:(s + 1) * 512],
                                 [(woutT[:, fc, oc * 128:(oc + 1) * 128], fb[:, fc, :])
                                  for fc in range(4)])
                    b.pe.inc(mm)
                    pem = b.pe.n
                    def evo(v, oc=oc, s=s, pem=pem, sslot=sslot):
                        v.wait_ge(b.pe.h, pem)
                        b.dv.inc(v.tensor_copy(ssb[:, oc, :], ps_a[:, s * 512:(s + 1) * 512]))
                    b.V(evo)
                    ra.done(s, b.dv)
            b.T(t_wo)
            dvm = b.dv.n
            def st_s(sy, i=i, dvm=dvm):
                sy.wait_ge(b.dv.h, dvm)
                b.dsy.dma(sy.dma_start(out=s_r[:, :, i * TC:(i + 1) * TC], in_=ssb[:, :, :]))
            b.SY(st_s)
            ssb_rot.done(0, b.dsy)
        b.barrier()

    g_acc = bes.enter_context(nc.sbuf_tensor("l0_gacc", [128, NDC, S], BF16))
    onesb = b.onesb
    FH = 4096
    NFC = FH // 128
    h_guard = [None]
    with ExitStack() as fes:
        sb = lambda n, sh, dt: fes.enter_context(nc.sbuf_tensor(n, sh, dt))
        psn = lambda n, sh: fes.enter_context(nc.psum_tensor(n, sh, F32))
        b1c = sb("l0f_b1", [128, E, NFC], F32)
        b2b = sb("l0f_b2", [8, D], BF16)
        grow = sb("l0f_grow", [1, 2, TC], BF16)
        gb = sb("l0f_gb", [128, TC], F32)
        gt1 = sb("l0f_gt1", [128, TC], F32)
        ps_h = psn("l0f_psh", [128, 1024])
        ps_g = psn("l0f_psg", [128, 1024])
        ps_k = psn("l0f_psk", [128, 512])

        def lds(sy):
            b.dsy.dma(sy.dma_start(out=b1c[:, :, :], in_=iaps["w_l0_b1"][:, :, :]))
        b.SY(lds)
        def ldb2(g):
            b.dgp.dma(g.dma_start(out=b2b[:, :], in_=iaps["w_l0_b2"][:, :]))
        b.G(ldb2)

        w1_rot = Rot(1)
        w2_rot = Rot(1)
        xn_rot = Rot(2)
        hst_rot = Rot(2)
        rh = Rot(2)
        rg = Rot(2)
        kg = Rot(2)
        grow_rot = Rot(1)

        dgm0 = b.dgp.n
        for i in range(NTC):
            def t_b2(t, i=i, dgm0=dgm0):
                if i == 0:
                    t.wait_ge(b.dgp.h, dgm0)
                for dc in range(NDC):
                    s = rg.acquire(t)
                    mm = t.matmul(ps_g[:, s * 512:(s + 1) * 512],
                                  b2b[0:8, dc * 128:(dc + 1) * 128],
                                  gates_bf[0:8, i * TC:(i + 1) * TC], start=True, stop=True)
                    b.pe.inc(mm)
                    pem = b.pe.n
                    def evb(v, dc=dc, s=s, pem=pem, i=i):
                        v.wait_ge(b.pe.h, pem)
                        b.dv.inc(v.tensor_copy(g_acc[:, dc, i * TC:(i + 1) * TC],
                                               ps_g[:, s * 512:(s + 1) * 512]))
                    b.V(evb)
                    rg.done(s, b.dv)
            b.T(t_b2)

        for e in range(E):
            p1es = ExitStack()
            w1 = p1es.enter_context(nc.sbuf_tensor(f"l0f_w1_{e}", [128, NDC, FH], BF16))
            xnc = p1es.enter_context(nc.sbuf_tensor(f"l0f_xnc_{e}", [128, 2, NDC, TC], BF16))
            hst = p1es.enter_context(nc.sbuf_tensor(f"l0f_hst_{e}", [128, 2, 8, TC], BF16))
            hg0 = h_guard[0]
            dvg0 = b.dv.n
            def ldw1(g, e=e, hg0=hg0, dvg0=dvg0):
                if hg0 is not None:
                    g.wait_ge(hg0[0].h, hg0[1])
                    g.wait_ge(b.dv.h, dvg0)
                w1_rot.acquire(g)
                b.dgp.dma(g.dma_start(out=w1[:, :, :],
                                      in_=iaps["w_l0_W1"][e].rearrange("(c p) f -> p c f", p=128)))
            b.G(ldw1)
            dgm = b.dgp.n
            for i in range(NTC):
                xpar = [None]
                def ldx(sy, i=i, xpar=xpar, hg0=hg0):
                    if i == 0 and hg0 is not None:
                        sy.wait_ge(hg0[0].h, hg0[1])
                    xpar[0] = xn_rot.acquire(sy)
                    b.dsy.dma(sy.dma_start(out=xnc[:, xpar[0]], in_=xn_r[:, :, i * TC:(i + 1) * TC]))
                b.SY(ldx)
                xpar = xpar[0]
                dsym = b.dsy.n
                for hg in range(4):
                    hpar = [None]
                    def pickh(sc, hpar=hpar):
                        hpar[0] = hst_rot.acquire(sc)
                    b.SC(pickh)
                    hpar = hpar[0]
                    def t_h(t, e=e, i=i, hg=hg, xpar=xpar, dgm=dgm, dsym=dsym, hpar=hpar):
                        if hg == 0:
                            t.wait_ge(b.dgp.h, dgm)
                            t.wait_ge(b.dsy.h, dsym)
                        if h_guard[0] is not None and hg == 0 and i == 0:
                            t.wait_ge(h_guard[0][0].h, h_guard[0][1])
                        for hh in range(8):
                            hc = hg * 8 + hh
                            s = rh.acquire(t)
                            mm = mmgroup(t, ps_h[:, s * 512:(s + 1) * 512],
                                         [(w1[:, c, hc * 128:(hc + 1) * 128], xnc[:, xpar, c, :])
                                          for c in range(NDC)])
                            b.pe.inc(mm)
                            pem = b.pe.n
                            def evh(sc, hh=hh, hc=hc, s=s, pem=pem, e=e, hpar=hpar):
                                sc.wait_ge(b.pe.h, pem)
                                b.ac.inc(sc.activation(hst[:, hpar, hh, :],
                                                       ps_h[:, s * 512:(s + 1) * 512],
                                                       AF.Gelu, bias=b1c[:, e, hc:hc + 1]))
                            b.SC(evh)
                            rh.done(s, b.ac)
                    b.T(t_h)
                    acm = b.ac.n
                    def st_h(sy, i=i, hg=hg, hpar=hpar, acm=acm):
                        sy.wait_ge(b.ac.h, acm)
                        b.dsy.dma(sy.dma_start(
                            out=h_r[:, hg * 8:(hg + 1) * 8, i * TC:(i + 1) * TC],
                            in_=hst[:, hpar]))
                    b.SY(st_h)
                    hst_rot.done(hpar, b.dsy)
                xn_rot.done(xpar, b.pe)
            p1_store_mark = b.dsy.n
            w1_rot.done(0, b.pe)
            p1_pe_mark = b.pe.n
            p1_ac_mark = b.ac.n
            p1es.close()
            if DEBUG_DUMP and e == 0:
                dbg_hh = nc.dram_tensor("dbg_hh", [4096, S], BF16, kind="ExternalOutput").ap()
                def dmp_hh(sy, p1_store_mark=p1_store_mark):
                    sy.wait_ge(b.dsy.h, p1_store_mark)
                    b.dsy.dma(sy.dma_start(out=dbg_hh[:, :], in_=h_hbm[:, :]))
                b.SY(dmp_hh)

            p2es = ExitStack()
            w2 = p2es.enter_context(nc.sbuf_tensor(f"l0f_w2_{e}", [128, NFC, D], BF16))
            hld = p2es.enter_context(nc.sbuf_tensor(f"l0f_hld_{e}", [128, 3, 16, TC], BF16))
            hld_rot = Rot(3)
            def ldw2(g, e=e, p1_pe_mark=p1_pe_mark, p1_ac_mark=p1_ac_mark):
                g.wait_ge(b.pe.h, p1_pe_mark)
                g.wait_ge(b.ac.h, p1_ac_mark)
                g.wait_ge(b.dsy.h, p1_store_mark)
                w2_rot.acquire(g)
                b.dgp.dma(g.dma_start(out=w2[:, :, :],
                                      in_=iaps["w_l0_W2"][e].rearrange("(c p) d -> p c d", p=128)))
            b.G(ldw2)
            dgm2 = b.dgp.n
            for i in range(NTC):
                hpar = [None, None]
                def ldh(sy, i=i, hpar=hpar, p1_store_mark=p1_store_mark,
                        p1_pe_mark=p1_pe_mark, p1_ac_mark=p1_ac_mark):
                    sy.wait_ge(b.dsy.h, p1_store_mark)
                    if i == 0:
                        sy.wait_ge(b.pe.h, p1_pe_mark)
                        sy.wait_ge(b.ac.h, p1_ac_mark)
                    hpar[0] = hld_rot.acquire(sy)
                    b.dsy.dma(sy.dma_start(out=hld[:, hpar[0]],
                                           in_=h_r[:, 0:16, i * TC:(i + 1) * TC]))
                    hpar[1] = hld_rot.acquire(sy)
                    b.dsy.dma(sy.dma_start(out=hld[:, hpar[1]],
                                           in_=h_r[:, 16:32, i * TC:(i + 1) * TC]))
                b.SY(ldh)
                dsym = b.dsy.n

                gs2 = [None]
                def ldgr(sc, e=e, i=i, gs2=gs2):
                    gs2[0] = grow_rot.acquire(sc)
                    b.dsc.dma(sc.dma_start(out=grow[:, gs2[0], :],
                                           in_=gates_bf[e:e + 1, i * TC:(i + 1) * TC]))
                b.SC(ldgr)
                gs2 = gs2[0]
                grm = b.dsc.n
                def t_k(t, e=e, i=i, grm=grm, gs2=gs2):
                    t.wait_ge(b.dsc.h, grm)
                    kg.acquire(t)
                    mm = t.matmul(ps_k[:, 0:512], onesb[0:1, 0:128],
                                  grow[0:1, gs2, :], start=True, stop=True)
                    b.pe.inc(mm)
                b.T(t_k)
                grow_rot.done(gs2, b.pe)
                pem = b.pe.n
                def ev_k(v, pem=pem):
                    v.wait_ge(b.pe.h, pem)
                    b.dv.inc(v.tensor_copy(gb[:, :], ps_k[:, 0:512]))
                b.V(ev_k)
                kg.done(0, b.dv)
                gbm = b.dv.n

                def t_d(t, e=e, i=i, hpar=hpar, dgm2=dgm2, dsym=dsym):
                    t.wait_ge(b.dgp.h, dgm2)
                    t.wait_ge(b.dsy.h, dsym)
                    for dc in range(NDC):
                        s = rg.acquire(t)
                        pairs = [(w2[:, fc, dc * 128:(dc + 1) * 128],
                                  hld[:, hpar[fc // 16], fc % 16, :])
                                 for fc in range(NFC)]
                        mm = mmgroup(t, ps_g[:, s * 512:(s + 1) * 512], pairs)
                        b.pe.inc(mm)
                        pem = b.pe.n
                        def evg(v, dc=dc, s=s, pem=pem, e=e, i=i, gbm=gbm):
                            v.wait_ge(b.pe.h, pem)
                            v.wait_ge(b.dv.h, gbm)
                            v.tensor_tensor(gt1[:, :], ps_g[:, s * 512:(s + 1) * 512],
                                            gb[:, :], ALU.mult)
                            b.dv.inc(v.tensor_tensor(g_acc[:, dc, i * TC:(i + 1) * TC],
                                                     gt1[:, :],
                                                     g_acc[:, dc, i * TC:(i + 1) * TC],
                                                     ALU.add))
                        b.V(evg)
                        rg.done(s, b.dv)
                b.T(t_d)
                hld_rot.done(hpar[0], b.pe)
                hld_rot.done(hpar[1], b.pe)
            w2_rot.done(0, b.pe)
            grow_rot.done(0, b.pe)
            h_guard[0] = (b.pe, b.pe.n)
            p2es.close()
        b.barrier()
    emit_moe_join(b, x_src, x_dst, s_hbm, g_acc)
    bes.close()


def emit_routing_sparsemixer(b, logits_sb, scr, gates_bf, ident, ps_t):
    sh = (128, NTILE, E)
    m1b = scr["m1b"][:, :, :]
    m2b = scr["m2b"][:, :, :]
    m1 = scr["m1"][:, :, :]
    m2 = scr["m2"][:, :, :]
    oh1 = scr["oh1"][:, :, :]
    oh2 = scr["oh2"][:, :, :]
    t1 = scr["t1"][:, :, :]
    t2 = scr["t2"][:, :, :]
    e1 = scr["e1"][:, :, :]
    k1 = scr["k1"][:, :, :]
    k2 = scr["k2"][:, :, :]
    d1 = scr["d1"][:, :, :]
    lg = logits_sb[:, :, :]

    emit_top2(b, logits_sb, scr)
    dvm = b.dv.n
    def sA(sc, dvm=dvm):
        sc.wait_ge(b.dv.h, dvm)
        b.ac.inc(sc.activation(t2, lg, AF.Abs))
    b.SC(sA)
    acm = b.ac.n
    def v1(v, acm=acm):
        v.wait_ge(b.ac.h, acm)
        v.tensor_tensor(d1, lg, m1b.broadcast_to(sh), ALU.subtract)
        v.tensor_tensor(k1, t2, m1b.broadcast_to(sh), ALU.max)
        v.scalar_tensor_tensor(k1, k1, 0.02, d1, ALU.mult, ALU.add)
        v.tensor_single_scalar(k1, k1, 0.0, ALU.is_ge)
        v.tensor_tensor(e1, lg, m2b.broadcast_to(sh), ALU.subtract)
        v.tensor_tensor(k2, t2, m2b.broadcast_to(sh), ALU.max)
        v.scalar_tensor_tensor(k2, k2, 0.02, e1, ALU.mult, ALU.add)
        v.tensor_single_scalar(k2, k2, 0.0, ALU.is_ge)
        v.tensor_scalar(t1, oh1, -1.0, 1.0, ALU.mult, ALU.add)
        v.tensor_mul(k2, k2, t1)
        b.dv.inc(v.tensor_single_scalar(e1, e1, 0.0, ALU.min))
    b.V(v1)
    dvm = b.dv.n
    def sE(sc, dvm=dvm):
        sc.wait_ge(b.dv.h, dvm)
        sc.activation(d1, d1, AF.Exp)
        b.ac.inc(sc.activation(e1, e1, AF.Exp))
    b.SC(sE)
    acm = b.ac.n
    def v2(v, acm=acm):
        v.wait_ge(b.ac.h, acm)
        v.tensor_mul(d1, d1, k1)
        v.tensor_mul(e1, e1, k2)
        v.tensor_reduce(m1, d1, AX, ALU.add)
        b.dv.inc(v.tensor_reduce(m2, e1, AX, ALU.add))
    b.V(v2)
    bounce_copy(b, [(m1b, m1), (m2b, m2)])
    def v3(v):
        v.reciprocal(m1, m1b)
        b.dv.inc(v.reciprocal(m2, m2b))
    b.V(v3)
    bounce_copy(b, [(m1b, m1), (m2b, m2)])
    def v4(v):
        v.tensor_mul(t1, oh1, m1b.broadcast_to(sh))
        v.tensor_mul(t2, oh2, m2b.broadcast_to(sh))
        b.dv.inc(v.tensor_add(e1, t1, t2))
    b.V(v4)
    _gates_transpose(b, e1, gates_bf, ident, ps_t)


def emit_block2(b, iaps, x_src, x_dst, xn_hbm, s_hbm, h_hbm, y_hbm,
                ident, identb, ones, rows):
    nc = b.nc
    xn_r = xn_hbm.rearrange("(c p) t -> p c t", p=128)
    s_r = s_hbm.rearrange("(c p) t -> p c t", p=128)
    y_r = y_hbm.rearrange("(c p) t -> p c t", p=128)
    lat_r = h_hbm.rearrange("(c p) t -> p c t", p=128)
    lat_rows = h_hbm

    bes = ExitStack()
    gates_bf = bes.enter_context(nc.sbuf_tensor("l2_gbf", [8, S], BF16))

    with ExitStack() as res_:
        sb = lambda n, sh, dt: res_.enter_context(nc.sbuf_tensor(n, sh, dt))
        wr_sb = sb("l2_wr", [128, NDC, E], F32)
        logits_sb = sb("l2_lg", [128, NTILE, E], F32)
        def ldr(sy):
            b.dsy.dma(sy.dma_start(out=wr_sb[:, :, :],
                                   in_=iaps["w_l2_Wr"].rearrange("(c p) e -> p c e", p=128)))
        b.SY(ldr)
        b.barrier()
        emit_stage1(b, x_src, xn_hbm, ones, rows, wr_sb=wr_sb, logits_sb=logits_sb)
        scr = {k: sb("l2_scr_" + k, [128, NTILE, E], F32)
               for k in ("oh1", "oh2", "t1", "t2", "e1", "k1", "k2", "d1")}
        for k in ("m1", "m2", "m1b", "m2b"):
            scr[k] = sb("l2_" + k, [128, NTILE, 1], F32)
        ps_t = res_.enter_context(nc.psum_tensor("l2_pst", [128, 2048], F32))
        emit_routing_sparsemixer(b, logits_sb, scr, gates_bf, ident, ps_t)
        b.barrier()

    with ExitStack() as aes:
        sb = lambda n, sh, dt: aes.enter_context(nc.sbuf_tensor(n, sh, dt))
        wq = sb("l2a_wq", [128, NDC, D], BF16)
        wk = sb("l2a_wk", [128, NDC, D], BF16)
        wv = sb("l2a_wv", [128, NDC, D], BF16)
        xnc = sb("l2a_xnc", [128, 2, NDC, TC], BF16)
        lst = sb("l2a_lst", [128, 2, NDC, TC], BF16)
        ps_a = aes.enter_context(nc.psum_tensor("l2a_ps", [128, 1024], F32))
        def ldw(g):
            b.dgp.dma(g.dma_start(out=wq[:, :, :], in_=iaps["w_l2_Wqd"].rearrange("(c p) o -> p c o", p=128)))
            b.dgp.dma(g.dma_start(out=wk[:, :, :], in_=iaps["w_l2_Wkd"].rearrange("(c p) o -> p c o", p=128)))
            b.dgp.dma(g.dma_start(out=wv[:, :, :], in_=iaps["w_l2_Wvd"].rearrange("(c p) o -> p c o", p=128)))
        b.G(ldw)
        b.barrier()
        xn_rot = Rot(2)
        lst_rot = Rot(2)
        ra = Rot(2)
        for i in range(NTC):
            par = [None]
            def ld(sy, i=i, par=par):
                par[0] = xn_rot.acquire(sy)
                b.dsy.dma(sy.dma_start(out=xnc[:, par[0]], in_=xn_r[:, :, i * TC:(i + 1) * TC]))
            b.SY(ld)
            par = par[0]
            dsym = b.dsy.n
            for pi, wmat in ((0, wq), (1, wk), (2, wv)):
                lpar = [None]
                def pickl(v, lpar=lpar):
                    lpar[0] = lst_rot.acquire(v)
                b.V(pickl)
                lpar = lpar[0]
                def t_l(t, pi=pi, wmat=wmat, par=par, dsym=dsym, lpar=lpar):
                    if pi == 0:
                        t.wait_ge(b.dsy.h, dsym)
                    for oc in range(NDC):
                        s = ra.acquire(t)
                        mm = mmgroup(t, ps_a[:, s * 512:(s + 1) * 512],
                                     [(wmat[:, c, oc * 128:(oc + 1) * 128], xnc[:, par, c, :])
                                      for c in range(NDC)])
                        b.pe.inc(mm)
                        pem = b.pe.n
                        def evl(v, oc=oc, s=s, pem=pem, lpar=lpar):
                            v.wait_ge(b.pe.h, pem)
                            b.dv.inc(v.tensor_copy(lst[:, lpar, oc, :], ps_a[:, s * 512:(s + 1) * 512]))
                        b.V(evl)
                        ra.done(s, b.dv)
                b.T(t_l)
                dvm = b.dv.n
                def st_l(sy, i=i, pi=pi, lpar=lpar, dvm=dvm):
                    sy.wait_ge(b.dv.h, dvm)
                    b.dsy.dma(sy.dma_start(out=lat_r[:, pi * 8:(pi + 1) * 8, i * TC:(i + 1) * TC],
                                           in_=lst[:, lpar]))
                b.SY(st_l)
                lst_rot.done(lpar, b.dsy)
            xn_rot.done(par, b.pe)
        b.barrier()

    with ExitStack() as hes:
        sb = lambda n, sh, dt: hes.enter_context(nc.sbuf_tensor(n, sh, dt))
        psn = lambda n, sh: hes.enter_context(nc.psum_tensor(n, sh, F32))
        upq = sb("l2b_upq", [64, 96], BF16)
        upk = sb("l2b_upk", [64, 96], BF16)
        upv = sb("l2b_upv", [64, 64], BF16)
        cosq = sb("l2b_cosq", [32, S], F32)
        sinq = sb("l2b_sinq", [32, S], F32)
        cosk = sb("l2b_cosk", [32, S], F32)
        sink = sb("l2b_sink", [32, S], F32)
        msk = sb("l2b_msk", [128, 4, TC], BF16)
        qlat = sb("l2b_qlat", [64, S], BF16)
        klat = sb("l2b_klat", [64, S], BF16)
        vlat = sb("l2b_vlat", [64, S], BF16)
        q_aug = sb("l2b_qaug", [128, S], BF16)
        k_aug = sb("l2b_kaug", [128, S], BF16)
        vsb = sb("l2b_vsb", [64, S], BF16)
        vaug = sb("l2b_vaug", [128, NTILE, 65], BF16)
        qsq = sb("l2b_qsq", [64, S], F32)
        rsb = sb("l2b_rsb", [1, 2 * S], F32)
        rtmp = sb("l2b_rt", [32, 2, TC], F32)
        psb = sb("l2b_psb", [128, 2, TC], BF16)
        ytmp = sb("l2b_yt", [64, TC], F32)
        dtmp = sb("l2b_dt", [64, TC], F32)
        dtmp2 = sb("l2b_dt2", [64, TC], F32)
        yout = sb("l2b_yo", [64, 2, TC], BF16)
        ps_s = psn("l2b_pss", [128, 1024])
        ps_vt = hes.enter_context(nc.psum_tensor("l2b_psvt", [128, 2, 1024], BF16))
        ps_o = psn("l2b_pso", [65, 512])
        ps_u = psn("l2b_psu", [96, 512])
        ps_bc = psn("l2b_psbc", [64, 512])
        ps_r = psn("l2b_psr", [1, 512])

        def ldc(g):
            b.dgp.dma(g.dma_start(out=upq[:, :], in_=iaps["w_l2_upq"][:, :]))
            b.dgp.dma(g.dma_start(out=upk[:, :], in_=iaps["w_l2_upk"][:, :]))
            b.dgp.dma(g.dma_start(out=upv[:, :], in_=iaps["w_l2_upv"][:, :]))
            b.dgp.dma(g.dma_start(out=msk[:, :, :], in_=iaps["c_l2mask"][:, :, :]))
        b.G(ldc)
        def ldc2(sy):
            b.dsy.dma(sy.dma_start(out=cosq[:, :], in_=iaps["c_l2cosq"][:, :]))
            b.dsy.dma(sy.dma_start(out=sinq[:, :], in_=iaps["c_l2sinq"][:, :]))
            b.dsy.dma(sy.dma_start(out=cosk[:, :], in_=iaps["c_l2cosk"][:, :]))
            b.dsy.dma(sy.dma_start(out=sink[:, :], in_=iaps["c_l2sink"][:, :]))
        b.SY(ldc2)
        b.barrier()
        def v_rot(v):
            v.tensor_scalar_mul(upq[:, 64:96:2], upq[:, 33:64:2], -1.0)
            v.tensor_copy(upq[:, 65:96:2], upq[:, 32:64:2])
            v.tensor_scalar_mul(upk[:, 64:96:2], upk[:, 33:64:2], -1.0)
            b.dv.inc(v.tensor_copy(upk[:, 65:96:2], upk[:, 32:64:2]))
        b.V(v_rot)
        b.barrier()

        lat_rot = Rot(1)
        rs = Rot(2)
        rp = Rot(2)
        ru = Rot(1)
        yrot = Rot(2)
        for h in range(16):
            def ldl(sy, h=h):
                lat_rot.acquire(sy)
                b.dsy.dma(sy.dma_start(out=qlat[:, :], in_=lat_rows[h * 64:(h + 1) * 64, :]))
                b.dsy.dma(sy.dma_start(out=klat[:, :], in_=lat_rows[1024 + h * 64:1024 + (h + 1) * 64, :]))
                b.dsy.dma(sy.dma_start(out=vlat[:, :], in_=lat_rows[2048 + h * 64:2048 + (h + 1) * 64, :]))
            b.SY(ldl)
            dsym = b.dsy.n

            for pi in range(8):
                sl = slice(pi * TC, (pi + 1) * TC)
                def t_up(t, pi=pi, dsym=dsym):
                    if pi == 0:
                        t.wait_ge(b.dsy.h, dsym)
                    ru.acquire(t)
                    mm = t.matmul(ps_u[:, :], upq[:, :], qlat[:, pi * TC:(pi + 1) * TC],
                                  start=True, stop=True)
                    b.pe.inc(mm)
                b.T(t_up)
                pem = b.pe.n
                def ev_q(sc, pi=pi, pem=pem, sl=sl):
                    sc.wait_ge(b.pe.h, pem)
                    b.ac.inc(sc.activation(q_aug[0:32, sl], ps_u[0:32, :], AF.Copy, scale=0.125))
                b.SC(ev_q)
                acm = b.ac.n
                def ev_qe(v, pi=pi, sl=sl, acm=acm):
                    v.wait_ge(b.ac.h, acm)
                    v.tensor_tensor(rtmp[:, 0, :], ps_u[32:64, :], cosq[:, sl], ALU.mult)
                    v.tensor_tensor(rtmp[:, 1, :], ps_u[64:96, :], sinq[:, sl], ALU.mult)
                    b.dv.inc(v.tensor_add(q_aug[32:64, sl], rtmp[:, 0, :], rtmp[:, 1, :]))
                b.V(ev_qe)
                ru.done(0, b.dv)
                def t_upk(t, pi=pi):
                    ru.acquire(t)
                    mm = t.matmul(ps_u[:, :], upk[:, :], klat[:, pi * TC:(pi + 1) * TC],
                                  start=True, stop=True)
                    b.pe.inc(mm)
                b.T(t_upk)
                pem = b.pe.n
                def ev_k(sc, pi=pi, pem=pem, sl=sl):
                    sc.wait_ge(b.pe.h, pem)
                    b.ac.inc(sc.copy(k_aug[0:32, sl], ps_u[0:32, :]))
                b.SC(ev_k)
                acm = b.ac.n
                def ev_ke(v, pi=pi, sl=sl, acm=acm):
                    v.wait_ge(b.ac.h, acm)
                    v.tensor_tensor(rtmp[:, 0, :], ps_u[32:64, :], cosk[:, sl], ALU.mult)
                    v.tensor_tensor(rtmp[:, 1, :], ps_u[64:96, :], sink[:, sl], ALU.mult)
                    b.dv.inc(v.tensor_add(k_aug[32:64, sl], rtmp[:, 0, :], rtmp[:, 1, :]))
                b.V(ev_ke)
                ru.done(0, b.dv)

            rr_rot = Rot(1)
            for which, src_aug, off in (("q", q_aug, 0), ("k", k_aug, S)):
                dvm0 = b.dv.n
                pem0 = b.pe.n
                def s_sq2(sc, src_aug=src_aug, dvm0=dvm0, pem0=pem0):
                    sc.wait_ge(b.dv.h, dvm0)
                    sc.wait_ge(b.pe.h, pem0)
                    b.ac.inc(sc.activation(qsq[:, :], src_aug[0:64, :], AF.Square))
                b.SC(s_sq2)
                acm = b.ac.n
                for pi in range(8):
                    def t_r(t, pi=pi, acm=acm):
                        if pi == 0:
                            t.wait_ge(b.ac.h, acm)
                        rr_rot.acquire(t)
                        mm = t.matmul(ps_r[:, :], ones[0:64, 0:1], qsq[:, pi * TC:(pi + 1) * TC],
                                      start=True, stop=True)
                        b.pe.inc(mm)
                    b.T(t_r)
                    pem = b.pe.n
                    def ev_r(v, pi=pi, pem=pem, off=off):
                        v.wait_ge(b.pe.h, pem)
                        b.dv.inc(v.tensor_copy(rsb[0:1, off + pi * TC:off + (pi + 1) * TC], ps_r[:, :]))
                    b.V(ev_r)
                    rr_rot.done(0, b.dv)
            dvm = b.dv.n
            def v_km(v, dvm=dvm):
                b.dv.inc(v.tensor_reduce(rows[0:1, 0:1], rsb[0:1, S:2 * S], AX, ALU.max))
            b.V(v_km)
            bounce_copy(b, [(rows[0:1, 4:5], rows[0:1, 0:1])])
            dvm = b.dv.n
            def s_nb(sc, dvm=dvm):
                sc.wait_ge(b.dv.h, dvm)
                sc.activation(rows[0:1, 8:9], rows[0:1, 4:5], AF.Sqrt)
                b.ac.inc(sc.activation(rsb[0:1, 0:S], rsb[0:1, 0:S], AF.Sqrt))
            b.SC(s_nb)
            acm = b.ac.n
            def v_nb(v, acm=acm):
                v.wait_ge(b.ac.h, acm)
                b.dv.inc(v.tensor_scalar(q_aug[64:65, :], rsb[0:1, 0:S],
                                         rows[0:1, 8:9], -1.01, ALU.mult, ALU.mult))
            b.V(v_nb)
            def v_ones(v):
                b.dv.inc(v.memset(k_aug[64:65, :], 1.0))
            b.V(v_ones)

            for pi in range(8):
                def t_uv(t, pi=pi):
                    ru.acquire(t)
                    mm = t.matmul(ps_u[0:64, :], upv[:, :], vlat[:, pi * TC:(pi + 1) * TC],
                                  start=True, stop=True)
                    b.pe.inc(mm)
                b.T(t_uv)
                pem = b.pe.n
                def ev_uv(v, pi=pi, pem=pem):
                    v.wait_ge(b.pe.h, pem)
                    b.dv.inc(v.tensor_copy(vsb[:, pi * TC:(pi + 1) * TC], ps_u[0:64, :]))
                b.V(ev_uv)
                ru.done(0, b.dv)

            def v_vz(v):
                b.dv.inc(v.memset(vaug[:, :, 64:65], 1.0))
            b.V(v_vz)
            dvm = b.dv.n
            rv = Rot(2)
            for kt in range(NTILE):
                def t_v(t, kt=kt, dvm=dvm):
                    if kt == 0:
                        t.wait_ge(b.dv.h, dvm)
                    s = rv.acquire(t)
                    mm = t.transpose(ps_vt[:, s, 0:64],
                                     vsb[:, kt * 128:(kt + 1) * 128], identb[0:64, 0:64])
                    b.pe.inc(mm)
                    pem = b.pe.n
                    def ev_v(v, kt=kt, s=s, pem=pem):
                        v.wait_ge(b.pe.h, pem)
                        b.dv.inc(v.tensor_copy(vaug[:, kt, 0:64], ps_vt[:, s, 0:64]))
                    b.V(ev_v)
                    rv.done(s, b.dv)
                b.T(t_v)

            if DEBUG_DUMP and h == 0:
                dbg_qa = nc.dram_tensor("dbg_qa", [128, S], BF16, kind="ExternalOutput").ap()
                dbg_ka = nc.dram_tensor("dbg_ka", [128, S], BF16, kind="ExternalOutput").ap()
                dbg_va = nc.dram_tensor("dbg_va", [128, NTILE * 65], BF16, kind="ExternalOutput").ap()
                dvq = b.dv.n
                def dmpq(sy, dvq=dvq):
                    sy.wait_ge(b.dv.h, dvq)
                    b.dsy.dma(sy.dma_start(out=dbg_qa[:, :], in_=q_aug[:, :]))
                    b.dsy.dma(sy.dma_start(out=dbg_ka[:, :], in_=k_aug[:, :]))
                    b.dsy.dma(sy.dma_start(out=dbg_va.rearrange("p (a c) -> p a c", c=65), in_=vaug[:, :, :]))
                b.SY(dmpq)
            dvm_all = b.dv.n
            po_rot = globals().get("_l2_po_rot")
            if po_rot is None or h == 0:
                po_rot = Rot(1)
                globals()["_l2_po_rot"] = po_rot
            for qp in range(8):
                nkt = 4 * qp + 4
                pend = [None]
                def emit_av(t, info, nkt=nkt):
                    kt, sl2, acm_, dvm_ = info
                    t.wait_ge(b.ac.h, acm_)
                    if dvm_:
                        t.wait_ge(b.dv.h, dvm_)
                    if kt == 0:
                        po_rot.acquire(t)
                    mm = t.matmul(ps_o[:, :], vaug[:, kt, :], psb[:, sl2, :],
                                  start=(kt == 0), stop=(kt == nkt - 1))
                    b.pe.inc(mm)
                for kt in range(nkt):
                    sref = [None]
                    def t_sw(t, sref=sref, qp=qp, kt=kt, dvm_all=dvm_all):
                        if qp == 0 and kt == 0:
                            t.wait_ge(b.dv.h, dvm_all)
                        s = rs.acquire(t)
                        sref[0] = s
                        mm = t.matmul(ps_s[:, s * 512:(s + 1) * 512],
                                      k_aug[0:65, kt * 128:(kt + 1) * 128],
                                      q_aug[0:65, qp * TC:(qp + 1) * TC],
                                      start=True, stop=True)
                        b.pe.inc(mm)
                        if pend[0] is not None:
                            emit_av(t, pend[0])
                            pend[0] = None
                    b.T(t_sw)
                    s = sref[0]
                    pem = b.pe.n
                    sl2ref = [None]
                    def s_e(sc, s=s, pem=pem, sl2ref=sl2ref):
                        sc.wait_ge(b.pe.h, pem)
                        sl2 = rp.acquire(sc)
                        sl2ref[0] = sl2
                        b.ac.inc(sc.activation(psb[:, sl2, :], ps_s[:, s * 512:(s + 1) * 512], AF.Exp))
                    b.SC(s_e)
                    sl2 = sl2ref[0]
                    rs.done(s, b.ac)
                    acm_ = b.ac.n
                    dvm_ = 0
                    r = kt - 4 * qp
                    if r >= 0:
                        def v_m(v, sl2=sl2, r=r, acm_=acm_):
                            v.wait_ge(b.ac.h, acm_)
                            b.dv.inc(v.tensor_mul(psb[:, sl2, :], psb[:, sl2, :], msk[:, r, :]))
                        b.V(v_m)
                        dvm_ = b.dv.n
                    pend[0] = (kt, sl2, acm_, dvm_)
                    rp.done(sl2, b.pe)
                def t_fl(t):
                    if pend[0] is not None:
                        emit_av(t, pend[0])
                        pend[0] = None
                b.T(t_fl)

                pem = b.pe.n
                def s_den(sc, pem=pem):
                    sc.wait_ge(b.pe.h, pem)
                    sc.copy(rows[0:1, 16:528], ps_o[64:65, :])
                    b.ac.inc(sc.copy(ytmp[:, :], ps_o[0:64, :]))
                b.SC(s_den)
                acm = b.ac.n
                def t_bc(t, acm=acm):
                    t.wait_ge(b.ac.h, acm)
                    mm = t.matmul(ps_bc[:, :], ones[0:1, 0:64], rows[0:1, 16:528],
                                  start=True, stop=True)
                    b.pe.inc(mm)
                b.T(t_bc)
                pem = b.pe.n
                def v_rc(v, pem=pem):
                    v.wait_ge(b.pe.h, pem)
                    b.dv.inc(v.reciprocal(dtmp[:, :], ps_bc[:, :]))
                b.V(v_rc)
                dvm = b.dv.n
                def s_rb(sc, dvm=dvm):
                    sc.wait_ge(b.dv.h, dvm)
                    b.ac.inc(sc.copy(dtmp2[:, :], dtmp[:, :]))
                b.SC(s_rb)
                acm = b.ac.n
                ypar = [None]
                def v_y(v, acm=acm, ypar=ypar):
                    v.wait_ge(b.ac.h, acm)
                    ypar[0] = yrot.acquire(v)
                    b.dv.inc(v.tensor_mul(yout[:, ypar[0], :], ytmp[:, :], dtmp2[:, :]))
                b.V(v_y)
                po_rot.done(0, b.dv)
                ypar = ypar[0]
                dvm = b.dv.n
                def st_y(sy, h=h, qp=qp, ypar=ypar, dvm=dvm):
                    sy.wait_ge(b.dv.h, dvm)
                    b.dsy.dma(sy.dma_start(out=y_hbm[h * 64:(h + 1) * 64, qp * TC:(qp + 1) * TC],
                                           in_=yout[:, ypar, :]))
                b.SY(st_y)
                yrot.done(ypar, b.dsy)
            lat_rot.done(0, b.pe)
        b.barrier()

    with ExitStack() as ces:
        sb = lambda n, sh, dt: ces.enter_context(nc.sbuf_tensor(n, sh, dt))
        wc = sb("l2c_wc", [128, NDC, D], BF16)
        ych = sb("l2c_y", [128, 2, NDC, TC], BF16)
        ssb = sb("l2c_s", [128, 2, NDC, TC], BF16)
        ps_a = ces.enter_context(nc.psum_tensor("l2c_ps", [128, 1024], F32))
        def ldw(g):
            b.dgp.dma(g.dma_start(out=wc[:, :, :], in_=iaps["w_l2_Wc"].rearrange("(c p) o -> p c o", p=128)))
        b.G(ldw)
        b.barrier()
        y_rot = Rot(2)
        s_rot = Rot(2)
        ra = Rot(2)
        for i in range(NTC):
            par = [None]
            def ld(sy, i=i, par=par):
                par[0] = y_rot.acquire(sy)
                b.dsy.dma(sy.dma_start(out=ych[:, par[0]], in_=y_r[:, :, i * TC:(i + 1) * TC]))
            b.SY(ld)
            par = par[0]
            dsym = b.dsy.n
            spar = [None]
            def picks(v, spar=spar):
                spar[0] = s_rot.acquire(v)
            b.V(picks)
            spar = spar[0]
            def t_c(t, par=par, dsym=dsym, spar=spar):
                t.wait_ge(b.dsy.h, dsym)
                for oc in range(NDC):
                    s = ra.acquire(t)
                    mm = mmgroup(t, ps_a[:, s * 512:(s + 1) * 512],
                                 [(wc[:, c, oc * 128:(oc + 1) * 128], ych[:, par, c, :])
                                  for c in range(NDC)])
                    b.pe.inc(mm)
                    pem = b.pe.n
                    def evc(v, oc=oc, s=s, pem=pem, spar=spar):
                        v.wait_ge(b.pe.h, pem)
                        b.dv.inc(v.tensor_copy(ssb[:, spar, oc, :], ps_a[:, s * 512:(s + 1) * 512]))
                    b.V(evc)
                    ra.done(s, b.dv)
            b.T(t_c)
            y_rot.done(par, b.pe)
            dvm = b.dv.n
            def st_s(sy, i=i, spar=spar, dvm=dvm):
                sy.wait_ge(b.dv.h, dvm)
                b.dsy.dma(sy.dma_start(out=s_r[:, :, i * TC:(i + 1) * TC], in_=ssb[:, spar]))
            b.SY(st_s)
            s_rot.done(spar, b.dsy)
        b.barrier()

    g_acc = bes.enter_context(nc.sbuf_tensor("l2_gacc", [128, NDC, S], BF16))
    onesb = b.onesb
    FH = 2048
    NFC = FH // 128
    h_guard = [None]
    with ExitStack() as fes:
        sb = lambda n, sh, dt: fes.enter_context(nc.sbuf_tensor(n, sh, dt))
        psn = lambda n, sh: fes.enter_context(nc.psum_tensor(n, sh, F32))
        grow = sb("l2f_grow", [1, S], BF16)
        gb = sb("l2f_gb", [128, TC], F32)
        gt1 = sb("l2f_gt1", [128, TC], F32)
        st1 = sb("l2f_st1", [128, 2, TC], F32)
        ps_h = psn("l2f_psh", [128, 1024])
        ps_p = psn("l2f_psp", [128, 1024])
        ps_g = psn("l2f_psg", [128, 1024])
        ps_k = psn("l2f_psk", [128, 512])

        w1_rot = Rot(1)
        w2_rot = Rot(1)
        xn_rot = Rot(2)
        hst_rot = Rot(2)
        rh = Rot(2)
        rp2 = Rot(2)
        st1_rot = Rot(2)
        rg = Rot(2)
        kg = Rot(1)
        grow_rot = Rot(1)
        for e in range(E):
            p1es = ExitStack()
            wg1 = p1es.enter_context(nc.sbuf_tensor(f"l2f_wg_{e}", [128, NDC, FH], BF16))
            wu1 = p1es.enter_context(nc.sbuf_tensor(f"l2f_wu_{e}", [128, NDC, FH], BF16))
            xnc = p1es.enter_context(nc.sbuf_tensor(f"l2f_xnc_{e}", [128, 2, NDC, TC], BF16))
            hst = p1es.enter_context(nc.sbuf_tensor(f"l2f_hst_{e}", [128, 2, 8, TC], BF16))
            hg0 = h_guard[0]
            dvg0 = b.dv.n
            def ldw1(g, e=e, hg0=hg0, dvg0=dvg0):
                if hg0 is not None:
                    g.wait_ge(hg0[0].h, hg0[1])
                    g.wait_ge(b.dv.h, dvg0)
                w1_rot.acquire(g)
                b.dgp.dma(g.dma_start(out=wg1[:, :, :],
                                      in_=iaps["w_l2_Wg"][e].rearrange("(c p) f -> p c f", p=128)))
                b.dgp.dma(g.dma_start(out=wu1[:, :, :],
                                      in_=iaps["w_l2_Wu"][e].rearrange("(c p) f -> p c f", p=128)))
            b.G(ldw1)
            dgm = b.dgp.n
            for i in range(NTC):
                xpar = [None]
                def ldx(sy, i=i, xpar=xpar, hg0=hg0):
                    if i == 0 and hg0 is not None:
                        sy.wait_ge(hg0[0].h, hg0[1])
                    xpar[0] = xn_rot.acquire(sy)
                    b.dsy.dma(sy.dma_start(out=xnc[:, xpar[0]], in_=xn_r[:, :, i * TC:(i + 1) * TC]))
                b.SY(ldx)
                xpar = xpar[0]
                dsym = b.dsy.n
                for hg in range(2):
                    hpar = [None]
                    def pickh(sc, hpar=hpar):
                        hpar[0] = hst_rot.acquire(sc)
                    b.SC(pickh)
                    hpar = hpar[0]
                    def t_h(t, e=e, i=i, hg=hg, xpar=xpar, dgm=dgm, dsym=dsym, hpar=hpar):
                        if hg == 0:
                            t.wait_ge(b.dgp.h, dgm)
                            t.wait_ge(b.dsy.h, dsym)
                        if h_guard[0] is not None and hg == 0 and i == 0:
                            t.wait_ge(h_guard[0][0].h, h_guard[0][1])
                        for hh in range(8):
                            hc = hg * 8 + hh
                            s = rh.acquire(t)
                            mm = mmgroup(t, ps_h[:, s * 512:(s + 1) * 512],
                                         [(wg1[:, c, hc * 128:(hc + 1) * 128], xnc[:, xpar, c, :])
                                          for c in range(NDC)])
                            b.pe.inc(mm)
                            s2 = rp2.acquire(t)
                            mm = mmgroup(t, ps_p[:, s2 * 512:(s2 + 1) * 512],
                                         [(wu1[:, c, hc * 128:(hc + 1) * 128], xnc[:, xpar, c, :])
                                          for c in range(NDC)])
                            b.pe.inc(mm)
                            pem = b.pe.n
                            s1ref = [None]
                            def evh(sc, hh=hh, s=s, pem=pem, s1ref=s1ref):
                                sc.wait_ge(b.pe.h, pem)
                                s1s = st1_rot.acquire(sc)
                                s1ref[0] = s1s
                                b.ac.inc(sc.activation(st1[:, s1s, :], ps_h[:, s * 512:(s + 1) * 512], AF.Silu))
                            b.SC(evh)
                            s1s = s1ref[0]
                            rh.done(s, b.ac)
                            acm = b.ac.n
                            def evh2(v, hh=hh, s2=s2, acm=acm, hpar=hpar, s1s=s1s):
                                v.wait_ge(b.ac.h, acm)
                                b.dv.inc(v.tensor_tensor(hst[:, hpar, hh, :], st1[:, s1s, :],
                                                         ps_p[:, s2 * 512:(s2 + 1) * 512], ALU.mult))
                            b.V(evh2)
                            rp2.done(s2, b.dv)
                            st1_rot.done(s1s, b.dv)
                    b.T(t_h)
                    dvm = b.dv.n
                    def st_h(sy, i=i, hg=hg, hpar=hpar, dvm=dvm):
                        sy.wait_ge(b.dv.h, dvm)
                        b.dsy.dma(sy.dma_start(
                            out=lat_r[:, hg * 8:(hg + 1) * 8, i * TC:(i + 1) * TC],
                            in_=hst[:, hpar]))
                    b.SY(st_h)
                    hst_rot.done(hpar, b.dsy)
                xn_rot.done(xpar, b.pe)
            p1_store_mark = b.dsy.n
            w1_rot.done(0, b.pe)
            p1_pe_mark = b.pe.n
            p1_dv_mark = b.dv.n
            p1es.close()

            p2es = ExitStack()
            w2 = p2es.enter_context(nc.sbuf_tensor(f"l2f_wd_{e}", [128, NFC, D], BF16))
            hld = p2es.enter_context(nc.sbuf_tensor(f"l2f_hld_{e}", [128, 2, NFC, TC], BF16))
            hld_rot = Rot(2)
            def ldw2(g, e=e, p1_pe_mark=p1_pe_mark, p1_dv_mark=p1_dv_mark):
                g.wait_ge(b.pe.h, p1_pe_mark)
                g.wait_ge(b.dv.h, p1_dv_mark)
                g.wait_ge(b.dsy.h, p1_store_mark)
                w2_rot.acquire(g)
                b.dgp.dma(g.dma_start(out=w2[:, :, :],
                                      in_=iaps["w_l2_Wd"][e].rearrange("(c p) d -> p c d", p=128)))
            b.G(ldw2)
            dgm2 = b.dgp.n
            def ldgr(sc, e=e):
                grow_rot.acquire(sc)
                b.dsc.dma(sc.dma_start(out=grow[:, :], in_=gates_bf[e:e + 1, :]))
            b.SC(ldgr)
            grm = b.dsc.n
            for i in range(NTC):
                hpar = [None]
                def ldh(sy, i=i, hpar=hpar, p1_store_mark=p1_store_mark,
                        p1_pe_mark=p1_pe_mark, p1_dv_mark=p1_dv_mark):
                    sy.wait_ge(b.dsy.h, p1_store_mark)
                    if i == 0:
                        sy.wait_ge(b.pe.h, p1_pe_mark)
                        sy.wait_ge(b.dv.h, p1_dv_mark)
                    hpar[0] = hld_rot.acquire(sy)
                    b.dsy.dma(sy.dma_start(out=hld[:, hpar[0]],
                                           in_=lat_r[:, 0:NFC, i * TC:(i + 1) * TC]))
                b.SY(ldh)
                hpar = hpar[0]
                dsym = b.dsy.n
                def t_k(t, e=e, i=i, grm=grm):
                    t.wait_ge(b.dsc.h, grm)
                    kg.acquire(t)
                    mm = t.matmul(ps_k[:, 0:512], onesb[0:1, 0:128],
                                  grow[0:1, i * TC:(i + 1) * TC], start=True, stop=True)
                    b.pe.inc(mm)
                b.T(t_k)
                pem = b.pe.n
                def ev_k(v, pem=pem):
                    v.wait_ge(b.pe.h, pem)
                    b.dv.inc(v.tensor_copy(gb[:, :], ps_k[:, 0:512]))
                b.V(ev_k)
                kg.done(0, b.dv)
                gbm = b.dv.n
                def t_d(t, e=e, i=i, hpar=hpar, dgm2=dgm2, dsym=dsym):
                    t.wait_ge(b.dgp.h, dgm2)
                    t.wait_ge(b.dsy.h, dsym)
                    for dc in range(NDC):
                        s = rg.acquire(t)
                        mm = mmgroup(t, ps_g[:, s * 512:(s + 1) * 512],
                                     [(w2[:, fc, dc * 128:(dc + 1) * 128], hld[:, hpar, fc, :])
                                      for fc in range(NFC)])
                        b.pe.inc(mm)
                        pem = b.pe.n
                        def evg(v, dc=dc, s=s, pem=pem, e=e, i=i, gbm=gbm):
                            v.wait_ge(b.pe.h, pem)
                            v.wait_ge(b.dv.h, gbm)
                            if e == 0:
                                b.dv.inc(v.tensor_tensor(g_acc[:, dc, i * TC:(i + 1) * TC],
                                                         ps_g[:, s * 512:(s + 1) * 512],
                                                         gb[:, :], ALU.mult))
                            else:
                                v.tensor_tensor(gt1[:, :], ps_g[:, s * 512:(s + 1) * 512],
                                                gb[:, :], ALU.mult)
                                b.dv.inc(v.tensor_tensor(g_acc[:, dc, i * TC:(i + 1) * TC],
                                                         gt1[:, :],
                                                         g_acc[:, dc, i * TC:(i + 1) * TC],
                                                         ALU.add))
                        b.V(evg)
                        rg.done(s, b.dv)
                b.T(t_d)
                hld_rot.done(hpar, b.pe)
            w2_rot.done(0, b.pe)
            h_guard[0] = (b.pe, b.pe.n)
            p2es.close()
        b.barrier()
    emit_moe_join(b, x_src, x_dst, s_hbm, g_acc)
    bes.close()



def _transpose_pass(b, nc, src_kind, i, pb, tin, tout, ident, src_ap, dst_ap,
                    in_rot, out_rot, ps_rot):
    par = [None]
    def ld(sy):
        par[0] = in_rot.acquire(sy)
        b.dsy.dma(sy.dma_start(out=tin[:, par[0]] if src_kind == "tm" else
                               tin[:, par[0]].rearrange("p (c q) -> p c q", q=128),
                               in_=src_ap))
    b.SY(ld)
    par = par[0]
    dsym = b.dsy.n
    oslot = [None]
    def pickv(v):
        oslot[0] = out_rot.acquire(v)
    b.V(pickv)
    def tp(t):
        t.wait_ge(b.dsy.h, dsym)
        for c in range(NDC):
            s = ps_rot.acquire(t)
            mm = t.transpose(pb[:, s * 512:s * 512 + 128], tin[:, par, c * 128:(c + 1) * 128], ident[:, :])
            b.pe.inc(mm)
            pem = b.pe.n
            def ev(v, c=c, s=s, pem=pem):
                v.wait_ge(b.pe.h, pem)
                b.dv.inc(v.tensor_copy(tout[:, oslot[0], c * 128:(c + 1) * 128],
                                       pb[:, s * 512:s * 512 + 128]))
            b.V(ev)
            ps_rot.done(s, b.dv)
    b.T(tp)
    in_rot.done(par, b.pe)
    oslot = oslot[0]
    dvm = b.dv.n
    def st(sy):
        sy.wait_ge(b.dv.h, dvm)
        b.dsy.dma(sy.dma_start(out=dst_ap,
                               in_=tout[:, oslot].rearrange("p (c q) -> p c q", q=128)
                               if src_kind == "tm" else tout[:, oslot]))
    b.SY(st)
    out_rot.done(oslot, b.dsy)


def build_program(host_arrays, blocks=(0, 1, 2, 3)):
    nc = bass.Bass("TRN2", target_bir_lowering=False, debug=False)
    import itertools
    _cnt = itertools.count()
    _orig_sb = nc.sbuf_tensor
    _orig_ps = nc.psum_tensor
    def _sb(name, shape, dtype, **kw):
        return _orig_sb(f"{name}_{next(_cnt)}", shape, dtype, **kw)
    def _ps(name, shape, dtype, **kw):
        return _orig_ps(f"{name}_{next(_cnt)}", shape, dtype, **kw)
    nc.sbuf_tensor = _sb
    nc.psum_tensor = _ps

    iaps = {}
    for name, arr in host_arrays.items():
        iaps[name] = nc.dram_tensor(name, list(arr.shape), F32, kind="ExternalInput").ap()
    x_in = nc.dram_tensor("x", [S, D], F32, kind="ExternalInput").ap()
    pfx = nc.dram_tensor("pfx", [8, 1], F32, kind="ExternalInput").ap()
    out = nc.dram_tensor("out", [S, D], F32, kind="ExternalOutput").ap()

    xfa = nc.dram_tensor("xfa", [D, S], F32).ap()
    xfb = nc.dram_tensor("xfb", [D, S], F32).ap()
    xn_hbm = nc.dram_tensor("xn_hbm", [D, S], BF16).ap()
    s_hbm = nc.dram_tensor("s_hbm", [D, S], BF16).ap()
    y_hbm = nc.dram_tensor("y_hbm", [D, S], BF16).ap()
    h_hbm = nc.dram_tensor("h_hbm", [4096, S], BF16).ap()
    cc_in = nc.dram_tensor("cc_in", [1, 16], F32).ap()
    cc_out = nc.dram_tensor("cc_out", [8, 16], F32).ap()

    es = ExitStack()
    with es:
        block = es.enter_context(nc.Block())
        b = Bld(nc, es, block)

        ident = b.sb("ident", [128, 128], F32)
        identb = b.sb("identb", [128, 128], BF16)
        onesb = b.sb("onesb", [1, 128], BF16)
        onescb = b.sb("onescb", [128, 1], BF16)
        ones = b.sb("ones", [128, 512], F32)
        tri = b.sb("tri", [128, 128], F32)
        rows = b.sb("rows", [1, 2048], F32)
        pfx_sb = b.sb("pfx_sb", [8, 1], F32)
        cpool = b.sb("cpool", [128, len(CONST_POOL)], F32)
        for j, val in enumerate(CONST_POOL):
            nc.const_aps.register(F32, float(val), cpool[:, j:j + 1])

        def ld_const(sy):
            b.dsy.dma(sy.dma_start(out=ident[:, :], in_=iaps["c_ident"][:, :]))
            b.dsy.dma(sy.dma_start(out=ones[:, :], in_=iaps["c_ones"][:, :]))
            b.dsy.dma(sy.dma_start(out=tri[:, :], in_=iaps["c_tri"][:, :]))
            b.dsy.dma(sy.dma_start(out=pfx_sb[:, :], in_=pfx[:, :]))
            b.dsy.dma(sy.dma_start(out=cpool[:, :], in_=iaps["c_pool"][:, :]))
        b.SY(ld_const)
        def ld_constg(g):
            b.dgp.dma(g.dma_start(out=identb[:, :], in_=iaps["c_ident"][:, :]))
            b.dgp.dma(g.dma_start(out=onesb[:, :], in_=iaps["c_ones"][0:1, 0:128]))
            b.dgp.dma(g.dma_start(out=onescb[:, :], in_=iaps["c_onescol"][:, :]))
        b.G(ld_constg)
        b.barrier()

        with ExitStack() as pes:
            pb = pes.enter_context(nc.psum_tensor("ppro", [128, 2048], F32))
            tin = pes.enter_context(nc.sbuf_tensor("p_in", [128, 2, D], F32))
            tout = pes.enter_context(nc.sbuf_tensor("p_out", [128, 2, D], F32))
            in_rot, out_rot, ps_rot = Rot(2), Rot(2), Rot(4)
            xfa_r = xfa.rearrange("(c p) t -> p c t", p=128)
            for i in range(NTILE):
                _transpose_pass(b, nc, "tm", i, pb, tin, tout, ident,
                                x_in[i * 128:(i + 1) * 128, :],
                                xfa_r[:, :, i * 128:(i + 1) * 128],
                                in_rot, out_rot, ps_rot)
            b.barrier()

        b.onesb = onesb
        b.onescb = onescb
        x_src, x_dst = xfa, xfb
        for li in blocks:
            if li == 3:
                emit_block3(b, iaps, x_src, x_dst, xn_hbm, s_hbm, ident, identb, ones, rows)
            elif li == 1:
                emit_block1(b, iaps, x_src, x_dst, xn_hbm, s_hbm, ident, identb, ones, rows)
            elif li == 0:
                emit_block0(b, iaps, x_src, x_dst, xn_hbm, s_hbm, h_hbm,
                            cc_in, cc_out, pfx_sb, ident, identb, ones, tri, rows)
            elif li == 2:
                emit_block2(b, iaps, x_src, x_dst, xn_hbm, s_hbm, h_hbm, y_hbm,
                            ident, identb, ones, rows)
            x_src, x_dst = x_dst, x_src
            b.barrier()

        if DEBUG_DUMP:
            dbg_xn = nc.dram_tensor("dbg_xn", [D, S], BF16, kind="ExternalOutput").ap()
            dbg_s = nc.dram_tensor("dbg_s", [D, S], BF16, kind="ExternalOutput").ap()
            dbg_y = nc.dram_tensor("dbg_y", [D, S], BF16, kind="ExternalOutput").ap()
            def dmp(sy):
                b.dsy.dma(sy.dma_start(out=dbg_xn[:, :], in_=xn_hbm[:, :]))
                b.dsy.dma(sy.dma_start(out=dbg_s[:, :], in_=s_hbm[:, :]))
                b.dsy.dma(sy.dma_start(out=dbg_y[:, :], in_=y_hbm[:, :]))
                sy.wait_ge(b.dsy.h, b.dsy.n)
            b.SY(dmp)

        with ExitStack() as pes:
            pb = pes.enter_context(nc.psum_tensor("pepi", [128, 2048], F32))
            tin = pes.enter_context(nc.sbuf_tensor("e_in", [128, 2, D], F32))
            tout = pes.enter_context(nc.sbuf_tensor("e_out", [128, 2, D], F32))
            in_rot, out_rot, ps_rot = Rot(2), Rot(2), Rot(4)
            xs_r = x_src.rearrange("(c p) t -> p c t", p=128)
            for i in range(NTILE):
                _transpose_pass(b, nc, "fm", i, pb, tin, tout, ident,
                                xs_r[:, :, i * 128:(i + 1) * 128],
                                out[i * 128:(i + 1) * 128, :],
                                in_rot, out_rot, ps_rot)
            def fin(sy):
                sy.wait_ge(b.dsy.h, b.dsy.n)
            b.SY(fin)
    return nc



def _run(inputs, blocks, x_override=None):
    host = host_prep(inputs)
    nc = build_program(host, blocks=blocks)
    x_full = np.asarray(x_override if x_override is not None else inputs["L0_attn_x"], dtype=np.float32)
    in_maps = []
    for c in range(8):
        m = dict(host)
        m["x"] = np.ascontiguousarray(x_full[c])
        p = np.zeros((8, 1), np.float32)
        p[:c] = 1.0
        m["pfx"] = p
        in_maps.append(m)
    res = run_bass_kernel_spmd(nc, in_maps, core_ids=list(range(8)))
    if DEBUG_DUMP:
        import ml_dtypes
        if "dbg_gb" in res.results[0]:
            np.save("/tmp/dbg_gb.npy", np.stack([np.asarray(r["dbg_gb"]) for r in res.results]))
        for key in ("dbg_den", "dbg_num"):
            if key in res.results[0]:
                np.save(f"/tmp/{key}.npy", np.stack([np.asarray(r[key]) for r in res.results]))
        for key in ("dbg_xn", "dbg_s", "dbg_g", "dbg_gt", "dbg_h", "dbg_w2", "dbg_hh", "dbg_y", "dbg_qa", "dbg_ka", "dbg_va", *[f"dbg_gi{i}" for i in range(8)]):
            if key in res.results[0]:
                np.save(f"/tmp/{key}.npy", np.stack(
                    [np.asarray(r[key]).view(ml_dtypes.bfloat16).astype(np.float32) for r in res.results]))
    return np.stack([np.asarray(r["out"]) for r in res.results])


def kernel(**inputs):
    return _run(inputs, blocks=(0, 1, 2, 3)).astype(np.float32)


# version 06
# speedup vs baseline: 1.1909x; 1.0810x over previous
import numpy as np
from contextlib import ExitStack
import concourse.bass as bass
import concourse.mybir as mybir
from concourse.bass_utils import run_bass_kernel_spmd

F32 = mybir.dt.float32
BF16 = mybir.dt.bfloat16
ALU = mybir.AluOpType
AF = mybir.ActivationFunctionType
AX = mybir.AxisListType.X

D = 1024
S = 4096
NDC = D // 128
TC = 512
NTC = S // TC
NTILE = S // 128
E = 8
CAPACITY = 5120

CONST_POOL = [0.0, 1.0, -1.0, 0.5, -0.5, 2.0, 1e-5, 1e-6, 0.1, 0.125,
              float(np.pi / 2), float(np.pi), float(2 * np.pi), 0.02, -30.0]
DEBUG_DUMP = False



def _pp(v):
    return np.ascontiguousarray(np.asarray(v, dtype=np.float32).reshape(NDC, 128).T)


def host_prep(inp):
    a = {}
    f32 = np.float32

    def T(x):
        return np.ascontiguousarray(np.asarray(x).T.astype(f32))

    def C(x):
        return np.ascontiguousarray(np.asarray(x).astype(f32))

    a["c_pool"] = np.ascontiguousarray(np.broadcast_to(np.array(CONST_POOL, dtype=f32), (128, len(CONST_POOL))))
    a["c_ident"] = np.eye(128, dtype=f32)
    a["c_ones"] = np.ones((128, 512), dtype=f32)
    a["c_onescol"] = np.ones((128, 1), dtype=f32)
    a["c_tri"] = np.ascontiguousarray(np.triu(np.ones((128, 128), dtype=f32)))
    m = np.zeros((128, 4, 512), dtype=f32)
    for r in range(4):
        for p in range(128):
            m[p, r, :] = (np.arange(512) >= (128 * r + p))
    a["c_l2mask"] = m
    DHE = 32
    inv = 1.0 / (10000.0 ** (np.arange(0, DHE, 2, dtype=np.float64) / DHE))
    emb = np.arange(S, dtype=np.float64)[:, None] * inv[None, :]
    emb = np.concatenate([emb, emb], axis=1)
    a["c_l2cosq"] = np.ascontiguousarray((np.cos(emb).T / 8.0).astype(f32))
    a["c_l2sinq"] = np.ascontiguousarray((np.sin(emb).T / 8.0).astype(f32))
    a["c_l2cosk"] = np.ascontiguousarray(np.cos(emb).T.astype(f32))
    a["c_l2sink"] = np.ascontiguousarray(np.sin(emb).T.astype(f32))
    dec = (0.5 ** (1.0 / np.linspace(2.0, 8.0, D))).astype(np.float64)
    a["c_l1dec"] = np.ascontiguousarray(
        np.broadcast_to(_pp(dec.astype(f32))[:, :, None], (128, NDC, TC)).astype(f32))
    a["c_l1beta"] = _pp((1.0 - dec).astype(f32))
    Bm = np.zeros((16, D), dtype=f32)
    for h in range(16):
        Bm[h, h * 64:(h + 1) * 64] = 1.0
    a["c_l3B"] = Bm

    a["w_l0_WinT"] = T(inp["L0_attn_W_in"])
    a["w_l0_WoutT"] = T(inp["L0_attn_W_out"])
    a["w_l0_logdec"] = C(inp["L0_attn_log_decay"]).reshape(2, 128).T.copy()
    a["w_l0_freq"] = C(inp["L0_attn_frequencies"]).reshape(2, 128).T.copy()
    a["w_l0_Wr"] = C(inp["L0_ffn_Wr"])
    a["w_l0_W1"] = C(inp["L0_ffn_W1"])
    a["w_l0_W2"] = C(inp["L0_ffn_W2"])
    a["w_l0_b1"] = np.ascontiguousarray(C(inp["L0_ffn_b1"]).reshape(E, 32, 128).transpose(2, 0, 1))
    a["w_l0_b2"] = C(inp["L0_ffn_b2"])

    a["w_l1_Wt"] = C(inp["L1_attn_Wt"])
    a["w_l1_Wo"] = C(inp["L1_attn_Wo"])
    a["w_l1_Wg"] = C(inp["L1_attn_Wg"])
    a["w_l1_bg"] = _pp(inp["L1_attn_bg"])
    a["w_l1_ln1w"] = _pp(inp["L1_attn_ln1_w"])
    a["w_l1_ln1b"] = _pp(inp["L1_attn_ln1_b"])
    a["w_l1_lncw"] = _pp(inp["L1_attn_lnc_w"])
    a["w_l1_lncb"] = _pp(inp["L1_attn_lnc_b"])
    a["w_l1_lngw"] = _pp(inp["L1_attn_lng_w"])
    a["w_l1_lngb"] = _pp(inp["L1_attn_lng_b"])
    a["w_l1_Wr"] = C(inp["L1_ffn_Wr"])
    a["w_l1_br"] = C(inp["L1_ffn_br"]).reshape(1, 8)
    a["w_l1_W1"] = C(inp["L1_ffn_W1"])
    a["w_l1_b1"] = np.ascontiguousarray(C(inp["L1_ffn_b1"]).reshape(E, 16, 128).transpose(2, 0, 1))
    a["w_l1_W2"] = C(inp["L1_ffn_W2"])
    a["w_l1_b2"] = C(inp["L1_ffn_b2"])

    a["w_l2_Wqd"] = C(inp["L2_attn_Wq_down"])
    a["w_l2_Wkd"] = C(inp["L2_attn_Wk_down"])
    a["w_l2_Wvd"] = C(inp["L2_attn_Wv_down"])
    lq = np.zeros((64, 96), dtype=f32)
    lq[:, 0:32] = C(inp["L2_attn_Wq_up_c"])
    lq[:, 32:64] = C(inp["L2_attn_Wq_up_e"])
    a["w_l2_upq"] = lq
    lk = np.zeros((64, 96), dtype=f32)
    lk[:, 0:32] = C(inp["L2_attn_Wk_up_c"])
    lk[:, 32:64] = C(inp["L2_attn_Wk_up_e"])
    a["w_l2_upk"] = lk
    a["w_l2_upv"] = C(inp["L2_attn_Wv_up"])
    a["w_l2_Wc"] = C(inp["L2_attn_Wc"])
    a["w_l2_Wr"] = T(inp["L2_ffn_router_w"])
    a["w_l2_Wg"] = np.ascontiguousarray(np.asarray(inp["L2_ffn_w_gate"]).transpose(0, 2, 1).astype(f32))
    a["w_l2_Wu"] = np.ascontiguousarray(np.asarray(inp["L2_ffn_w_up"]).transpose(0, 2, 1).astype(f32))
    a["w_l2_Wd"] = np.ascontiguousarray(np.asarray(inp["L2_ffn_w_down"]).transpose(0, 2, 1).astype(f32))

    a["w_l3_dec"] = C(inp["L3_attn_decay"]).reshape(16, 2)
    a["w_l3_WqT"] = T(inp["L3_attn_Wq"])
    a["w_l3_WvT"] = T(inp["L3_attn_Wv"])
    a["w_l3_WoT"] = T(inp["L3_attn_Wo"])
    a["w_l3_WgT"] = T(inp["L3_attn_Wg"])
    a["w_l3_lng"] = _pp(inp["L3_attn_ln_g"])
    a["w_l3_lnb"] = _pp(inp["L3_attn_ln_b"])
    a["w_l3_WT"] = T(inp["L3_ffn_W"])
    a["w_l3_b"] = _pp(inp["L3_ffn_b"])
    a["w_l3_Vr"] = C(inp["L3_ffn_V_r"])
    a["w_l3_CT"] = T(inp["L3_ffn_C"])
    return a



class Sem:
    __slots__ = ("h", "n")

    def __init__(self, nc, es, name):
        self.h = es.enter_context(nc.semaphore(name))
        self.n = 0

    def inc(self, instr, k=1):
        instr.then_inc(self.h, k)
        self.n += k
        return instr

    def dma(self, instr):
        return self.inc(instr, 16)

    def wait(self, eng, n=None):
        eng.wait_ge(self.h, self.n if n is None else n)


class Rot:

    def __init__(self, nslots):
        self.marks = [None] * nslots
        self.i = 0

    def acquire(self, eng):
        s = self.i % len(self.marks)
        self.i += 1
        g = self.marks[s]
        if g is not None:
            eng.wait_ge(g[0].h, g[1])
        return s

    def done(self, s, sem):
        self.marks[s] = (sem, sem.n)


class Bld:
    def __init__(self, nc, es, block):
        self.nc = nc
        self.es = es
        self.block = block
        self.pe = Sem(nc, es, "pe")
        self.dv = Sem(nc, es, "dv")
        self.ac = Sem(nc, es, "ac")
        self.po = Sem(nc, es, "po")
        self.dsy = Sem(nc, es, "dsy")
        self.dgp = Sem(nc, es, "dgp")
        self.dsc = Sem(nc, es, "dsc")
        self.cc = Sem(nc, es, "cc")
        self.all_sems = [self.pe, self.dv, self.ac, self.po,
                         self.dsy, self.dgp, self.dsc, self.cc]

    def T(self, f): self.block.tensor(f)
    def V(self, f): self.block.vector(f)
    def SC(self, f): self.block.scalar(f)
    def G(self, f): self.block.gpsimd(f)
    def SY(self, f): self.block.sync(f)

    def barrier(self):
        snap = [(s, s.n) for s in self.all_sems]

        def w(eng):
            for s, n in snap:
                if n:
                    eng.wait_ge(s.h, n)
        self.T(w); self.V(w); self.SC(w); self.G(w); self.SY(w)

    def sb(self, name, shape, dtype):
        return self.es.enter_context(self.nc.sbuf_tensor(name, shape, dtype))


def mmgroup(t, ps_ap, pairs):
    mm = None
    n = len(pairs)
    for j, (l, r) in enumerate(pairs):
        mm = t.matmul(ps_ap, l, r, start=(j == 0), stop=(j == n - 1))
    return mm



def emit_ln_chunk(b, xc, xn_out, work, eps, affine=None):
    acc, sq, rows = work["acc"], work["sq"], work["rows"]
    ps_st, ps_bc, ones = work["ps_st"], work["ps_bc"], work["ones"]
    onesb = b.onesb
    xc_bf = xc.dtype == BF16 if hasattr(xc, "dtype") else False

    pem0 = b.pe.n
    dvm0 = b.dv.n
    def s1(sc, pem0=pem0, dvm0=dvm0):
        sc.wait_ge(b.pe.h, pem0)
        sc.wait_ge(b.dv.h, dvm0)
        last = None
        for c in range(NDC):
            last = sc.activation(sq[:, c, :], xc[:, c, :], AF.Square)
        b.ac.inc(last)
    b.SC(s1)

    acm = b.ac.n
    try:
        xdt = xc.dtype
    except Exception:
        xdt = F32
    lhs_x = b.onescb[:, 0:1] if xdt == BF16 else ones[:, 0:1]
    dsym0 = b.dsy.n
    def t1(t, acm=acm, lhs_x=lhs_x, pem0=pem0, dvm0=dvm0, dsym0=dsym0):
        t.wait_ge(b.dv.h, dvm0)
        t.wait_ge(b.dsy.h, dsym0)
        for c in range(NDC):
            t.matmul(ps_st[0:1, 0:512], lhs_x, xc[:, c, :],
                     start=(c == 0), stop=(c == NDC - 1))
        t.wait_ge(b.ac.h, acm)
        mm = None
        for c in range(NDC):
            mm = t.matmul(ps_st[0:1, 512:1024], ones[:, 0:1], sq[:, c, :],
                          start=(c == 0), stop=(c == NDC - 1))
        b.pe.inc(mm)
    b.T(t1)

    pem = b.pe.n
    def v3(v, pem=pem):
        v.wait_ge(b.pe.h, pem)
        v.tensor_scalar_mul(rows[0:1, 0:512], ps_st[0:1, 0:512], 1.0 / 1024.0)
        v.tensor_scalar_mul(rows[0:1, 512:1024], ps_st[0:1, 512:1024], 1.0 / 1024.0)
        v.tensor_mul(rows[0:1, 1024:1536], rows[0:1, 0:512], rows[0:1, 0:512])
        b.dv.inc(v.tensor_sub(rows[0:1, 512:1024], rows[0:1, 512:1024], rows[0:1, 1024:1536]))
    b.V(v3)

    dvm = b.dv.n
    def s2(sc, dvm=dvm):
        sc.wait_ge(b.dv.h, dvm)
        b.ac.inc(sc.activation(rows[0:1, 1024:1536], rows[0:1, 512:1024], AF.Sqrt, bias=float(eps)))
    b.SC(s2)

    acm = b.ac.n
    def v4(v, acm=acm):
        v.wait_ge(b.ac.h, acm)
        b.dv.inc(v.reciprocal(rows[0:1, 1536:2048], rows[0:1, 1024:1536]))
    b.V(v4)

    dvm = b.dv.n
    def t2(t, dvm=dvm):
        t.wait_ge(b.dv.h, dvm)
        t.matmul(ps_bc[:, 0:512], ones[0:1, 0:128], rows[0:1, 0:512], start=True, stop=True)
        mm = t.matmul(ps_bc[:, 512:1024], ones[0:1, 0:128], rows[0:1, 1536:2048], start=True, stop=True)
        b.pe.inc(mm)
    b.T(t2)

    pem = b.pe.n
    def v5(v, pem=pem):
        v.wait_ge(b.pe.h, pem)
        last = None
        for c in range(NDC):
            v.tensor_sub(xn_out[:, c, :], xc[:, c, :], ps_bc[:, 0:512])
            last = v.tensor_mul(xn_out[:, c, :], xn_out[:, c, :], ps_bc[:, 512:1024])
            if affine is not None:
                wcol, bcol = affine
                last = v.tensor_scalar(xn_out[:, c, :], xn_out[:, c, :],
                                       wcol[:, c:c + 1], bcol[:, c:c + 1],
                                       ALU.mult, ALU.add)
        b.dv.inc(last)
    b.V(v5)



def emit_stage1(b, x_src, xn_hbm, ones, rows, wr_sb=None, logits_sb=None, br_row=None):
    nc = b.nc
    x_src_r = x_src.rearrange("(c p) t -> p c t", p=128)
    xn_r = xn_hbm.rearrange("(c p) t -> p c t", p=128)
    with ExitStack() as es:
        sb = lambda n, sh, dt: es.enter_context(nc.sbuf_tensor(n, sh, dt))
        xc = sb("s1_xc", [128, 2, NDC, TC], F32)
        sq = sb("s1_sq", [128, NDC, TC], F32)
        acc = sb("s1_acc", [128, TC], F32)
        xnb = sb("s1_xnb", [128, 2, NDC, TC], BF16)
        ps_st = es.enter_context(nc.psum_tensor("s1_ps_st", [128, 1024], F32))
        ps_bc = es.enter_context(nc.psum_tensor("s1_ps_bc", [128, 1024], F32))
        ps_r = es.enter_context(nc.psum_tensor("s1_ps_r", [128, 64], F32)) if wr_sb is not None else None
        work = dict(acc=acc, sq=sq, rows=rows, ps_st=ps_st, ps_bc=ps_bc, ones=ones)

        xc_rot = Rot(2)
        xnb_rot = Rot(2)
        r_guard = [None]
        for i in range(NTC):
            par = [None]
            def ldw(sy, i=i, par=par):
                par[0] = xc_rot.acquire(sy)
                b.dsy.dma(sy.dma_start(out=xc[:, par[0]], in_=x_src_r[:, :, i * TC:(i + 1) * TC]))
            b.SY(ldw)
            par = par[0]
            dsym = b.dsy.n

            pem0 = b.pe.n
            def w0(v, dsym=dsym, pem0=pem0):
                v.wait_ge(b.dsy.h, dsym)
                v.wait_ge(b.pe.h, pem0)
            b.V(w0)
            def w0s(sc, dsym=dsym, pem0=pem0):
                sc.wait_ge(b.dsy.h, dsym)
                sc.wait_ge(b.pe.h, pem0)
            b.SC(w0s)

            emit_ln_chunk(b, xc[:, par], sq, work, eps=1e-5)
            dv_ln = b.dv.n
            xc_rot.done(par, b.dv)

            sslot = [None]
            def castw(v, par=par, sslot=sslot):
                sslot[0] = xnb_rot.acquire(v)
                last = None
                for c in range(NDC):
                    last = v.tensor_copy(xnb[:, sslot[0], c, :], sq[:, c, :])
                b.dv.inc(last)
            b.V(castw)
            sslot = sslot[0]
            dvm = b.dv.n
            def stn(sy, i=i, sslot=sslot, dvm=dvm):
                sy.wait_ge(b.dv.h, dvm)
                b.dsy.dma(sy.dma_start(out=xn_r[:, :, i * TC:(i + 1) * TC], in_=xnb[:, sslot]))
            b.SY(stn)
            xnb_rot.done(sslot, b.dsy)

            if wr_sb is not None:
                def rt(t, i=i, dv_ln=dv_ln):
                    t.wait_ge(b.dv.h, dv_ln)
                    if r_guard[0] is not None:
                        t.wait_ge(r_guard[0][0].h, r_guard[0][1])
                    mm = None
                    for ts in range(4):
                        pairs = [(sq[:, c, ts * 128:(ts + 1) * 128], wr_sb[:, c, :])
                                 for c in range(NDC)]
                        if br_row is not None:
                            pairs.append((work["ones"][0:1, 0:128], br_row[0:1, 0:8]))
                        mm = mmgroup(t, ps_r[:, ts * 16:ts * 16 + 8], pairs)
                    b.pe.inc(mm)
                b.T(rt)
                pem = b.pe.n
                def ev(v, i=i, pem=pem):
                    v.wait_ge(b.pe.h, pem)
                    last = None
                    for ts in range(4):
                        last = v.tensor_copy(logits_sb[:, i * 4 + ts, :], ps_r[:, ts * 16:ts * 16 + 8])
                    b.dv.inc(last)
                b.V(ev)
                r_guard[0] = (b.dv, b.dv.n)
        b.barrier()



def emit_block3(b, iaps, x_src, x_dst, xn_hbm, s_hbm, ident, identb, ones, rows):
    nc = b.nc
    emit_stage1(b, x_src, xn_hbm, ones, rows)

    xn_r = xn_hbm.rearrange("(c p) t -> p c t", p=128)
    s_r = s_hbm.rearrange("(c p) t -> p c t", p=128)
    x_src_r = x_src.rearrange("(c p) t -> p c t", p=128)
    x_dst_r = x_dst.rearrange("(c p) t -> p c t", p=128)

    with ExitStack() as aes:
        sb = lambda n, sh, dt: aes.enter_context(nc.sbuf_tensor(n, sh, dt))
        psn = lambda n, sh: aes.enter_context(nc.psum_tensor(n, sh, F32))
        wq = sb("l3_wq", [128, NDC, D], BF16)
        wv = sb("l3_wv", [128, NDC, D], BF16)
        wg = sb("l3_wg", [128, NDC, D], BF16)
        wo = sb("l3_wo", [128, NDC, D], BF16)
        dec16 = sb("l3_dec", [16, 2], F32)
        lamc = sb("l3_lam", [128, NDC], F32)
        l3B = sb("l3_B", [16, D], F32)
        lam_tile = sb("l3_lamt", [128, NDC, TC], F32)
        lng = sb("l3_lng", [128, NDC], F32)
        lnb = sb("l3_lnb", [128, NDC], F32)
        carry = sb("l3_carry", [128, NDC], F32)

        def ldw(g):
            for nm, dst in (("w_l3_WqT", wq), ("w_l3_WvT", wv), ("w_l3_WgT", wg), ("w_l3_WoT", wo)):
                b.dgp.dma(g.dma_start(out=dst[:, :, :],
                                      in_=iaps[nm].rearrange("(c p) o -> p c o", p=128)))
        b.G(ldw)

        def lds(sy):
            b.dsy.dma(sy.dma_start(out=dec16[:, :], in_=iaps["w_l3_dec"][:, :]))
            b.dsy.dma(sy.dma_start(out=l3B[:, :], in_=iaps["c_l3B"][:, :]))
            b.dsy.dma(sy.dma_start(out=lng[:, :], in_=iaps["w_l3_lng"][:, :]))
            b.dsy.dma(sy.dma_start(out=lnb[:, :], in_=iaps["w_l3_lnb"][:, :]))
        b.SY(lds)

        ps_q = psn("l3_psq", [128, 1024])
        ps_g = psn("l3_psg", [128, 1024])
        ps_st = psn("l3_psst", [128, 1024])
        ps_bc = psn("l3_psbc", [128, 1024])
        b.barrier()

        def t_lam(t):
            last = None
            for c in range(NDC):
                last = t.matmul(ps_q[:, c:c + 1], l3B[:, c * 128:(c + 1) * 128],
                                dec16[:, 1:2], start=True, stop=True)
            b.pe.inc(last)
        b.T(t_lam)
        pem = b.pe.n
        def v_lam(v, pem=pem):
            v.wait_ge(b.pe.h, pem)
            for c in range(NDC):
                v.tensor_copy(lamc[:, c:c + 1], ps_q[:, c:c + 1])
            for c in range(NDC):
                v.tensor_scalar_mul(lam_tile[:, c, :], ones[:, 0:TC], lamc[:, c:c + 1])
            b.dv.inc(v.memset(carry[:, :], 0.0))
        b.V(v_lam)
        b.barrier()

        xnc = sb("l3_xnc", [128, 2, NDC, TC], BF16)
        qsb = sb("l3_q", [128, NDC, TC], BF16)
        vsb = sb("l3_v", [128, NDC, TC], F32)
        ysb = sb("l3_y", [128, NDC, TC], F32)
        sq = sb("l3_sq", [128, NDC, TC], F32)
        accw = sb("l3_acc", [128, TC], F32)
        gate = sb("l3_gate", [128, NDC, TC], BF16)
        yg = sb("l3_yg", [128, NDC, TC], BF16)
        ssb = sb("l3_s", [128, 2, NDC, TC], BF16)
        work = dict(acc=accw, sq=sq, rows=rows, ps_st=ps_st, ps_bc=ps_bc, ones=ones)

        xnc_rot = Rot(2)
        ssb_rot = Rot(2)
        rq, rg = Rot(2), Rot(2)
        rv = rq
        for i in range(NTC):
            par = [None]
            def ld(sy, par=par):
                par[0] = xnc_rot.acquire(sy)
                b.dsy.dma(sy.dma_start(out=xnc[:, par[0]], in_=xn_r[:, :, i * TC:(i + 1) * TC]))
            b.SY(ld)
            par = par[0]
            dsym = b.dsy.n

            def t_qvg(t, par=par, dsym=dsym):
                t.wait_ge(b.dsy.h, dsym)
                for oc in range(NDC):
                    s = rq.acquire(t)
                    mm = mmgroup(t, ps_q[:, s * 512:(s + 1) * 512],
                                 [(wq[:, c, oc * 128:(oc + 1) * 128], xnc[:, par, c, :]) for c in range(NDC)])
                    b.pe.inc(mm)
                    pem = b.pe.n
                    def evq(v, oc=oc, s=s, pem=pem):
                        v.wait_ge(b.pe.h, pem)
                        b.dv.inc(v.tensor_copy(qsb[:, oc, :], ps_q[:, s * 512:(s + 1) * 512]))
                    b.V(evq)
                    rq.done(s, b.dv)
                    s = rv.acquire(t)
                    mm = mmgroup(t, ps_q[:, s * 512:(s + 1) * 512],
                                 [(wv[:, c, oc * 128:(oc + 1) * 128], xnc[:, par, c, :]) for c in range(NDC)])
                    b.pe.inc(mm)
                    pem = b.pe.n
                    def evv(v, oc=oc, s=s, pem=pem):
                        v.wait_ge(b.pe.h, pem)
                        b.dv.inc(v.tensor_copy(vsb[:, oc, :], ps_q[:, s * 512:(s + 1) * 512]))
                    b.V(evv)
                    rv.done(s, b.dv)
                for oc in range(NDC):
                    s = rg.acquire(t)
                    mm = mmgroup(t, ps_g[:, s * 512:(s + 1) * 512],
                                 [(wg[:, c, oc * 128:(oc + 1) * 128], xnc[:, par, c, :]) for c in range(NDC)])
                    b.pe.inc(mm)
                    pem = b.pe.n
                    def evg(sc, oc=oc, s=s, pem=pem):
                        sc.wait_ge(b.pe.h, pem)
                        b.ac.inc(sc.activation(gate[:, oc, :], ps_g[:, s * 512:(s + 1) * 512], AF.Silu))
                    b.SC(evg)
                    rg.done(s, b.ac)
            b.T(t_qvg)
            xnc_rot.done(par, b.pe)

            dvm = b.dv.n
            def v_scan(v, dvm=dvm):
                v.wait_ge(b.dv.h, dvm)
                last = None
                for c in range(NDC):
                    last = v.tensor_tensor_scan(ysb[:, c, :], lam_tile[:, c, :], vsb[:, c, :],
                                                carry[:, c:c + 1], ALU.mult, ALU.add)
                b.dv.inc(last)
            b.V(v_scan)
            dvm = b.dv.n
            def s_carry(sc, dvm=dvm):
                sc.wait_ge(b.dv.h, dvm)
                last = None
                for c in range(NDC):
                    last = sc.copy(carry[:, c:c + 1], ysb[:, c, TC - 1:TC])
                b.ac.inc(last)
            b.SC(s_carry)
            acm2 = b.ac.n
            def v_mul(v, acm2=acm2):
                v.wait_ge(b.ac.h, acm2)
                last = None
                for c in range(NDC):
                    last = v.tensor_mul(ysb[:, c, :], ysb[:, c, :], qsb[:, c, :])
                b.dv.inc(last)
            b.V(v_mul)

            emit_ln_chunk(b, ysb, ysb, work, eps=1e-5, affine=(lng, lnb))

            acm = b.ac.n
            def v_yg(v, acm=acm):
                v.wait_ge(b.ac.h, acm)
                last = None
                for c in range(NDC):
                    last = v.tensor_mul(yg[:, c, :], ysb[:, c, :], gate[:, c, :])
                b.dv.inc(last)
            b.V(v_yg)
            dvm = b.dv.n

            sslot = [None]
            def pick(v, sslot=sslot):
                sslot[0] = ssb_rot.acquire(v)
            b.V(pick)
            sslot = sslot[0]

            def t_wo(t, par=par, dvm=dvm, sslot=sslot):
                t.wait_ge(b.dv.h, dvm)
                for oc in range(NDC):
                    s = rq.acquire(t)
                    mm = mmgroup(t, ps_q[:, s * 512:(s + 1) * 512],
                                 [(wo[:, c, oc * 128:(oc + 1) * 128], yg[:, c, :]) for c in range(NDC)])
                    b.pe.inc(mm)
                    pem = b.pe.n
                    def evs(v, oc=oc, s=s, pem=pem, sslot=sslot):
                        v.wait_ge(b.pe.h, pem)
                        b.dv.inc(v.tensor_copy(ssb[:, sslot, oc, :], ps_q[:, s * 512:(s + 1) * 512]))
                    b.V(evs)
                    rq.done(s, b.dv)
            b.T(t_wo)
            dvm = b.dv.n
            def st_s(sy, i=i, sslot=sslot, dvm=dvm):
                sy.wait_ge(b.dv.h, dvm)
                b.dsy.dma(sy.dma_start(out=s_r[:, :, i * TC:(i + 1) * TC], in_=ssb[:, sslot]))
            b.SY(st_s)
            ssb_rot.done(sslot, b.dsy)
        b.barrier()

    with ExitStack() as fes:
        sb = lambda n, sh, dt: fes.enter_context(nc.sbuf_tensor(n, sh, dt))
        wT = sb("l3f_wT", [128, NDC, D], BF16)
        vr = sb("l3f_vr", [128, NDC, 32], BF16)
        cT = sb("l3f_cT", [32, D], BF16)
        bcol = sb("l3f_bcol", [128, NDC], F32)
        ps_go = fes.enter_context(nc.psum_tensor("l3f_psg", [128, 1024], F32))
        ps_rr = fes.enter_context(nc.psum_tensor("l3f_psr", [32, 512], F32))

        def ldw(g):
            b.dgp.dma(g.dma_start(out=wT[:, :, :], in_=iaps["w_l3_WT"].rearrange("(c p) o -> p c o", p=128)))
            b.dgp.dma(g.dma_start(out=vr[:, :, :], in_=iaps["w_l3_Vr"].rearrange("(c p) r -> p c r", p=128)))
            b.dgp.dma(g.dma_start(out=cT[:, :], in_=iaps["w_l3_CT"][:, :]))
        b.G(ldw)
        def lds(sy):
            b.dsy.dma(sy.dma_start(out=bcol[:, :], in_=iaps["w_l3_b"][:, :]))
        b.SY(lds)

        xnc = sb("l3f_xnc", [128, 2, NDC, TC], BF16)
        xcc = sb("l3f_xc", [128, 2, NDC, TC], F32)
        scc = sb("l3f_sc", [128, 2, NDC, TC], BF16)
        rsb = sb("l3f_r", [32, TC], BF16)
        gtmp = sb("l3f_gt", [128, 2, TC], F32)
        xout = sb("l3f_xo", [128, 2, NDC, TC], F32)

        b.barrier()
        in_rot = Rot(2)
        out_rot = Rot(2)
        go_rot = Rot(2)
        gt_rot = Rot(2)
        rr_guard = [None]
        for i in range(NTC):
            par = [None]
            def ld(sy, i=i, par=par):
                par[0] = in_rot.acquire(sy)
                b.dsy.dma(sy.dma_start(out=xnc[:, par[0]], in_=xn_r[:, :, i * TC:(i + 1) * TC]))
                b.dsy.dma(sy.dma_start(out=xcc[:, par[0]], in_=x_src_r[:, :, i * TC:(i + 1) * TC]))
                b.dsy.dma(sy.dma_start(out=scc[:, par[0]], in_=s_r[:, :, i * TC:(i + 1) * TC]))
            b.SY(ld)
            par = par[0]
            dsym = b.dsy.n

            def t_r(t, par=par, dsym=dsym):
                t.wait_ge(b.dsy.h, dsym)
                if rr_guard[0] is not None:
                    t.wait_ge(rr_guard[0][0].h, rr_guard[0][1])
                mm = mmgroup(t, ps_rr[:, :], [(vr[:, c, :], xnc[:, par, c, :]) for c in range(NDC)])
                b.pe.inc(mm)
            b.T(t_r)
            pem = b.pe.n
            def ev_r(v, pem=pem):
                v.wait_ge(b.pe.h, pem)
                b.dv.inc(v.tensor_copy(rsb[:, :], ps_rr[:, :]))
            b.V(ev_r)
            rr_guard[0] = (b.dv, b.dv.n)
            dvm = b.dv.n

            oslot = [None]
            def pick(v, oslot=oslot):
                oslot[0] = out_rot.acquire(v)
            b.V(pick)
            oslot = oslot[0]

            for oc in range(NDC):
                sref = [None]
                def t_gw(t, oc=oc, par=par, dvm=dvm, sref=sref):
                    if oc == 0:
                        t.wait_ge(b.dv.h, dvm)
                    s = go_rot.acquire(t)
                    sref[0] = s
                    pairs = [(wT[:, c, oc * 128:(oc + 1) * 128], xnc[:, par, c, :]) for c in range(NDC)]
                    pairs.append((cT[:, oc * 128:(oc + 1) * 128], rsb[:, :]))
                    mm = mmgroup(t, ps_go[:, s * 512:(s + 1) * 512], pairs)
                    b.pe.inc(mm)
                b.T(t_gw)
                s = sref[0]
                pem = b.pe.n
                gslot = [None]
                def fuse(sc, oc=oc, s=s, pem=pem, gslot=gslot):
                    sc.wait_ge(b.pe.h, pem)
                    gs = gt_rot.acquire(sc)
                    gslot[0] = gs
                    b.ac.inc(sc.activation(gtmp[:, gs, :], ps_go[:, s * 512:(s + 1) * 512],
                                           AF.Silu, bias=bcol[:, oc:oc + 1]))
                b.SC(fuse)
                gslot = gslot[0]
                go_rot.done(s, b.ac)
                acm = b.ac.n
                def join(v, oc=oc, par=par, acm=acm, gslot=gslot, oslot=oslot):
                    v.wait_ge(b.ac.h, acm)
                    v.tensor_mul(gtmp[:, gslot, :], gtmp[:, gslot, :], scc[:, par, oc, :])
                    b.dv.inc(v.tensor_add(xout[:, oslot, oc, :], xcc[:, par, oc, :], gtmp[:, gslot, :]))
                b.V(join)
                gt_rot.done(gslot, b.dv)
            in_rot.done(par, b.dv)
            dvm = b.dv.n
            def st_x(sy, i=i, oslot=oslot, dvm=dvm):
                sy.wait_ge(b.dv.h, dvm)
                b.dsy.dma(sy.dma_start(out=x_dst_r[:, :, i * TC:(i + 1) * TC], in_=xout[:, oslot]))
            b.SY(st_x)
            out_rot.done(oslot, b.dsy)
        b.barrier()


def emit_first_max_ks(v, src_ap, m_bc, oh, t1, t2, sh):
    P, NT_, Ecnt = sh
    v.tensor_tensor(oh, src_ap, m_bc, ALU.is_ge)
    v.tensor_copy(t1, oh)
    v.tensor_add(t1[:, :, 1:8], oh[:, :, 1:8], oh[:, :, 0:7])
    v.tensor_copy(t2, t1)
    v.tensor_add(t2[:, :, 2:8], t1[:, :, 2:8], t1[:, :, 0:6])
    v.tensor_copy(t1, t2)
    v.tensor_add(t1[:, :, 4:8], t2[:, :, 4:8], t2[:, :, 0:4])
    v.tensor_single_scalar(t1, t1, 1.0, ALU.is_le)
    v.tensor_mul(oh, oh, t1)


def bounce_copy(b, pairs):
    dvm = b.dv.n
    def sc_(sc, dvm=dvm):
        sc.wait_ge(b.dv.h, dvm)
        last = None
        for dst, srcap in pairs:
            last = sc.copy(dst, srcap)
        b.ac.inc(last)
    b.SC(sc_)
    acm = b.ac.n
    def v_(v, acm=acm):
        v.wait_ge(b.ac.h, acm)
    b.V(v_)


def emit_top2(b, logits_sb, scr):
    sh = (128, NTILE, E)
    m1 = scr["m1"][:, :, :]
    m2 = scr["m2"][:, :, :]
    m1b = scr["m1b"][:, :, :]
    m2b = scr["m2b"][:, :, :]
    oh1 = scr["oh1"][:, :, :]
    oh2 = scr["oh2"][:, :, :]
    t1 = scr["t1"][:, :, :]
    t2 = scr["t2"][:, :, :]
    e1 = scr["e1"][:, :, :]
    lg = logits_sb[:, :, :]

    def va(v):
        b.dv.inc(v.tensor_reduce(m1, lg, AX, ALU.max))
    b.V(va)
    bounce_copy(b, [(m1b, m1)])
    def vb(v):
        emit_first_max_ks(v, lg, m1b.broadcast_to(sh), oh1, t1, t2, sh)
        v.scalar_tensor_tensor(t1, oh1, -1e30, lg, ALU.mult, ALU.add)
        b.dv.inc(v.tensor_reduce(m2, t1, AX, ALU.max))
    b.V(vb)
    bounce_copy(b, [(m2b, m2)])
    def vc(v):
        emit_first_max_ks(v, t1, m2b.broadcast_to(sh), oh2, t2, e1, sh)
    b.V(vc)


def emit_routing_topk(b, logits_sb, scr, gates_bf, ident, ps_t, out_also=None):
    nc = b.nc
    sh = (128, NTILE, E)
    m1 = scr["m1"][:, :, :]
    m2 = scr["m2"][:, :, :]
    m1b = scr["m1b"][:, :, :]
    m2b = scr["m2b"][:, :, :]
    oh1 = scr["oh1"][:, :, :]
    oh2 = scr["oh2"][:, :, :]
    t1 = scr["t1"][:, :, :]
    e1 = scr["e1"][:, :, :]
    lg = logits_sb[:, :, :]
    emit_top2(b, logits_sb, scr)
    def vd(v):
        b.dv.inc(v.tensor_tensor(e1, lg, m1b.broadcast_to(sh), ALU.subtract))
    b.V(vd)
    dvm = b.dv.n
    def s1(sc, dvm=dvm):
        sc.wait_ge(b.dv.h, dvm)
        b.ac.inc(sc.activation(e1, e1, AF.Exp))
    b.SC(s1)
    acm = b.ac.n
    def v2(v, acm=acm):
        v.wait_ge(b.ac.h, acm)
        v.tensor_add(t1, oh1, oh2)
        v.tensor_mul(e1, e1, t1)
        b.dv.inc(v.tensor_reduce(m2, e1, AX, ALU.add))
    b.V(v2)
    bounce_copy(b, [(m2b, m2)])
    def v3(v):
        b.dv.inc(v.reciprocal(m1, m2b))
    b.V(v3)
    bounce_copy(b, [(m1b, m1)])
    def v4(v):
        b.dv.inc(v.tensor_mul(e1, e1, m1b.broadcast_to(sh)))
    b.V(v4)
    _gates_transpose(b, e1, gates_bf, ident, ps_t)


def _gates_transpose(b, gtm, gates_bf, ident, ps_t):
    dvm = b.dv.n
    rot = Rot(4)
    for i in range(NTILE):
        def tp(t, i=i, dvm=dvm):
            if i == 0:
                t.wait_ge(b.dv.h, dvm)
            s = rot.acquire(t)
            mm = t.transpose(ps_t[0:8, s * 512:s * 512 + 128], gtm[:, i, :], ident[:, :])
            b.pe.inc(mm)
            pem = b.pe.n
            def ev(v, i=i, s=s, pem=pem):
                v.wait_ge(b.pe.h, pem)
                b.dv.inc(v.tensor_copy(gates_bf[0:8, i * 128:(i + 1) * 128],
                                       ps_t[0:8, s * 512:s * 512 + 128]))
            b.V(ev)
            rot.done(s, b.dv)
        b.T(tp)


def emit_moe_join(b, x_src, x_dst, s_hbm, g_acc):
    nc = b.nc
    x_src_r = x_src.rearrange("(c p) t -> p c t", p=128)
    x_dst_r = x_dst.rearrange("(c p) t -> p c t", p=128)
    s_r = s_hbm.rearrange("(c p) t -> p c t", p=128)
    with ExitStack() as es:
        sb = lambda n, sh, dt: es.enter_context(nc.sbuf_tensor(n, sh, dt))
        xcc = sb("j_xc", [128, 2, NDC, TC], F32)
        scc = sb("j_sc", [128, 2, NDC, TC], BF16)
        gt = sb("j_gt", [128, NDC, TC], F32)
        xo = sb("j_xo", [128, 2, NDC, TC], F32)
        in_rot, out_rot = Rot(2), Rot(2)
        for i in range(NTC):
            par = [None]
            def ld(sy, i=i, par=par):
                par[0] = in_rot.acquire(sy)
                b.dsy.dma(sy.dma_start(out=xcc[:, par[0]], in_=x_src_r[:, :, i * TC:(i + 1) * TC]))
                b.dsy.dma(sy.dma_start(out=scc[:, par[0]], in_=s_r[:, :, i * TC:(i + 1) * TC]))
            b.SY(ld)
            par = par[0]
            dsym = b.dsy.n
            acm0 = b.ac.n
            def s_silu(sc, i=i, acm0=acm0):
                sc.wait_ge(b.ac.h, acm0)
                last = None
                for c in range(NDC):
                    last = sc.activation(gt[:, c, :], g_acc[:, c, i * TC:(i + 1) * TC], AF.Silu)
                b.ac.inc(last)
            b.SC(s_silu)
            acm = b.ac.n
            oslot = [None]
            def vj(v, par=par, acm=acm, dsym=dsym, oslot=oslot):
                v.wait_ge(b.ac.h, acm)
                v.wait_ge(b.dsy.h, dsym)
                oslot[0] = out_rot.acquire(v)
                last = None
                for c in range(NDC):
                    v.tensor_mul(gt[:, c, :], gt[:, c, :], scc[:, par, c, :])
                    last = v.tensor_add(xo[:, oslot[0], c, :], xcc[:, par, c, :], gt[:, c, :])
                b.dv.inc(last)
            b.V(vj)
            in_rot.done(par, b.dv)
            oslot = oslot[0]
            dvj = b.dv.n
            def s_guard(sc, dvj=dvj):
                sc.wait_ge(b.dv.h, dvj)
            b.SC(s_guard)
            dvm = b.dv.n
            def st(sy, i=i, oslot=oslot, dvm=dvm):
                sy.wait_ge(b.dv.h, dvm)
                b.dsy.dma(sy.dma_start(out=x_dst_r[:, :, i * TC:(i + 1) * TC], in_=xo[:, oslot]))
            b.SY(st)
            out_rot.done(oslot, b.dsy)
    b.barrier()


def emit_block1(b, iaps, x_src, x_dst, xn_hbm, s_hbm, ident, identb, ones, rows):
    nc = b.nc
    xn_r = xn_hbm.rearrange("(c p) t -> p c t", p=128)
    s_r = s_hbm.rearrange("(c p) t -> p c t", p=128)

    bes = ExitStack()
    gates_bf = bes.enter_context(nc.sbuf_tensor("l1_gbf", [8, S], BF16))

    with ExitStack() as res_:
        sb = lambda n, sh, dt: res_.enter_context(nc.sbuf_tensor(n, sh, dt))
        wr_sb = sb("l1_wr", [128, NDC, E], F32)
        br_row = sb("l1_br", [1, 8], F32)
        logits_sb = sb("l1_lg", [128, NTILE, E], F32)

        def ldr(sy):
            b.dsy.dma(sy.dma_start(out=wr_sb[:, :, :],
                                   in_=iaps["w_l1_Wr"].rearrange("(c p) e -> p c e", p=128)))
            b.dsy.dma(sy.dma_start(out=br_row[:, :], in_=iaps["w_l1_br"][:, :]))
        b.SY(ldr)
        b.barrier()
        emit_stage1(b, x_src, xn_hbm, ones, rows, wr_sb=wr_sb, logits_sb=logits_sb, br_row=br_row)

        scr = {k: sb("l1_scr_" + k, [128, NTILE, E], F32)
               for k in ("oh1", "oh2", "t1", "t2", "e1")}
        for k in ("m1", "m2", "m1b", "m2b"):
            scr[k] = sb("l1_" + k, [128, NTILE, 1], F32)
        ps_t = res_.enter_context(nc.psum_tensor("l1_pst", [128, 2048], F32))
        emit_routing_topk(b, logits_sb, scr, gates_bf, ident, ps_t)
        b.barrier()

    with ExitStack() as aes:
        sb = lambda n, sh, dt: aes.enter_context(nc.sbuf_tensor(n, sh, dt))
        psn = lambda n, sh: aes.enter_context(nc.psum_tensor(n, sh, F32))
        wt = sb("l1_wt", [128, NDC, D], BF16)
        wo = sb("l1_wo", [128, NDC, D], BF16)
        wgm = sb("l1_wgm", [128, 2 * NDC, D], BF16)
        dec_t = sb("l1_dect", [128, NDC, TC], F32)
        beta = sb("l1_beta", [128, NDC], F32)
        ln1w = sb("l1_ln1w", [128, NDC], F32)
        ln1b = sb("l1_ln1b", [128, NDC], F32)
        lncw = sb("l1_lncw", [128, NDC], F32)
        lncb = sb("l1_lncb", [128, NDC], F32)
        lngw = sb("l1_lngw", [128, NDC], F32)
        lngb = sb("l1_lngb", [128, NDC], F32)
        bg = sb("l1_bg", [128, NDC], F32)
        carry = sb("l1_carry", [128, NDC], F32)

        def ldw(g):
            b.dgp.dma(g.dma_start(out=wt[:, :, :], in_=iaps["w_l1_Wt"].rearrange("(c p) o -> p c o", p=128)))
            b.dgp.dma(g.dma_start(out=wo[:, :, :], in_=iaps["w_l1_Wo"].rearrange("(c p) o -> p c o", p=128)))
            b.dgp.dma(g.dma_start(out=wgm[:, :, :], in_=iaps["w_l1_Wg"].rearrange("(c p) o -> p c o", p=128)))
        b.G(ldw)
        def lds(sy):
            b.dsy.dma(sy.dma_start(out=dec_t[:, :, :], in_=iaps["c_l1dec"][:, :, :]))
            for nm, dst in (("c_l1beta", beta), ("w_l1_ln1w", ln1w), ("w_l1_ln1b", ln1b),
                            ("w_l1_lncw", lncw), ("w_l1_lncb", lncb), ("w_l1_lngw", lngw),
                            ("w_l1_lngb", lngb), ("w_l1_bg", bg)):
                b.dsy.dma(sy.dma_start(out=dst[:, :], in_=iaps[nm][:, :]))
        b.SY(lds)

        xnc = sb("l1_xnc", [128, 1, NDC, TC], BF16)
        xf = sb("l1_xf", [128, NDC, TC], F32)
        xb = sb("l1_xb", [128, NDC, TC], BF16)
        cx = sb("l1_cx", [128, NDC, TC], F32)
        cxnb = sb("l1_cxnb", [128, NDC, TC], BF16)
        outb = sb("l1_outb", [128, NDC, TC], BF16)
        gate = sb("l1_gate", [128, NDC, TC], BF16)
        ssb = sb("l1_ssb", [128, 1, NDC, TC], BF16)
        sq = sb("l1_sq", [128, NDC, TC], F32)
        accw = sb("l1_acc", [128, TC], F32)
        ps_a = psn("l1_psa", [128, 1024])
        ps_st = psn("l1_psst", [128, 1024])
        ps_bc = psn("l1_psbc", [128, 1024])
        work = dict(acc=accw, sq=sq, rows=rows, ps_st=ps_st, ps_bc=ps_bc, ones=ones)

        b.barrier()
        def vz(v):
            b.dv.inc(v.memset(carry[:, :], 0.0))
        b.V(vz)

        xnc_rot = Rot(1)
        ssb_rot = Rot(1)
        ra = Rot(2)
        for i in range(NTC):
            par = [None]
            def ld(sy, i=i, par=par):
                par[0] = xnc_rot.acquire(sy)
                b.dsy.dma(sy.dma_start(out=xnc[:, par[0]], in_=xn_r[:, :, i * TC:(i + 1) * TC]))
            b.SY(ld)
            par = par[0]
            dsym = b.dsy.n
            def w0(v, dsym=dsym):
                v.wait_ge(b.dsy.h, dsym)
            b.V(w0)
            def w0s(sc, dsym=dsym):
                sc.wait_ge(b.dsy.h, dsym)
            b.SC(w0s)

            emit_ln_chunk(b, xnc[:, par], xf, work, eps=1e-6, affine=(ln1w, ln1b))
            def vcast(v):
                last = None
                for c in range(NDC):
                    last = v.tensor_copy(xb[:, c, :], xf[:, c, :])
                b.dv.inc(last)
            b.V(vcast)
            xnc_rot.done(par, b.dv)
            dvm = b.dv.n

            def t_u(t, dvm=dvm):
                t.wait_ge(b.dv.h, dvm)
                for oc in range(NDC):
                    s = ra.acquire(t)
                    mm = mmgroup(t, ps_a[:, s * 512:(s + 1) * 512],
                                 [(wt[:, c, oc * 128:(oc + 1) * 128], xb[:, c, :]) for c in range(NDC)])
                    b.pe.inc(mm)
                    pem = b.pe.n
                    def evu(v, oc=oc, s=s, pem=pem):
                        v.wait_ge(b.pe.h, pem)
                        b.dv.inc(v.tensor_scalar(sq[:, oc, :], ps_a[:, s * 512:(s + 1) * 512],
                                                 beta[:, oc:oc + 1], None, ALU.mult))
                    b.V(evu)
                    ra.done(s, b.dv)
            b.T(t_u)

            dvm = b.dv.n
            def v_scan(v, dvm=dvm):
                v.wait_ge(b.dv.h, dvm)
                last = None
                for c in range(NDC):
                    last = v.tensor_tensor_scan(cx[:, c, :], dec_t[:, c, :], sq[:, c, :],
                                                carry[:, c:c + 1], ALU.mult, ALU.add)
                b.dv.inc(last)
            b.V(v_scan)
            dvm = b.dv.n
            def s_carry(sc, dvm=dvm):
                sc.wait_ge(b.dv.h, dvm)
                last = None
                for c in range(NDC):
                    last = sc.copy(carry[:, c:c + 1], cx[:, c, TC - 1:TC])
                b.ac.inc(last)
            b.SC(s_carry)
            acm = b.ac.n
            def vb1(v, acm=acm):
                v.wait_ge(b.ac.h, acm)
            b.V(vb1)

            emit_ln_chunk(b, cx, cxnb, work, eps=1e-6, affine=(lncw, lncb))
            dvm = b.dv.n

            def t_o(t, dvm=dvm):
                t.wait_ge(b.dv.h, dvm)
                for oc in range(NDC):
                    s = ra.acquire(t)
                    mm = mmgroup(t, ps_a[:, s * 512:(s + 1) * 512],
                                 [(wo[:, c, oc * 128:(oc + 1) * 128], cxnb[:, c, :]) for c in range(NDC)])
                    b.pe.inc(mm)
                    pem = b.pe.n
                    def evo(v, oc=oc, s=s, pem=pem):
                        v.wait_ge(b.pe.h, pem)
                        b.dv.inc(v.tensor_copy(outb[:, oc, :], ps_a[:, s * 512:(s + 1) * 512]))
                    b.V(evo)
                    ra.done(s, b.dv)
            b.T(t_o)
            dvm = b.dv.n

            def t_g(t, dvm=dvm):
                t.wait_ge(b.dv.h, dvm)
                for oc in range(NDC):
                    s = ra.acquire(t)
                    pairs = [(wgm[:, c, oc * 128:(oc + 1) * 128], xb[:, c, :]) for c in range(NDC)]
                    pairs += [(wgm[:, NDC + c, oc * 128:(oc + 1) * 128], outb[:, c, :]) for c in range(NDC)]
                    mm = mmgroup(t, ps_a[:, s * 512:(s + 1) * 512], pairs)
                    b.pe.inc(mm)
                    pem = b.pe.n
                    def evl(v, oc=oc, s=s, pem=pem):
                        v.wait_ge(b.pe.h, pem)
                        b.dv.inc(v.tensor_scalar(cx[:, oc, :], ps_a[:, s * 512:(s + 1) * 512],
                                                 bg[:, oc:oc + 1], None, ALU.add))
                    b.V(evl)
                    ra.done(s, b.dv)
            b.T(t_g)

            emit_ln_chunk(b, cx, cx, work, eps=1e-6, affine=(lngw, lngb))
            dvm = b.dv.n
            def s_sig(sc, dvm=dvm):
                sc.wait_ge(b.dv.h, dvm)
                last = None
                for c in range(NDC):
                    last = sc.activation(gate[:, c, :], cx[:, c, :], AF.Sigmoid)
                b.ac.inc(last)
            b.SC(s_sig)
            acm = b.ac.n

            sslot = [None]
            def v_blend(v, par=par, acm=acm, sslot=sslot):
                v.wait_ge(b.ac.h, acm)
                sslot[0] = ssb_rot.acquire(v)
                last = None
                for c in range(NDC):
                    v.tensor_sub(sq[:, c, :], outb[:, c, :], xf[:, c, :])
                    v.tensor_mul(sq[:, c, :], sq[:, c, :], gate[:, c, :])
                    last = v.tensor_add(ssb[:, sslot[0], c, :], xf[:, c, :], sq[:, c, :])
                b.dv.inc(last)
            b.V(v_blend)
            sslot = sslot[0]
            dvm = b.dv.n
            def st_s(sy, i=i, sslot=sslot, dvm=dvm):
                sy.wait_ge(b.dv.h, dvm)
                b.dsy.dma(sy.dma_start(out=s_r[:, :, i * TC:(i + 1) * TC], in_=ssb[:, sslot]))
            b.SY(st_s)
            ssb_rot.done(sslot, b.dsy)
        b.barrier()

    g_acc = bes.enter_context(nc.sbuf_tensor("l1_gacc", [128, NDC, S], BF16))
    with ExitStack() as fes:
        sb = lambda n, sh, dt: fes.enter_context(nc.sbuf_tensor(n, sh, dt))
        psn = lambda n, sh: fes.enter_context(nc.psum_tensor(n, sh, F32))
        FH = 2048
        NFC = FH // 128
        w1 = sb("l1f_w1", [128, 1, NDC, FH], BF16)
        w2 = sb("l1f_w2", [128, NFC, D], BF16)
        b1c = sb("l1f_b1", [128, E, NFC], F32)
        b2b = sb("l1f_b2", [8, D], BF16)
        xnc = sb("l1f_xnc", [128, 1, NDC, TC], BF16)
        grow = sb("l1f_grow", [1, 2, TC], BF16)
        hsb = sb("l1f_h", [128, 2, NFC, TC], BF16)
        gb = sb("l1f_gb", [128, 2, TC], F32)
        ps_h = psn("l1f_psh", [128, 1024])
        ps_g = psn("l1f_psg", [128, 1024])
        ps_k = psn("l1f_psk", [128, 1024])

        def lds(sy):
            b.dsy.dma(sy.dma_start(out=b1c[:, :, :], in_=iaps["w_l1_b1"][:, :, :]))
        b.SY(lds)
        def ldb2(g):
            b.dgp.dma(g.dma_start(out=b2b[:, :], in_=iaps["w_l1_b2"][:, :]))
        b.G(ldb2)

        onesb = b.onesb
        w1_rot = Rot(1)
        w2_rot = Rot(1)
        xn_rot = Rot(1)
        h_rot = Rot(2)
        gbrot = Rot(2)
        rh = Rot(2)
        rg = Rot(2)
        kg = Rot(2)
        grow_rot = Rot(2)
        def ldw1_0(g):
            w1_rot.acquire(g)
            b.dgp.dma(g.dma_start(out=w1[:, 0],
                                  in_=iaps["w_l1_W1"][0].rearrange("(c p) f -> p c f", p=128)))
        b.G(ldw1_0)
        dgm_w1_next = [b.dgp.n]
        for e in range(E):
            wpar = 0
            dgm_w1 = dgm_w1_next[0]
            def ldw2(g, e=e):
                w2_rot.acquire(g)
                b.dgp.dma(g.dma_start(out=w2[:, :, :],
                                      in_=iaps["w_l1_W2"][e].rearrange("(c p) d -> p c d", p=128)))
            b.G(ldw2)
            dgm_w2 = b.dgp.n
            pend = [None]
            def emit_td(t, info, e=e, dgm_w2=dgm_w2):
                i, hs, dvm_ = info
                t.wait_ge(b.dv.h, dvm_)
                t.wait_ge(b.dgp.h, dgm_w2)
                for dc in range(NDC):
                    s = rg.acquire(t)
                    pairs = []
                    if e == 0:
                        pairs.append((b2b[0:8, dc * 128:(dc + 1) * 128],
                                      gates_bf[0:8, i * TC:(i + 1) * TC]))
                    pairs += [(w2[:, fc, dc * 128:(dc + 1) * 128], hsb[:, hs, fc, :])
                              for fc in range(NFC)]
                    mm = mmgroup(t, ps_g[:, s * 512:(s + 1) * 512], pairs)
                    b.pe.inc(mm)
                    pem = b.pe.n
                    def evg(v, dc=dc, s=s, pem=pem, e=e, i=i):
                        v.wait_ge(b.pe.h, pem)
                        if e == 0:
                            b.dv.inc(v.tensor_copy(g_acc[:, dc, i * TC:(i + 1) * TC],
                                                   ps_g[:, s * 512:(s + 1) * 512]))
                        else:
                            b.dv.inc(v.tensor_tensor(g_acc[:, dc, i * TC:(i + 1) * TC],
                                                     ps_g[:, s * 512:(s + 1) * 512],
                                                     g_acc[:, dc, i * TC:(i + 1) * TC], ALU.add))
                    b.V(evg)
                    rg.done(s, b.dv)
            for i in range(NTC):
                xpar = [None]
                def ldx(sy, i=i, xpar=xpar):
                    xpar[0] = xn_rot.acquire(sy)
                    b.dsy.dma(sy.dma_start(out=xnc[:, xpar[0]], in_=xn_r[:, :, i * TC:(i + 1) * TC]))
                b.SY(ldx)
                xpar = xpar[0]
                dsym = b.dsy.n

                gs2 = [None]
                def ldgr(sc, e=e, i=i, gs2=gs2):
                    gs2[0] = grow_rot.acquire(sc)
                    b.dsc.dma(sc.dma_start(out=grow[:, gs2[0], :],
                                           in_=gates_bf[e:e + 1, i * TC:(i + 1) * TC]))
                b.SC(ldgr)
                gs2 = gs2[0]
                grm = b.dsc.n
                kgs = [None]
                def t_k(t, e=e, i=i, grm=grm, kgs=kgs, gs2=gs2):
                    t.wait_ge(b.dsc.h, grm)
                    kgs[0] = kg.acquire(t)
                    mm = t.matmul(ps_k[:, kgs[0] * 512:(kgs[0] + 1) * 512], onesb[0:1, 0:128],
                                  grow[0:1, gs2, :], start=True, stop=True)
                    b.pe.inc(mm)
                b.T(t_k)
                grow_rot.done(gs2, b.pe)
                kgs = kgs[0]
                pem = b.pe.n
                gbs = [None]
                def ev_k(v, pem=pem, kgs=kgs, gbs=gbs):
                    v.wait_ge(b.pe.h, pem)
                    gbs[0] = gbrot.acquire(v)
                    b.dv.inc(v.tensor_copy(gb[:, gbs[0], :], ps_k[:, kgs * 512:(kgs + 1) * 512]))
                b.V(ev_k)
                gbs = gbs[0]
                kg.done(kgs, b.dv)
                gbm = b.dv.n

                hs = [None]
                def pickh(sc, hs=hs):
                    hs[0] = h_rot.acquire(sc)
                b.SC(pickh)
                hs = hs[0]
                def t_h(t, e=e, i=i, xpar=xpar, dgm_w1=dgm_w1, dsym=dsym, hs=hs, wpar=wpar):
                    t.wait_ge(b.dgp.h, dgm_w1)
                    t.wait_ge(b.dsy.h, dsym)
                    for hc in range(NFC):
                        s = rh.acquire(t)
                        mm = mmgroup(t, ps_h[:, s * 512:(s + 1) * 512],
                                     [(w1[:, wpar, c, hc * 128:(hc + 1) * 128], xnc[:, xpar, c, :])
                                      for c in range(NDC)])
                        b.pe.inc(mm)
                        pem = b.pe.n
                        def evh(sc, hc=hc, s=s, pem=pem, e=e, hs=hs):
                            sc.wait_ge(b.pe.h, pem)
                            b.ac.inc(sc.activation(hsb[:, hs, hc, :], ps_h[:, s * 512:(s + 1) * 512],
                                                   AF.Relu, bias=b1c[:, e, hc:hc + 1]))
                        b.SC(evh)
                        rh.done(s, b.ac)
                    if pend[0] is not None:
                        emit_td(t, pend[0])
                        pend[0] = None
                b.T(t_h)
                xn_rot.done(xpar, b.pe)
                acm = b.ac.n
                def v_hg(v, acm=acm, gbm=gbm, hs=hs, gbs=gbs):
                    v.wait_ge(b.ac.h, acm)
                    v.wait_ge(b.dv.h, gbm)
                    b.dv.inc(v.tensor_tensor(
                        hsb[:, hs, :, :], hsb[:, hs, :, :],
                        gb[:, gbs, :].rearrange("p (o t) -> p o t", o=1).broadcast_to((128, NFC, TC)),
                        ALU.mult))
                b.V(v_hg)
                gbrot.done(gbs, b.dv)
                pend[0] = (i, hs, b.dv.n)
                h_rot.done(hs, b.pe)
            w1_rot.done(0, b.pe)
            if e + 1 < E:
                def ldw1_n(g, e=e):
                    w1_rot.acquire(g)
                    b.dgp.dma(g.dma_start(out=w1[:, 0],
                                          in_=iaps["w_l1_W1"][e + 1].rearrange("(c p) f -> p c f", p=128)))
                b.G(ldw1_n)
                dgm_w1_next[0] = b.dgp.n
            def t_tail(t):
                if pend[0] is not None:
                    emit_td(t, pend[0])
                    pend[0] = None
            b.T(t_tail)
            w2_rot.done(0, b.pe)
        b.barrier()

    emit_moe_join(b, x_src, x_dst, s_hbm, g_acc)
    bes.close()


def emit_block0(b, iaps, x_src, x_dst, xn_hbm, s_hbm, h_hbm, cc_in, cc_out, pfx_sb,
                ident, identb, ones, tri, rows):
    nc = b.nc
    xn_r = xn_hbm.rearrange("(c p) t -> p c t", p=128)
    s_r = s_hbm.rearrange("(c p) t -> p c t", p=128)
    h_r = h_hbm.rearrange("(c p) t -> p c t", p=128)

    bes = ExitStack()
    gates_bf = bes.enter_context(nc.sbuf_tensor("l0_gbf", [8, S], BF16))

    with ExitStack() as res_:
        sb = lambda n, sh, dt: res_.enter_context(nc.sbuf_tensor(n, sh, dt))
        wr_sb = sb("l0_wr", [128, NDC, E], F32)
        logits_sb = sb("l0_lg", [128, NTILE, E], F32)

        def ldr(sy):
            b.dsy.dma(sy.dma_start(out=wr_sb[:, :, :],
                                   in_=iaps["w_l0_Wr"].rearrange("(c p) e -> p c e", p=128)))
        b.SY(ldr)
        b.barrier()
        emit_stage1(b, x_src, xn_hbm, ones, rows, wr_sb=wr_sb, logits_sb=logits_sb)

        scr = {k: sb("l0_scr_" + k, [128, NTILE, E], F32)
               for k in ("oh1", "oh2", "t1", "t2", "e1")}
        for k in ("m1", "m2", "m1b", "m2b"):
            scr[k] = sb("l0_" + k, [128, NTILE, 1], F32)
        emit_top2(b, logits_sb, scr)

        ohs = sb("l0_ohs", [128, NTILE, 16], F32)
        w1t = sb("l0_w1t", [128, NTILE, 1], F32)
        rank_sb = sb("l0_rank", [128, NTILE, 16], F32)
        carry16 = sb("l0_carry16", [1, 16], F32)
        offs_bc = sb("l0_offs", [128, 16], F32)
        ccsb = sb("l0_ccsb", [8, 16], F32)
        ps_rk = res_.enter_context(nc.psum_tensor("l0_psrk", [128, 512], F32))
        ps_ct = res_.enter_context(nc.psum_tensor("l0_psct", [128, 512], F32))

        def v5(v):
            v.tensor_copy(ohs[:, :, 0:8], scr["oh1"][:, :, :])
            v.tensor_copy(ohs[:, :, 8:16], scr["oh2"][:, :, :])
            b.dv.inc(v.tensor_sub(scr["t1"][:, :, 0:1], scr["m1b"][:, :, :], scr["m2b"][:, :, :]))
        b.V(v5)
        dvm = b.dv.n
        def s5(sc, dvm=dvm):
            sc.wait_ge(b.dv.h, dvm)
            b.ac.inc(sc.activation(w1t[:, :, :], scr["t1"][:, :, 0:1], AF.Sigmoid))
        b.SC(s5)

        dvm = b.dv.n
        acm = b.ac.n
        cguard = [None]
        for ti in range(NTILE):
            def t_rk(t, ti=ti, dvm=dvm, acm=acm):
                if ti == 0:
                    t.wait_ge(b.dv.h, dvm)
                    t.wait_ge(b.ac.h, acm)
                else:
                    t.wait_ge(cguard[0][0].h, cguard[0][1])
                pairs = [(tri[:, :], ohs[:, ti, :])]
                if ti > 0:
                    pairs.append((ones[0:1, 0:128], carry16[0:1, :]))
                mm = mmgroup(t, ps_rk[:, 0:16], pairs)
                b.pe.inc(mm)
                mm = t.matmul(ps_ct[0:1, 0:16], ones[:, 0:1], ohs[:, ti, :],
                              start=(ti == 0), stop=True)
                b.pe.inc(mm)
            b.T(t_rk)
            pem = b.pe.n
            def ev_rk(v, ti=ti, pem=pem):
                v.wait_ge(b.pe.h, pem)
                v.tensor_copy(rank_sb[:, ti, :], ps_rk[:, 0:16])
                b.dv.inc(v.tensor_copy(carry16[0:1, :], ps_ct[0:1, 0:16]))
            b.V(ev_rk)
            cguard[0] = (b.dv, b.dv.n)

        dvm = b.dv.n
        def st_cc(sy, dvm=dvm):
            sy.wait_ge(b.dv.h, dvm)
            b.dsy.dma(sy.dma_start(out=cc_in[:, :], in_=carry16[0:1, :]))
        b.SY(st_cc)
        dsym = b.dsy.n
        def do_cc(g, dsym=dsym):
            g.wait_ge(b.dsy.h, dsym)
            b.cc.inc(g.collective_compute(
                "AllGather", ALU.bypass,
                replica_groups=[list(range(8))],
                ins=[cc_in[:, :]],
                outs=[cc_out[:, :]]), 1)
        b.G(do_cc)
        ccm = b.cc.n
        def ld_cc(sy, ccm=ccm):
            sy.wait_ge(b.cc.h, ccm)
            b.dsy.dma(sy.dma_start(out=ccsb[:, :], in_=cc_out[:, :]))
        b.SY(ld_cc)
        dsym = b.dsy.n
        def t_off(t, dsym=dsym):
            t.wait_ge(b.dsy.h, dsym)
            mm = t.matmul(ps_ct[0:1, 32:48], pfx_sb[:, :], ccsb[:, :], start=True, stop=True)
            b.pe.inc(mm)
        b.T(t_off)
        pem = b.pe.n
        def ev_off(v, pem=pem):
            v.wait_ge(b.pe.h, pem)
            b.dv.inc(v.tensor_copy(rows[0:1, 0:16], ps_ct[0:1, 32:48]))
        b.V(ev_off)
        dvm = b.dv.n
        def t_offb(t, dvm=dvm):
            t.wait_ge(b.dv.h, dvm)
            mm = t.matmul(ps_rk[:, 32:48], ones[0:1, 0:128], rows[0:1, 0:16], start=True, stop=True)
            b.pe.inc(mm)
        b.T(t_offb)
        pem = b.pe.n
        def ev_offb(v, pem=pem):
            v.wait_ge(b.pe.h, pem)
            b.dv.inc(v.tensor_copy(offs_bc[:, :], ps_rk[:, 32:48]))
        b.V(ev_offb)

        def v6(v):
            sh16 = (128, NTILE, 16)
            v.tensor_tensor(rank_sb[:, :, :], rank_sb[:, :, :],
                            offs_bc[:, :].rearrange("p (o f) -> p o f", o=1).broadcast_to(sh16),
                            ALU.add)
            v.tensor_single_scalar(rank_sb[:, :, :], rank_sb[:, :, :], float(CAPACITY), ALU.is_le)
            v.tensor_mul(rank_sb[:, :, :], rank_sb[:, :, :], ohs[:, :, :])
            sh8 = (128, NTILE, 8)
            v.tensor_mul(scr["t2"][:, :, :], rank_sb[:, :, 0:8], w1t[:, :, :].broadcast_to(sh8))
            v.tensor_scalar(scr["t1"][:, :, 0:1], w1t[:, :, :], -1.0, 1.0, ALU.mult, ALU.add)
            v.tensor_mul(scr["e1"][:, :, :], rank_sb[:, :, 8:16],
                         scr["t1"][:, :, 0:1].broadcast_to(sh8))
            b.dv.inc(v.tensor_add(scr["e1"][:, :, :], scr["e1"][:, :, :], scr["t2"][:, :, :]))
        b.V(v6)
        ps_t = res_.enter_context(nc.psum_tensor("l0_pst", [128, 2048], F32))
        _gates_transpose(b, scr["e1"][:, :, :], gates_bf, ident, ps_t)
        b.barrier()

    with ExitStack() as aes:
        sb = lambda n, sh, dt: aes.enter_context(nc.sbuf_tensor(n, sh, dt))
        psn = lambda n, sh: aes.enter_context(nc.psum_tensor(n, sh, F32))
        winT = sb("l0_win", [128, NDC, 512], BF16)
        woutT = sb("l0_wout", [128, 4, D], BF16)
        ldec = sb("l0_ldec", [128, 2, 1], F32)
        freq = sb("l0_freq", [128, 2, 1], F32)
        dcol = sb("l0_dcol", [128, 2, 1], F32)
        c1 = sb("l0_c1", [128, 2, 1], F32)
        s1v = sb("l0_s1", [128, 2, 1], F32)
        csc = sb("l0_csc", [128, 2, 1], F32)
        ssc = sb("l0_ssc", [128, 2, 1], F32)
        Ct = sb("l0_C", [128, 2, S], F32)
        St = sb("l0_S", [128, 2, S], F32)
        tmp2 = sb("l0_tmp2", [128, 2, 2048], F32)
        dec_tile = sb("l0_dect", [128, 4, TC], F32)

        def ldw(g):
            b.dgp.dma(g.dma_start(out=winT[:, :, :],
                                  in_=iaps["w_l0_WinT"].rearrange("(c p) f -> p c f", p=128)))
            b.dgp.dma(g.dma_start(out=woutT[:, :, :],
                                  in_=iaps["w_l0_WoutT"].rearrange("(c p) o -> p c o", p=128)))
        b.G(ldw)
        def lds(sy):
            b.dsy.dma(sy.dma_start(out=ldec[:, :, 0], in_=iaps["w_l0_logdec"][:, :]))
            b.dsy.dma(sy.dma_start(out=freq[:, :, 0], in_=iaps["w_l0_freq"][:, :]))
        b.SY(lds)
        dsym = b.dsy.n
        def s_trig(sc, dsym=dsym):
            sc.wait_ge(b.dsy.h, dsym)
            sc.activation(dcol[:, :, :], ldec[:, :, :], AF.Sigmoid)
            sc.activation(c1[:, :, :], freq[:, :, :], AF.Sin, bias=float(np.pi / 2), scale=0.1)
            b.ac.inc(sc.activation(s1v[:, :, :], freq[:, :, :], AF.Sin, scale=0.1))
        b.SC(s_trig)
        acm = b.ac.n
        def v_init(v, acm=acm):
            v.wait_ge(b.ac.h, acm)
            for j in range(4):
                v.tensor_scalar_mul(dec_tile[:, j, :], ones[:, 0:TC], dcol[:, j % 2, 0:1])
            v.memset(Ct[:, :, 0:1], 1.0)
            v.memset(St[:, :, 0:1], 0.0)
            v.tensor_copy(Ct[:, :, 1:2], c1[:, :, :])
            b.dv.inc(v.tensor_copy(St[:, :, 1:2], s1v[:, :, :]))
        b.V(v_init)

        vv = 2
        while vv < S:
            w = min(vv - 1, S - vv)
            dvm = b.dv.n
            def sc_b(sc, vv=vv, dvm=dvm):
                sc.wait_ge(b.dv.h, dvm)
                sc.copy(csc[:, :, :], Ct[:, :, vv - 1:vv])
                b.ac.inc(sc.copy(ssc[:, :, :], St[:, :, vv - 1:vv]))
            b.SC(sc_b)
            acm = b.ac.n
            def v_dbl(v, vv=vv, w=w, acm=acm):
                v.wait_ge(b.ac.h, acm)
                last = None
                for cr in range(2):
                    v.tensor_scalar_mul(tmp2[:, cr, 0:w], St[:, cr, 1:1 + w], ssc[:, cr, 0:1])
                    v.scalar_tensor_tensor(Ct[:, cr, vv:vv + w], Ct[:, cr, 1:1 + w],
                                           csc[:, cr, 0:1], tmp2[:, cr, 0:w],
                                           ALU.mult, ALU.subtract)
                    v.tensor_scalar_mul(tmp2[:, cr, 0:w], Ct[:, cr, 1:1 + w], ssc[:, cr, 0:1])
                    last = v.scalar_tensor_tensor(St[:, cr, vv:vv + w], St[:, cr, 1:1 + w],
                                                  csc[:, cr, 0:1], tmp2[:, cr, 0:w],
                                                  ALU.mult, ALU.add)
                b.dv.inc(last)
            b.V(v_dbl)
            vv = vv + w

        xnc = sb("l0_xnc", [128, 2, NDC, TC], BF16)
        u = sb("l0_u", [128, 4, TC], F32)
        us = sb("l0_us", [128, 4, TC], F32)
        cxs = sb("l0_cx", [128, 4, TC], F32)
        c2t = sb("l0_c2", [128, 2, TC], F32)
        s2t = sb("l0_s2", [128, 2, TC], F32)
        fb = sb("l0_fb", [128, 4, TC], BF16)
        ssb = sb("l0_ssb", [128, NDC, TC], BF16)
        carry = sb("l0_carry", [128, 4], F32)
        ps_a = psn("l0_psa", [128, 1024])
        b.barrier()

        def vz(v):
            b.dv.inc(v.memset(carry[:, :], 0.0))
        b.V(vz)

        xn_rot = Rot(2)
        ssb_rot = Rot(1)
        ra = Rot(2)
        for i in range(NTC):
            par = [None]
            def ld(sy, i=i, par=par):
                par[0] = xn_rot.acquire(sy)
                b.dsy.dma(sy.dma_start(out=xnc[:, par[0]], in_=xn_r[:, :, i * TC:(i + 1) * TC]))
            b.SY(ld)
            par = par[0]
            dsym = b.dsy.n

            def t_sp(t, par=par, dsym=dsym):
                t.wait_ge(b.dsy.h, dsym)
                for fc in range(4):
                    s = ra.acquire(t)
                    mm = mmgroup(t, ps_a[:, s * 512:(s + 1) * 512],
                                 [(winT[:, c, fc * 128:(fc + 1) * 128], xnc[:, par, c, :])
                                  for c in range(NDC)])
                    b.pe.inc(mm)
                    pem = b.pe.n
                    def evs(v, fc=fc, s=s, pem=pem):
                        v.wait_ge(b.pe.h, pem)
                        b.dv.inc(v.tensor_copy(u[:, fc, :], ps_a[:, s * 512:(s + 1) * 512]))
                    b.V(evs)
                    ra.done(s, b.dv)
            b.T(t_sp)
            xn_rot.done(par, b.pe)

            def v_tw(v, i=i):
                sl = slice(i * TC, (i + 1) * TC)
                last = None
                for cr in range(2):
                    v.tensor_mul(us[:, cr, :], u[:, cr, :], Ct[:, cr, sl])
                    v.tensor_mul(cxs[:, cr, :], u[:, 2 + cr, :], St[:, cr, sl])
                    v.tensor_add(us[:, cr, :], us[:, cr, :], cxs[:, cr, :])
                    v.tensor_mul(us[:, 2 + cr, :], u[:, 2 + cr, :], Ct[:, cr, sl])
                    v.tensor_mul(cxs[:, 2 + cr, :], u[:, cr, :], St[:, cr, sl])
                    last = v.tensor_sub(us[:, 2 + cr, :], us[:, 2 + cr, :], cxs[:, 2 + cr, :])
                b.dv.inc(last)
            b.V(v_tw)

            dvm = b.dv.n
            def v_scan(v, dvm=dvm):
                v.wait_ge(b.dv.h, dvm)
                last = None
                for j in range(4):
                    last = v.tensor_tensor_scan(cxs[:, j, :], dec_tile[:, j, :], us[:, j, :],
                                                carry[:, j:j + 1], ALU.mult, ALU.add)
                b.dv.inc(last)
            b.V(v_scan)
            dvm = b.dv.n
            def s_carry(sc, dvm=dvm):
                sc.wait_ge(b.dv.h, dvm)
                last = None
                for j in range(4):
                    last = sc.copy(carry[:, j:j + 1], cxs[:, j, TC - 1:TC])
                b.ac.inc(last)
            b.SC(s_carry)
            acm = b.ac.n

            def v_ut(v, i=i, acm=acm):
                v.wait_ge(b.ac.h, acm)
                sl = slice(i * TC, (i + 1) * TC)
                for cr in range(2):
                    v.tensor_scalar_mul(c2t[:, cr, :], St[:, cr, sl], s1v[:, cr, 0:1])
                    v.scalar_tensor_tensor(c2t[:, cr, :], Ct[:, cr, sl], c1[:, cr, 0:1],
                                           c2t[:, cr, :], ALU.mult, ALU.subtract)
                    v.tensor_scalar_mul(s2t[:, cr, :], Ct[:, cr, sl], s1v[:, cr, 0:1])
                    v.scalar_tensor_tensor(s2t[:, cr, :], St[:, cr, sl], c1[:, cr, 0:1],
                                           s2t[:, cr, :], ALU.mult, ALU.add)
                last = None
                for cr in range(2):
                    v.tensor_mul(us[:, cr, :], cxs[:, cr, :], c2t[:, cr, :])
                    v.tensor_mul(us[:, 2 + cr, :], cxs[:, 2 + cr, :], s2t[:, cr, :])
                    v.tensor_sub(fb[:, cr, :], us[:, cr, :], us[:, 2 + cr, :])
                    v.tensor_mul(us[:, cr, :], cxs[:, cr, :], s2t[:, cr, :])
                    v.tensor_mul(us[:, 2 + cr, :], cxs[:, 2 + cr, :], c2t[:, cr, :])
                    last = v.tensor_add(fb[:, 2 + cr, :], us[:, cr, :], us[:, 2 + cr, :])
                b.dv.inc(last)
            b.V(v_ut)
            dvm = b.dv.n

            sslot = [None]
            def pick(v, sslot=sslot):
                sslot[0] = ssb_rot.acquire(v)
            b.V(pick)
            sslot = sslot[0]
            def t_wo(t, dvm=dvm, sslot=sslot):
                t.wait_ge(b.dv.h, dvm)
                for oc in range(NDC):
                    s = ra.acquire(t)
                    mm = mmgroup(t, ps_a[:, s * 512:(s + 1) * 512],
                                 [(woutT[:, fc, oc * 128:(oc + 1) * 128], fb[:, fc, :])
                                  for fc in range(4)])
                    b.pe.inc(mm)
                    pem = b.pe.n
                    def evo(v, oc=oc, s=s, pem=pem, sslot=sslot):
                        v.wait_ge(b.pe.h, pem)
                        b.dv.inc(v.tensor_copy(ssb[:, oc, :], ps_a[:, s * 512:(s + 1) * 512]))
                    b.V(evo)
                    ra.done(s, b.dv)
            b.T(t_wo)
            dvm = b.dv.n
            def st_s(sy, i=i, dvm=dvm):
                sy.wait_ge(b.dv.h, dvm)
                b.dsy.dma(sy.dma_start(out=s_r[:, :, i * TC:(i + 1) * TC], in_=ssb[:, :, :]))
            b.SY(st_s)
            ssb_rot.done(0, b.dsy)
        b.barrier()

    g_acc = bes.enter_context(nc.sbuf_tensor("l0_gacc", [128, NDC, S], BF16))
    onesb = b.onesb
    FH = 4096
    NFC = FH // 128
    h_guard = [None]
    with ExitStack() as fes:
        sb = lambda n, sh, dt: fes.enter_context(nc.sbuf_tensor(n, sh, dt))
        psn = lambda n, sh: fes.enter_context(nc.psum_tensor(n, sh, F32))
        b1c = sb("l0f_b1", [128, E, NFC], F32)
        b2b = sb("l0f_b2", [8, D], BF16)
        grow = sb("l0f_grow", [1, 2, TC], BF16)
        gb = sb("l0f_gb", [128, TC], F32)
        gt1 = sb("l0f_gt1", [128, TC], F32)
        ps_h = psn("l0f_psh", [128, 1024])
        ps_g = psn("l0f_psg", [128, 1024])
        ps_k = psn("l0f_psk", [128, 512])

        def lds(sy):
            b.dsy.dma(sy.dma_start(out=b1c[:, :, :], in_=iaps["w_l0_b1"][:, :, :]))
        b.SY(lds)
        def ldb2(g):
            b.dgp.dma(g.dma_start(out=b2b[:, :], in_=iaps["w_l0_b2"][:, :]))
        b.G(ldb2)

        w1_rot = Rot(1)
        w2_rot = Rot(1)
        xn_rot = Rot(2)
        hst_rot = Rot(2)
        rh = Rot(2)
        rg = Rot(2)
        kg = Rot(2)
        grow_rot = Rot(1)

        dgm0 = b.dgp.n
        for i in range(NTC):
            def t_b2(t, i=i, dgm0=dgm0):
                if i == 0:
                    t.wait_ge(b.dgp.h, dgm0)
                for dc in range(NDC):
                    s = rg.acquire(t)
                    mm = t.matmul(ps_g[:, s * 512:(s + 1) * 512],
                                  b2b[0:8, dc * 128:(dc + 1) * 128],
                                  gates_bf[0:8, i * TC:(i + 1) * TC], start=True, stop=True)
                    b.pe.inc(mm)
                    pem = b.pe.n
                    def evb(v, dc=dc, s=s, pem=pem, i=i):
                        v.wait_ge(b.pe.h, pem)
                        b.dv.inc(v.tensor_copy(g_acc[:, dc, i * TC:(i + 1) * TC],
                                               ps_g[:, s * 512:(s + 1) * 512]))
                    b.V(evb)
                    rg.done(s, b.dv)
            b.T(t_b2)

        for e in range(E):
            p1es = ExitStack()
            w1 = p1es.enter_context(nc.sbuf_tensor(f"l0f_w1_{e}", [128, NDC, FH], BF16))
            xnc = p1es.enter_context(nc.sbuf_tensor(f"l0f_xnc_{e}", [128, 2, NDC, TC], BF16))
            hst = p1es.enter_context(nc.sbuf_tensor(f"l0f_hst_{e}", [128, 2, 8, TC], BF16))
            hg0 = h_guard[0]
            dvg0 = b.dv.n
            def ldw1(g, e=e, hg0=hg0, dvg0=dvg0):
                if hg0 is not None:
                    g.wait_ge(hg0[0].h, hg0[1])
                    g.wait_ge(b.dv.h, dvg0)
                w1_rot.acquire(g)
                b.dgp.dma(g.dma_start(out=w1[:, :, :],
                                      in_=iaps["w_l0_W1"][e].rearrange("(c p) f -> p c f", p=128)))
            b.G(ldw1)
            dgm = b.dgp.n
            for i in range(NTC):
                xpar = [None]
                def ldx(sy, i=i, xpar=xpar, hg0=hg0):
                    if i == 0 and hg0 is not None:
                        sy.wait_ge(hg0[0].h, hg0[1])
                    xpar[0] = xn_rot.acquire(sy)
                    b.dsy.dma(sy.dma_start(out=xnc[:, xpar[0]], in_=xn_r[:, :, i * TC:(i + 1) * TC]))
                b.SY(ldx)
                xpar = xpar[0]
                dsym = b.dsy.n
                for hg in range(4):
                    hpar = [None]
                    def pickh(sc, hpar=hpar):
                        hpar[0] = hst_rot.acquire(sc)
                    b.SC(pickh)
                    hpar = hpar[0]
                    def t_h(t, e=e, i=i, hg=hg, xpar=xpar, dgm=dgm, dsym=dsym, hpar=hpar):
                        if hg == 0:
                            t.wait_ge(b.dgp.h, dgm)
                            t.wait_ge(b.dsy.h, dsym)
                        if h_guard[0] is not None and hg == 0 and i == 0:
                            t.wait_ge(h_guard[0][0].h, h_guard[0][1])
                        for hh in range(8):
                            hc = hg * 8 + hh
                            s = rh.acquire(t)
                            mm = mmgroup(t, ps_h[:, s * 512:(s + 1) * 512],
                                         [(w1[:, c, hc * 128:(hc + 1) * 128], xnc[:, xpar, c, :])
                                          for c in range(NDC)])
                            b.pe.inc(mm)
                            pem = b.pe.n
                            def evh(sc, hh=hh, hc=hc, s=s, pem=pem, e=e, hpar=hpar):
                                sc.wait_ge(b.pe.h, pem)
                                b.ac.inc(sc.activation(hst[:, hpar, hh, :],
                                                       ps_h[:, s * 512:(s + 1) * 512],
                                                       AF.Gelu, bias=b1c[:, e, hc:hc + 1]))
                            b.SC(evh)
                            rh.done(s, b.ac)
                    b.T(t_h)
                    acm = b.ac.n
                    def st_h(sy, i=i, hg=hg, hpar=hpar, acm=acm):
                        sy.wait_ge(b.ac.h, acm)
                        b.dsy.dma(sy.dma_start(
                            out=h_r[:, hg * 8:(hg + 1) * 8, i * TC:(i + 1) * TC],
                            in_=hst[:, hpar]))
                    b.SY(st_h)
                    hst_rot.done(hpar, b.dsy)
                xn_rot.done(xpar, b.pe)
            p1_store_mark = b.dsy.n
            w1_rot.done(0, b.pe)
            p1_pe_mark = b.pe.n
            p1_ac_mark = b.ac.n
            p1es.close()
            if DEBUG_DUMP and e == 0:
                dbg_hh = nc.dram_tensor("dbg_hh", [4096, S], BF16, kind="ExternalOutput").ap()
                def dmp_hh(sy, p1_store_mark=p1_store_mark):
                    sy.wait_ge(b.dsy.h, p1_store_mark)
                    b.dsy.dma(sy.dma_start(out=dbg_hh[:, :], in_=h_hbm[:, :]))
                b.SY(dmp_hh)

            p2es = ExitStack()
            w2 = p2es.enter_context(nc.sbuf_tensor(f"l0f_w2_{e}", [128, NFC, D], BF16))
            hld = p2es.enter_context(nc.sbuf_tensor(f"l0f_hld_{e}", [128, 3, 16, TC], BF16))
            hld_rot = Rot(3)
            def ldw2(g, e=e, p1_pe_mark=p1_pe_mark, p1_ac_mark=p1_ac_mark):
                g.wait_ge(b.pe.h, p1_pe_mark)
                g.wait_ge(b.ac.h, p1_ac_mark)
                g.wait_ge(b.dsy.h, p1_store_mark)
                w2_rot.acquire(g)
                b.dgp.dma(g.dma_start(out=w2[:, :, :],
                                      in_=iaps["w_l0_W2"][e].rearrange("(c p) d -> p c d", p=128)))
            b.G(ldw2)
            dgm2 = b.dgp.n
            for i in range(NTC):
                hpar = [None, None]
                def ldh(sy, i=i, hpar=hpar, p1_store_mark=p1_store_mark,
                        p1_pe_mark=p1_pe_mark, p1_ac_mark=p1_ac_mark):
                    sy.wait_ge(b.dsy.h, p1_store_mark)
                    if i == 0:
                        sy.wait_ge(b.pe.h, p1_pe_mark)
                        sy.wait_ge(b.ac.h, p1_ac_mark)
                    hpar[0] = hld_rot.acquire(sy)
                    b.dsy.dma(sy.dma_start(out=hld[:, hpar[0]],
                                           in_=h_r[:, 0:16, i * TC:(i + 1) * TC]))
                    hpar[1] = hld_rot.acquire(sy)
                    b.dsy.dma(sy.dma_start(out=hld[:, hpar[1]],
                                           in_=h_r[:, 16:32, i * TC:(i + 1) * TC]))
                b.SY(ldh)
                dsym = b.dsy.n

                gs2 = [None]
                def ldgr(sc, e=e, i=i, gs2=gs2):
                    gs2[0] = grow_rot.acquire(sc)
                    b.dsc.dma(sc.dma_start(out=grow[:, gs2[0], :],
                                           in_=gates_bf[e:e + 1, i * TC:(i + 1) * TC]))
                b.SC(ldgr)
                gs2 = gs2[0]
                grm = b.dsc.n
                def t_k(t, e=e, i=i, grm=grm, gs2=gs2):
                    t.wait_ge(b.dsc.h, grm)
                    kg.acquire(t)
                    mm = t.matmul(ps_k[:, 0:512], onesb[0:1, 0:128],
                                  grow[0:1, gs2, :], start=True, stop=True)
                    b.pe.inc(mm)
                b.T(t_k)
                grow_rot.done(gs2, b.pe)
                pem = b.pe.n
                def ev_k(v, pem=pem):
                    v.wait_ge(b.pe.h, pem)
                    b.dv.inc(v.tensor_copy(gb[:, :], ps_k[:, 0:512]))
                b.V(ev_k)
                kg.done(0, b.dv)
                gbm = b.dv.n

                def t_d(t, e=e, i=i, hpar=hpar, dgm2=dgm2, dsym=dsym):
                    t.wait_ge(b.dgp.h, dgm2)
                    t.wait_ge(b.dsy.h, dsym)
                    for dc in range(NDC):
                        s = rg.acquire(t)
                        pairs = [(w2[:, fc, dc * 128:(dc + 1) * 128],
                                  hld[:, hpar[fc // 16], fc % 16, :])
                                 for fc in range(NFC)]
                        mm = mmgroup(t, ps_g[:, s * 512:(s + 1) * 512], pairs)
                        b.pe.inc(mm)
                        pem = b.pe.n
                        def evg(v, dc=dc, s=s, pem=pem, e=e, i=i, gbm=gbm):
                            v.wait_ge(b.pe.h, pem)
                            v.wait_ge(b.dv.h, gbm)
                            v.tensor_tensor(gt1[:, :], ps_g[:, s * 512:(s + 1) * 512],
                                            gb[:, :], ALU.mult)
                            b.dv.inc(v.tensor_tensor(g_acc[:, dc, i * TC:(i + 1) * TC],
                                                     gt1[:, :],
                                                     g_acc[:, dc, i * TC:(i + 1) * TC],
                                                     ALU.add))
                        b.V(evg)
                        rg.done(s, b.dv)
                b.T(t_d)
                hld_rot.done(hpar[0], b.pe)
                hld_rot.done(hpar[1], b.pe)
            w2_rot.done(0, b.pe)
            grow_rot.done(0, b.pe)
            h_guard[0] = (b.pe, b.pe.n)
            p2es.close()
        b.barrier()
    emit_moe_join(b, x_src, x_dst, s_hbm, g_acc)
    bes.close()


def emit_routing_sparsemixer(b, logits_sb, scr, gates_bf, ident, ps_t):
    sh = (128, NTILE, E)
    m1b = scr["m1b"][:, :, :]
    m2b = scr["m2b"][:, :, :]
    m1 = scr["m1"][:, :, :]
    m2 = scr["m2"][:, :, :]
    oh1 = scr["oh1"][:, :, :]
    oh2 = scr["oh2"][:, :, :]
    t1 = scr["t1"][:, :, :]
    t2 = scr["t2"][:, :, :]
    e1 = scr["e1"][:, :, :]
    k1 = scr["k1"][:, :, :]
    k2 = scr["k2"][:, :, :]
    d1 = scr["d1"][:, :, :]
    lg = logits_sb[:, :, :]

    emit_top2(b, logits_sb, scr)
    dvm = b.dv.n
    def sA(sc, dvm=dvm):
        sc.wait_ge(b.dv.h, dvm)
        b.ac.inc(sc.activation(t2, lg, AF.Abs))
    b.SC(sA)
    acm = b.ac.n
    def v1(v, acm=acm):
        v.wait_ge(b.ac.h, acm)
        v.tensor_tensor(d1, lg, m1b.broadcast_to(sh), ALU.subtract)
        v.tensor_tensor(k1, t2, m1b.broadcast_to(sh), ALU.max)
        v.scalar_tensor_tensor(k1, k1, 0.02, d1, ALU.mult, ALU.add)
        v.tensor_single_scalar(k1, k1, 0.0, ALU.is_ge)
        v.tensor_tensor(e1, lg, m2b.broadcast_to(sh), ALU.subtract)
        v.tensor_tensor(k2, t2, m2b.broadcast_to(sh), ALU.max)
        v.scalar_tensor_tensor(k2, k2, 0.02, e1, ALU.mult, ALU.add)
        v.tensor_single_scalar(k2, k2, 0.0, ALU.is_ge)
        v.tensor_scalar(t1, oh1, -1.0, 1.0, ALU.mult, ALU.add)
        v.tensor_mul(k2, k2, t1)
        b.dv.inc(v.tensor_single_scalar(e1, e1, 0.0, ALU.min))
    b.V(v1)
    dvm = b.dv.n
    def sE(sc, dvm=dvm):
        sc.wait_ge(b.dv.h, dvm)
        sc.activation(d1, d1, AF.Exp)
        b.ac.inc(sc.activation(e1, e1, AF.Exp))
    b.SC(sE)
    acm = b.ac.n
    def v2(v, acm=acm):
        v.wait_ge(b.ac.h, acm)
        v.tensor_mul(d1, d1, k1)
        v.tensor_mul(e1, e1, k2)
        v.tensor_reduce(m1, d1, AX, ALU.add)
        b.dv.inc(v.tensor_reduce(m2, e1, AX, ALU.add))
    b.V(v2)
    bounce_copy(b, [(m1b, m1), (m2b, m2)])
    def v3(v):
        v.reciprocal(m1, m1b)
        b.dv.inc(v.reciprocal(m2, m2b))
    b.V(v3)
    bounce_copy(b, [(m1b, m1), (m2b, m2)])
    def v4(v):
        v.tensor_mul(t1, oh1, m1b.broadcast_to(sh))
        v.tensor_mul(t2, oh2, m2b.broadcast_to(sh))
        b.dv.inc(v.tensor_add(e1, t1, t2))
    b.V(v4)
    _gates_transpose(b, e1, gates_bf, ident, ps_t)


def emit_block2(b, iaps, x_src, x_dst, xn_hbm, s_hbm, h_hbm, y_hbm,
                ident, identb, ones, rows):
    nc = b.nc
    xn_r = xn_hbm.rearrange("(c p) t -> p c t", p=128)
    s_r = s_hbm.rearrange("(c p) t -> p c t", p=128)
    y_r = y_hbm.rearrange("(c p) t -> p c t", p=128)
    lat_r = h_hbm.rearrange("(c p) t -> p c t", p=128)
    lat_rows = h_hbm

    bes = ExitStack()
    gates_bf = bes.enter_context(nc.sbuf_tensor("l2_gbf", [8, S], BF16))

    with ExitStack() as res_:
        sb = lambda n, sh, dt: res_.enter_context(nc.sbuf_tensor(n, sh, dt))
        wr_sb = sb("l2_wr", [128, NDC, E], F32)
        logits_sb = sb("l2_lg", [128, NTILE, E], F32)
        def ldr(sy):
            b.dsy.dma(sy.dma_start(out=wr_sb[:, :, :],
                                   in_=iaps["w_l2_Wr"].rearrange("(c p) e -> p c e", p=128)))
        b.SY(ldr)
        b.barrier()
        emit_stage1(b, x_src, xn_hbm, ones, rows, wr_sb=wr_sb, logits_sb=logits_sb)
        scr = {k: sb("l2_scr_" + k, [128, NTILE, E], F32)
               for k in ("oh1", "oh2", "t1", "t2", "e1", "k1", "k2", "d1")}
        for k in ("m1", "m2", "m1b", "m2b"):
            scr[k] = sb("l2_" + k, [128, NTILE, 1], F32)
        ps_t = res_.enter_context(nc.psum_tensor("l2_pst", [128, 2048], F32))
        emit_routing_sparsemixer(b, logits_sb, scr, gates_bf, ident, ps_t)
        b.barrier()

    with ExitStack() as aes:
        sb = lambda n, sh, dt: aes.enter_context(nc.sbuf_tensor(n, sh, dt))
        wq = sb("l2a_wq", [128, NDC, D], BF16)
        wk = sb("l2a_wk", [128, NDC, D], BF16)
        wv = sb("l2a_wv", [128, NDC, D], BF16)
        xnc = sb("l2a_xnc", [128, 2, NDC, TC], BF16)
        lst = sb("l2a_lst", [128, 2, NDC, TC], BF16)
        ps_a = aes.enter_context(nc.psum_tensor("l2a_ps", [128, 1024], F32))
        def ldw(g):
            b.dgp.dma(g.dma_start(out=wq[:, :, :], in_=iaps["w_l2_Wqd"].rearrange("(c p) o -> p c o", p=128)))
            b.dgp.dma(g.dma_start(out=wk[:, :, :], in_=iaps["w_l2_Wkd"].rearrange("(c p) o -> p c o", p=128)))
            b.dgp.dma(g.dma_start(out=wv[:, :, :], in_=iaps["w_l2_Wvd"].rearrange("(c p) o -> p c o", p=128)))
        b.G(ldw)
        b.barrier()
        xn_rot = Rot(2)
        lst_rot = Rot(2)
        ra = Rot(2)
        for i in range(NTC):
            par = [None]
            def ld(sy, i=i, par=par):
                par[0] = xn_rot.acquire(sy)
                b.dsy.dma(sy.dma_start(out=xnc[:, par[0]], in_=xn_r[:, :, i * TC:(i + 1) * TC]))
            b.SY(ld)
            par = par[0]
            dsym = b.dsy.n
            for pi, wmat in ((0, wq), (1, wk), (2, wv)):
                lpar = [None]
                def pickl(v, lpar=lpar):
                    lpar[0] = lst_rot.acquire(v)
                b.V(pickl)
                lpar = lpar[0]
                def t_l(t, pi=pi, wmat=wmat, par=par, dsym=dsym, lpar=lpar):
                    if pi == 0:
                        t.wait_ge(b.dsy.h, dsym)
                    for oc in range(NDC):
                        s = ra.acquire(t)
                        mm = mmgroup(t, ps_a[:, s * 512:(s + 1) * 512],
                                     [(wmat[:, c, oc * 128:(oc + 1) * 128], xnc[:, par, c, :])
                                      for c in range(NDC)])
                        b.pe.inc(mm)
                        pem = b.pe.n
                        def evl(v, oc=oc, s=s, pem=pem, lpar=lpar):
                            v.wait_ge(b.pe.h, pem)
                            b.dv.inc(v.tensor_copy(lst[:, lpar, oc, :], ps_a[:, s * 512:(s + 1) * 512]))
                        b.V(evl)
                        ra.done(s, b.dv)
                b.T(t_l)
                dvm = b.dv.n
                def st_l(sy, i=i, pi=pi, lpar=lpar, dvm=dvm):
                    sy.wait_ge(b.dv.h, dvm)
                    b.dsy.dma(sy.dma_start(out=lat_r[:, pi * 8:(pi + 1) * 8, i * TC:(i + 1) * TC],
                                           in_=lst[:, lpar]))
                b.SY(st_l)
                lst_rot.done(lpar, b.dsy)
            xn_rot.done(par, b.pe)
        b.barrier()

    with ExitStack() as hes:
        sb = lambda n, sh, dt: hes.enter_context(nc.sbuf_tensor(n, sh, dt))
        psn = lambda n, sh: hes.enter_context(nc.psum_tensor(n, sh, F32))
        upq = sb("l2b_upq", [64, 96], BF16)
        upk = sb("l2b_upk", [64, 96], BF16)
        upv = sb("l2b_upv", [64, 64], BF16)
        cosq = sb("l2b_cosq", [32, S], F32)
        sinq = sb("l2b_sinq", [32, S], F32)
        cosk = sb("l2b_cosk", [32, S], F32)
        sink = sb("l2b_sink", [32, S], F32)
        msk = sb("l2b_msk", [128, 4, TC], BF16)
        qlat = sb("l2b_qlat", [64, S], BF16)
        klat = sb("l2b_klat", [64, S], BF16)
        vlat = sb("l2b_vlat", [64, S], BF16)
        q_aug = sb("l2b_qaug", [128, S], BF16)
        k_aug = sb("l2b_kaug", [128, S], BF16)
        vsb = sb("l2b_vsb", [64, S], BF16)
        vaug = sb("l2b_vaug", [128, NTILE, 65], BF16)
        qsq = sb("l2b_qsq", [64, S], F32)
        rsb = sb("l2b_rsb", [1, 2 * S], F32)
        rtmp = sb("l2b_rt", [32, 2, TC], F32)
        psb = sb("l2b_psb", [128, 4, TC], BF16)
        ytmp = sb("l2b_yt", [64, TC], F32)
        dtmp = sb("l2b_dt", [64, TC], F32)
        dtmp2 = sb("l2b_dt2", [64, TC], F32)
        yout = sb("l2b_yo", [64, 2, TC], BF16)
        ps_s = psn("l2b_pss", [128, 1536])
        ps_vt = hes.enter_context(nc.psum_tensor("l2b_psvt", [128, 2, 1024], BF16))
        ps_o = psn("l2b_pso", [65, 512])
        ps_u = ps_s
        ps_bc = psn("l2b_psbc", [64, 512])

        def ldc(g):
            b.dgp.dma(g.dma_start(out=upq[:, :], in_=iaps["w_l2_upq"][:, :]))
            b.dgp.dma(g.dma_start(out=upk[:, :], in_=iaps["w_l2_upk"][:, :]))
            b.dgp.dma(g.dma_start(out=upv[:, :], in_=iaps["w_l2_upv"][:, :]))
            b.dgp.dma(g.dma_start(out=msk[:, :, :], in_=iaps["c_l2mask"][:, :, :]))
        b.G(ldc)
        def ldc2(sy):
            b.dsy.dma(sy.dma_start(out=cosq[:, :], in_=iaps["c_l2cosq"][:, :]))
            b.dsy.dma(sy.dma_start(out=sinq[:, :], in_=iaps["c_l2sinq"][:, :]))
            b.dsy.dma(sy.dma_start(out=cosk[:, :], in_=iaps["c_l2cosk"][:, :]))
            b.dsy.dma(sy.dma_start(out=sink[:, :], in_=iaps["c_l2sink"][:, :]))
        b.SY(ldc2)
        b.barrier()
        def v_rot(v):
            v.tensor_scalar_mul(upq[:, 64:96:2], upq[:, 33:64:2], -1.0)
            v.tensor_copy(upq[:, 65:96:2], upq[:, 32:64:2])
            v.tensor_scalar_mul(upk[:, 64:96:2], upk[:, 33:64:2], -1.0)
            b.dv.inc(v.tensor_copy(upk[:, 65:96:2], upk[:, 32:64:2]))
        b.V(v_rot)
        b.barrier()

        lat_rot = Rot(1)
        rs = Rot(3)
        rp = Rot(4)
        ru = rs
        yrot = Rot(2)
        for h in range(16):
            def ldl(sy, h=h):
                lat_rot.acquire(sy)
                b.dsy.dma(sy.dma_start(out=qlat[:, :], in_=lat_rows[h * 64:(h + 1) * 64, :]))
                b.dsy.dma(sy.dma_start(out=klat[:, :], in_=lat_rows[1024 + h * 64:1024 + (h + 1) * 64, :]))
                b.dsy.dma(sy.dma_start(out=vlat[:, :], in_=lat_rows[2048 + h * 64:2048 + (h + 1) * 64, :]))
            b.SY(ldl)
            dsym = b.dsy.n

            for pi in range(8):
                sl = slice(pi * TC, (pi + 1) * TC)
                us_ = [None]
                def t_up(t, pi=pi, dsym=dsym, us_=us_):
                    if pi == 0:
                        t.wait_ge(b.dsy.h, dsym)
                    us_[0] = ru.acquire(t)
                    mm = t.matmul(ps_u[0:96, us_[0] * 512:(us_[0] + 1) * 512],
                                  upq[:, :], qlat[:, pi * TC:(pi + 1) * TC],
                                  start=True, stop=True)
                    b.pe.inc(mm)
                b.T(t_up)
                us_ = us_[0]
                pem = b.pe.n
                def ev_q(sc, pi=pi, pem=pem, sl=sl, us_=us_):
                    sc.wait_ge(b.pe.h, pem)
                    b.ac.inc(sc.activation(q_aug[0:32, sl], ps_u[0:32, us_ * 512:(us_ + 1) * 512],
                                           AF.Copy, scale=0.125))
                b.SC(ev_q)
                acm = b.ac.n
                def ev_qe(v, pi=pi, sl=sl, acm=acm, us_=us_):
                    v.wait_ge(b.ac.h, acm)
                    v.tensor_tensor(rtmp[:, 0, :], ps_u[32:64, us_ * 512:(us_ + 1) * 512], cosq[:, sl], ALU.mult)
                    v.tensor_tensor(rtmp[:, 1, :], ps_u[64:96, us_ * 512:(us_ + 1) * 512], sinq[:, sl], ALU.mult)
                    b.dv.inc(v.tensor_add(q_aug[32:64, sl], rtmp[:, 0, :], rtmp[:, 1, :]))
                b.V(ev_qe)
                ru.done(us_, b.dv)
                usk = [None]
                def t_upk(t, pi=pi, usk=usk):
                    usk[0] = ru.acquire(t)
                    mm = t.matmul(ps_u[0:96, usk[0] * 512:(usk[0] + 1) * 512],
                                  upk[:, :], klat[:, pi * TC:(pi + 1) * TC],
                                  start=True, stop=True)
                    b.pe.inc(mm)
                b.T(t_upk)
                usk = usk[0]
                pem = b.pe.n
                def ev_k(sc, pi=pi, pem=pem, sl=sl, usk=usk):
                    sc.wait_ge(b.pe.h, pem)
                    b.ac.inc(sc.copy(k_aug[0:32, sl], ps_u[0:32, usk * 512:(usk + 1) * 512]))
                b.SC(ev_k)
                acm = b.ac.n
                def ev_ke(v, pi=pi, sl=sl, acm=acm, usk=usk):
                    v.wait_ge(b.ac.h, acm)
                    v.tensor_tensor(rtmp[:, 0, :], ps_u[32:64, usk * 512:(usk + 1) * 512], cosk[:, sl], ALU.mult)
                    v.tensor_tensor(rtmp[:, 1, :], ps_u[64:96, usk * 512:(usk + 1) * 512], sink[:, sl], ALU.mult)
                    b.dv.inc(v.tensor_add(k_aug[32:64, sl], rtmp[:, 0, :], rtmp[:, 1, :]))
                b.V(ev_ke)
                ru.done(usk, b.dv)

            rr_rot = Rot(1)
            for which, src_aug, off in (("q", q_aug, 0), ("k", k_aug, S)):
                dvm0 = b.dv.n
                pem0 = b.pe.n
                def s_sq2(sc, src_aug=src_aug, dvm0=dvm0, pem0=pem0):
                    sc.wait_ge(b.dv.h, dvm0)
                    sc.wait_ge(b.pe.h, pem0)
                    b.ac.inc(sc.activation(qsq[:, :], src_aug[0:64, :], AF.Square))
                b.SC(s_sq2)
                acm = b.ac.n
                for pi in range(8):
                    def t_r(t, pi=pi, acm=acm):
                        if pi == 0:
                            t.wait_ge(b.ac.h, acm)
                        rr_rot.acquire(t)
                        mm = t.matmul(ps_bc[0:1, :], ones[0:64, 0:1], qsq[:, pi * TC:(pi + 1) * TC],
                                      start=True, stop=True)
                        b.pe.inc(mm)
                    b.T(t_r)
                    pem = b.pe.n
                    def ev_r(v, pi=pi, pem=pem, off=off):
                        v.wait_ge(b.pe.h, pem)
                        b.dv.inc(v.tensor_copy(rsb[0:1, off + pi * TC:off + (pi + 1) * TC], ps_bc[0:1, :]))
                    b.V(ev_r)
                    rr_rot.done(0, b.dv)
            dvm = b.dv.n
            def v_km(v, dvm=dvm):
                b.dv.inc(v.tensor_reduce(rows[0:1, 0:1], rsb[0:1, S:2 * S], AX, ALU.max))
            b.V(v_km)
            bounce_copy(b, [(rows[0:1, 4:5], rows[0:1, 0:1])])
            dvm = b.dv.n
            def s_nb(sc, dvm=dvm):
                sc.wait_ge(b.dv.h, dvm)
                sc.activation(rows[0:1, 8:9], rows[0:1, 4:5], AF.Sqrt)
                b.ac.inc(sc.activation(rsb[0:1, 0:S], rsb[0:1, 0:S], AF.Sqrt))
            b.SC(s_nb)
            acm = b.ac.n
            def v_nb(v, acm=acm):
                v.wait_ge(b.ac.h, acm)
                b.dv.inc(v.tensor_scalar(q_aug[64:65, :], rsb[0:1, 0:S],
                                         rows[0:1, 8:9], -1.01, ALU.mult, ALU.mult))
            b.V(v_nb)
            def v_ones(v):
                b.dv.inc(v.memset(k_aug[64:65, :], 1.0))
            b.V(v_ones)

            for pi in range(8):
                uv_ = [None]
                def t_uv(t, pi=pi, uv_=uv_):
                    uv_[0] = ru.acquire(t)
                    mm = t.matmul(ps_u[0:64, uv_[0] * 512:(uv_[0] + 1) * 512],
                                  upv[:, :], vlat[:, pi * TC:(pi + 1) * TC],
                                  start=True, stop=True)
                    b.pe.inc(mm)
                b.T(t_uv)
                uv_ = uv_[0]
                pem = b.pe.n
                def ev_uv(v, pi=pi, pem=pem, uv_=uv_):
                    v.wait_ge(b.pe.h, pem)
                    b.dv.inc(v.tensor_copy(vsb[:, pi * TC:(pi + 1) * TC],
                                           ps_u[0:64, uv_ * 512:(uv_ + 1) * 512]))
                b.V(ev_uv)
                ru.done(uv_, b.dv)

            def v_vz(v):
                b.dv.inc(v.memset(vaug[:, :, 64:65], 1.0))
            b.V(v_vz)
            dvm = b.dv.n
            rv = Rot(2)
            for kt in range(NTILE):
                def t_v(t, kt=kt, dvm=dvm):
                    if kt == 0:
                        t.wait_ge(b.dv.h, dvm)
                    s = rv.acquire(t)
                    mm = t.transpose(ps_vt[:, s, 0:64],
                                     vsb[:, kt * 128:(kt + 1) * 128], identb[0:64, 0:64])
                    b.pe.inc(mm)
                    pem = b.pe.n
                    def ev_v(v, kt=kt, s=s, pem=pem):
                        v.wait_ge(b.pe.h, pem)
                        b.dv.inc(v.tensor_copy(vaug[:, kt, 0:64], ps_vt[:, s, 0:64]))
                    b.V(ev_v)
                    rv.done(s, b.dv)
                b.T(t_v)

            if DEBUG_DUMP and h == 0:
                dbg_qa = nc.dram_tensor("dbg_qa", [128, S], BF16, kind="ExternalOutput").ap()
                dbg_ka = nc.dram_tensor("dbg_ka", [128, S], BF16, kind="ExternalOutput").ap()
                dbg_va = nc.dram_tensor("dbg_va", [128, NTILE * 65], BF16, kind="ExternalOutput").ap()
                dvq = b.dv.n
                def dmpq(sy, dvq=dvq):
                    sy.wait_ge(b.dv.h, dvq)
                    b.dsy.dma(sy.dma_start(out=dbg_qa[:, :], in_=q_aug[:, :]))
                    b.dsy.dma(sy.dma_start(out=dbg_ka[:, :], in_=k_aug[:, :]))
                    b.dsy.dma(sy.dma_start(out=dbg_va.rearrange("p (a c) -> p a c", c=65), in_=vaug[:, :, :]))
                b.SY(dmpq)
            dvm_all = b.dv.n
            po_rot = globals().get("_l2_po_rot")
            if po_rot is None or h == 0:
                po_rot = Rot(1)
                globals()["_l2_po_rot"] = po_rot
            for qp in range(8):
                nkt = 4 * qp + 4
                pend = []
                def emit_av(t, info, nkt=nkt):
                    kt, sl2, acm_, dvm_ = info
                    t.wait_ge(b.ac.h, acm_)
                    if dvm_:
                        t.wait_ge(b.dv.h, dvm_)
                    if kt == 0:
                        po_rot.acquire(t)
                    mm = t.matmul(ps_o[:, :], vaug[:, kt, :], psb[:, sl2, :],
                                  start=(kt == 0), stop=(kt == nkt - 1))
                    b.pe.inc(mm)
                for kt in range(nkt):
                    sref = [None]
                    def t_sw(t, sref=sref, qp=qp, kt=kt, dvm_all=dvm_all):
                        if qp == 0 and kt == 0:
                            t.wait_ge(b.dv.h, dvm_all)
                        s = rs.acquire(t)
                        sref[0] = s
                        mm = t.matmul(ps_s[:, s * 512:(s + 1) * 512],
                                      k_aug[0:65, kt * 128:(kt + 1) * 128],
                                      q_aug[0:65, qp * TC:(qp + 1) * TC],
                                      start=True, stop=True)
                        b.pe.inc(mm)
                        if len(pend) >= 3:
                            emit_av(t, pend.pop(0))
                    b.T(t_sw)
                    s = sref[0]
                    pem = b.pe.n
                    sl2ref = [None]
                    def s_e(sc, s=s, pem=pem, sl2ref=sl2ref):
                        sc.wait_ge(b.pe.h, pem)
                        sl2 = rp.acquire(sc)
                        sl2ref[0] = sl2
                        b.ac.inc(sc.activation(psb[:, sl2, :], ps_s[:, s * 512:(s + 1) * 512], AF.Exp))
                    b.SC(s_e)
                    sl2 = sl2ref[0]
                    rs.done(s, b.ac)
                    acm_ = b.ac.n
                    dvm_ = 0
                    r = kt - 4 * qp
                    if r >= 0:
                        def v_m(v, sl2=sl2, r=r, acm_=acm_):
                            v.wait_ge(b.ac.h, acm_)
                            b.dv.inc(v.tensor_mul(psb[:, sl2, :], psb[:, sl2, :], msk[:, r, :]))
                        b.V(v_m)
                        dvm_ = b.dv.n
                    pend.append((kt, sl2, acm_, dvm_))
                    rp.done(sl2, b.pe)
                def t_fl(t):
                    while pend:
                        emit_av(t, pend.pop(0))
                b.T(t_fl)

                pem = b.pe.n
                def s_den(sc, pem=pem):
                    sc.wait_ge(b.pe.h, pem)
                    sc.copy(rows[0:1, 16:528], ps_o[64:65, :])
                    b.ac.inc(sc.copy(ytmp[:, :], ps_o[0:64, :]))
                b.SC(s_den)
                acm = b.ac.n
                def t_bc(t, acm=acm):
                    t.wait_ge(b.ac.h, acm)
                    mm = t.matmul(ps_bc[:, :], ones[0:1, 0:64], rows[0:1, 16:528],
                                  start=True, stop=True)
                    b.pe.inc(mm)
                b.T(t_bc)
                pem = b.pe.n
                def v_rc(v, pem=pem):
                    v.wait_ge(b.pe.h, pem)
                    b.dv.inc(v.reciprocal(dtmp[:, :], ps_bc[:, :]))
                b.V(v_rc)
                dvm = b.dv.n
                def s_rb(sc, dvm=dvm):
                    sc.wait_ge(b.dv.h, dvm)
                    b.ac.inc(sc.copy(dtmp2[:, :], dtmp[:, :]))
                b.SC(s_rb)
                acm = b.ac.n
                ypar = [None]
                def v_y(v, acm=acm, ypar=ypar):
                    v.wait_ge(b.ac.h, acm)
                    ypar[0] = yrot.acquire(v)
                    b.dv.inc(v.tensor_mul(yout[:, ypar[0], :], ytmp[:, :], dtmp2[:, :]))
                b.V(v_y)
                po_rot.done(0, b.dv)
                ypar = ypar[0]
                dvm = b.dv.n
                def st_y(sy, h=h, qp=qp, ypar=ypar, dvm=dvm):
                    sy.wait_ge(b.dv.h, dvm)
                    b.dsy.dma(sy.dma_start(out=y_hbm[h * 64:(h + 1) * 64, qp * TC:(qp + 1) * TC],
                                           in_=yout[:, ypar, :]))
                b.SY(st_y)
                yrot.done(ypar, b.dsy)
            lat_rot.done(0, b.pe)
        b.barrier()

    with ExitStack() as ces:
        sb = lambda n, sh, dt: ces.enter_context(nc.sbuf_tensor(n, sh, dt))
        wc = sb("l2c_wc", [128, NDC, D], BF16)
        ych = sb("l2c_y", [128, 2, NDC, TC], BF16)
        ssb = sb("l2c_s", [128, 2, NDC, TC], BF16)
        ps_a = ces.enter_context(nc.psum_tensor("l2c_ps", [128, 1024], F32))
        def ldw(g):
            b.dgp.dma(g.dma_start(out=wc[:, :, :], in_=iaps["w_l2_Wc"].rearrange("(c p) o -> p c o", p=128)))
        b.G(ldw)
        b.barrier()
        y_rot = Rot(2)
        s_rot = Rot(2)
        ra = Rot(2)
        for i in range(NTC):
            par = [None]
            def ld(sy, i=i, par=par):
                par[0] = y_rot.acquire(sy)
                b.dsy.dma(sy.dma_start(out=ych[:, par[0]], in_=y_r[:, :, i * TC:(i + 1) * TC]))
            b.SY(ld)
            par = par[0]
            dsym = b.dsy.n
            spar = [None]
            def picks(v, spar=spar):
                spar[0] = s_rot.acquire(v)
            b.V(picks)
            spar = spar[0]
            def t_c(t, par=par, dsym=dsym, spar=spar):
                t.wait_ge(b.dsy.h, dsym)
                for oc in range(NDC):
                    s = ra.acquire(t)
                    mm = mmgroup(t, ps_a[:, s * 512:(s + 1) * 512],
                                 [(wc[:, c, oc * 128:(oc + 1) * 128], ych[:, par, c, :])
                                  for c in range(NDC)])
                    b.pe.inc(mm)
                    pem = b.pe.n
                    def evc(v, oc=oc, s=s, pem=pem, spar=spar):
                        v.wait_ge(b.pe.h, pem)
                        b.dv.inc(v.tensor_copy(ssb[:, spar, oc, :], ps_a[:, s * 512:(s + 1) * 512]))
                    b.V(evc)
                    ra.done(s, b.dv)
            b.T(t_c)
            y_rot.done(par, b.pe)
            dvm = b.dv.n
            def st_s(sy, i=i, spar=spar, dvm=dvm):
                sy.wait_ge(b.dv.h, dvm)
                b.dsy.dma(sy.dma_start(out=s_r[:, :, i * TC:(i + 1) * TC], in_=ssb[:, spar]))
            b.SY(st_s)
            s_rot.done(spar, b.dsy)
        b.barrier()

    g_acc = bes.enter_context(nc.sbuf_tensor("l2_gacc", [128, NDC, S], BF16))
    onesb = b.onesb
    FH = 2048
    NFC = FH // 128
    h_guard = [None]
    with ExitStack() as fes:
        sb = lambda n, sh, dt: fes.enter_context(nc.sbuf_tensor(n, sh, dt))
        psn = lambda n, sh: fes.enter_context(nc.psum_tensor(n, sh, F32))
        grow = sb("l2f_grow", [1, S], BF16)
        gb = sb("l2f_gb", [128, TC], F32)
        gt1 = sb("l2f_gt1", [128, TC], F32)
        st1 = sb("l2f_st1", [128, 2, TC], F32)
        ps_h = psn("l2f_psh", [128, 1024])
        ps_p = psn("l2f_psp", [128, 1024])
        ps_g = psn("l2f_psg", [128, 1024])
        ps_k = psn("l2f_psk", [128, 512])

        w1_rot = Rot(1)
        w2_rot = Rot(1)
        xn_rot = Rot(2)
        hst_rot = Rot(2)
        rh = Rot(2)
        rp2 = Rot(2)
        st1_rot = Rot(2)
        rg = Rot(2)
        kg = Rot(1)
        grow_rot = Rot(1)
        for e in range(E):
            p1es = ExitStack()
            wg1 = p1es.enter_context(nc.sbuf_tensor(f"l2f_wg_{e}", [128, NDC, FH], BF16))
            wu1 = p1es.enter_context(nc.sbuf_tensor(f"l2f_wu_{e}", [128, NDC, FH], BF16))
            xnc = p1es.enter_context(nc.sbuf_tensor(f"l2f_xnc_{e}", [128, 2, NDC, TC], BF16))
            hst = p1es.enter_context(nc.sbuf_tensor(f"l2f_hst_{e}", [128, 2, 8, TC], BF16))
            hg0 = h_guard[0]
            dvg0 = b.dv.n
            def ldw1(g, e=e, hg0=hg0, dvg0=dvg0):
                if hg0 is not None:
                    g.wait_ge(hg0[0].h, hg0[1])
                    g.wait_ge(b.dv.h, dvg0)
                w1_rot.acquire(g)
                b.dgp.dma(g.dma_start(out=wg1[:, :, :],
                                      in_=iaps["w_l2_Wg"][e].rearrange("(c p) f -> p c f", p=128)))
                b.dgp.dma(g.dma_start(out=wu1[:, :, :],
                                      in_=iaps["w_l2_Wu"][e].rearrange("(c p) f -> p c f", p=128)))
            b.G(ldw1)
            dgm = b.dgp.n
            for i in range(NTC):
                xpar = [None]
                def ldx(sy, i=i, xpar=xpar, hg0=hg0):
                    if i == 0 and hg0 is not None:
                        sy.wait_ge(hg0[0].h, hg0[1])
                    xpar[0] = xn_rot.acquire(sy)
                    b.dsy.dma(sy.dma_start(out=xnc[:, xpar[0]], in_=xn_r[:, :, i * TC:(i + 1) * TC]))
                b.SY(ldx)
                xpar = xpar[0]
                dsym = b.dsy.n
                for hg in range(2):
                    hpar = [None]
                    def pickh(sc, hpar=hpar):
                        hpar[0] = hst_rot.acquire(sc)
                    b.SC(pickh)
                    hpar = hpar[0]
                    def t_h(t, e=e, i=i, hg=hg, xpar=xpar, dgm=dgm, dsym=dsym, hpar=hpar):
                        if hg == 0:
                            t.wait_ge(b.dgp.h, dgm)
                            t.wait_ge(b.dsy.h, dsym)
                        if h_guard[0] is not None and hg == 0 and i == 0:
                            t.wait_ge(h_guard[0][0].h, h_guard[0][1])
                        for hh in range(8):
                            hc = hg * 8 + hh
                            s = rh.acquire(t)
                            mm = mmgroup(t, ps_h[:, s * 512:(s + 1) * 512],
                                         [(wg1[:, c, hc * 128:(hc + 1) * 128], xnc[:, xpar, c, :])
                                          for c in range(NDC)])
                            b.pe.inc(mm)
                            s2 = rp2.acquire(t)
                            mm = mmgroup(t, ps_p[:, s2 * 512:(s2 + 1) * 512],
                                         [(wu1[:, c, hc * 128:(hc + 1) * 128], xnc[:, xpar, c, :])
                                          for c in range(NDC)])
                            b.pe.inc(mm)
                            pem = b.pe.n
                            s1ref = [None]
                            def evh(sc, hh=hh, s=s, pem=pem, s1ref=s1ref):
                                sc.wait_ge(b.pe.h, pem)
                                s1s = st1_rot.acquire(sc)
                                s1ref[0] = s1s
                                b.ac.inc(sc.activation(st1[:, s1s, :], ps_h[:, s * 512:(s + 1) * 512], AF.Silu))
                            b.SC(evh)
                            s1s = s1ref[0]
                            rh.done(s, b.ac)
                            acm = b.ac.n
                            def evh2(v, hh=hh, s2=s2, acm=acm, hpar=hpar, s1s=s1s):
                                v.wait_ge(b.ac.h, acm)
                                b.dv.inc(v.tensor_tensor(hst[:, hpar, hh, :], st1[:, s1s, :],
                                                         ps_p[:, s2 * 512:(s2 + 1) * 512], ALU.mult))
                            b.V(evh2)
                            rp2.done(s2, b.dv)
                            st1_rot.done(s1s, b.dv)
                    b.T(t_h)
                    dvm = b.dv.n
                    def st_h(sy, i=i, hg=hg, hpar=hpar, dvm=dvm):
                        sy.wait_ge(b.dv.h, dvm)
                        b.dsy.dma(sy.dma_start(
                            out=lat_r[:, hg * 8:(hg + 1) * 8, i * TC:(i + 1) * TC],
                            in_=hst[:, hpar]))
                    b.SY(st_h)
                    hst_rot.done(hpar, b.dsy)
                xn_rot.done(xpar, b.pe)
            p1_store_mark = b.dsy.n
            w1_rot.done(0, b.pe)
            p1_pe_mark = b.pe.n
            p1_dv_mark = b.dv.n
            p1es.close()

            p2es = ExitStack()
            w2 = p2es.enter_context(nc.sbuf_tensor(f"l2f_wd_{e}", [128, NFC, D], BF16))
            hld = p2es.enter_context(nc.sbuf_tensor(f"l2f_hld_{e}", [128, 2, NFC, TC], BF16))
            hld_rot = Rot(2)
            def ldw2(g, e=e, p1_pe_mark=p1_pe_mark, p1_dv_mark=p1_dv_mark):
                g.wait_ge(b.pe.h, p1_pe_mark)
                g.wait_ge(b.dv.h, p1_dv_mark)
                g.wait_ge(b.dsy.h, p1_store_mark)
                w2_rot.acquire(g)
                b.dgp.dma(g.dma_start(out=w2[:, :, :],
                                      in_=iaps["w_l2_Wd"][e].rearrange("(c p) d -> p c d", p=128)))
            b.G(ldw2)
            dgm2 = b.dgp.n
            def ldgr(sc, e=e):
                grow_rot.acquire(sc)
                b.dsc.dma(sc.dma_start(out=grow[:, :], in_=gates_bf[e:e + 1, :]))
            b.SC(ldgr)
            grm = b.dsc.n
            for i in range(NTC):
                hpar = [None]
                def ldh(sy, i=i, hpar=hpar, p1_store_mark=p1_store_mark,
                        p1_pe_mark=p1_pe_mark, p1_dv_mark=p1_dv_mark):
                    sy.wait_ge(b.dsy.h, p1_store_mark)
                    if i == 0:
                        sy.wait_ge(b.pe.h, p1_pe_mark)
                        sy.wait_ge(b.dv.h, p1_dv_mark)
                    hpar[0] = hld_rot.acquire(sy)
                    b.dsy.dma(sy.dma_start(out=hld[:, hpar[0]],
                                           in_=lat_r[:, 0:NFC, i * TC:(i + 1) * TC]))
                b.SY(ldh)
                hpar = hpar[0]
                dsym = b.dsy.n
                def t_k(t, e=e, i=i, grm=grm):
                    t.wait_ge(b.dsc.h, grm)
                    kg.acquire(t)
                    mm = t.matmul(ps_k[:, 0:512], onesb[0:1, 0:128],
                                  grow[0:1, i * TC:(i + 1) * TC], start=True, stop=True)
                    b.pe.inc(mm)
                b.T(t_k)
                pem = b.pe.n
                def ev_k(v, pem=pem):
                    v.wait_ge(b.pe.h, pem)
                    b.dv.inc(v.tensor_copy(gb[:, :], ps_k[:, 0:512]))
                b.V(ev_k)
                kg.done(0, b.dv)
                gbm = b.dv.n
                def t_d(t, e=e, i=i, hpar=hpar, dgm2=dgm2, dsym=dsym):
                    t.wait_ge(b.dgp.h, dgm2)
                    t.wait_ge(b.dsy.h, dsym)
                    for dc in range(NDC):
                        s = rg.acquire(t)
                        mm = mmgroup(t, ps_g[:, s * 512:(s + 1) * 512],
                                     [(w2[:, fc, dc * 128:(dc + 1) * 128], hld[:, hpar, fc, :])
                                      for fc in range(NFC)])
                        b.pe.inc(mm)
                        pem = b.pe.n
                        def evg(v, dc=dc, s=s, pem=pem, e=e, i=i, gbm=gbm):
                            v.wait_ge(b.pe.h, pem)
                            v.wait_ge(b.dv.h, gbm)
                            if e == 0:
                                b.dv.inc(v.tensor_tensor(g_acc[:, dc, i * TC:(i + 1) * TC],
                                                         ps_g[:, s * 512:(s + 1) * 512],
                                                         gb[:, :], ALU.mult))
                            else:
                                v.tensor_tensor(gt1[:, :], ps_g[:, s * 512:(s + 1) * 512],
                                                gb[:, :], ALU.mult)
                                b.dv.inc(v.tensor_tensor(g_acc[:, dc, i * TC:(i + 1) * TC],
                                                         gt1[:, :],
                                                         g_acc[:, dc, i * TC:(i + 1) * TC],
                                                         ALU.add))
                        b.V(evg)
                        rg.done(s, b.dv)
                b.T(t_d)
                hld_rot.done(hpar, b.pe)
            w2_rot.done(0, b.pe)
            h_guard[0] = (b.pe, b.pe.n)
            p2es.close()
        b.barrier()
    emit_moe_join(b, x_src, x_dst, s_hbm, g_acc)
    bes.close()



def _transpose_pass(b, nc, src_kind, i, pb, tin, tout, ident, src_ap, dst_ap,
                    in_rot, out_rot, ps_rot):
    par = [None]
    def ld(sy):
        par[0] = in_rot.acquire(sy)
        b.dsy.dma(sy.dma_start(out=tin[:, par[0]] if src_kind == "tm" else
                               tin[:, par[0]].rearrange("p (c q) -> p c q", q=128),
                               in_=src_ap))
    b.SY(ld)
    par = par[0]
    dsym = b.dsy.n
    oslot = [None]
    def pickv(v):
        oslot[0] = out_rot.acquire(v)
    b.V(pickv)
    def tp(t):
        t.wait_ge(b.dsy.h, dsym)
        for c in range(NDC):
            s = ps_rot.acquire(t)
            mm = t.transpose(pb[:, s * 512:s * 512 + 128], tin[:, par, c * 128:(c + 1) * 128], ident[:, :])
            b.pe.inc(mm)
            pem = b.pe.n
            def ev(v, c=c, s=s, pem=pem):
                v.wait_ge(b.pe.h, pem)
                b.dv.inc(v.tensor_copy(tout[:, oslot[0], c * 128:(c + 1) * 128],
                                       pb[:, s * 512:s * 512 + 128]))
            b.V(ev)
            ps_rot.done(s, b.dv)
    b.T(tp)
    in_rot.done(par, b.pe)
    oslot = oslot[0]
    dvm = b.dv.n
    def st(sy):
        sy.wait_ge(b.dv.h, dvm)
        b.dsy.dma(sy.dma_start(out=dst_ap,
                               in_=tout[:, oslot].rearrange("p (c q) -> p c q", q=128)
                               if src_kind == "tm" else tout[:, oslot]))
    b.SY(st)
    out_rot.done(oslot, b.dsy)


def build_program(host_arrays, blocks=(0, 1, 2, 3)):
    nc = bass.Bass("TRN2", target_bir_lowering=False, debug=False)
    import itertools
    _cnt = itertools.count()
    _orig_sb = nc.sbuf_tensor
    _orig_ps = nc.psum_tensor
    def _sb(name, shape, dtype, **kw):
        return _orig_sb(f"{name}_{next(_cnt)}", shape, dtype, **kw)
    def _ps(name, shape, dtype, **kw):
        return _orig_ps(f"{name}_{next(_cnt)}", shape, dtype, **kw)
    nc.sbuf_tensor = _sb
    nc.psum_tensor = _ps
    for _eng in (nc.tensor, nc.vector, nc.scalar, nc.gpsimd, nc.sync):
        _cache = {}
        _orig_wg = _eng.wait_ge
        def _mk(orig, cache):
            def wg(sem, val, **kw):
                key = getattr(sem, "num", None) or id(sem)
                if cache.get(key, -1) >= val:
                    return None
                cache[key] = val
                return orig(sem, val, **kw)
            return wg
        _eng.wait_ge = _mk(_orig_wg, _cache)

    iaps = {}
    for name, arr in host_arrays.items():
        iaps[name] = nc.dram_tensor(name, list(arr.shape), F32, kind="ExternalInput").ap()
    x_in = nc.dram_tensor("x", [S, D], F32, kind="ExternalInput").ap()
    pfx = nc.dram_tensor("pfx", [8, 1], F32, kind="ExternalInput").ap()
    out = nc.dram_tensor("out", [S, D], F32, kind="ExternalOutput").ap()

    xfa = nc.dram_tensor("xfa", [D, S], F32).ap()
    xfb = nc.dram_tensor("xfb", [D, S], F32).ap()
    xn_hbm = nc.dram_tensor("xn_hbm", [D, S], BF16).ap()
    s_hbm = nc.dram_tensor("s_hbm", [D, S], BF16).ap()
    y_hbm = nc.dram_tensor("y_hbm", [D, S], BF16).ap()
    h_hbm = nc.dram_tensor("h_hbm", [4096, S], BF16).ap()
    cc_in = nc.dram_tensor("cc_in", [1, 16], F32).ap()
    cc_out = nc.dram_tensor("cc_out", [8, 16], F32).ap()

    es = ExitStack()
    with es:
        block = es.enter_context(nc.Block())
        b = Bld(nc, es, block)

        ident = b.sb("ident", [128, 128], F32)
        identb = b.sb("identb", [128, 128], BF16)
        onesb = b.sb("onesb", [1, 128], BF16)
        onescb = b.sb("onescb", [128, 1], BF16)
        ones = b.sb("ones", [128, 512], F32)
        tri = b.sb("tri", [128, 128], F32)
        rows = b.sb("rows", [1, 2048], F32)
        pfx_sb = b.sb("pfx_sb", [8, 1], F32)
        cpool = b.sb("cpool", [128, len(CONST_POOL)], F32)
        for j, val in enumerate(CONST_POOL):
            nc.const_aps.register(F32, float(val), cpool[:, j:j + 1])

        def ld_const(sy):
            b.dsy.dma(sy.dma_start(out=ident[:, :], in_=iaps["c_ident"][:, :]))
            b.dsy.dma(sy.dma_start(out=ones[:, :], in_=iaps["c_ones"][:, :]))
            b.dsy.dma(sy.dma_start(out=tri[:, :], in_=iaps["c_tri"][:, :]))
            b.dsy.dma(sy.dma_start(out=pfx_sb[:, :], in_=pfx[:, :]))
            b.dsy.dma(sy.dma_start(out=cpool[:, :], in_=iaps["c_pool"][:, :]))
        b.SY(ld_const)
        def ld_constg(g):
            b.dgp.dma(g.dma_start(out=identb[:, :], in_=iaps["c_ident"][:, :]))
            b.dgp.dma(g.dma_start(out=onesb[:, :], in_=iaps["c_ones"][0:1, 0:128]))
            b.dgp.dma(g.dma_start(out=onescb[:, :], in_=iaps["c_onescol"][:, :]))
        b.G(ld_constg)
        b.barrier()

        with ExitStack() as pes:
            pb = pes.enter_context(nc.psum_tensor("ppro", [128, 2048], F32))
            tin = pes.enter_context(nc.sbuf_tensor("p_in", [128, 2, D], F32))
            tout = pes.enter_context(nc.sbuf_tensor("p_out", [128, 2, D], F32))
            in_rot, out_rot, ps_rot = Rot(2), Rot(2), Rot(4)
            xfa_r = xfa.rearrange("(c p) t -> p c t", p=128)
            for i in range(NTILE):
                _transpose_pass(b, nc, "tm", i, pb, tin, tout, ident,
                                x_in[i * 128:(i + 1) * 128, :],
                                xfa_r[:, :, i * 128:(i + 1) * 128],
                                in_rot, out_rot, ps_rot)
            b.barrier()

        b.onesb = onesb
        b.onescb = onescb
        x_src, x_dst = xfa, xfb
        for li in blocks:
            if li == 3:
                emit_block3(b, iaps, x_src, x_dst, xn_hbm, s_hbm, ident, identb, ones, rows)
            elif li == 1:
                emit_block1(b, iaps, x_src, x_dst, xn_hbm, s_hbm, ident, identb, ones, rows)
            elif li == 0:
                emit_block0(b, iaps, x_src, x_dst, xn_hbm, s_hbm, h_hbm,
                            cc_in, cc_out, pfx_sb, ident, identb, ones, tri, rows)
            elif li == 2:
                emit_block2(b, iaps, x_src, x_dst, xn_hbm, s_hbm, h_hbm, y_hbm,
                            ident, identb, ones, rows)
            x_src, x_dst = x_dst, x_src
            b.barrier()

        if DEBUG_DUMP:
            dbg_xn = nc.dram_tensor("dbg_xn", [D, S], BF16, kind="ExternalOutput").ap()
            dbg_s = nc.dram_tensor("dbg_s", [D, S], BF16, kind="ExternalOutput").ap()
            dbg_y = nc.dram_tensor("dbg_y", [D, S], BF16, kind="ExternalOutput").ap()
            def dmp(sy):
                b.dsy.dma(sy.dma_start(out=dbg_xn[:, :], in_=xn_hbm[:, :]))
                b.dsy.dma(sy.dma_start(out=dbg_s[:, :], in_=s_hbm[:, :]))
                b.dsy.dma(sy.dma_start(out=dbg_y[:, :], in_=y_hbm[:, :]))
                sy.wait_ge(b.dsy.h, b.dsy.n)
            b.SY(dmp)

        with ExitStack() as pes:
            pb = pes.enter_context(nc.psum_tensor("pepi", [128, 2048], F32))
            tin = pes.enter_context(nc.sbuf_tensor("e_in", [128, 2, D], F32))
            tout = pes.enter_context(nc.sbuf_tensor("e_out", [128, 2, D], F32))
            in_rot, out_rot, ps_rot = Rot(2), Rot(2), Rot(4)
            xs_r = x_src.rearrange("(c p) t -> p c t", p=128)
            for i in range(NTILE):
                _transpose_pass(b, nc, "fm", i, pb, tin, tout, ident,
                                xs_r[:, :, i * 128:(i + 1) * 128],
                                out[i * 128:(i + 1) * 128, :],
                                in_rot, out_rot, ps_rot)
            def fin(sy):
                sy.wait_ge(b.dsy.h, b.dsy.n)
            b.SY(fin)
    return nc



def _run(inputs, blocks, x_override=None):
    host = host_prep(inputs)
    nc = build_program(host, blocks=blocks)
    x_full = np.asarray(x_override if x_override is not None else inputs["L0_attn_x"], dtype=np.float32)
    in_maps = []
    for c in range(8):
        m = dict(host)
        m["x"] = np.ascontiguousarray(x_full[c])
        p = np.zeros((8, 1), np.float32)
        p[:c] = 1.0
        m["pfx"] = p
        in_maps.append(m)
    res = run_bass_kernel_spmd(nc, in_maps, core_ids=list(range(8)))
    if DEBUG_DUMP:
        import ml_dtypes
        if "dbg_gb" in res.results[0]:
            np.save("/tmp/dbg_gb.npy", np.stack([np.asarray(r["dbg_gb"]) for r in res.results]))
        for key in ("dbg_den", "dbg_num"):
            if key in res.results[0]:
                np.save(f"/tmp/{key}.npy", np.stack([np.asarray(r[key]) for r in res.results]))
        for key in ("dbg_xn", "dbg_s", "dbg_g", "dbg_gt", "dbg_h", "dbg_w2", "dbg_hh", "dbg_y", "dbg_qa", "dbg_ka", "dbg_va", *[f"dbg_gi{i}" for i in range(8)]):
            if key in res.results[0]:
                np.save(f"/tmp/{key}.npy", np.stack(
                    [np.asarray(r[key]).view(ml_dtypes.bfloat16).astype(np.float32) for r in res.results]))
    return np.stack([np.asarray(r["out"]) for r in res.results])


def kernel(**inputs):
    return _run(inputs, blocks=(0, 1, 2, 3)).astype(np.float32)
